# version 02
# baseline (speedup 1.0000x reference)
import sys
sys.path.insert(0, "/opt/trn_rl_repo")

import numpy as np
import concourse.bass as bass
import concourse.mybir as mybir
from concourse.bass_utils import run_bass_kernel_spmd

F32 = mybir.dt.float32
BF16 = mybir.dt.bfloat16
I32 = mybir.dt.int32
AF = mybir.ActivationFunctionType
ALU = mybir.AluOpType
AX = mybir.AxisListType

B, S, D = 2, 4096, 2048
NCORE = 8
TPC = 1024
NCH = D // 128
NEG = -1.0e30

L0_HD = 128
PAY_Q, PAY_K, PAY_V = 0, 2048, 2560
PAYR = 3072

_uid = [0]


def uid():
    _uid[0] += 1
    return _uid[0]


class Sem:
    def __init__(self, nc, name):
        self.h = nc.alloc_semaphore(name)
        self.v = 0

    def inc(self, n=1):
        self.v += n
        return self.v



_WRAP_OPS = {
    "tensor_copy", "tensor_tensor", "tensor_scalar", "scalar_tensor_tensor",
    "tensor_reduce", "reciprocal", "activation", "copy", "mul", "add",
    "square", "sqrt", "memset", "select", "reduce_max", "reduce_sum",
    "tensor_add", "tensor_sub", "tensor_mul", "tensor_max",
}


class InstWrap:

    def __init__(self, proxy):
        self._p = proxy

    def then_inc(self, sh, n=1):
        p = self._p
        p._e.wait_ge(p._semh, p._cnt[0])
        p._e.nop().then_inc(sh, n)
        return self


class EngProxy:

    def __init__(self, e, semh, cnt):
        self._e = e
        self._semh = semh
        self._cnt = cnt

    def __getattr__(self, name):
        f = getattr(self._e, name)
        if name in _WRAP_OPS:
            def g(*a, **k):
                if self._cnt[0] > 0:
                    self._e.wait_ge(self._semh, self._cnt[0])
                inst = f(*a, **k)
                inst.then_inc(self._semh, 1)
                self._cnt[0] += 1
                return InstWrap(self)
            return g
        if name == "nop":
            def g2():
                if self._cnt[0] > 0:
                    self._e.wait_ge(self._semh, self._cnt[0])
                return f()
            return g2
        return f


class Ph:
    ENGS = ("pe", "dve", "act", "pool", "sp")

    def __init__(self, name=""):
        self.name = name
        self.uses_reg = False
        self.q = {e: [] for e in self.ENGS}

    def add(self, eng, fn):
        self.q[eng].append(fn)

    def pe(self, fn): self.add("pe", fn)
    def dve(self, fn): self.add("dve", fn)
    def act(self, fn): self.add("act", fn)
    def pool(self, fn): self.add("pool", fn)
    def sp(self, fn): self.add("sp", fn)

    def wait(self, eng, sem, val):
        self.add(eng, lambda e, s=sem.h, v=val: e.wait_ge(s, v))

    def emit(self, nc, prog=None):
        with nc.Block() as b:
            if self.q["pe"]:
                @b.tensor
                def _(e):
                    for f in self.q["pe"]:
                        f(e)
            if self.q["dve"]:
                @b.vector
                def _(e):
                    ep = EngProxy(e, prog.self_dve.h, prog.cnt_dve)
                    for f in self.q["dve"]:
                        f(ep)
            if self.q["act"]:
                @b.scalar
                def _(e):
                    ep = EngProxy(e, prog.self_act.h, prog.cnt_act)
                    for f in self.q["act"]:
                        f(ep)
            if self.q["pool"]:
                if self.uses_reg:
                    @b.gpsimd
                    def _(e):
                        with e.register(f"rp{uid()}") as r:
                            e._shared_reg = r
                            for f in self.q["pool"]:
                                f(e)
                else:
                    @b.gpsimd
                    def _(e):
                        for f in self.q["pool"]:
                            f(e)
            if self.q["sp"]:
                @b.sync
                def _(e):
                    for f in self.q["sp"]:
                        f(e)


class Prog:
    def __init__(self, name):
        self.nc = bass.Bass("TRN2", target_bir_lowering=False)
        self.phases = []
        self.sems = {}
        self.ps = [self.nc.alloc_psum_tensor(f"ps{i}", [128, 512], F32)
                   for i in range(7)]
        self.psm = self.nc.alloc_psum_tensor("psmisc", [128, 512], F32)
        nc = self.nc
        self.wbuf = [sbuf(nc, f"wbuf{i}", [128, 1024], BF16) for i in range(2)]
        self.scbf = [sbuf(nc, f"scbf{i}", [128, 1024], BF16) for i in range(4)]
        self.rowb = sbuf(nc, "rowb", [1, 3072], F32)
        self.gidx = 0
        self.grp_par = 0
        self.pf_marks = [0, 0]
        self.self_dve = Sem(nc, "self_dve")
        self.self_act = Sem(nc, "self_act")
        self.cnt_dve = [0]
        self.cnt_act = [0]
        self.bcA = sbuf(nc, "bcA", [128, 1024], F32)
        self.bcB = sbuf(nc, "bcB", [128, 1024], F32)
        self.tmpF = sbuf(nc, "tmpF", [128, 1024], F32)
        self.arena_h = None
        self.arena_off = 0
        self.arena_sz = 0

    def init_arena(self):
        import concourse.mybir as _mb
        rem = self.nc.sbuf_bytes_remaining - 1024
        self.arena_h = self.nc.alloc_sbuf_tensor("blockarena", [128, rem], _mb.dt.uint8)
        self.arena_base = self.nc.lookup_mloc(self.arena_h).addr
        self.arena_sz = rem
        self.arena_off = 0

    def ablock(self, name, shape, dtype):
        import concourse.mybir as _mb
        per = 1
        for s in shape[1:]:
            per *= s
        per *= _mb.dt.size(dtype)
        per = (per + 31) // 32 * 32
        assert self.arena_off + per <= self.arena_sz, \
            f"block arena overflow: {name} needs {per}, have {self.arena_sz - self.arena_off}"
        t = self.nc.alloc_sbuf_tensor_at(name, shape, dtype,
                                         offset=self.arena_base + self.arena_off)
        self.arena_off += per
        return t

    def arena_reset(self):
        self.arena_off = 0

    def sem(self, name):
        if name not in self.sems:
            self.sems[name] = Sem(self.nc, name)
        return self.sems[name]

    def phase(self, name=""):
        p = Ph(name)
        self.phases.append(p)
        return p

    def emit(self):
        for p in self.phases:
            p.emit(self.nc, self)


def sbuf(nc, name, shape, dtype):
    return nc.alloc_sbuf_tensor(name, shape, dtype)


def emit_proj(pr, ph, wdram, F, xn_s, consumer, n_dch=NCH, w_off=0, row_len=None,
              tts=(0, 1), xn_cols=None):
    if row_len is None:
        row_len = F
    wl = [pr.sem("wl0"), pr.sem("wl1")]
    wf = pr.sem("wf")
    pf = pr.sem("pf")
    if xn_cols is None:
        def xn_cols(dc, tt):
            return xn_s[:, dc * 1024 + tt * 512: dc * 1024 + tt * 512 + 512]

    assert n_dch % 4 == 0
    n_dg = n_dch // 4
    n_ft = F // 128
    groups = [(i, min(2, n_ft - i)) for i in range(0, n_ft, 2)]

    gidx_load = pr.gidx
    for (ft0, nft) in groups:
        cols = nft * 128
        for dg in range(n_dg):
            gidx = gidx_load
            gidx_load += 1
            par = gidx % 2
            if gidx >= 2:
                ph.wait("pool", wf, gidx - 1)
            s = bass.AP(wdram, w_off + dg * 512 * row_len + ft0 * 128,
                        [[row_len, 128], [128 * row_len, 4], [1, cols]])
            d = bass.AP(pr.wbuf[par], 0, [[1024, 128], [cols, 4], [1, cols]])
            wl[par].inc(16)
            ph.pool(lambda e, s=s, d=d, sh=wl[par].h:
                    e.dma_start(d, s).then_inc(sh, 16))

    for (ft0, nft) in groups:
        cols = nft * 128
        ntiles = nft * len(tts)
        grp = pr.grp_par % 2
        pr.grp_par += 1
        banks = [pr.ps[0], pr.ps[1], pr.ps[2], pr.ps[3]] if grp == 0 else \
                [pr.ps[4], pr.ps[5], pr.ps[6], pr.psm]
        ph.wait("pe", pf, pr.pf_marks[-2])
        for dg in range(n_dg):
            gidx = wf.v
            par = gidx % 2
            ph.wait("pe", wl[par], 16 * (gidx // 2 + 1))
            for j in range(4):
                st = (dg == 0 and j == 0)
                sp_ = (dg == n_dg - 1 and j == 3)
                for f in range(nft):
                    for ti, tt in enumerate(tts):
                        lhs = pr.wbuf[par][:, j * cols + f * 128: j * cols + f * 128 + 128]
                        rhs = xn_cols(dg * 4 + j, tt)
                        out = banks[f * len(tts) + ti][:, :]
                        is_last = (j == 3 and f == nft - 1 and ti == len(tts) - 1)
                        if is_last:
                            wf.inc(1)
                            ph.pe(lambda e, o=out, l=lhs, r=rhs, s0=st, s1=sp_,
                                  swf=wf.h: e.matmul(o, l, r, start=s0, stop=s1)
                                  .then_inc(swf, 1))
                        else:
                            ph.pe(lambda e, o=out, l=lhs, r=rhs, s0=st, s1=sp_:
                                  e.matmul(o, l, r, start=s0, stop=s1))
        wf_tgt = wf.v
        for f in range(nft):
            for ti, tt in enumerate(tts):
                consumer(ph, ft0 + f, tt, banks[f * len(tts) + ti], wf, wf_tgt, pf)
        pr.pf_marks.append(pf.v)
    pr.gidx = gidx_load


def emit_down(pr, ph, wdram, n_hch, h_cols, consumer, w_off=0, row_len=D, tt=0,
              hc0=0, dtg_stride=None):
    wl = [pr.sem("wl0"), pr.sem("wl1")]
    wf = pr.sem("wf")
    pf = pr.sem("pf")
    gidx_load = pr.gidx
    hgroups = []
    i = 0
    while i < n_hch:
        n = min(4, n_hch - i)
        hgroups.append((i, n))
        i += n
    if dtg_stride is None:
        dtg_stride = 256
    for dtg in range(8):
        for (h0, nh) in hgroups:
            gidx = gidx_load
            gidx_load += 1
            par = gidx % 2
            if gidx >= 2:
                ph.wait("pool", wf, gidx - 1)
            s = bass.AP(wdram, w_off + (hc0 + h0) * 128 * row_len + dtg * dtg_stride,
                        [[row_len, 128], [128 * row_len, nh], [1, 256]])
            d = bass.AP(pr.wbuf[par], 0, [[1024, 128], [256, nh], [1, 256]])
            wl[par].inc(16)
            ph.pool(lambda e, s=s, d=d, sh=wl[par].h:
                    e.dma_start(d, s).then_inc(sh, 16))
    h_ready = pf.v
    for dtg in range(8):
        grp = pr.grp_par % 2
        pr.grp_par += 1
        banks = [pr.ps[0], pr.ps[1]] if grp == 0 else [pr.ps[4], pr.ps[5]]
        ph.wait("pe", pf, max(pr.pf_marks[-2], h_ready))
        for gi, (h0, nh) in enumerate(hgroups):
            gidx = wf.v
            par = gidx % 2
            ph.wait("pe", wl[par], 16 * (gidx // 2 + 1))
            for j in range(nh):
                hc = hc0 + h0 + j
                st = (gi == 0 and j == 0)
                sp_ = (gi == len(hgroups) - 1 and j == nh - 1)
                for f in range(2):
                    lhs = pr.wbuf[par][:, j * 256 + f * 128: j * 256 + f * 128 + 128]
                    rhs = h_cols(hc)
                    out = banks[f][:, :]
                    is_last = (j == nh - 1 and f == 1)
                    if is_last:
                        wf.inc(1)
                        ph.pe(lambda e, o=out, l=lhs, r=rhs, s0=st, s1=sp_, swf=wf.h:
                              e.matmul(o, l, r, start=s0, stop=s1).then_inc(swf, 1))
                    else:
                        ph.pe(lambda e, o=out, l=lhs, r=rhs, s0=st, s1=sp_:
                              e.matmul(o, l, r, start=s0, stop=s1))
        wf_tgt = wf.v
        for f in range(2):
            consumer(ph, dtg * 2 + f, banks[f], wf, wf_tgt, pf)
        pr.pf_marks.append(pf.v)
    pr.gidx = gidx_load


def emit_ln(pr, x_s, xn_s, cons, tag, rms_w=None, eps=1e-5, src_bf=False):
    rms = rms_w is not None
    xb = [pr.scbf[0], pr.scbf[1]]
    sq = [pr.scbf[2], pr.scbf[3]]
    rows = pr.rowb
    bc_r, bc_m, tmp = pr.bcA, pr.bcB, pr.tmpF

    sX = pr.sem(f"lnx_{tag}")
    sQ = pr.sem(f"lnq_{tag}")
    sF = pr.sem(f"lnf_{tag}")
    sS = pr.sem(f"lns_{tag}")
    sX0, sQ0, sF0, sS0 = sX.v, sQ.v, sF.v, sS.v

    ph = pr.phase(f"ln_{tag}_stats")
    for ch in range(NCH):
        par = ch % 2
        if not src_bf:
            if ch >= 2:
                ph.wait("dve", sF, sF0 + 2 * (ch - 2) + 2)
            sX.inc(1)
            ph.dve(lambda e, c=ch, p=par, sh=sX.h: e.tensor_copy(
                xb[p][:, :], x_s[:, c * 1024:(c + 1) * 1024]).then_inc(sh, 1))
        if ch >= 2:
            ph.wait("act", sF, sF0 + 2 * (ch - 2) + 2)
        if not src_bf:
            ph.wait("act", sX, sX0 + ch + 1)
        sQ.inc(1)
        if src_bf:
            ph.act(lambda e, c=ch, p=par, sh=sQ.h: e.activation(
                sq[p][:, :], x_s[:, c * 1024:(c + 1) * 1024], AF.Square).then_inc(sh, 1))
        else:
            ph.act(lambda e, p=par, sh=sQ.h: e.activation(
                sq[p][:, :], xb[p][:, :], AF.Square).then_inc(sh, 1))
        ph.wait("pe", sQ, sQ0 + ch + 1)
        st = (ch == 0)
        sp_ = (ch == NCH - 1)
        slots = [pr.psm[0:1, :], pr.psm[32:33, :], pr.psm[64:65, :], pr.ps[4][0:1, :]]
        for tt in range(2):
            if not rms:
                xbsrc = (x_s[:, ch * 1024 + tt * 512: ch * 1024 + tt * 512 + 512]
                         if src_bf else xb[par][:, tt * 512:(tt + 1) * 512])
                ph.pe(lambda e, s0=st, s1=sp_, o=slots[tt], xs=xbsrc: e.matmul(
                    o, cons["ones_col_bf"][:, :], xs, start=s0, stop=s1))
            sF.inc(1)
            ph.pe(lambda e, p=par, t=tt, s0=st, s1=sp_, sh=sF.h, o=slots[2 + tt]: e.matmul(
                o, cons["ones_col_bf"][:, :],
                sq[p][:, t * 512:(t + 1) * 512], start=s0, stop=s1).then_inc(sh, 1))

    ph2 = pr.phase(f"ln_{tag}_mid")
    inv_d = 1.0 / D

    def rowmath(e):
        if not rms:
            e.tensor_copy(rows[0:1, 0:512], pr.psm[0:1, :])
            e.tensor_copy(rows[0:1, 512:1024], pr.psm[32:33, :])
        e.tensor_copy(rows[0:1, 1024:1536], pr.psm[64:65, :])
        e.tensor_copy(rows[0:1, 1536:2048], pr.ps[4][0:1, :])
        if not rms:
            e.tensor_scalar(rows[0:1, 0:1024], rows[0:1, 0:1024], inv_d, None, ALU.mult)
        e.tensor_scalar(rows[0:1, 1024:2048], rows[0:1, 1024:2048], inv_d, None, ALU.mult)
        if not rms:
            e.tensor_tensor(rows[0:1, 2048:3072], rows[0:1, 0:1024],
                            rows[0:1, 0:1024], ALU.mult)
            e.tensor_tensor(rows[0:1, 1024:2048], rows[0:1, 1024:2048],
                            rows[0:1, 2048:3072], ALU.subtract)
    ph2.dve(rowmath)
    sS.inc(1)
    ph2.dve(lambda e, sh=sS.h: e.nop().then_inc(sh, 1))
    ph2.wait("act", sS, sS0 + 1)
    epsap = cons["cvals"][0:1, (0 if eps == 1e-5 else 1):(1 if eps == 1e-5 else 2)]
    ph2.act(lambda e, b=epsap: e.activation(rows[0:1, 2048:3072], rows[0:1, 1024:2048],
                                            AF.Sqrt, bias=b))
    sS.inc(1)
    ph2.act(lambda e, sh=sS.h: e.nop().then_inc(sh, 1))
    ph2.wait("dve", sS, sS0 + 2)
    ph2.dve(lambda e: e.reciprocal(rows[0:1, 2048:3072], rows[0:1, 2048:3072]))
    if not rms:
        ph2.dve(lambda e: e.scalar_tensor_tensor(
            rows[0:1, 0:1024], rows[0:1, 0:1024], -1.0, rows[0:1, 2048:3072],
            ALU.mult, ALU.mult))
    sS.inc(1)
    ph2.dve(lambda e, sh=sS.h: e.nop().then_inc(sh, 1))
    ph2.wait("pe", sS, sS0 + 3)
    mms = []
    for tt in range(2):
        mms.append((pr.ps[tt], rows[0:1, 2048 + tt * 512: 2048 + (tt + 1) * 512]))
        if not rms:
            mms.append((pr.ps[2 + tt], rows[0:1, tt * 512:(tt + 1) * 512]))
    for i, (bank, rsrc) in enumerate(mms):
        if i == len(mms) - 1:
            sS.inc(1)
            ph2.pe(lambda e, b=bank, r=rsrc, sh=sS.h: e.matmul(
                b[:, :], cons["ones_row_f32"][:, :], r, start=True, stop=True)
                .then_inc(sh, 1))
        else:
            ph2.pe(lambda e, b=bank, r=rsrc: e.matmul(
                b[:, :], cons["ones_row_f32"][:, :], r, start=True, stop=True))
    ph2.wait("dve", sS, sS0 + 4)
    ph2.dve(lambda e: e.tensor_copy(bc_r[:, 0:512], pr.ps[0][:, :]))
    ph2.dve(lambda e: e.tensor_copy(bc_r[:, 512:1024], pr.ps[1][:, :]))
    if not rms:
        ph2.dve(lambda e: e.tensor_copy(bc_m[:, 0:512], pr.ps[2][:, :]))
        ph2.dve(lambda e: e.tensor_copy(bc_m[:, 512:1024], pr.ps[3][:, :]))

    ph3 = pr.phase(f"ln_{tag}_norm")
    for ch in range(NCH):
        ph3.dve(lambda e, c=ch: e.tensor_tensor(
            tmp[:, :], x_s[:, c * 1024:(c + 1) * 1024], bc_r[:, :], ALU.mult))
        if rms:
            ph3.dve(lambda e, c=ch: e.tensor_scalar(
                xn_s[:, c * 1024:(c + 1) * 1024], tmp[:, :],
                rms_w[:, c:c + 1], None, ALU.mult))
        else:
            ph3.dve(lambda e, c=ch: e.tensor_tensor(
                xn_s[:, c * 1024:(c + 1) * 1024], tmp[:, :], bc_m[:, :], ALU.add))


def alloc_consts(pr):
    nc = pr.nc
    cons = {}
    cons["ones_col_bf"] = sbuf(nc, "ones_col_bf", [128, 1], BF16)
    cons["ones_row_bf"] = sbuf(nc, "ones_row_bf", [1, 128], BF16)
    cons["ones_row_f32"] = sbuf(nc, "ones_row_f32", [1, 128], F32)
    cons["ident_bf"] = sbuf(nc, "ident_bf", [128, 128], BF16)
    cons["ident_f32"] = sbuf(nc, "ident_f32", [128, 128], F32)
    cons["ones_col_f32"] = sbuf(nc, "ones_col_f32", [128, 1], F32)
    cons["cvals"] = sbuf(nc, "cvals", [1, 8], F32)
    it_cv = nc.inline_tensor(np.array([[1e-5, 1e-6, 0, 0, 0, 0, 0, 0]], np.float32), name="it_cv")
    it_onec = nc.inline_tensor(np.ones((128, 1), np.float32), name="it_onec")
    it_oner = nc.inline_tensor(np.ones((1, 128), np.float32), name="it_oner")
    it_id = nc.inline_tensor(np.eye(128, dtype=np.float32), name="it_id")
    ph = pr.phase("consts")
    ds = pr.sem("dmap")
    for dst, src in ((cons["ones_col_bf"], it_onec), (cons["ones_row_bf"], it_oner),
                     (cons["ones_row_f32"], it_oner), (cons["ident_bf"], it_id),
                     (cons["ident_f32"], it_id), (cons["ones_col_f32"], it_onec),
                     (cons["cvals"], it_cv)):
        ds.inc(16)
        ph.pool(lambda e, d=dst, s=src, sh=ds.h: e.dma_start(d[:, :], s[:, :]).then_inc(sh, 16))
    ph.wait("pool", ds, ds.v)
    return cons


def load_stream(pr, ph, x_dram, x_s, dsem):
    src = bass.AP(x_dram, 0, [[1024, 128], [128 * 1024, 16], [1, 1024]])
    dst = bass.AP(x_s, 0, [[16 * 1024, 128], [1024, 16], [1, 1024]])
    dsem.inc(16)
    ph.sp(lambda e, s=src, d=dst, sh=dsem.h: e.dma_start(d, s).then_inc(sh, 16))


def store_stream(pr, ph, x_s, y_dram, dsem):
    src = bass.AP(x_s, 0, [[16 * 1024, 128], [1024, 16], [1, 1024]])
    dst = bass.AP(y_dram, 0, [[1024, 128], [128 * 1024, 16], [1, 1024]])
    dsem.inc(16)
    ph.sp(lambda e, s=src, d=dst, sh=dsem.h: e.dma_start(d, s).then_inc(sh, 16))


def emit_l0a(pr, cons, x_in, y_out, P=""):
    nc = pr.nc
    wq = nc.dram_tensor(P + "wq", [D, D], F32, kind="ExternalInput")
    wk = nc.dram_tensor(P + "wk", [D, 512], F32, kind="ExternalInput")
    wv = nc.dram_tensor(P + "wv", [D, 512], F32, kind="ExternalInput")
    wo = nc.dram_tensor(P + "wo", [D, D], F32, kind="ExternalInput")
    qcos = nc.dram_tensor(P + "qcos", [128, TPC], F32, kind="ExternalInput")
    qsin = nc.dram_tensor(P + "qsin", [128, TPC], F32, kind="ExternalInput")
    kcos = nc.dram_tensor(P + "kcos", [128, TPC], F32, kind="ExternalInput")
    ksin = nc.dram_tensor(P + "ksin", [128, TPC], F32, kind="ExternalInput")
    cmask = nc.dram_tensor(P + "cmask", [512, 512], F32, kind="ExternalInput")
    meta = nc.dram_tensor(P + "meta", [1, 128], I32, kind="ExternalInput")
    pay_i = nc.dram_tensor(P + "pay_i", [PAYR, TPC], BF16)
    pay_o = nc.dram_tensor(P + "pay_o", [4 * PAYR, TPC], BF16)
    opay_i = nc.dram_tensor(P + "opay_i", [512, S], BF16)
    opay_o = nc.dram_tensor(P + "opay_o", [2048, S], BF16)

    dsem = pr.sem("dma")
    ccs = pr.sem("cc")

    x_s = pr.ablock(P + "x_s", [128, NCH * 1024], F32)
    xn_s = pr.ablock(P + "xn_s", [128, NCH * 1024], BF16)
    qro = pr.ablock(P + "qro", [128, NCH * 1024], BF16)
    kro = pr.ablock(P + "kro", [128, 4 * 1024], BF16)
    vts = pr.ablock(P + "vts", [128, 4 * 1024], BF16)
    ropes = {}
    for nm in ("qcos", "qsin", "kcos", "ksin"):
        ropes[nm] = pr.ablock(P + f"r_{nm}", [128, TPC], BF16)
    cmask_s = pr.ablock(P + "cmask_s", [128, 4 * 512], F32)
    meta_s = pr.ablock(P + "meta_s", [1, 128], I32)
    normrow = pr.ablock(P + "normrow", [1, 1024], BF16)
    rsc = [pr.tmpF[:, 0:512], pr.tmpF[:, 512:1024]]

    ph = pr.phase("load")
    dmap = pr.sem("dmap")
    load_stream(pr, ph, x_in, x_s, dsem)
    for nm, t in (("qcos", qcos), ("qsin", qsin), ("kcos", kcos), ("ksin", ksin)):
        dmap.inc(16)
        ph.pool(lambda e, d=ropes[nm], s=t, sh=dmap.h: e.dma_start(d[:, :], s[:, :]).then_inc(sh, 16))
    csrc = bass.AP(cmask, 0, [[512, 128], [128 * 512, 4], [1, 512]])
    cdst = bass.AP(cmask_s, 0, [[4 * 512, 128], [512, 4], [1, 512]])
    dsem.inc(16)
    ph.sp(lambda e, s=csrc, d=cdst, sh=dsem.h: e.dma_start(d, s).then_inc(sh, 16))
    dsem.inc(16)
    ph.sp(lambda e, sh=dsem.h: e.dma_start(meta_s[:, :], meta[:, :]).then_inc(sh, 16))
    ph.wait("sp", dsem, dsem.v)
    ph.wait("pool", dmap, dmap.v)

    emit_ln(pr, x_s, xn_s, cons, "l0")

    def rope_consumer(out_s, costab, sintab):
        def consumer(ph_, fg, tt, pt, pg, pgtgt, pf):
            ph_.wait("dve", pg, pgtgt)
            co = costab[:, tt * 512:(tt + 1) * 512]
            si = sintab[:, tt * 512:(tt + 1) * 512]
            ocol = fg * 1024 + tt * 512

            def f(e, pt=pt, co=co, si=si, oc=ocol):
                e.tensor_tensor(rsc[0], pt[:, :], co, ALU.mult)
                e.tensor_tensor(rsc[1][0:64, :], pt[64:128, :], si[0:64, :], ALU.mult)
                e.tensor_tensor(rsc[1][64:128, :], pt[0:64, :], si[64:128, :], ALU.mult)
                e.tensor_tensor(out_s[:, oc:oc + 512], rsc[0], rsc[1], ALU.add)
            ph_.dve(f)
            pf.inc(1)
            ph_.dve(lambda e, s=pf.h: e.nop().then_inc(s, 1))
        return consumer

    def plain_consumer(out_s):
        def consumer(ph_, fg, tt, pt, pg, pgtgt, pf):
            ph_.wait("dve", pg, pgtgt)
            ocol = fg * 1024 + tt * 512
            ph_.dve(lambda e, pt=pt, oc=ocol: e.tensor_copy(out_s[:, oc:oc + 512], pt[:, :]))
            pf.inc(1)
            ph_.dve(lambda e, s=pf.h: e.nop().then_inc(s, 1))
        return consumer

    emit_proj(pr, pr.phase("proj_q"), wq, D, xn_s,
              rope_consumer(qro, ropes["qcos"], ropes["qsin"]))
    emit_proj(pr, pr.phase("proj_k"), wk, 512, xn_s,
              rope_consumer(kro, ropes["kcos"], ropes["ksin"]))
    emit_proj(pr, pr.phase("proj_v"), wv, 512, xn_s, plain_consumer(vts))

    php3 = pr.phase("payload")
    for src, dstoff, nch, rowlen in ((qro, PAY_Q, 16, NCH * 1024),
                                     (kro, PAY_K, 4, 4 * 1024),
                                     (vts, PAY_V, 4, 4 * 1024)):
        s_ap = bass.AP(src, 0, [[rowlen, 128], [1024, nch], [1, 1024]])
        d_ap = bass.AP(pay_i, dstoff * TPC, [[1024, 128], [128 * 1024, nch], [1, 1024]])
        dsem.inc(16)
        php3.sp(lambda e, s=s_ap, d=d_ap, sh=dsem.h: e.dma_start(d, s).then_inc(sh, 16))
    php3.wait("sp", dsem, dsem.v)

    phg = pr.phase("gather_in")
    ccs.inc(1)
    phg.pool(lambda e, sh=ccs.h: e.collective_compute(
        "AllGather", ALU.bypass, replica_groups=[[0, 1, 2, 3], [4, 5, 6, 7]],
        ins=[pay_i.ap()], outs=[pay_o.ap()]).then_inc(sh, 1))
    phg.wait("pool", ccs, ccs.v)

    pha = pr.phase("attn")
    pha.uses_reg = True
    kT = qro
    KOF, VOF, VTOK, QOF = 0, 4096, 8192, 12288
    pt_s = [pr.ablock(P + f"pt{i}", [128, 512], BF16) for i in range(2)]
    es_s = pr.bcA[:, 0:512]
    nbf = pr.ablock(P + "nbf", [1, 512], BF16)
    bcS = pr.bcA[:, 512:1024]
    outt = pr.ablock(P + "outt", [128, 512], BF16)
    sqk = pr.scbf[0]
    rowb = pr.rowb

    sLD = pr.sem("aload")
    sSC = pr.sem("asc")
    sE = pr.sem("aE")
    sP = pr.sem("aP")
    sAV = pr.sem("aAV")
    sVT = pr.sem("aVT")
    sVE = pr.sem("aVE")
    sEP = pr.sem("aEP")
    sBC = pr.sem("aBC")
    sQQ = pr.sem("aQQ")
    sBN = pr.sem("aBN")
    aq = [pr.sem("aq0"), pr.sem("aq1")]

    def pool_kv(e):
        r0 = e._shared_reg
        e.reg_load(r0, meta_s[0:1, 0:1])
        e.dma_start(
            bass.AP(kT, KOF, [[NCH * 1024, 128], [1024, 4], [1, 1024]]),
            bass.AP(pay_o, r0, [[1024, 128], [PAYR * 1024, 4], [1, 1024]])
        ).then_inc(sLD.h, 16)
        e.reg_add(r0, r0, 512 * 1024)
        e.dma_start(
            bass.AP(qro, VOF, [[NCH * 1024, 128], [1024, 4], [1, 1024]]),
            bass.AP(pay_o, r0, [[1024, 128], [PAYR * 1024, 4], [1, 1024]])
        ).then_inc(sLD.h, 16)
    pha.pool(pool_kv)
    sLD.inc(32)
    unit_ld = sLD.v

    pha.wait("pe", sLD, unit_ld)
    for ti in range(32):
        par = ti % 2
        if ti >= 2:
            pha.wait("pe", sVE, sVE.v - 1)
        sVT.inc(1)
        pha.pe(lambda e, t=ti, p=par, sh=sVT.h: e.matmul(
            pr.ps[2 + p][:, 0:128], qro[:, VOF + t * 128: VOF + (t + 1) * 128],
            cons["ident_bf"][:, :], start=True, stop=True).then_inc(sh, 1))
        pha.wait("dve", sVT, sVT.v)
        sVE.inc(1)
        pha.dve(lambda e, t=ti, p=par, sh=sVE.h: e.tensor_copy(
            qro[:, VTOK + t * 128: VTOK + (t + 1) * 128],
            pr.ps[2 + p][:, 0:128]).then_inc(sh, 1))

    pha.wait("act", sLD, unit_ld)
    sEP0k = sEP.v
    for rb in range(4):
        sQQ.inc(1)
        pha.act(lambda e, rb=rb, sh=sQQ.h: e.activation(
            sqk[:, :], kT[:, KOF + rb * 1024: KOF + (rb + 1) * 1024],
            AF.Square).then_inc(sh, 1))
        pha.wait("pe", sQQ, sQQ.v)
        for tt in range(2):
            i = rb * 2 + tt
            if i > 0:
                pha.wait("pe", sEP, sEP0k + i)
            sBC.inc(1)
            pha.pe(lambda e, t=tt, sh=sBC.h: e.matmul(
                pr.ps[6][0:1, :], cons["ones_col_bf"][:, :],
                sqk[:, t * 512:(t + 1) * 512], start=True, stop=True).then_inc(sh, 1))
            pha.wait("dve", sBC, sBC.v)
            sEP.inc(1)
            pha.dve(lambda e, i=i, sh=sEP.h: e.tensor_reduce(
                rowb[0:1, 1027 + i:1028 + i], pr.ps[6][0:1, :], AX.X, ALU.max)
                .then_inc(sh, 1))
        if rb < 3:
            pha.wait("act", sBC, sBC.v)
    pha.wait("dve", sEP, sEP.v)
    pha.dve(lambda e: e.tensor_reduce(rowb[0:1, 1024:1025], rowb[0:1, 1027:1035],
                                      AX.X, ALU.max))
    pha.dve(lambda e: e.tensor_scalar(rowb[0:1, 1026:1027], rowb[0:1, 1024:1025],
                                      -1.0 / 22.62742, None, ALU.mult))
    sBN.inc(1)
    pha.dve(lambda e, sh=sBN.h: e.nop().then_inc(sh, 1))
    bounds_ready = sBN.v

    qld_free = [None]
    sqq = pr.scbf[1]
    for h in range(4):
        for st in range(8):
            it = h * 8 + st
            if st == 0:
                aq[0].inc(16)

                def load_q(e, h=h, w=qld_free[0]):
                    r2 = e._shared_reg
                    if w is not None:
                        e.wait_ge(sAV.h, w)
                    e.reg_load(r2, meta_s[0:1, 4 + h * 4: 5 + h * 4])
                    e.dma_start(
                        bass.AP(qro, QOF, [[NCH * 1024, 128], [1024, 4], [1, 1024]]),
                        bass.AP(pay_o, r2, [[1024, 128], [PAYR * 1024, 4], [1, 1024]])
                    ).then_inc(aq[0].h, 16)
                pha.pool(load_q)
            q_tgt = aq[0].v
            qcols = qro[:, QOF + st * 512: QOF + st * 512 + 512]

            pha.wait("act", aq[0], q_tgt)
            if it > 0:
                pha.wait("act", sSC, sSC.v)
            sQQ.inc(1)
            pha.act(lambda e, qc=qcols, sh=sQQ.h: e.activation(
                sqq[:, 0:512], qc, AF.Square).then_inc(sh, 1))
            pha.wait("pe", sQQ, sQQ.v)
            pha.wait("pe", sEP, sEP.v)
            sBC.inc(1)
            pha.pe(lambda e, sh=sBC.h: e.matmul(
                pr.ps[6][0:1, :], cons["ones_col_bf"][:, :], sqq[:, 0:512],
                start=True, stop=True).then_inc(sh, 1))
            pha.wait("dve", sBN, bounds_ready)
            pha.wait("dve", sBC, sBC.v)
            pha.dve(lambda e: e.tensor_scalar(
                rowb[0:1, 0:512], pr.ps[6][0:1, :],
                rowb[0:1, 1026:1027], -5.65685, ALU.mult, ALU.add))
            pha.dve(lambda e: e.tensor_copy(nbf[0:1, :], rowb[0:1, 0:512]))
            sEP.inc(1)
            pha.dve(lambda e, sh=sEP.h: e.nop().then_inc(sh, 1))
            negb_tgt = sEP.v

            n_tt = 4 * (st + 1)
            sAV0 = sAV.v
            sP0 = sP.v
            for tt in range(n_tt):
                par = tt % 2
                pha.wait("pe", aq[0], q_tgt)
                if tt == 0:
                    pha.wait("pe", sEP, negb_tgt)
                if tt >= 2:
                    pha.wait("pe", sP, sP0 + tt - 1)
                pha.pe(lambda e, t=tt, p=par, qc=qcols: e.matmul(
                    pr.ps[p][:, :], kT[:, KOF + t * 128: KOF + (t + 1) * 128],
                    qc, start=True, stop=False))
                sSC.inc(1)
                pha.pe(lambda e, p=par, sh=sSC.h: e.matmul(
                    pr.ps[p][:, :], cons["ones_row_bf"][:, :], nbf[0:1, :],
                    start=False, stop=True).then_inc(sh, 1))
                diag = (tt >= 4 * st)
                if diag:
                    kk = tt - 4 * st
                    pha.wait("dve", sSC, sSC.v)
                    pha.wait("dve", sP, sP0 + tt)
                    sE.inc(1)
                    pha.dve(lambda e, p=par, k=kk, sh=sE.h: e.tensor_tensor(
                        es_s, pr.ps[p][:, :],
                        cmask_s[:, k * 512:(k + 1) * 512], ALU.add).then_inc(sh, 1))
                    pha.wait("act", sE, sE.v)
                    if tt >= 2:
                        pha.wait("act", sAV, sAV0 + tt - 1)
                    sP.inc(1)
                    pha.act(lambda e, p2=par, sh=sP.h: e.activation(
                        pt_s[p2][:, :], es_s, AF.Exp).then_inc(sh, 1))
                else:
                    pha.wait("act", sSC, sSC.v)
                    if tt >= 2:
                        pha.wait("act", sAV, sAV0 + tt - 1)
                    sP.inc(1)
                    pha.act(lambda e, p=par, p2=par, sh=sP.h: e.activation(
                        pt_s[p2][:, :], pr.ps[p][:, :], AF.Exp).then_inc(sh, 1))
                pha.wait("pe", sP, sP.v)
                pha.pe(lambda e, t=tt, p2=par, st_=(tt == 0), sp2=(tt == n_tt - 1):
                       e.matmul(
                    pr.ps[4][:, 0:512], qro[:, VTOK + t * 128: VTOK + (t + 1) * 128],
                    pt_s[p2][:, :], start=st_, stop=sp2))
                sAV.inc(1)
                pha.pe(lambda e, p2=par, st_=(tt == 0), sp_=(tt == n_tt - 1), sh=sAV.h:
                       e.matmul(pr.psm[0:1, :], cons["ones_col_bf"][:, :],
                                pt_s[p2][:, :], start=st_, stop=sp_).then_inc(sh, 1))
            if st == 7:
                qld_free[0] = sAV.v

            pha.wait("dve", sAV, sAV.v)
            pha.dve(lambda e: e.tensor_scalar(rowb[0:1, 512:1024], pr.psm[0:1, :],
                                              1e-30, None, ALU.max))
            pha.dve(lambda e: e.reciprocal(rowb[0:1, 512:1024], rowb[0:1, 512:1024]))
            sEP.inc(1)
            pha.dve(lambda e, sh=sEP.h: e.nop().then_inc(sh, 1))
            pha.wait("pe", sEP, sEP.v)
            sBC.inc(1)
            pha.pe(lambda e, sh=sBC.h: e.matmul(
                pr.ps[6][:, :], cons["ones_row_f32"][:, :], rowb[0:1, 512:1024],
                start=True, stop=True).then_inc(sh, 1))
            pha.wait("dve", sBC, sBC.v)
            pha.dve(lambda e: e.tensor_copy(bcS, pr.ps[6][:, :]))
            pha.wait("dve", dsem, dsem.v)
            pha.dve(lambda e: e.tensor_tensor(outt[:, :], pr.ps[4][:, 0:512],
                                              bcS, ALU.mult))
            sEP.inc(1)
            pha.dve(lambda e, sh=sEP.h: e.nop().then_inc(sh, 1))
            out_tgt = sEP.v
            pha.wait("sp", sEP, out_tgt)
            osrc = bass.AP(outt, 0, [[512, 128], [1, 512]])
            odst = bass.AP(opay_i, h * 128 * S + st * 512, [[S, 128], [1, 512]])
            dsem.inc(16)
            pha.sp(lambda e, s1=osrc, d1=odst, sh=dsem.h: e.dma_start(d1, s1).then_inc(sh, 16))
            pha.wait("pe", sEP, out_tgt)

    pha.wait("sp", dsem, dsem.v)

    phg2 = pr.phase("gather_out")
    ccs.inc(1)
    phg2.pool(lambda e, sh=ccs.h: e.collective_compute(
        "AllGather", ALU.bypass, replica_groups=[[0, 1, 2, 3], [4, 5, 6, 7]],
        ins=[opay_i.ap()], outs=[opay_o.ap()]).then_inc(sh, 1))
    phg2.wait("pool", ccs, ccs.v)

    phr = pr.phase("read_out")
    phr.uses_reg = True
    ao = qro

    def read_own(e):
            r3 = e._shared_reg
            e.reg_load(r3, meta_s[0:1, 60:61])
            e.dma_start(bass.AP(ao, 0, [[NCH * 1024, 128], [1024, 16], [1, 1024]]),
                        bass.AP(opay_o, r3, [[S, 128], [128 * S, 16], [1, 1024]])).then_inc(sLD.h, 16)
    phr.pool(read_own)
    sLD.inc(16)
    phr.wait("pool", sLD, sLD.v)

    def resid_consumer(ph_, fg, tt, pt, pg, pgtgt, pf):
        ph_.wait("dve", pg, pgtgt)
        col = fg * 1024 + tt * 512
        ph_.dve(lambda e, pt=pt, c=col: e.tensor_tensor(
            x_s[:, c:c + 512], x_s[:, c:c + 512], pt[:, :], ALU.add))
        pf.inc(1)
        ph_.dve(lambda e, s=pf.h: e.nop().then_inc(s, 1))

    emit_proj(pr, pr.phase("proj_o"), wo, D, ao, resid_consumer)

    phs = pr.phase("store")
    store_stream(pr, phs, x_s, y_out, dsem)
    phs.wait("sp", dsem, dsem.v)
    pr.arena_reset()


def build_l0a(upto=None):
    pr = Prog("l0a")
    nc = pr.nc
    x_in = nc.dram_tensor("x", [D, TPC], F32, kind="ExternalInput")
    y_out = nc.dram_tensor("y", [D, TPC], F32, kind="ExternalOutput")
    cons = alloc_consts(pr)
    pr.init_arena()
    emit_l0a(pr, cons, x_in, y_out)
    if upto is not None:
        names = [p.name for p in pr.phases]
        idx = names.index(upto)
        pr.phases = pr.phases[:idx + 1]
        tr = pr.sem("trunc")
        phs2 = pr.phase("store_trunc")
        x_s = nc.find_inst_by_name and None
        pr.phases = pr.phases
    pr.emit()
    return pr.nc


def rope_tables():
    hd = L0_HD
    freqs = 1.0 / (10000.0 ** (np.arange(0, hd, 2, dtype=np.float64) / hd))
    t = np.arange(S, dtype=np.float64)
    ang = t[:, None] * freqs[None, :]
    cos = np.concatenate([np.cos(ang), np.cos(ang)], axis=1)
    sin = np.sin(ang)
    ssin = np.concatenate([-sin, sin], axis=1)
    return cos.T.astype(np.float32), ssin.T.astype(np.float32)


def causal_masks():
    m = np.zeros((4, 128, 512), np.float32)
    ti = np.arange(128)[:, None]
    sj = np.arange(512)[None, :]
    for k in range(4):
        m[k] = np.where(ti + 128 * k <= sj, 0.0, NEG)
    return np.ascontiguousarray(m.reshape(512, 512))


def l0a_meta(core):
    kvg = core % 4
    m = np.zeros(128, np.int32)
    m[0] = (PAY_K + kvg * 128) * TPC
    for h in range(4):
        for rb in range(4):
            m[4 + h * 4 + rb] = (rb * PAYR + PAY_Q + (4 * kvg + h) * 128) * TPC
    m[60] = (core % 4) * TPC
    return m.reshape(1, 128)


_NC_CACHE = {}


def get_nc(name, builder):
    if name not in _NC_CACHE:
        _NC_CACHE[name] = builder()
    return _NC_CACHE[name]


def maps_l0a(inputs, c, P=""):
    cos, ssin = rope_tables()
    cm = causal_masks()
    scale = 1.0 / np.sqrt(float(L0_HD))
    t0 = (c % 4) * TPC
    return {
        P + "wq": inputs["L0_attn_wq"], P + "wk": inputs["L0_attn_wk"],
        P + "wv": inputs["L0_attn_wv"], P + "wo": inputs["L0_attn_wo"],
        P + "qcos": np.ascontiguousarray(cos[:, t0:t0 + TPC] * scale),
        P + "qsin": np.ascontiguousarray(ssin[:, t0:t0 + TPC] * scale),
        P + "kcos": np.ascontiguousarray(cos[:, t0:t0 + TPC]),
        P + "ksin": np.ascontiguousarray(ssin[:, t0:t0 + TPC]),
        P + "cmask": cm,
        P + "meta": l0a_meta(c),
    }


def run_l0a(x_shards, inputs, upto=None):
    nc = get_nc(f"l0a{upto}", lambda: build_l0a(upto))
    in_maps = [dict(x=x_shards[c], **maps_l0a(inputs, c)) for c in range(NCORE)]
    res = run_bass_kernel_spmd(nc, in_maps, core_ids=list(range(NCORE)))
    return [res.results[c]["y"] for c in range(NCORE)]


def shard_stream(x):
    out = []
    for c in range(NCORE):
        b, r = c // 4, c % 4
        out.append(np.ascontiguousarray(x[b, r * TPC:(r + 1) * TPC, :].T.astype(np.float32)))
    return out


def unshard_stream(shards):
    x = np.empty((B, S, D), np.float32)
    for c in range(NCORE):
        b, r = c // 4, c % 4
        x[b, r * TPC:(r + 1) * TPC, :] = shards[c].T
    return x


def kernel(**inputs):
    return kernel_fused(inputs)


def emit_l0f(pr, cons, x_in, y_out, P="", only=None):
    nc = pr.nc
    gwT = nc.dram_tensor(P + "gwT", [D, 8], F32, kind="ExternalInput")
    wg3 = nc.dram_tensor(P + "wg3", [8 * D, 1408], F32, kind="ExternalInput")
    wu3 = nc.dram_tensor(P + "wu3", [8 * D, 1408], F32, kind="ExternalInput")
    wd3 = nc.dram_tensor(P + "wd3", [8 * 1408, D], F32, kind="ExternalInput")
    shg = nc.dram_tensor(P + "shg", [128, 16], F32, kind="ExternalInput")
    shwg = nc.dram_tensor(P + "shwg", [D, 2816], F32, kind="ExternalInput")
    shwu = nc.dram_tensor(P + "shwu", [D, 2816], F32, kind="ExternalInput")
    shwd = nc.dram_tensor(P + "shwd", [2816, D], F32, kind="ExternalInput")
    dsem = pr.sem("dma")
    dmap = pr.sem("dmap")

    x_s = pr.ablock(P + "x_s", [128, NCH * 1024], F32)
    xn_s = pr.ablock(P + "xn_s", [128, NCH * 1024], BF16)
    hb = pr.ablock(P + "hb", [128, 22 * 512], BF16)
    y_acc = pr.ablock(P + "y_acc", [128, 16 * 512], F32)
    combrow = pr.ablock(P + "combrow", [1, 9 * 1024], BF16)
    gwT_s = pr.ablock(P + "gwT_s", [128, 16 * 8], F32)
    shg_s = pr.ablock(P + "shg_s", [128, 16], BF16)
    lgt = pr.ablock(P + "lgt", [128, 64], F32)
    cb_tok = pr.ablock(P + "cb_tok", [128, 8], F32)

    ph = pr.phase("load")
    load_stream(pr, ph, x_in, x_s, dsem)
    gsrc = bass.AP(gwT, 0, [[8, 128], [128 * 8, 16], [1, 8]])
    gdst = bass.AP(gwT_s, 0, [[128, 128], [8, 16], [1, 8]])
    dsem.inc(16)
    ph.sp(lambda e, s=gsrc, d=gdst, sh=dsem.h: e.dma_start(d, s).then_inc(sh, 16))
    dmap.inc(16)
    ph.pool(lambda e, sh=dmap.h: e.dma_start(shg_s[:, :], shg[:, :]).then_inc(sh, 16))
    ph.wait("sp", dsem, dsem.v)
    ph.wait("pool", dmap, dmap.v)

    emit_ln(pr, x_s, xn_s, cons, "l0f")
    rstd_row = pr.rowb[0:1, 2048:3072]
    negm_row = pr.rowb[0:1, 0:1024]

    phr = pr.phase("router")
    rtP = pr.sem("rtP")
    rtD = pr.sem("rtD")
    rtA = pr.sem("rtA")
    for dc in range(NCH):
        st, sp_ = dc == 0, dc == NCH - 1
        if sp_:
            rtP.inc(1)
            phr.pe(lambda e, dc=dc, sh=rtP.h: e.matmul(
                pr.ps[5][0:1, 0:8], cons["ones_col_f32"][:, :],
                gwT_s[:, dc * 8:(dc + 1) * 8], start=dc == 0, stop=True).then_inc(sh, 1))
        else:
            phr.pe(lambda e, dc=dc, s0=st: e.matmul(
                pr.ps[5][0:1, 0:8], cons["ones_col_f32"][:, :],
                gwT_s[:, dc * 8:(dc + 1) * 8], start=s0, stop=False))
    sg_done = []
    for t2 in range(2):
        for dc in range(NCH):
            sp_ = dc == NCH - 1
            if sp_:
                rtP.inc(1)
                phr.pe(lambda e, dc=dc, t2=t2, sh=rtP.h: e.matmul(
                    pr.ps[6][0:1, t2 * 256: t2 * 256 + 512] if False else pr.ps[6][0:1, 0:512],
                    shg_s[:, dc:dc + 1],
                    xn_s[:, dc * 1024 + t2 * 512: dc * 1024 + t2 * 512 + 512],
                    start=dc == 0, stop=True).then_inc(sh, 1))
            else:
                phr.pe(lambda e, dc=dc, t2=t2: e.matmul(
                    pr.ps[6][0:1, 0:512], shg_s[:, dc:dc + 1],
                    xn_s[:, dc * 1024 + t2 * 512: dc * 1024 + t2 * 512 + 512],
                    start=dc == 0, stop=False))
        phr.wait("act", rtP, rtP.v)
        rtA.inc(1)
        phr.act(lambda e, t2=t2, sh=rtA.h: e.activation(
            combrow[0:1, 8 * 1024 + t2 * 512: 8 * 1024 + (t2 + 1) * 512],
            pr.ps[6][0:1, 0:512], AF.Sigmoid).then_inc(sh, 1))
        sg_done.append(rtA.v)
        phr.wait("pe", rtA, rtA.v)

    phr.wait("dve", rtP, 1)
    rtD.inc(1)
    phr.dve(lambda e, sh=rtD.h: e.tensor_copy(
        lgt[0:1, 48:56], pr.ps[5][0:1, 0:8]).then_inc(sh, 1))

    for ts in range(8):
        for dc in range(NCH):
            sp_ = dc == NCH - 1
            lhs = x_s[:, dc * 1024 + ts * 128: dc * 1024 + ts * 128 + 128]
            if sp_:
                rtP.inc(1)
                phr.pe(lambda e, l=lhs, dc=dc, sh=rtP.h: e.matmul(
                    pr.ps[0][:, 0:8], l, gwT_s[:, dc * 8:(dc + 1) * 8],
                    start=dc == 0, stop=True).then_inc(sh, 1))
            else:
                phr.pe(lambda e, l=lhs, dc=dc: e.matmul(
                    pr.ps[0][:, 0:8], l, gwT_s[:, dc * 8:(dc + 1) * 8],
                    start=dc == 0, stop=False))
        if ts == 0:
            phr.wait("pe", rtD, rtD.v)
        rtP.inc(1)
        phr.pe(lambda e, ts=ts, sh=rtP.h: e.matmul(
            pr.ps[1][:, 0:8], negm_row[0:1, ts * 128:(ts + 1) * 128],
            lgt[0:1, 48:56], start=True, stop=True).then_inc(sh, 1))
        rtP.inc(1)
        phr.pe(lambda e, ts=ts, sh=rtP.h: e.matmul(
            pr.ps[2][:, 0:1], rstd_row[0:1, ts * 128:(ts + 1) * 128],
            cons["ones_row_f32"][0:1, 0:1], start=True, stop=True).then_inc(sh, 1))
        pe_tgt = rtP.v
        phr.wait("dve", rtP, pe_tgt)
        rtD.inc(1)

        def rmath(e, ts=ts, sh=rtD.h):
            e.tensor_copy(lgt[:, 37:38], pr.ps[2][:, 0:1])
            e.tensor_copy(lgt[:, 40:48], pr.ps[0][:, 0:8])
            e.scalar_tensor_tensor(lgt[:, 0:8], lgt[:, 40:48], lgt[:, 37:38],
                                   pr.ps[1][:, 0:8], ALU.mult, ALU.add)
            e.tensor_reduce(lgt[:, 32:33], lgt[:, 0:8], AX.X, ALU.max)
            e.tensor_scalar(lgt[:, 8:16], lgt[:, 0:8], lgt[:, 32:33], None, ALU.is_equal)
            e.scalar_tensor_tensor(lgt[:, 16:24], lgt[:, 8:16], NEG, lgt[:, 0:8],
                                   ALU.mult, ALU.add)
            e.tensor_reduce(lgt[:, 33:34], lgt[:, 16:24], AX.X, ALU.max)
            e.tensor_tensor(lgt[:, 34:35], lgt[:, 32:33], lgt[:, 33:34], ALU.subtract)
            e.tensor_scalar(lgt[:, 24:32], lgt[:, 0:8], lgt[:, 33:34], None, ALU.is_equal).then_inc(sh, 1)
        phr.dve(rmath)
        phr.wait("act", rtD, rtD.v)
        rtA.inc(1)
        phr.act(lambda e, sh=rtA.h: e.activation(
            lgt[:, 35:36], lgt[:, 34:35], AF.Sigmoid).then_inc(sh, 1))
        phr.wait("dve", rtA, rtA.v)
        rtD.inc(1)

        def rmath2(e, sh=rtD.h):
            e.tensor_scalar(lgt[:, 36:37], lgt[:, 35:36], -1.0, 1.0, ALU.mult, ALU.add)
            e.tensor_scalar(cb_tok[:, 0:8], lgt[:, 8:16], lgt[:, 35:36], None, ALU.mult)
            e.tensor_scalar(lgt[:, 24:32], lgt[:, 24:32], lgt[:, 36:37], None, ALU.mult)
            e.tensor_tensor(cb_tok[:, 0:8], cb_tok[:, 0:8], lgt[:, 24:32], ALU.add).then_inc(sh, 1)
        phr.dve(rmath2)
        phr.wait("pe", rtD, rtD.v)
        rtP.inc(1)
        phr.pe(lambda e, sh=rtP.h: e.matmul(
            pr.ps[3][0:8, 0:128], cb_tok[:, 0:8], cons["ident_f32"][:, :],
            start=True, stop=True).then_inc(sh, 1))
        phr.wait("dve", rtP, rtP.v)
        rtD.inc(1)
        phr.dve(lambda e, ts=ts, sh=rtD.h: e.tensor_copy(
            pr.bcA[0:8, ts * 128:(ts + 1) * 128], pr.ps[3][0:8, 0:128]).then_inc(sh, 1))
        phr.wait("pe", rtD, rtD.v)
    phr.wait("pool", rtD, rtD.v)
    dmap.inc(16)
    phr.pool(lambda e, sh=dmap.h: e.dma_start(
        bass.AP(combrow, 0, [[8192, 1], [1, 8192]]),
        bass.AP(pr.bcA, 0, [[1024, 8], [1, 1024]])).then_inc(sh, 16))
    phr.wait("pool", dmap, dmap.v)

    sCA = pr.sem("sCA")
    hb_free_at = [0]

    def mk_bc(ph_, e_idx, t2):
        grp = pr.grp_par % 2
        pr.grp_par += 1
        bank = pr.ps[0] if grp == 0 else pr.ps[4]
        pf = pr.sem("pf")
        ph_.wait("pe", pf, pr.pf_marks[-2])
        bcs = pr.sem("bcs")
        bcs.inc(1)
        ph_.pe(lambda e, ei=e_idx, t2=t2, b=bank, sh=bcs.h: e.matmul(
            b[:, :], cons["ones_row_bf"][:, :],
            combrow[0:1, ei * 1024 + t2 * 512: ei * 1024 + t2 * 512 + 512],
            start=True, stop=True).then_inc(sh, 1))
        ph_.wait("dve", bcs, bcs.v)
        pf.inc(1)
        ph_.dve(lambda e, b=bank, sh=pf.h: e.tensor_copy(
            pr.bcB[:, 0:512], b[:, :]).then_inc(sh, 1))
        pr.pf_marks.append(pf.v)

    def silu_consumer(hb_guard):
        def consumer(ph_, fg, tt, bank, wf, wf_tgt, pf):
            ph_.wait("act", wf, wf_tgt)
            if hb_guard[0]:
                ph_.wait("act", wf, hb_guard[0])
                hb_guard[0] = 0
            sCA.inc(1)
            ph_.act(lambda e, fg=fg, b=bank, sh=sCA.h: e.activation(
                hb[:, fg * 512:(fg + 1) * 512], b[:, :], AF.Silu).then_inc(sh, 1))
            ph_.wait("dve", sCA, sCA.v)
            pf.inc(1)
            ph_.dve(lambda e, sh=pf.h: e.nop().then_inc(sh, 1))
        return consumer

    def h_consumer(ph_, fg, tt, bank, wf, wf_tgt, pf):
        ph_.wait("dve", wf, wf_tgt)
        pf.inc(1)

        def f(e, fg=fg, b=bank, sh=pf.h):
            e.tensor_tensor(pr.tmpF[:, 0:512], b[:, :], hb[:, fg * 512:(fg + 1) * 512],
                            ALU.mult)
            e.tensor_tensor(hb[:, fg * 512:(fg + 1) * 512], pr.tmpF[:, 0:512],
                            pr.bcB[:, 0:512], ALU.mult).then_inc(sh, 1)
        ph_.dve(f)

    def yacc_consumer(first):
        def consumer(ph_, dt, bank, wf, wf_tgt, pf):
            ph_.wait("dve", wf, wf_tgt)
            pf.inc(1)
            if first:
                ph_.dve(lambda e, dt=dt, b=bank, sh=pf.h: e.tensor_copy(
                    y_acc[:, dt * 512:(dt + 1) * 512], b[:, :]).then_inc(sh, 1))
            else:
                ph_.dve(lambda e, dt=dt, b=bank, sh=pf.h: e.tensor_tensor(
                    y_acc[:, dt * 512:(dt + 1) * 512],
                    y_acc[:, dt * 512:(dt + 1) * 512], b[:, :], ALU.add).then_inc(sh, 1))
        return consumer

    experts = [("sh",)] + [(e,) for e in range(8)]
    if only is not None:
        experts = [(e,) for e in only]
    for t2 in range(2):
        phe = pr.phase(f"experts_t{t2}")
        for i, (e_id,) in enumerate(experts):
            ei = 8 if e_id == "sh" else e_id
            mk_bc(phe, ei, t2)
            hb_guard = [hb_free_at[0]]
            if e_id == "sh":
                emit_proj(pr, phe, shwg, 2816, xn_s, silu_consumer(hb_guard), tts=(t2,))
                emit_proj(pr, phe, shwu, 2816, xn_s, h_consumer, tts=(t2,))
                emit_down(pr, phe, shwd, 22,
                          lambda hc: hb[:, hc * 512:(hc + 1) * 512],
                          yacc_consumer(i == 0))
            else:
                emit_proj(pr, phe, wg3, 1408, xn_s, silu_consumer(hb_guard),
                          w_off=e_id * D * 1408, row_len=1408, tts=(t2,))
                emit_proj(pr, phe, wu3, 1408, xn_s, h_consumer,
                          w_off=e_id * D * 1408, row_len=1408, tts=(t2,))
                emit_down(pr, phe, wd3, 11,
                          lambda hc: hb[:, hc * 512:(hc + 1) * 512],
                          yacc_consumer(i == 0), w_off=e_id * 1408 * D)
            hb_free_at[0] = pr.sem("wf").v
        pha2 = pr.phase(f"resid_t{t2}")
        for ch in range(NCH):
            pha2.dve(lambda e, ch=ch, t2=t2: e.tensor_tensor(
                x_s[:, ch * 1024 + t2 * 512: ch * 1024 + t2 * 512 + 512],
                x_s[:, ch * 1024 + t2 * 512: ch * 1024 + t2 * 512 + 512],
                y_acc[:, ch * 512:(ch + 1) * 512], ALU.add))

    phs = pr.phase("store")
    store_stream(pr, phs, x_s, y_out, dsem)
    phs.wait("sp", dsem, dsem.v)
    pr.arena_reset()


def build_l0f(upto=None, only=None):
    pr = Prog("l0f")
    nc = pr.nc
    x_in = nc.dram_tensor("x", [D, TPC], F32, kind="ExternalInput")
    y_out = nc.dram_tensor("y", [D, TPC], F32, kind="ExternalOutput")
    cons = alloc_consts(pr)
    pr.init_arena()
    emit_l0f(pr, cons, x_in, y_out, only=only)
    pr.emit()
    return pr.nc


def maps_l0f(inputs, c, P=""):
    gwT = np.ascontiguousarray(np.asarray(inputs["L0_ffn_gate_w"], np.float32).T)
    wg3 = np.ascontiguousarray(np.asarray(inputs["L0_ffn_w_gate"], np.float32).reshape(8 * D, 1408))
    wu3 = np.ascontiguousarray(np.asarray(inputs["L0_ffn_w_up"], np.float32).reshape(8 * D, 1408))
    wd3 = np.ascontiguousarray(np.asarray(inputs["L0_ffn_w_down"], np.float32).reshape(8 * 1408, D))
    shg = np.ascontiguousarray(np.asarray(inputs["L0_ffn_shgate_w"], np.float32).reshape(16, 128).T)
    return {
        P + "gwT": gwT, P + "wg3": wg3, P + "wu3": wu3, P + "wd3": wd3,
        P + "shg": shg, P + "shwg": inputs["L0_ffn_shw_gate"],
        P + "shwu": inputs["L0_ffn_shw_up"], P + "shwd": inputs["L0_ffn_shw_down"],
    }


def run_l0f(x_shards, inputs, upto=None):
    nc = get_nc(f"l0f{upto}", lambda: build_l0f(upto))
    in_maps = [dict(x=x_shards[c], **maps_l0f(inputs, c)) for c in range(NCORE)]
    res = run_bass_kernel_spmd(nc, in_maps, core_ids=list(range(NCORE)))
    return [res.results[c]["y"] for c in range(NCORE)]


PAY1_Q, PAY1_K, PAY1_V = 0, 1024, 2048
PAYL1 = 4096
RSQ128 = float(1.0 / np.sqrt(128.0))


def emit_l1a(pr, cons, x_in, y_out, P=""):
    nc = pr.nc
    wq = nc.dram_tensor(P + "wq", [D, 1024], F32, kind="ExternalInput")
    wk = nc.dram_tensor(P + "wk", [D, 1024], F32, kind="ExternalInput")
    wv = nc.dram_tensor(P + "wv", [D, D], F32, kind="ExternalInput")
    wig = nc.dram_tensor(P + "wig", [D, 8], F32, kind="ExternalInput")
    wfg = nc.dram_tensor(P + "wfg", [D, 8], F32, kind="ExternalInput")
    bigc = nc.dram_tensor(P + "bigc", [8, 2], F32, kind="ExternalInput")
    wog = nc.dram_tensor(P + "wog", [D, D], F32, kind="ExternalInput")
    wout = nc.dram_tensor(P + "wout", [D, D], F32, kind="ExternalInput")
    wgate = nc.dram_tensor(P + "wgate", [D, 8192], F32, kind="ExternalInput")
    wup = nc.dram_tensor(P + "wup", [D, 8192], F32, kind="ExternalInput")
    wdown = nc.dram_tensor(P + "wdown", [8192, D], F32, kind="ExternalInput")
    n1c = nc.dram_tensor(P + "n1c", [128, 16], F32, kind="ExternalInput")
    n2c = nc.dram_tensor(P + "n2c", [128, 16], F32, kind="ExternalInput")
    cmask = nc.dram_tensor(P + "cmask", [512, 512], F32, kind="ExternalInput")
    meta = nc.dram_tensor(P + "meta", [1, 128], I32, kind="ExternalInput")
    pay_i = nc.dram_tensor(P + "pay_i", [PAYL1, TPC], BF16)
    pay_o = nc.dram_tensor(P + "pay_o", [4 * PAYL1, TPC], BF16)
    pg_i = nc.dram_tensor(P + "pg_i", [16, TPC], F32)
    pg_o = nc.dram_tensor(P + "pg_o", [64, TPC], F32)
    opay_i = nc.dram_tensor(P + "opay_i", [512, S], BF16)
    opay_o = nc.dram_tensor(P + "opay_o", [2048, S], BF16)

    selmy = nc.dram_tensor(P + "selmy", [8, 256], F32, kind="ExternalInput")
    i8my = nc.dram_tensor(P + "i8my", [8, 2], F32, kind="ExternalInput")
    mhmy = nc.dram_tensor(P + "mhmy", [128, 4], F32, kind="ExternalInput")

    dsem = pr.sem("dma")
    dmap = pr.sem("dmap")
    ccs = pr.sem("cc")

    arena = pr.ablock(P + "arena", [128, NCH * 1024], F32)
    abase = nc.lookup_mloc(arena).addr
    x_s = arena
    g_in = pr.ablock(P + "g_in", [128, NCH * 1024], BF16)
    h_in = pr.ablock(P + "h_in", [128, NCH * 1024], BF16)
    B1 = pr.ablock(P + "B1", [8, 4096], F32)
    B2 = pr.ablock(P + "B2", [8, 4096], F32)
    bbase = nc.lookup_mloc(B1).addr
    acol = pr.ablock(P + "acol", [128, 64], F32)
    n1_s = pr.ablock(P + "n1_s", [128, 16], F32)
    n2_s = pr.ablock(P + "n2_s", [128, 16], F32)
    mh_s = pr.ablock(P + "mh_s", [128, 4], F32)
    big_s = pr.ablock(P + "big_s", [8, 2], F32)
    cmask_s = pr.ablock(P + "cmask_s", [128, 4 * 512], BF16)
    meta_s = pr.ablock(P + "meta_s", [1, 128], I32)
    sel_s = pr.ablock(P + "sel_s", [8, 256], F32)
    i8_s = pr.ablock(P + "i8_s", [8, 2], F32)

    qkvT = nc.alloc_sbuf_tensor_at(P + "qkvT", [128, 32 * 1024], BF16, offset=abase)
    B3 = nc.alloc_sbuf_tensor_at(P + "B3", [8, 4096], F32, offset=abase)
    gI = nc.alloc_sbuf_tensor_at(P + "gI", [8, 4096], F32, offset=abase + 16384)
    EF = nc.alloc_sbuf_tensor_at(P + "EF", [128, 512], F32, offset=abase)
    PT2 = nc.alloc_sbuf_tensor_at(P + "PT2", [128, 1024], BF16, offset=abase + 2048)
    outst = nc.alloc_sbuf_tensor_at(P + "outst", [128, 512], BF16, offset=abase + 4096)
    sqsc = nc.alloc_sbuf_tensor_at(P + "sqsc", [128, 512], BF16, offset=abase + 5120)
    kTA = nc.alloc_sbuf_tensor_at(P + "kTA", [128, 4096], BF16, offset=abase + 16384)
    vstage = nc.alloc_sbuf_tensor_at(P + "vstage", [128, 8192], BF16, offset=abase + 24576)
    vtok = nc.alloc_sbuf_tensor_at(P + "vtok", [128, 8192], BF16, offset=abase + 40960)
    qslot = nc.alloc_sbuf_tensor_at(P + "qslot", [128, 4096], BF16, offset=abase + 57344)
    hn = nc.alloc_sbuf_tensor_at(P + "hn", [128, NCH * 1024], BF16, offset=bbase)
    hb = hn

    ph = pr.phase("load")
    load_stream(pr, ph, x_in, x_s, dsem)
    for d, s in ((n1_s, n1c), (n2_s, n2c), (mh_s, mhmy), (big_s, bigc),
                 (meta_s, meta), (sel_s, selmy), (i8_s, i8my)):
        dsem.inc(16)
        ph.sp(lambda e, d=d, s=s, sh=dsem.h: e.dma_start(d[:, :], s[:, :]).then_inc(sh, 16))
    csrc = bass.AP(cmask, 0, [[512, 128], [128 * 512, 4], [1, 512]])
    cdst = bass.AP(cmask_s, 0, [[4 * 512, 128], [512, 4], [1, 512]])
    dmap.inc(16)
    ph.pool(lambda e, s=csrc, d=cdst, sh=dmap.h: e.dma_start(d, s).then_inc(sh, 16))
    ph.wait("sp", dsem, dsem.v)
    ph.wait("pool", dmap, dmap.v)

    emit_ln(pr, x_s, g_in, cons, "l1")
    emit_ln(pr, g_in, h_in, cons, "l1r", rms_w=n1_s, eps=1e-6, src_bf=True)

    def stage_consumer(colbase, scale=None):
        def consumer(ph_, fg, tt, bank, wf, wf_tgt, pf):
            ph_.wait("dve", wf, wf_tgt)
            oc = colbase + fg * 1024 + tt * 512
            pf.inc(1)
            if scale is None:
                ph_.dve(lambda e, b=bank, oc=oc, sh=pf.h: e.tensor_copy(
                    qkvT[:, oc:oc + 512], b[:, :]).then_inc(sh, 1))
            else:
                ph_.dve(lambda e, b=bank, oc=oc, sh=pf.h: e.tensor_scalar(
                    qkvT[:, oc:oc + 512], b[:, :], scale, None, ALU.mult).then_inc(sh, 1))
        return consumer

    php = pr.phase("proj_qkv")
    emit_proj(pr, php, wq, 1024, h_in, stage_consumer(0, RSQ128))
    emit_proj(pr, php, wk, 1024, h_in, stage_consumer(8 * 1024))
    emit_proj(pr, php, wv, D, h_in, stage_consumer(16 * 1024))

    phgp = pr.phase("proj_gates")
    gsem = pr.sem("l1gp")
    gsd = pr.sem("l1gd")
    gsa = pr.sem("l1ga")
    gw_s = pr.ablock(P + "l1gw", [128, 2 * 16 * 8], BF16)
    for gi, w in enumerate((wig, wfg)):
        gsrc = bass.AP(w, 0, [[8, 128], [128 * 8, 16], [1, 8]])
        gdst = bass.AP(gw_s, gi * 128, [[256, 128], [8, 16], [1, 8]])
        dmap.inc(16)
        phgp.pool(lambda e, s=gsrc, d=gdst, sh=dmap.h: e.dma_start(d, s).then_inc(sh, 16))
    phgp.wait("pe", dmap, dmap.v)
    for gi in range(2):
        for t2 in range(2):
            for dc in range(NCH):
                sp_ = dc == NCH - 1
                lhs = gw_s[:, gi * 128 + dc * 8: gi * 128 + (dc + 1) * 8]
                if sp_:
                    gsem.inc(1)
                    phgp.pe(lambda e, l=lhs, dc=dc, t2=t2, sh=gsem.h: e.matmul(
                        pr.ps[6][0:8, 0:512], l,
                        h_in[:, dc * 1024 + t2 * 512: dc * 1024 + t2 * 512 + 512],
                        start=dc == 0, stop=True).then_inc(sh, 1))
                else:
                    phgp.pe(lambda e, l=lhs, dc=dc, t2=t2: e.matmul(
                        pr.ps[6][0:8, 0:512], l,
                        h_in[:, dc * 1024 + t2 * 512: dc * 1024 + t2 * 512 + 512],
                        start=dc == 0, stop=False))
            phgp.wait("dve", gsem, gsem.v)
            gsd.inc(1)
            phgp.dve(lambda e, gi=gi, sh=gsd.h: e.tensor_scalar(
                pr.bcA[0:8, 0:512], pr.ps[6][0:8, 0:512],
                big_s[0:8, gi:gi + 1], None, ALU.add).then_inc(sh, 1))
            phgp.wait("act", gsd, gsd.v)
            gsa.inc(1)
            phgp.act(lambda e, sh=gsa.h: e.activation(
                pr.bcB[0:8, 0:512], pr.bcA[0:8, 0:512], AF.Tanh,
                scale=1.0 / 15.0).then_inc(sh, 1))
            phgp.wait("dve", gsa, gsa.v)
            phgp.wait("dve", dsem, dsem.v)
            gsd.inc(1)
            phgp.dve(lambda e, sh=gsd.h: e.tensor_scalar(
                pr.bcA[0:8, 512:1024], pr.bcB[0:8, 0:512], 15.0, None,
                ALU.mult).then_inc(sh, 1))
            phgp.wait("sp", gsd, gsd.v)
            dsem.inc(16)
            phgp.sp(lambda e, gi=gi, t2=t2, sh=dsem.h: e.dma_start(
                bass.AP(pg_i, gi * 8 * TPC + t2 * 512, [[TPC, 8], [1, 512]]),
                pr.bcA[0:8, 512:1024]).then_inc(sh, 16))
            phgp.wait("pe", gsd, gsd.v)
    phgp.wait("sp", dsem, dsem.v)

    php3 = pr.phase("payload")
    s_ap = bass.AP(qkvT, 0, [[32 * 1024, 128], [1024, 32], [1, 1024]])
    d_ap = bass.AP(pay_i, 0, [[1024, 128], [128 * 1024, 32], [1, 1024]])
    dsem.inc(16)
    php3.sp(lambda e, s=s_ap, d=d_ap, sh=dsem.h: e.dma_start(d, s).then_inc(sh, 16))
    php3.wait("sp", dsem, dsem.v)

    phga = pr.phase("gather_in")
    ccs.inc(1)
    phga.pool(lambda e, sh=ccs.h: e.collective_compute(
        "AllGather", ALU.bypass, replica_groups=[[0, 1, 2, 3], [4, 5, 6, 7]],
        ins=[pay_i.ap()], outs=[pay_o.ap()]).then_inc(sh, 1))
    ccs.inc(1)
    phga.pool(lambda e, sh=ccs.h: e.collective_compute(
        "AllGather", ALU.bypass, replica_groups=[[0, 1, 2, 3], [4, 5, 6, 7]],
        ins=[pg_i.ap()], outs=[pg_o.ap()]).then_inc(sh, 1))
    phga.wait("pool", ccs, ccs.v)

    phw = pr.phase("gates")
    gisrc = bass.AP(pg_o, 0, [[TPC, 8], [16 * TPC, 4], [1, 1024]])
    gidst = bass.AP(gI, 0, [[4096, 8], [1024, 4], [1, 1024]])
    gfsrc = bass.AP(pg_o, 8 * TPC, [[TPC, 8], [16 * TPC, 4], [1, 1024]])
    gfdst = bass.AP(B2, 0, [[4096, 8], [1024, 4], [1, 1024]])
    dsem.inc(16)
    phw.sp(lambda e, s=gisrc, d=gidst, sh=dsem.h: e.dma_start(d, s).then_inc(sh, 16))
    dsem.inc(16)
    phw.sp(lambda e, s=gfsrc, d=gfdst, sh=dsem.h: e.dma_start(d, s).then_inc(sh, 16))
    phw.wait("dve", dsem, dsem.v)
    phw.wait("act", dsem, dsem.v)
    gsm = pr.sem("l1w")
    gwa = pr.sem("l1wx")
    gwa.inc(1)
    phw.act(lambda e, sh=gwa.h: e.activation(B3[0:8, :], B2[0:8, :], AF.Abs).then_inc(sh, 1))
    gwa.inc(1)
    phw.act(lambda e, sh=gwa.h: e.activation(B3[0:8, :], B3[0:8, :], AF.Exp,
                                             scale=-1.0).then_inc(sh, 1))
    gwa.inc(1)
    phw.act(lambda e, sh=gwa.h: e.activation(
        B3[0:8, :], B3[0:8, :], AF.Ln,
        bias=cons["ones_col_f32"][0:8, 0:1]).then_inc(sh, 1))
    phw.wait("dve", gwa, gwa.v)
    gsm.inc(1)
    phw.dve(lambda e, sh=gsm.h: e.tensor_scalar(B2[0:8, :], B2[0:8, :], 0.0, None,
                                                ALU.min).then_inc(sh, 1))
    gsm.inc(1)
    phw.dve(lambda e, sh=gsm.h: e.tensor_tensor(B2[0:8, :], B2[0:8, :], B3[0:8, :],
                                                ALU.subtract).then_inc(sh, 1))
    bufs = [B2, B3]
    w = 1
    pi = 0
    while w < 4096:
        a_, b_ = bufs[pi], bufs[1 - pi]
        gsm.inc(1)
        phw.dve(lambda e, a=a_, b=b_, w=w, sh=gsm.h: e.tensor_copy(
            b[0:8, 0:w], a[0:8, 0:w]).then_inc(sh, 1))
        gsm.inc(1)
        phw.dve(lambda e, a=a_, b=b_, w=w, sh=gsm.h: e.tensor_tensor(
            b[0:8, w:4096], a[0:8, w:4096], a[0:8, 0:4096 - w], ALU.add).then_inc(sh, 1))
        pi = 1 - pi
        w *= 2
    fcum = bufs[pi]
    gsm.inc(1)
    phw.dve(lambda e, f=fcum, sh=gsm.h: e.tensor_tensor(
        gI[0:8, :], gI[0:8, :], f[0:8, :], ALU.subtract).then_inc(sh, 1))
    other = bufs[1 - pi]
    gsm.inc(1)
    phw.dve(lambda e, o=other, sh=gsm.h: e.tensor_copy(o[0:8, :], gI[0:8, :]).then_inc(sh, 1))
    bufs2 = [other, B1]
    w = 1
    pi2 = 0
    while w < 4096:
        a_, b_ = bufs2[pi2], bufs2[1 - pi2]
        gsm.inc(1)
        phw.dve(lambda e, a=a_, b=b_, w=w, sh=gsm.h: e.tensor_copy(
            b[0:8, 0:w], a[0:8, 0:w]).then_inc(sh, 1))
        gsm.inc(1)
        phw.dve(lambda e, a=a_, b=b_, w=w, sh=gsm.h: e.tensor_tensor(
            b[0:8, w:4096], a[0:8, w:4096], a[0:8, 0:4096 - w], ALU.max).then_inc(sh, 1))
        pi2 = 1 - pi2
        w *= 2
    cmbuf = bufs2[pi2]
    gsm.inc(1)
    phw.dve(lambda e, c=cmbuf, f=fcum, sh=gsm.h: e.tensor_tensor(
        B2[0:8, :], f[0:8, :], c[0:8, :], ALU.add).then_inc(sh, 1))
    gsm.inc(1)
    phw.dve(lambda e, c=cmbuf, sh=gsm.h: e.tensor_scalar(
        B1[0:8, :], c[0:8, :], -1.0, None, ALU.mult).then_inc(sh, 1))
    phw.wait("pe", gsm, gsm.v)
    tsp = pr.sem("l1tp")
    tse = pr.sem("l1te")
    for blk in range(32):
        par = blk % 2
        if blk >= 2:
            phw.wait("pe", tse, tse.v - 1)
        tsp.inc(1)
        phw.pe(lambda e, b=blk, p=par, sh=tsp.h: e.matmul(
            pr.ps[2 + p][0:128, 0:2], gI[0:8, b * 128:(b + 1) * 128],
            i8_s[0:8, 0:2], start=True, stop=True).then_inc(sh, 1))
        phw.wait("dve", tsp, tsp.v)
        tse.inc(1)
        phw.dve(lambda e, b=blk, p=par, sh=tse.h: e.tensor_copy(
            acol[:, b * 2:(b + 1) * 2], pr.ps[2 + p][0:128, 0:2]).then_inc(sh, 1))

    pha = pr.phase("attn")
    pha.uses_reg = True
    sSC = pr.sem("asc")
    sE = pr.sem("aE")
    sP = pr.sem("aP")
    sM = pr.sem("aM")
    sAV = pr.sem("aAV")
    sVT = pr.sem("aVT")
    sVE = pr.sem("aVE")
    sEP = pr.sem("aEP")
    sBC = pr.sem("aBC")
    sLD = pr.sem("aload")
    aq = [pr.sem("aq0"), pr.sem("aq1")]
    rowb = pr.rowb

    qld_free_at = {}
    for hl in range(2):
        def unit_loads(e, hl=hl, w=None if hl == 0 else sAV.v):
            r0 = e._shared_reg
            if w is not None:
                e.wait_ge(sAV.h, w)
            e.reg_load(r0, meta_s[0:1, hl:hl + 1])
            e.dma_start(
                bass.AP(kTA, 0, [[4096, 128], [1024, 4], [1, 1024]]),
                bass.AP(pay_o, r0, [[1024, 128], [PAYL1 * 1024, 4], [1, 1024]])
            ).then_inc(sLD.h, 16)
            for dv in range(2):
                e.reg_load(r0, meta_s[0:1, 16 + 2 * hl + dv: 17 + 2 * hl + dv])
                e.dma_start(
                    bass.AP(vstage, dv * 4096, [[8192, 128], [1024, 4], [1, 1024]]),
                    bass.AP(pay_o, r0, [[1024, 128], [PAYL1 * 1024, 4], [1, 1024]])
                ).then_inc(sLD.h, 16)
        pha.pool(unit_loads)
        sLD.inc(48)
        unit_ld = sLD.v

        pha.wait("pe", sLD, unit_ld)
        for ti in range(64):
            tt, dv = ti // 2, ti % 2
            par = ti % 2
            if ti >= 2:
                pha.wait("pe", sVE, sVE.v - 1)
            sVT.inc(1)
            pha.pe(lambda e, t=tt, d=dv, p=par, sh=sVT.h: e.matmul(
                pr.ps[2 + p][:, 0:128],
                vstage[:, d * 4096 + t * 128: d * 4096 + (t + 1) * 128],
                cons["ident_bf"][:, :], start=True, stop=True).then_inc(sh, 1))
            pha.wait("dve", sVT, sVT.v)
            sVE.inc(1)
            pha.dve(lambda e, t=tt, d=dv, p=par, sh=sVE.h: e.tensor_copy(
                vtok[:, t * 256 + d * 128: t * 256 + (d + 1) * 128],
                pr.ps[2 + p][:, 0:128]).then_inc(sh, 1))

        for st in range(8):
            it = hl * 8 + st
            if st == 0:
                aq[0].inc(16)

                def load_q(e, hl=hl, w=qld_free_at.get(0)):
                    r2 = e._shared_reg
                    if w is not None:
                        e.wait_ge(sAV.h, w)
                    e.reg_load(r2, meta_s[0:1, 4 + hl * 4: 5 + hl * 4])
                    e.dma_start(
                        bass.AP(qslot, 0, [[4096, 128], [1024, 4], [1, 1024]]),
                        bass.AP(pay_o, r2, [[1024, 128], [PAYL1 * 1024, 4], [1, 1024]])
                    ).then_inc(aq[0].h, 16)
                pha.pool(load_q)
            q_tgt = aq[0].v
            qcols = qslot[:, st * 512: st * 512 + 512]

            pha.wait("pe", sEP, sEP.v)
            pha.wait("pe", sP, sP.v)
            sBC.inc(1)
            pha.pe(lambda e, hl=hl, st=st, sh=sBC.h: e.matmul(
                pr.ps[6][:, :], sel_s[0:8, hl * 128:(hl + 1) * 128],
                B1[0:8, st * 512:(st + 1) * 512], start=True, stop=True).then_inc(sh, 1))
            sBC.inc(1)
            pha.pe(lambda e, hl=hl, st=st, sh=sBC.h: e.matmul(
                pr.psm[32:33, 0:512], i8_s[0:8, hl:hl + 1],
                B2[0:8, st * 512:(st + 1) * 512], start=True, stop=True).then_inc(sh, 1))
            pha.wait("dve", sBC, sBC.v - 1)
            sEP.inc(1)
            pha.dve(lambda e, sh=sEP.h: e.tensor_copy(
                pr.bcB[:, 0:512], pr.ps[6][:, :]).then_inc(sh, 1))
            pha.wait("act", sBC, sBC.v)
            sP.inc(1)
            pha.act(lambda e, sh=sP.h: e.activation(
                rowb[0:1, 1024:1536], pr.psm[32:33, 0:512], AF.Exp,
                scale=-1.0).then_inc(sh, 1))
            bc_ready = sEP.v
            nf_ready = sP.v

            n_tt = 4 * (st + 1)
            sAV0 = sAV.v
            sM0 = sM.v
            sP0 = sP.v
            for tt in range(n_tt):
                par = tt % 2
                pha.wait("pe", aq[0], q_tgt)
                if tt >= 2:
                    pha.wait("pe", sM, sM0 + tt - 1)
                sSC.inc(1)
                pha.pe(lambda e, t=tt, p=par, qc=qcols, sh=sSC.h: e.matmul(
                    pr.ps[p][:, :], kTA[:, t * 128:(t + 1) * 128], qc,
                    start=True, stop=True).then_inc(sh, 1))
                pha.wait("dve", sEP, bc_ready)
                if tt >= 1:
                    pha.wait("dve", sP, sP0 + tt)
                sE.inc(1)
                pha.dve(lambda e, t=tt, hl=hl, sh=sE.h: e.tensor_scalar(
                    EF[:, :], pr.bcB[:, 0:512],
                    acol[:, t * 2 + hl: t * 2 + hl + 1], None,
                    ALU.add).then_inc(sh, 1))
                diag = (tt >= 4 * st)
                if diag:
                    kk = tt - 4 * st
                    sE.inc(1)
                    pha.dve(lambda e, k=kk, sh=sE.h: e.tensor_tensor(
                        EF[:, :], EF[:, :],
                        cmask_s[:, k * 512:(k + 1) * 512], ALU.add).then_inc(sh, 1))
                pha.wait("act", sE, sE.v)
                if tt >= 2:
                    pha.wait("act", sM, sM0 + tt - 1)
                sP.inc(1)
                pha.act(lambda e, p=par, sh=sP.h: e.activation(
                    PT2[:, p * 512:(p + 1) * 512], EF[:, :], AF.Exp).then_inc(sh, 1))
                pha.wait("dve", sP, sP.v)
                pha.wait("dve", sSC, sSC.v)
                sM.inc(1)
                pha.dve(lambda e, p=par, sh=sM.h: e.tensor_tensor(
                    PT2[:, p * 512:(p + 1) * 512], pr.ps[p][:, :],
                    PT2[:, p * 512:(p + 1) * 512], ALU.mult).then_inc(sh, 1))
                pha.wait("pe", sM, sM.v)
                pha.pe(lambda e, t=tt, p=par, st_=(tt == 0), sp_=(tt == n_tt - 1):
                       e.matmul(pr.ps[4][:, :],
                                vtok[:, t * 256: t * 256 + 128],
                                PT2[:, p * 512:(p + 1) * 512], start=st_, stop=sp_))
                pha.pe(lambda e, t=tt, p=par, st_=(tt == 0), sp_=(tt == n_tt - 1):
                       e.matmul(pr.ps[5][:, :],
                                vtok[:, t * 256 + 128: t * 256 + 256],
                                PT2[:, p * 512:(p + 1) * 512], start=st_, stop=sp_))
                sAV.inc(1)
                pha.pe(lambda e, p=par, st_=(tt == 0), sp_=(tt == n_tt - 1), sh=sAV.h:
                       e.matmul(pr.psm[0:1, 0:512], cons["ones_col_bf"][:, :],
                                PT2[:, p * 512:(p + 1) * 512], start=st_, stop=sp_)
                       .then_inc(sh, 1))
            if st == 7:
                qld_free_at[0] = sAV.v

            pha.wait("act", sAV, sAV.v)
            sP.inc(1)
            pha.act(lambda e, sh=sP.h: e.activation(
                rowb[0:1, 0:512], pr.psm[0:1, 0:512], AF.Abs).then_inc(sh, 1))
            pha.wait("dve", sP, sP.v)
            pha.wait("dve", sP, nf_ready)

            def epi_rows(e, sh=sEP.h):
                e.tensor_tensor(rowb[0:1, 0:512], rowb[0:1, 0:512],
                                rowb[0:1, 1024:1536], ALU.max)
                e.reciprocal(rowb[0:1, 512:1024], rowb[0:1, 0:512]).then_inc(sh, 1)
            sEP.inc(1)
            pha.dve(epi_rows)
            pha.wait("act", sEP, sEP.v)
            for dv in range(2):
                sP.inc(1)
                pha.act(lambda e, d=dv, sh=sP.h: e.activation(
                    sqsc[:, 0:512], pr.ps[4 + d][:, :], AF.Square).then_inc(sh, 1))
                pha.wait("pe", sP, sP.v)
                sBC.inc(1)
                pha.pe(lambda e, d=dv, sh=sBC.h: e.matmul(
                    pr.psm[64:65, 0:512], cons["ones_col_bf"][:, :], sqsc[:, 0:512],
                    start=d == 0, stop=d == 1).then_inc(sh, 1))
                if dv == 0:
                    pha.wait("act", sBC, sBC.v)
            pha.wait("dve", sBC, sBC.v)

            def epi_rows2(e, sh=sEP.h):
                e.tensor_tensor(rowb[0:1, 1536:2048], pr.psm[64:65, 0:512],
                                rowb[0:1, 512:1024], ALU.mult)
                e.tensor_tensor(rowb[0:1, 1536:2048], rowb[0:1, 1536:2048],
                                rowb[0:1, 512:1024], ALU.mult)
                e.tensor_scalar(rowb[0:1, 1536:2048], rowb[0:1, 1536:2048],
                                1.0 / 256.0, 1e-6, ALU.mult, ALU.add).then_inc(sh, 1)
            sEP.inc(1)
            pha.dve(epi_rows2)
            pha.wait("act", sEP, sEP.v)
            sP.inc(1)
            pha.act(lambda e, sh=sP.h: e.activation(
                rowb[0:1, 1536:2048], rowb[0:1, 1536:2048], AF.Sqrt).then_inc(sh, 1))
            pha.wait("dve", sP, sP.v)

            def epi_rows3(e, sh=sEP.h):
                e.reciprocal(rowb[0:1, 1536:2048], rowb[0:1, 1536:2048])
                e.tensor_tensor(rowb[0:1, 1536:2048], rowb[0:1, 1536:2048],
                                rowb[0:1, 512:1024], ALU.mult).then_inc(sh, 1)
            sEP.inc(1)
            pha.dve(epi_rows3)
            pha.wait("pe", sEP, sEP.v)
            sBC.inc(1)
            pha.pe(lambda e, sh=sBC.h: e.matmul(
                pr.ps[6][:, :], cons["ones_row_f32"][:, :], rowb[0:1, 1536:2048],
                start=True, stop=True).then_inc(sh, 1))
            pha.wait("dve", sBC, sBC.v)
            sEP.inc(1)
            pha.dve(lambda e, sh=sEP.h: e.tensor_copy(
                pr.bcA[:, 0:512], pr.ps[6][:, :]).then_inc(sh, 1))
            for dv in range(2):
                pha.wait("dve", sEP, sEP.v)
                pha.wait("dve", dsem, dsem.v)
                sEP.inc(1)
                pha.dve(lambda e, d=dv, hl=hl, sh=sEP.h: e.scalar_tensor_tensor(
                    outst[:, :], pr.ps[4 + d][:, :],
                    mh_s[:, 2 * hl + d: 2 * hl + d + 1],
                    pr.bcA[:, 0:512], ALU.mult, ALU.mult).then_inc(sh, 1))
                pha.wait("sp", sEP, sEP.v)
                dsem.inc(16)
                pha.sp(lambda e, d=dv, hl=hl, st=st, sh=dsem.h: e.dma_start(
                    bass.AP(opay_i, (hl * 256 + d * 128) * S + st * 512,
                            [[S, 128], [1, 512]]),
                    outst[:, :]).then_inc(sh, 16))
            pha.wait("pe", sEP, sEP.v)

    pha.wait("sp", dsem, dsem.v)

    phg2 = pr.phase("gather_out")
    ccs.inc(1)
    phg2.pool(lambda e, sh=ccs.h: e.collective_compute(
        "AllGather", ALU.bypass, replica_groups=[[0, 1, 2, 3], [4, 5, 6, 7]],
        ins=[opay_i.ap()], outs=[opay_o.ap()]).then_inc(sh, 1))
    phg2.wait("pool", ccs, ccs.v)

    phr = pr.phase("read_out")
    phr.uses_reg = True

    def read_own(e):
        r3 = e._shared_reg
        e.reg_load(r3, meta_s[0:1, 60:61])
        e.dma_start(bass.AP(hn, 0, [[NCH * 1024, 128], [1024, 16], [1, 1024]]),
                    bass.AP(opay_o, r3, [[S, 128], [128 * S, 16], [1, 1024]])).then_inc(sLD.h, 16)
    phr.pool(read_own)
    sLD.inc(16)
    load_stream(pr, phr, x_in, x_s, dsem)
    phr.wait("pool", sLD, sLD.v)
    phr.wait("sp", dsem, dsem.v)

    sCA = pr.sem("l1sg")

    def og_consumer(ph_, fg, tt, bank, wf, wf_tgt, pf):
        ph_.wait("act", wf, wf_tgt)
        ph_.wait("act", pf, pf.v)
        sCA.inc(1)
        ph_.act(lambda e, b=bank, sh=sCA.h: e.activation(
            pr.scbf[2][:, 0:512], b[:, :], AF.Sigmoid).then_inc(sh, 1))
        ph_.wait("dve", sCA, sCA.v)
        oc = fg * 1024 + tt * 512
        pf.inc(1)
        ph_.dve(lambda e, oc=oc, sh=pf.h: e.tensor_tensor(
            hn[:, oc:oc + 512], hn[:, oc:oc + 512], pr.scbf[2][:, 0:512],
            ALU.mult).then_inc(sh, 1))

    emit_proj(pr, pr.phase("proj_og"), wog, D, h_in, og_consumer)

    def wout_consumer(ph_, fg, tt, bank, wf, wf_tgt, pf):
        ph_.wait("dve", wf, wf_tgt)
        oc = fg * 1024 + tt * 512
        pf.inc(1)

        def f(e, oc=oc, b=bank, sh=pf.h):
            e.tensor_tensor(g_in[:, oc:oc + 512], g_in[:, oc:oc + 512], b[:, :],
                            ALU.add)
            e.tensor_tensor(x_s[:, oc:oc + 512], x_s[:, oc:oc + 512],
                            g_in[:, oc:oc + 512], ALU.add).then_inc(sh, 1)
        ph_.dve(f)

    emit_proj(pr, pr.phase("proj_wout"), wout, D, hn, wout_consumer)

    emit_ln(pr, g_in, h_in, cons, "l1r2", rms_w=n2_s, eps=1e-6, src_bf=True)

    phm = pr.phase("mlp")
    hb_free_at = [0]
    sMS = pr.sem("l1ms")

    def msilu_consumer(hb_guard):
        def consumer(ph_, fg, tt, bank, wf, wf_tgt, pf):
            ph_.wait("act", wf, wf_tgt)
            if hb_guard[0]:
                ph_.wait("act", wf, hb_guard[0])
                hb_guard[0] = 0
            sMS.inc(1)
            ph_.act(lambda e, fg=fg, b=bank, sh=sMS.h: e.activation(
                hb[:, fg * 512:(fg + 1) * 512], b[:, :], AF.Silu).then_inc(sh, 1))
            ph_.wait("dve", sMS, sMS.v)
            pf.inc(1)
            ph_.dve(lambda e, sh=pf.h: e.nop().then_inc(sh, 1))
        return consumer

    def mh_consumer(ph_, fg, tt, bank, wf, wf_tgt, pf):
        ph_.wait("dve", wf, wf_tgt)
        pf.inc(1)
        ph_.dve(lambda e, fg=fg, b=bank, sh=pf.h: e.tensor_tensor(
            hb[:, fg * 512:(fg + 1) * 512], b[:, :],
            hb[:, fg * 512:(fg + 1) * 512], ALU.mult).then_inc(sh, 1))

    for t2 in range(2):
        for half in range(2):
            def mdown_consumer(ph_, dt, bank, wf, wf_tgt, pf, t2=t2):
                ph_.wait("dve", wf, wf_tgt)
                pf.inc(1)
                ph_.dve(lambda e, dt=dt, b=bank, sh=pf.h: e.tensor_tensor(
                    x_s[:, dt * 1024 + t2 * 512: dt * 1024 + t2 * 512 + 512],
                    x_s[:, dt * 1024 + t2 * 512: dt * 1024 + t2 * 512 + 512],
                    b[:, :], ALU.add).then_inc(sh, 1))
            hb_guard = [hb_free_at[0]]
            emit_proj(pr, phm, wgate, 4096, h_in, msilu_consumer(hb_guard),
                      row_len=8192, w_off=half * 4096, tts=(t2,))
            emit_proj(pr, phm, wup, 4096, h_in, mh_consumer,
                      row_len=8192, w_off=half * 4096, tts=(t2,))
            emit_down(pr, phm, wdown, 32,
                      lambda hc: hb[:, hc * 512:(hc + 1) * 512],
                      mdown_consumer, w_off=half * 4096 * D)
            hb_free_at[0] = pr.sem("wf").v

    phs = pr.phase("store")
    store_stream(pr, phs, x_s, y_out, dsem)
    phs.wait("sp", dsem, dsem.v)
    pr.arena_reset()


def build_l1a(upto=None):
    pr = Prog("l1a")
    nc = pr.nc
    x_in = nc.dram_tensor("x", [D, TPC], F32, kind="ExternalInput")
    y_out = nc.dram_tensor("y", [D, TPC], F32, kind="ExternalOutput")
    cons = alloc_consts(pr)
    pr.init_arena()
    emit_l1a(pr, cons, x_in, y_out)
    pr.emit()
    return pr.nc


def l1a_meta(core):
    r = core % 4
    m = np.zeros(128, np.int32)
    for hl in range(2):
        hg = 2 * r + hl
        m[hl] = (PAY1_K + hg * 128) * TPC
        for dv in range(2):
            m[16 + 2 * hl + dv] = (PAY1_V + hg * 256 + dv * 128) * TPC
        for rb in range(4):
            m[4 + hl * 4 + rb] = (rb * PAYL1 + PAY1_Q + hg * 128) * TPC
    m[60] = r * TPC
    return m.reshape(1, 128)


def colw(v):
    return np.ascontiguousarray(np.asarray(v, np.float32).reshape(16, 128).T)


def maps_l1a(inputs, c, P=""):
    cm = causal_masks()
    bigc = np.stack([np.asarray(inputs["L1_attn_b_ig"], np.float32),
                     np.asarray(inputs["L1_attn_b_fg"], np.float32)], axis=1)
    bigc = np.ascontiguousarray(bigc)
    r = c % 4
    selmy = np.zeros((8, 256), np.float32)
    i8my = np.zeros((8, 2), np.float32)
    for hl in range(2):
        hg = 2 * r + hl
        selmy[hg, hl * 128:(hl + 1) * 128] = 1.0
        i8my[hg, hl] = 1.0
    mhfull = colw(inputs["L1_attn_mh_w"])
    mhmy = np.zeros((128, 4), np.float32)
    for hl in range(2):
        hg = 2 * r + hl
        for dv in range(2):
            mhmy[:, 2 * hl + dv] = mhfull[:, 2 * hg + dv]
    return {
        P + "wq": inputs["L1_attn_wq"], P + "wk": inputs["L1_attn_wk"],
        P + "wv": inputs["L1_attn_wv"], P + "wig": inputs["L1_attn_w_ig"],
        P + "wfg": inputs["L1_attn_w_fg"], P + "bigc": bigc,
        P + "wog": inputs["L1_attn_w_og"], P + "wout": inputs["L1_attn_w_out"],
        P + "wgate": inputs["L1_attn_w_gate"], P + "wup": inputs["L1_attn_w_up"],
        P + "wdown": inputs["L1_attn_w_down"],
        P + "n1c": colw(inputs["L1_attn_norm1_w"]),
        P + "n2c": colw(inputs["L1_attn_norm2_w"]),
        P + "selmy": np.ascontiguousarray(selmy),
        P + "i8my": np.ascontiguousarray(i8my),
        P + "mhmy": np.ascontiguousarray(mhmy),
        P + "cmask": cm, P + "meta": l1a_meta(c),
    }


def run_l1a(x_shards, inputs, upto=None):
    nc = get_nc(f"l1a{upto}", lambda: build_l1a(upto))
    in_maps = [dict(x=x_shards[c], **maps_l1a(inputs, c)) for c in range(NCORE)]
    res = run_bass_kernel_spmd(nc, in_maps, core_ids=list(range(NCORE)))
    return [res.results[c]["y"] for c in range(NCORE)]


def maps_l1f(inputs, c, P=""):
    lut = np.ascontiguousarray(np.asarray(inputs["L1_ffn_lut"], np.float32).reshape(1, 16))
    gi = np.asarray(inputs["L1_ffn_gate_idx"], np.int32)
    ui = np.asarray(inputs["L1_ffn_up_idx"], np.int32)
    di = np.asarray(inputs["L1_ffn_down_idx"], np.int32)
    return {
        P + "lut": lut,
        P + "gsl": np.ascontiguousarray(gi[:, 256 * c:256 * (c + 1)]),
        P + "usl": np.ascontiguousarray(ui[:, 256 * c:256 * (c + 1)]),
        P + "dsl": np.ascontiguousarray(di[256 * c:256 * (c + 1), :]),
    }


def run_l1f(x_shards, inputs, upto=None):
    nc = get_nc(f"l1f{upto}", lambda: build_l1f(upto))
    in_maps = [dict(x=x_shards[c], **maps_l1f(inputs, c)) for c in range(NCORE)]
    res = run_bass_kernel_spmd(nc, in_maps, core_ids=list(range(NCORE)))
    return [res.results[c]["y"] for c in range(NCORE)]


def build_fused():
    pr = Prog("fused")
    nc = pr.nc
    x0 = nc.dram_tensor("x", [D, TPC], F32, kind="ExternalInput")
    y = nc.dram_tensor("y", [D, TPC], F32, kind="ExternalOutput")
    s1 = nc.dram_tensor("s1", [D, TPC], F32)
    s2 = nc.dram_tensor("s2", [D, TPC], F32)
    s3 = nc.dram_tensor("s3", [D, TPC], F32)
    cons = alloc_consts(pr)
    pr.init_arena()
    emit_l0a(pr, cons, x0, s1, P="a_")
    emit_l0f(pr, cons, s1, s2, P="f_")
    emit_l1a(pr, cons, s2, s3, P="b_")
    emit_l1f(pr, cons, s3, y, P="g_")
    pr.emit()
    return pr.nc


def kernel_fused(inputs):
    nc = get_nc("fused", build_fused)
    x_shards = shard_stream(np.asarray(inputs["L0_attn_x"], np.float32))
    in_maps = []
    for c in range(NCORE):
        m = {"x": x_shards[c]}
        m.update(maps_l0a(inputs, c, "a_"))
        m.update(maps_l0f(inputs, c, "f_"))
        m.update(maps_l1a(inputs, c, "b_"))
        m.update(maps_l1f(inputs, c, "g_"))
        in_maps.append(m)
    res = run_bass_kernel_spmd(nc, in_maps, core_ids=list(range(NCORE)))
    return unshard_stream([res.results[c]["y"] for c in range(NCORE)])


PAY1_Q, PAY1_K, PAY1_V = 0, 1024, 2048
PAYL1 = 4096
RSQ128 = float(1.0 / np.sqrt(128.0))


def emit_l1f(pr, cons, x_in, y_out, P=""):
    nc = pr.nc
    lut = nc.dram_tensor(P + "lut", [1, 16], F32, kind="ExternalInput")
    gsl = nc.dram_tensor(P + "gsl", [5504, 256], I32, kind="ExternalInput")
    usl = nc.dram_tensor(P + "usl", [5504, 256], I32, kind="ExternalInput")
    dsl = nc.dram_tensor(P + "dsl", [256, 5504], I32, kind="ExternalInput")
    wg_i = nc.dram_tensor(P + "wg_i", [256, 5504], BF16)
    wg_o = nc.dram_tensor(P + "wg_o", [2048, 5504], BF16)
    wu_i = nc.dram_tensor(P + "wu_i", [256, 5504], BF16)
    wu_o = nc.dram_tensor(P + "wu_o", [2048, 5504], BF16)
    wd_i = nc.dram_tensor(P + "wd_i", [5504, 256], BF16)
    wd_o = nc.dram_tensor(P + "wd_o", [8 * 5504, 256], BF16)

    dsem = pr.sem("dma")
    dmap = pr.sem("dmap")
    ccs = pr.sem("cc")

    x_s = pr.ablock(P + "x_s", [128, NCH * 1024], F32)
    hn = pr.ablock(P + "hn", [128, NCH * 1024], BF16)
    hb = pr.ablock(P + "hb", [128, 22 * 512], BF16)
    lutbc = pr.ablock(P + "lutbc", [128, 16], F32)
    lutrow = pr.ablock(P + "lutrow", [1, 16], F32)
    idxs = [pr.ablock(P + f"idxs{i}", [128, 128], I32) for i in range(2)]
    idxb = pr.ablock(P + "idxb", [128, 128], BF16)
    macc = pr.ablock(P + "macc", [128, 128], BF16)
    mtmp = pr.ablock(P + "mtmp", [128, 128], BF16)
    wrow = pr.ablock(P + "wrow", [128, 5504], BF16)

    ph = pr.phase("load")
    load_stream(pr, ph, x_in, x_s, dsem)
    dsem.inc(16)
    ph.sp(lambda e, sh=dsem.h: e.dma_start(lutrow[:, :], lut[:, :]).then_inc(sh, 16))
    ph.wait("sp", dsem, dsem.v)
    lsm = pr.sem("lf")
    ph.wait("pe", dsem, dsem.v)
    lsm.inc(1)
    ph.pe(lambda e, sh=lsm.h: e.matmul(pr.ps[0][:, 0:16], cons["ones_row_f32"][:, :],
                                       lutrow[0:1, :], start=True, stop=True).then_inc(sh, 1))
    ph.wait("dve", lsm, lsm.v)
    lsm.inc(1)
    ph.dve(lambda e, sh=lsm.h: e.tensor_copy(lutbc[:, :], pr.ps[0][:, 0:16]).then_inc(sh, 1))

    emit_ln(pr, x_s, hn, cons, "l1f")

    phm = pr.phase("mat")
    msd = [pr.sem("mI0"), pr.sem("mI1")]
    msv = pr.sem("mV")
    msp = pr.sem("mP")
    mse = pr.sem("mE")
    msw = pr.sem("mW")
    phm.wait("dve", lsm, lsm.v)

    mt_n = [0]

    def mat_tile(idx_dram, row_len, r0, c0, par, wcol):
        if mt_n[0] >= 2:
            phm.wait("sp", msv, mt_n[0] - 1)
        mt_n[0] += 1
        msd[par].inc(16)
        phm.sp(lambda e, s=bass.AP(idx_dram, r0 * row_len + c0,
                                   [[row_len, 128], [1, 128]]),
               d=idxs[par], sh=msd[par].h: e.dma_start(d[:, :], s).then_inc(sh, 16))
        tgt = msd[par].v
        phm.wait("dve", msd[par], tgt)
        msv.inc(1)

        def mat(e, p=par, sh=msv.h):
            e.tensor_copy(idxb[:, :], idxs[p][:, :])
            e.tensor_scalar(macc[:, :], idxb[:, :], 0.0, lutbc[:, 0:1],
                            ALU.is_equal, ALU.mult)
            for j in range(1, 16):
                e.tensor_scalar(mtmp[:, :], idxb[:, :], float(j), lutbc[:, j:j + 1],
                                ALU.is_equal, ALU.mult)
                e.tensor_tensor(macc[:, :], macc[:, :], mtmp[:, :], ALU.add)
            e.nop().then_inc(sh, 1)
        phm.dve(mat)
        phm.wait("pe", msv, msv.v)
        phm.wait("pe", mse, mse.v)
        msp.inc(1)
        phm.pe(lambda e, sh=msp.h: e.matmul(pr.ps[1][:, 0:128], macc[:, :],
                                            cons["ident_bf"][:, :],
                                            start=True, stop=True).then_inc(sh, 1))
        phm.wait("dve", msp, msp.v)
        mse.inc(1)
        phm.dve(lambda e, wc=wcol, sh=mse.h: e.tensor_copy(
            wrow[:, wc:wc + 128], pr.ps[1][:, 0:128]).then_inc(sh, 1))

    for mi, (idx_dram, ag_in) in enumerate(((gsl, wg_i), (usl, wu_i))):
        for hc2 in range(2):
            for ot in range(43):
                mat_tile(idx_dram, 256, ot * 128, hc2 * 128, ot % 2, ot * 128)
            phm.wait("sp", mse, mse.v)
            msw.inc(16)
            phm.sp(lambda e, ag=ag_in, hc2=hc2, sh=msw.h: e.dma_start(
                bass.AP(ag, hc2 * 128 * 5504, [[5504, 128], [1, 5504]]),
                wrow[:, :]).then_inc(sh, 16))
            phm.wait("dve", msw, msw.v)
    for it in range(43):
        for dt2 in range(2):
            mat_tile(dsl, 5504, dt2 * 128, it * 128, dt2 % 2, dt2 * 128)
        phm.wait("sp", mse, mse.v)
        msw.inc(16)
        phm.sp(lambda e, it=it, sh=msw.h: e.dma_start(
            bass.AP(wd_i, it * 128 * 256, [[256, 128], [1, 256]]),
            wrow[:, 0:256]).then_inc(sh, 16))
        phm.wait("dve", msw, msw.v)

    phg = pr.phase("gather_w")
    for ag_i, ag_o in ((wg_i, wg_o), (wu_i, wu_o), (wd_i, wd_o)):
        ccs.inc(1)
        phg.pool(lambda e, i_=ag_i, o_=ag_o, sh=ccs.h: e.collective_compute(
            "AllGather", ALU.bypass, replica_groups=[[0, 1, 2, 3, 4, 5, 6, 7]],
            ins=[i_.ap()], outs=[o_.ap()]).then_inc(sh, 1))
    phg.wait("pool", ccs, ccs.v)

    phf = pr.phase("ffn")
    hb_free_at = [0]
    sMS = pr.sem("lfms")

    def fsilu_consumer(hb_guard):
        def consumer(ph_, fg, tt, bank, wf, wf_tgt, pf):
            ph_.wait("act", wf, wf_tgt)
            if hb_guard[0]:
                ph_.wait("act", wf, hb_guard[0])
                hb_guard[0] = 0
            sMS.inc(1)
            ph_.act(lambda e, fg=fg, b=bank, sh=sMS.h: e.activation(
                hb[:, fg * 512:(fg + 1) * 512], b[:, :], AF.Silu).then_inc(sh, 1))
            ph_.wait("dve", sMS, sMS.v)
            pf.inc(1)
            ph_.dve(lambda e, sh=pf.h: e.nop().then_inc(sh, 1))
        return consumer

    def fh_consumer(ph_, fg, tt, bank, wf, wf_tgt, pf):
        ph_.wait("dve", wf, wf_tgt)
        pf.inc(1)
        ph_.dve(lambda e, fg=fg, b=bank, sh=pf.h: e.tensor_tensor(
            hb[:, fg * 512:(fg + 1) * 512], b[:, :],
            hb[:, fg * 512:(fg + 1) * 512], ALU.mult).then_inc(sh, 1))

    halves = [(0, 2816, 22), (2816, 2688, 21)]
    for t2 in range(2):
        for (coff, fsz, nhc) in halves:
            def fdown_consumer(ph_, dt, bank, wf, wf_tgt, pf, t2=t2):
                ph_.wait("dve", wf, wf_tgt)
                pf.inc(1)
                ph_.dve(lambda e, dt=dt, b=bank, sh=pf.h: e.tensor_tensor(
                    x_s[:, dt * 1024 + t2 * 512: dt * 1024 + t2 * 512 + 512],
                    x_s[:, dt * 1024 + t2 * 512: dt * 1024 + t2 * 512 + 512],
                    b[:, :], ALU.add).then_inc(sh, 1))
            hb_guard = [hb_free_at[0]]
            emit_proj(pr, phf, wg_o, fsz, hn, fsilu_consumer(hb_guard),
                      row_len=5504, w_off=coff, tts=(t2,))
            emit_proj(pr, phf, wu_o, fsz, hn, fh_consumer,
                      row_len=5504, w_off=coff, tts=(t2,))
            emit_down(pr, phf, wd_o, nhc,
                      lambda hc, c0=coff: hb[:, (hc - c0 // 128) * 512:
                                             (hc - c0 // 128 + 1) * 512],
                      fdown_consumer, row_len=256, hc0=coff // 128,
                      dtg_stride=5504 * 256)
            hb_free_at[0] = pr.sem("wf").v

    phs = pr.phase("store")
    store_stream(pr, phs, x_s, y_out, dsem)
    phs.wait("sp", dsem, dsem.v)
    pr.arena_reset()


def build_l1f(upto=None):
    pr = Prog("l1f")
    nc = pr.nc
    x_in = nc.dram_tensor("x", [D, TPC], F32, kind="ExternalInput")
    y_out = nc.dram_tensor("y", [D, TPC], F32, kind="ExternalOutput")
    cons = alloc_consts(pr)
    pr.init_arena()
    emit_l1f(pr, cons, x_in, y_out)
    pr.emit()
    return pr.nc


PAY1_Q, PAY1_K, PAY1_V = 0, 1024, 2048
PAYL1 = 4096
RSQ128 = float(1.0 / np.sqrt(128.0))


PAY1_Q, PAY1_K, PAY1_V = 0, 1024, 2048
PAYL1 = 4096
RSQ128 = float(1.0 / np.sqrt(128.0))



# version 03
# speedup vs baseline: 1.0346x; 1.0346x over previous
import sys
sys.path.insert(0, "/opt/trn_rl_repo")

import numpy as np
import concourse.bass as bass
import concourse.mybir as mybir
from concourse.bass_utils import run_bass_kernel_spmd

F32 = mybir.dt.float32
BF16 = mybir.dt.bfloat16
I32 = mybir.dt.int32
AF = mybir.ActivationFunctionType
ALU = mybir.AluOpType
AX = mybir.AxisListType

B, S, D = 2, 4096, 2048
NCORE = 8
TPC = 1024
NCH = D // 128
NEG = -1.0e30

L0_HD = 128
PAY_Q, PAY_K, PAY_V = 0, 2048, 2560
PAYR = 3072

_uid = [0]


def uid():
    _uid[0] += 1
    return _uid[0]


class Sem:
    def __init__(self, nc, name):
        self.h = nc.alloc_semaphore(name)
        self.v = 0

    def inc(self, n=1):
        self.v += n
        return self.v



_WRAP_OPS = {
    "tensor_copy", "tensor_tensor", "tensor_scalar", "scalar_tensor_tensor",
    "tensor_reduce", "reciprocal", "activation", "copy", "mul", "add",
    "square", "sqrt", "memset", "select", "reduce_max", "reduce_sum",
    "tensor_add", "tensor_sub", "tensor_mul", "tensor_max",
}


class InstWrap:

    def __init__(self, proxy):
        self._p = proxy

    def then_inc(self, sh, n=1):
        p = self._p
        p._e.wait_ge(p._semh, p._cnt[0])
        p._e.nop().then_inc(sh, n)
        return self


class EngProxy:

    def __init__(self, e, semh, cnt):
        self._e = e
        self._semh = semh
        self._cnt = cnt

    def __getattr__(self, name):
        f = getattr(self._e, name)
        if name in _WRAP_OPS:
            def g(*a, **k):
                if self._cnt[0] > 0:
                    self._e.wait_ge(self._semh, self._cnt[0])
                inst = f(*a, **k)
                inst.then_inc(self._semh, 1)
                self._cnt[0] += 1
                return InstWrap(self)
            return g
        if name == "nop":
            def g2():
                if self._cnt[0] > 0:
                    self._e.wait_ge(self._semh, self._cnt[0])
                return f()
            return g2
        return f


class Ph:
    ENGS = ("pe", "dve", "act", "pool", "sp")

    def __init__(self, name=""):
        self.name = name
        self.uses_reg = False
        self.q = {e: [] for e in self.ENGS}

    def add(self, eng, fn):
        self.q[eng].append(fn)

    def pe(self, fn): self.add("pe", fn)
    def dve(self, fn): self.add("dve", fn)
    def act(self, fn): self.add("act", fn)
    def pool(self, fn): self.add("pool", fn)
    def sp(self, fn): self.add("sp", fn)

    def wait(self, eng, sem, val):
        self.add(eng, lambda e, s=sem.h, v=val: e.wait_ge(s, v))

    def emit(self, nc, prog=None):
        with nc.Block() as b:
            if self.q["pe"]:
                @b.tensor
                def _(e):
                    for f in self.q["pe"]:
                        f(e)
            if self.q["dve"]:
                @b.vector
                def _(e):
                    ep = EngProxy(e, prog.self_dve.h, prog.cnt_dve)
                    for f in self.q["dve"]:
                        f(ep)
            if self.q["act"]:
                @b.scalar
                def _(e):
                    ep = EngProxy(e, prog.self_act.h, prog.cnt_act)
                    for f in self.q["act"]:
                        f(ep)
            if self.q["pool"]:
                if self.uses_reg:
                    @b.gpsimd
                    def _(e):
                        with e.register(f"rp{uid()}") as r:
                            e._shared_reg = r
                            for f in self.q["pool"]:
                                f(e)
                else:
                    @b.gpsimd
                    def _(e):
                        for f in self.q["pool"]:
                            f(e)
            if self.q["sp"]:
                @b.sync
                def _(e):
                    for f in self.q["sp"]:
                        f(e)


class Prog:
    def __init__(self, name):
        self.nc = bass.Bass("TRN2", target_bir_lowering=False)
        self.phases = []
        self.sems = {}
        self.ps = [self.nc.alloc_psum_tensor(f"ps{i}", [128, 512], F32)
                   for i in range(7)]
        self.psm = self.nc.alloc_psum_tensor("psmisc", [128, 512], F32)
        nc = self.nc
        self.wbuf = [sbuf(nc, f"wbuf{i}", [128, 1024], BF16) for i in range(2)]
        self.scbf = [sbuf(nc, f"scbf{i}", [128, 1024], BF16) for i in range(4)]
        self.rowb = sbuf(nc, "rowb", [1, 3072], F32)
        self.gidx = 0
        self.grp_par = 0
        self.pf_marks = [0, 0]
        self.self_dve = Sem(nc, "self_dve")
        self.self_act = Sem(nc, "self_act")
        self.cnt_dve = [0]
        self.cnt_act = [0]
        self.bcA = sbuf(nc, "bcA", [128, 1024], F32)
        self.bcB = sbuf(nc, "bcB", [128, 1024], F32)
        self.tmpF = sbuf(nc, "tmpF", [128, 1024], F32)
        self.arena_h = None
        self.arena_off = 0
        self.arena_sz = 0

    def init_arena(self):
        import concourse.mybir as _mb
        rem = self.nc.sbuf_bytes_remaining - 1024
        self.arena_h = self.nc.alloc_sbuf_tensor("blockarena", [128, rem], _mb.dt.uint8)
        self.arena_base = self.nc.lookup_mloc(self.arena_h).addr
        self.arena_sz = rem
        self.arena_off = 0

    def ablock(self, name, shape, dtype):
        import concourse.mybir as _mb
        per = 1
        for s in shape[1:]:
            per *= s
        per *= _mb.dt.size(dtype)
        per = (per + 31) // 32 * 32
        assert self.arena_off + per <= self.arena_sz, \
            f"block arena overflow: {name} needs {per}, have {self.arena_sz - self.arena_off}"
        t = self.nc.alloc_sbuf_tensor_at(name, shape, dtype,
                                         offset=self.arena_base + self.arena_off)
        self.arena_off += per
        return t

    def arena_reset(self):
        self.arena_off = 0

    def sem(self, name):
        if name not in self.sems:
            self.sems[name] = Sem(self.nc, name)
        return self.sems[name]

    def phase(self, name=""):
        p = Ph(name)
        self.phases.append(p)
        return p

    def emit(self):
        for p in self.phases:
            p.emit(self.nc, self)


def sbuf(nc, name, shape, dtype):
    return nc.alloc_sbuf_tensor(name, shape, dtype)




def proj_chunks(n_dch, F):
    n_ft = F // 128
    out = []
    for i in range(0, n_ft, 2):
        nft = min(2, n_ft - i)
        for dg in range(n_dch // 4):
            out.append((nft * 128, dg, i))
    return out


def down_chunks(n_hch):
    hg = []
    i = 0
    while i < n_hch:
        n = min(4, n_hch - i)
        hg.append((i, n))
        i += n
    out = []
    for dtg in range(8):
        for (h0, nh) in hg:
            out.append((nh, h0, dtg))
    return out


def pack_proj(W, n_dch, F, col_off=0):
    parts = []
    for (cols, dg, ft0) in proj_chunks(n_dch, F):
        blk = W[dg * 512:(dg + 1) * 512, col_off + ft0 * 128: col_off + ft0 * 128 + cols]
        parts.append(np.ascontiguousarray(
            blk.reshape(4, 128, cols).transpose(1, 0, 2)).reshape(-1))
    return np.concatenate(parts)


def pack_down(W, n_hch, row_off=0):
    parts = []
    for (nh, h0, dtg) in down_chunks(n_hch):
        blk = W[row_off + h0 * 128: row_off + (h0 + nh) * 128,
                dtg * 256:(dtg + 1) * 256]
        parts.append(np.ascontiguousarray(
            blk.reshape(nh, 128, 256).transpose(1, 0, 2)).reshape(-1))
    return np.concatenate(parts)


def emit_proj(pr, ph, wdram, F, xn_s, consumer, n_dch=NCH, w_off=0, row_len=None,
              tts=(0, 1), xn_cols=None, packed=False):
    if row_len is None:
        row_len = F
    wl = [pr.sem("wl0"), pr.sem("wl1")]
    wf = pr.sem("wf")
    pf = pr.sem("pf")
    if xn_cols is None:
        def xn_cols(dc, tt):
            return xn_s[:, dc * 1024 + tt * 512: dc * 1024 + tt * 512 + 512]

    assert n_dch % 4 == 0
    n_dg = n_dch // 4
    n_ft = F // 128
    groups = [(i, min(2, n_ft - i)) for i in range(0, n_ft, 2)]

    gidx_load = pr.gidx
    if packed:
        if not hasattr(pr, "wpack_off"):
            pr.wpack_off = {}
        poff = pr.wpack_off.get(wdram.name, 0)
    for (ft0, nft) in groups:
        cols = nft * 128
        for dg in range(n_dg):
            gidx = gidx_load
            gidx_load += 1
            par = gidx % 2
            if gidx >= 2:
                ph.wait("pool", wf, gidx - 1)
            if packed:
                s = bass.AP(wdram, poff, [[4 * cols, 128], [cols, 4], [1, cols]])
                poff += 512 * cols
            else:
                s = bass.AP(wdram, w_off + dg * 512 * row_len + ft0 * 128,
                            [[row_len, 128], [128 * row_len, 4], [1, cols]])
            d = bass.AP(pr.wbuf[par], 0, [[1024, 128], [cols, 4], [1, cols]])
            wl[par].inc(16)
            ph.pool(lambda e, s=s, d=d, sh=wl[par].h:
                    e.dma_start(d, s).then_inc(sh, 16))
    if packed:
        pr.wpack_off[wdram.name] = poff

    for (ft0, nft) in groups:
        cols = nft * 128
        ntiles = nft * len(tts)
        grp = pr.grp_par % 2
        pr.grp_par += 1
        banks = [pr.ps[0], pr.ps[1], pr.ps[2], pr.ps[3]] if grp == 0 else \
                [pr.ps[4], pr.ps[5], pr.ps[6], pr.psm]
        ph.wait("pe", pf, pr.pf_marks[-2])
        for dg in range(n_dg):
            gidx = wf.v
            par = gidx % 2
            ph.wait("pe", wl[par], 16 * (gidx // 2 + 1))
            for j in range(4):
                st = (dg == 0 and j == 0)
                sp_ = (dg == n_dg - 1 and j == 3)
                for f in range(nft):
                    for ti, tt in enumerate(tts):
                        lhs = pr.wbuf[par][:, j * cols + f * 128: j * cols + f * 128 + 128]
                        rhs = xn_cols(dg * 4 + j, tt)
                        out = banks[f * len(tts) + ti][:, :]
                        is_last = (j == 3 and f == nft - 1 and ti == len(tts) - 1)
                        if is_last:
                            wf.inc(1)
                            ph.pe(lambda e, o=out, l=lhs, r=rhs, s0=st, s1=sp_,
                                  swf=wf.h: e.matmul(o, l, r, start=s0, stop=s1)
                                  .then_inc(swf, 1))
                        else:
                            ph.pe(lambda e, o=out, l=lhs, r=rhs, s0=st, s1=sp_:
                                  e.matmul(o, l, r, start=s0, stop=s1))
        wf_tgt = wf.v
        for f in range(nft):
            for ti, tt in enumerate(tts):
                consumer(ph, ft0 + f, tt, banks[f * len(tts) + ti], wf, wf_tgt, pf)
        pr.pf_marks.append(pf.v)
    pr.gidx = gidx_load


def emit_down(pr, ph, wdram, n_hch, h_cols, consumer, w_off=0, row_len=D, tt=0,
              hc0=0, dtg_stride=None, packed=False):
    wl = [pr.sem("wl0"), pr.sem("wl1")]
    wf = pr.sem("wf")
    pf = pr.sem("pf")
    gidx_load = pr.gidx
    hgroups = []
    i = 0
    while i < n_hch:
        n = min(4, n_hch - i)
        hgroups.append((i, n))
        i += n
    if dtg_stride is None:
        dtg_stride = 256
    if packed:
        if not hasattr(pr, "wpack_off"):
            pr.wpack_off = {}
        poff = pr.wpack_off.get(wdram.name, 0)
    for dtg in range(8):
        for (h0, nh) in hgroups:
            gidx = gidx_load
            gidx_load += 1
            par = gidx % 2
            if gidx >= 2:
                ph.wait("pool", wf, gidx - 1)
            if packed:
                s = bass.AP(wdram, poff, [[nh * 256, 128], [256, nh], [1, 256]])
                poff += nh * 128 * 256
            else:
                s = bass.AP(wdram, w_off + (hc0 + h0) * 128 * row_len + dtg * dtg_stride,
                            [[row_len, 128], [128 * row_len, nh], [1, 256]])
            d = bass.AP(pr.wbuf[par], 0, [[1024, 128], [256, nh], [1, 256]])
            wl[par].inc(16)
            ph.pool(lambda e, s=s, d=d, sh=wl[par].h:
                    e.dma_start(d, s).then_inc(sh, 16))
    if packed:
        pr.wpack_off[wdram.name] = poff
    h_ready = pf.v
    for dtg in range(8):
        grp = pr.grp_par % 2
        pr.grp_par += 1
        banks = [pr.ps[0], pr.ps[1]] if grp == 0 else [pr.ps[4], pr.ps[5]]
        ph.wait("pe", pf, max(pr.pf_marks[-2], h_ready))
        for gi, (h0, nh) in enumerate(hgroups):
            gidx = wf.v
            par = gidx % 2
            ph.wait("pe", wl[par], 16 * (gidx // 2 + 1))
            for j in range(nh):
                hc = hc0 + h0 + j
                st = (gi == 0 and j == 0)
                sp_ = (gi == len(hgroups) - 1 and j == nh - 1)
                for f in range(2):
                    lhs = pr.wbuf[par][:, j * 256 + f * 128: j * 256 + f * 128 + 128]
                    rhs = h_cols(hc)
                    out = banks[f][:, :]
                    is_last = (j == nh - 1 and f == 1)
                    if is_last:
                        wf.inc(1)
                        ph.pe(lambda e, o=out, l=lhs, r=rhs, s0=st, s1=sp_, swf=wf.h:
                              e.matmul(o, l, r, start=s0, stop=s1).then_inc(swf, 1))
                    else:
                        ph.pe(lambda e, o=out, l=lhs, r=rhs, s0=st, s1=sp_:
                              e.matmul(o, l, r, start=s0, stop=s1))
        wf_tgt = wf.v
        for f in range(2):
            consumer(ph, dtg * 2 + f, banks[f], wf, wf_tgt, pf)
        pr.pf_marks.append(pf.v)
    pr.gidx = gidx_load


def emit_ln(pr, x_s, xn_s, cons, tag, rms_w=None, eps=1e-5, src_bf=False):
    rms = rms_w is not None
    xb = [pr.scbf[0], pr.scbf[1]]
    sq = [pr.scbf[2], pr.scbf[3]]
    rows = pr.rowb
    bc_r, bc_m, tmp = pr.bcA, pr.bcB, pr.tmpF

    sX = pr.sem(f"lnx_{tag}")
    sQ = pr.sem(f"lnq_{tag}")
    sF = pr.sem(f"lnf_{tag}")
    sS = pr.sem(f"lns_{tag}")
    sX0, sQ0, sF0, sS0 = sX.v, sQ.v, sF.v, sS.v

    ph = pr.phase(f"ln_{tag}_stats")
    for ch in range(NCH):
        par = ch % 2
        if not src_bf:
            if ch >= 2:
                ph.wait("dve", sF, sF0 + 2 * (ch - 2) + 2)
            sX.inc(1)
            ph.dve(lambda e, c=ch, p=par, sh=sX.h: e.tensor_copy(
                xb[p][:, :], x_s[:, c * 1024:(c + 1) * 1024]).then_inc(sh, 1))
        if ch >= 2:
            ph.wait("act", sF, sF0 + 2 * (ch - 2) + 2)
        if not src_bf:
            ph.wait("act", sX, sX0 + ch + 1)
        sQ.inc(1)
        if src_bf:
            ph.act(lambda e, c=ch, p=par, sh=sQ.h: e.activation(
                sq[p][:, :], x_s[:, c * 1024:(c + 1) * 1024], AF.Square).then_inc(sh, 1))
        else:
            ph.act(lambda e, p=par, sh=sQ.h: e.activation(
                sq[p][:, :], xb[p][:, :], AF.Square).then_inc(sh, 1))
        ph.wait("pe", sQ, sQ0 + ch + 1)
        st = (ch == 0)
        sp_ = (ch == NCH - 1)
        slots = [pr.psm[0:1, :], pr.psm[32:33, :], pr.psm[64:65, :], pr.ps[4][0:1, :]]
        for tt in range(2):
            if not rms:
                xbsrc = (x_s[:, ch * 1024 + tt * 512: ch * 1024 + tt * 512 + 512]
                         if src_bf else xb[par][:, tt * 512:(tt + 1) * 512])
                ph.pe(lambda e, s0=st, s1=sp_, o=slots[tt], xs=xbsrc: e.matmul(
                    o, cons["ones_col_bf"][:, :], xs, start=s0, stop=s1))
            sF.inc(1)
            ph.pe(lambda e, p=par, t=tt, s0=st, s1=sp_, sh=sF.h, o=slots[2 + tt]: e.matmul(
                o, cons["ones_col_bf"][:, :],
                sq[p][:, t * 512:(t + 1) * 512], start=s0, stop=s1).then_inc(sh, 1))

    ph2 = pr.phase(f"ln_{tag}_mid")
    inv_d = 1.0 / D

    def rowmath(e):
        if not rms:
            e.tensor_copy(rows[0:1, 0:512], pr.psm[0:1, :])
            e.tensor_copy(rows[0:1, 512:1024], pr.psm[32:33, :])
        e.tensor_copy(rows[0:1, 1024:1536], pr.psm[64:65, :])
        e.tensor_copy(rows[0:1, 1536:2048], pr.ps[4][0:1, :])
        if not rms:
            e.tensor_scalar(rows[0:1, 0:1024], rows[0:1, 0:1024], inv_d, None, ALU.mult)
        e.tensor_scalar(rows[0:1, 1024:2048], rows[0:1, 1024:2048], inv_d, None, ALU.mult)
        if not rms:
            e.tensor_tensor(rows[0:1, 2048:3072], rows[0:1, 0:1024],
                            rows[0:1, 0:1024], ALU.mult)
            e.tensor_tensor(rows[0:1, 1024:2048], rows[0:1, 1024:2048],
                            rows[0:1, 2048:3072], ALU.subtract)
    ph2.dve(rowmath)
    sS.inc(1)
    ph2.dve(lambda e, sh=sS.h: e.nop().then_inc(sh, 1))
    ph2.wait("act", sS, sS0 + 1)
    epsap = cons["cvals"][0:1, (0 if eps == 1e-5 else 1):(1 if eps == 1e-5 else 2)]
    ph2.act(lambda e, b=epsap: e.activation(rows[0:1, 2048:3072], rows[0:1, 1024:2048],
                                            AF.Sqrt, bias=b))
    sS.inc(1)
    ph2.act(lambda e, sh=sS.h: e.nop().then_inc(sh, 1))
    ph2.wait("dve", sS, sS0 + 2)
    ph2.dve(lambda e: e.reciprocal(rows[0:1, 2048:3072], rows[0:1, 2048:3072]))
    if not rms:
        ph2.dve(lambda e: e.scalar_tensor_tensor(
            rows[0:1, 0:1024], rows[0:1, 0:1024], -1.0, rows[0:1, 2048:3072],
            ALU.mult, ALU.mult))
    sS.inc(1)
    ph2.dve(lambda e, sh=sS.h: e.nop().then_inc(sh, 1))
    ph2.wait("pe", sS, sS0 + 3)
    mms = []
    for tt in range(2):
        mms.append((pr.ps[tt], rows[0:1, 2048 + tt * 512: 2048 + (tt + 1) * 512]))
        if not rms:
            mms.append((pr.ps[2 + tt], rows[0:1, tt * 512:(tt + 1) * 512]))
    for i, (bank, rsrc) in enumerate(mms):
        if i == len(mms) - 1:
            sS.inc(1)
            ph2.pe(lambda e, b=bank, r=rsrc, sh=sS.h: e.matmul(
                b[:, :], cons["ones_row_f32"][:, :], r, start=True, stop=True)
                .then_inc(sh, 1))
        else:
            ph2.pe(lambda e, b=bank, r=rsrc: e.matmul(
                b[:, :], cons["ones_row_f32"][:, :], r, start=True, stop=True))
    ph2.wait("dve", sS, sS0 + 4)
    ph2.dve(lambda e: e.tensor_copy(bc_r[:, 0:512], pr.ps[0][:, :]))
    ph2.dve(lambda e: e.tensor_copy(bc_r[:, 512:1024], pr.ps[1][:, :]))
    if not rms:
        ph2.dve(lambda e: e.tensor_copy(bc_m[:, 0:512], pr.ps[2][:, :]))
        ph2.dve(lambda e: e.tensor_copy(bc_m[:, 512:1024], pr.ps[3][:, :]))

    ph3 = pr.phase(f"ln_{tag}_norm")
    for ch in range(NCH):
        ph3.dve(lambda e, c=ch: e.tensor_tensor(
            tmp[:, :], x_s[:, c * 1024:(c + 1) * 1024], bc_r[:, :], ALU.mult))
        if rms:
            ph3.dve(lambda e, c=ch: e.tensor_scalar(
                xn_s[:, c * 1024:(c + 1) * 1024], tmp[:, :],
                rms_w[:, c:c + 1], None, ALU.mult))
        else:
            ph3.dve(lambda e, c=ch: e.tensor_tensor(
                xn_s[:, c * 1024:(c + 1) * 1024], tmp[:, :], bc_m[:, :], ALU.add))


def alloc_consts(pr):
    nc = pr.nc
    cons = {}
    cons["ones_col_bf"] = sbuf(nc, "ones_col_bf", [128, 1], BF16)
    cons["ones_row_bf"] = sbuf(nc, "ones_row_bf", [1, 128], BF16)
    cons["ones_row_f32"] = sbuf(nc, "ones_row_f32", [1, 128], F32)
    cons["ident_bf"] = sbuf(nc, "ident_bf", [128, 128], BF16)
    cons["ident_f32"] = sbuf(nc, "ident_f32", [128, 128], F32)
    cons["ones_col_f32"] = sbuf(nc, "ones_col_f32", [128, 1], F32)
    cons["cvals"] = sbuf(nc, "cvals", [1, 8], F32)
    it_cv = nc.inline_tensor(np.array([[1e-5, 1e-6, 0, 0, 0, 0, 0, 0]], np.float32), name="it_cv")
    it_onec = nc.inline_tensor(np.ones((128, 1), np.float32), name="it_onec")
    it_oner = nc.inline_tensor(np.ones((1, 128), np.float32), name="it_oner")
    it_id = nc.inline_tensor(np.eye(128, dtype=np.float32), name="it_id")
    ph = pr.phase("consts")
    ds = pr.sem("dmap")
    for dst, src in ((cons["ones_col_bf"], it_onec), (cons["ones_row_bf"], it_oner),
                     (cons["ones_row_f32"], it_oner), (cons["ident_bf"], it_id),
                     (cons["ident_f32"], it_id), (cons["ones_col_f32"], it_onec),
                     (cons["cvals"], it_cv)):
        ds.inc(16)
        ph.pool(lambda e, d=dst, s=src, sh=ds.h: e.dma_start(d[:, :], s[:, :]).then_inc(sh, 16))
    ph.wait("pool", ds, ds.v)
    return cons


def load_stream(pr, ph, x_dram, x_s, dsem):
    src = bass.AP(x_dram, 0, [[1024, 128], [128 * 1024, 16], [1, 1024]])
    dst = bass.AP(x_s, 0, [[16 * 1024, 128], [1024, 16], [1, 1024]])
    dsem.inc(16)
    ph.sp(lambda e, s=src, d=dst, sh=dsem.h: e.dma_start(d, s).then_inc(sh, 16))


def store_stream(pr, ph, x_s, y_dram, dsem):
    src = bass.AP(x_s, 0, [[16 * 1024, 128], [1024, 16], [1, 1024]])
    dst = bass.AP(y_dram, 0, [[1024, 128], [128 * 1024, 16], [1, 1024]])
    dsem.inc(16)
    ph.sp(lambda e, s=src, d=dst, sh=dsem.h: e.dma_start(d, s).then_inc(sh, 16))


def emit_l0a(pr, cons, x_in, y_out, P=""):
    nc = pr.nc
    wq = nc.dram_tensor(P + "wq", [D, D], F32, kind="ExternalInput")
    wk = nc.dram_tensor(P + "wk", [D, 512], F32, kind="ExternalInput")
    wv = nc.dram_tensor(P + "wv", [D, 512], F32, kind="ExternalInput")
    wo = nc.dram_tensor(P + "wo", [D, D], F32, kind="ExternalInput")
    qcos = nc.dram_tensor(P + "qcos", [128, TPC], F32, kind="ExternalInput")
    qsin = nc.dram_tensor(P + "qsin", [128, TPC], F32, kind="ExternalInput")
    kcos = nc.dram_tensor(P + "kcos", [128, TPC], F32, kind="ExternalInput")
    ksin = nc.dram_tensor(P + "ksin", [128, TPC], F32, kind="ExternalInput")
    cmask = nc.dram_tensor(P + "cmask", [512, 512], F32, kind="ExternalInput")
    meta = nc.dram_tensor(P + "meta", [1, 128], I32, kind="ExternalInput")
    pay_i = nc.dram_tensor(P + "pay_i", [PAYR, TPC], BF16)
    pay_o = nc.dram_tensor(P + "pay_o", [4 * PAYR, TPC], BF16)
    opay_i = nc.dram_tensor(P + "opay_i", [512, S], BF16)
    opay_o = nc.dram_tensor(P + "opay_o", [2048, S], BF16)

    dsem = pr.sem("dma")
    ccs = pr.sem("cc")

    x_s = pr.ablock(P + "x_s", [128, NCH * 1024], F32)
    xn_s = pr.ablock(P + "xn_s", [128, NCH * 1024], BF16)
    qro = pr.ablock(P + "qro", [128, NCH * 1024], BF16)
    kro = pr.ablock(P + "kro", [128, 4 * 1024], BF16)
    vts = pr.ablock(P + "vts", [128, 4 * 1024], BF16)
    ropes = {}
    for nm in ("qcos", "qsin", "kcos", "ksin"):
        ropes[nm] = pr.ablock(P + f"r_{nm}", [128, TPC], BF16)
    cmask_s = pr.ablock(P + "cmask_s", [128, 4 * 512], F32)
    meta_s = pr.ablock(P + "meta_s", [1, 128], I32)
    normrow = pr.ablock(P + "normrow", [1, 1024], BF16)
    rsc = [pr.tmpF[:, 0:512], pr.tmpF[:, 512:1024]]

    ph = pr.phase("load")
    dmap = pr.sem("dmap")
    load_stream(pr, ph, x_in, x_s, dsem)
    for nm, t in (("qcos", qcos), ("qsin", qsin), ("kcos", kcos), ("ksin", ksin)):
        dmap.inc(16)
        ph.pool(lambda e, d=ropes[nm], s=t, sh=dmap.h: e.dma_start(d[:, :], s[:, :]).then_inc(sh, 16))
    csrc = bass.AP(cmask, 0, [[512, 128], [128 * 512, 4], [1, 512]])
    cdst = bass.AP(cmask_s, 0, [[4 * 512, 128], [512, 4], [1, 512]])
    dsem.inc(16)
    ph.sp(lambda e, s=csrc, d=cdst, sh=dsem.h: e.dma_start(d, s).then_inc(sh, 16))
    dsem.inc(16)
    ph.sp(lambda e, sh=dsem.h: e.dma_start(meta_s[:, :], meta[:, :]).then_inc(sh, 16))
    ph.wait("sp", dsem, dsem.v)
    ph.wait("pool", dmap, dmap.v)

    emit_ln(pr, x_s, xn_s, cons, "l0")

    def rope_consumer(out_s, costab, sintab):
        def consumer(ph_, fg, tt, pt, pg, pgtgt, pf):
            ph_.wait("dve", pg, pgtgt)
            co = costab[:, tt * 512:(tt + 1) * 512]
            si = sintab[:, tt * 512:(tt + 1) * 512]
            ocol = fg * 1024 + tt * 512

            def f(e, pt=pt, co=co, si=si, oc=ocol):
                e.tensor_tensor(rsc[0], pt[:, :], co, ALU.mult)
                e.tensor_tensor(rsc[1][0:64, :], pt[64:128, :], si[0:64, :], ALU.mult)
                e.tensor_tensor(rsc[1][64:128, :], pt[0:64, :], si[64:128, :], ALU.mult)
                e.tensor_tensor(out_s[:, oc:oc + 512], rsc[0], rsc[1], ALU.add)
            ph_.dve(f)
            pf.inc(1)
            ph_.dve(lambda e, s=pf.h: e.nop().then_inc(s, 1))
        return consumer

    def plain_consumer(out_s):
        def consumer(ph_, fg, tt, pt, pg, pgtgt, pf):
            ph_.wait("dve", pg, pgtgt)
            ocol = fg * 1024 + tt * 512
            ph_.dve(lambda e, pt=pt, oc=ocol: e.tensor_copy(out_s[:, oc:oc + 512], pt[:, :]))
            pf.inc(1)
            ph_.dve(lambda e, s=pf.h: e.nop().then_inc(s, 1))
        return consumer

    emit_proj(pr, pr.phase("proj_q"), wq, D, xn_s,
              rope_consumer(qro, ropes["qcos"], ropes["qsin"]), packed=True)
    emit_proj(pr, pr.phase("proj_k"), wk, 512, xn_s,
              rope_consumer(kro, ropes["kcos"], ropes["ksin"]), packed=True)
    emit_proj(pr, pr.phase("proj_v"), wv, 512, xn_s, plain_consumer(vts), packed=True)

    php3 = pr.phase("payload")
    for src, dstoff, nch, rowlen in ((qro, PAY_Q, 16, NCH * 1024),
                                     (kro, PAY_K, 4, 4 * 1024),
                                     (vts, PAY_V, 4, 4 * 1024)):
        s_ap = bass.AP(src, 0, [[rowlen, 128], [1024, nch], [1, 1024]])
        d_ap = bass.AP(pay_i, dstoff * TPC, [[1024, 128], [128 * 1024, nch], [1, 1024]])
        dsem.inc(16)
        php3.sp(lambda e, s=s_ap, d=d_ap, sh=dsem.h: e.dma_start(d, s).then_inc(sh, 16))
    php3.wait("sp", dsem, dsem.v)

    phg = pr.phase("gather_in")
    ccs.inc(1)
    phg.pool(lambda e, sh=ccs.h: e.collective_compute(
        "AllGather", ALU.bypass, replica_groups=[[0, 1, 2, 3], [4, 5, 6, 7]],
        ins=[pay_i.ap()], outs=[pay_o.ap()]).then_inc(sh, 1))
    phg.wait("pool", ccs, ccs.v)

    pha = pr.phase("attn")
    pha.uses_reg = True
    kT = qro
    KOF, VOF, VTOK, QOF = 0, 4096, 8192, 12288
    pt_s = [pr.ablock(P + f"pt{i}", [128, 512], BF16) for i in range(2)]
    es_s = pr.bcA[:, 0:512]
    nbf = pr.ablock(P + "nbf", [1, 512], BF16)
    bcS = pr.bcA[:, 512:1024]
    outt = pr.ablock(P + "outt", [128, 512], BF16)
    sqk = pr.scbf[0]
    rowb = pr.rowb

    sLD = pr.sem("aload")
    sSC = pr.sem("asc")
    sE = pr.sem("aE")
    sP = pr.sem("aP")
    sAV = pr.sem("aAV")
    sVT = pr.sem("aVT")
    sVE = pr.sem("aVE")
    sEP = pr.sem("aEP")
    sBC = pr.sem("aBC")
    sQQ = pr.sem("aQQ")
    sBN = pr.sem("aBN")
    aq = [pr.sem("aq0"), pr.sem("aq1")]

    def pool_kv(e):
        r0 = e._shared_reg
        e.reg_load(r0, meta_s[0:1, 0:1])
        e.dma_start(
            bass.AP(kT, KOF, [[NCH * 1024, 128], [1024, 4], [1, 1024]]),
            bass.AP(pay_o, r0, [[1024, 128], [PAYR * 1024, 4], [1, 1024]])
        ).then_inc(sLD.h, 16)
        e.reg_add(r0, r0, 512 * 1024)
        e.dma_start(
            bass.AP(qro, VOF, [[NCH * 1024, 128], [1024, 4], [1, 1024]]),
            bass.AP(pay_o, r0, [[1024, 128], [PAYR * 1024, 4], [1, 1024]])
        ).then_inc(sLD.h, 16)
    pha.pool(pool_kv)
    sLD.inc(32)
    unit_ld = sLD.v

    pha.wait("pe", sLD, unit_ld)
    for ti in range(32):
        par = ti % 2
        if ti >= 2:
            pha.wait("pe", sVE, sVE.v - 1)
        sVT.inc(1)
        pha.pe(lambda e, t=ti, p=par, sh=sVT.h: e.matmul(
            pr.ps[2 + p][:, 0:128], qro[:, VOF + t * 128: VOF + (t + 1) * 128],
            cons["ident_bf"][:, :], start=True, stop=True).then_inc(sh, 1))
        pha.wait("dve", sVT, sVT.v)
        sVE.inc(1)
        pha.dve(lambda e, t=ti, p=par, sh=sVE.h: e.tensor_copy(
            qro[:, VTOK + t * 128: VTOK + (t + 1) * 128],
            pr.ps[2 + p][:, 0:128]).then_inc(sh, 1))

    pha.wait("act", sLD, unit_ld)
    sEP0k = sEP.v
    for rb in range(4):
        sQQ.inc(1)
        pha.act(lambda e, rb=rb, sh=sQQ.h: e.activation(
            sqk[:, :], kT[:, KOF + rb * 1024: KOF + (rb + 1) * 1024],
            AF.Square).then_inc(sh, 1))
        pha.wait("pe", sQQ, sQQ.v)
        for tt in range(2):
            i = rb * 2 + tt
            if i > 0:
                pha.wait("pe", sEP, sEP0k + i)
            sBC.inc(1)
            pha.pe(lambda e, t=tt, sh=sBC.h: e.matmul(
                pr.ps[6][0:1, :], cons["ones_col_bf"][:, :],
                sqk[:, t * 512:(t + 1) * 512], start=True, stop=True).then_inc(sh, 1))
            pha.wait("dve", sBC, sBC.v)
            sEP.inc(1)
            pha.dve(lambda e, i=i, sh=sEP.h: e.tensor_reduce(
                rowb[0:1, 1027 + i:1028 + i], pr.ps[6][0:1, :], AX.X, ALU.max)
                .then_inc(sh, 1))
        if rb < 3:
            pha.wait("act", sBC, sBC.v)
    pha.wait("dve", sEP, sEP.v)
    pha.dve(lambda e: e.tensor_reduce(rowb[0:1, 1024:1025], rowb[0:1, 1027:1035],
                                      AX.X, ALU.max))
    pha.dve(lambda e: e.tensor_scalar(rowb[0:1, 1026:1027], rowb[0:1, 1024:1025],
                                      -1.0 / 22.62742, None, ALU.mult))
    sBN.inc(1)
    pha.dve(lambda e, sh=sBN.h: e.nop().then_inc(sh, 1))
    bounds_ready = sBN.v

    qld_free = [None]
    sqq = pr.scbf[1]
    for h in range(4):
        for st in range(8):
            it = h * 8 + st
            if st == 0:
                aq[0].inc(16)

                def load_q(e, h=h, w=qld_free[0]):
                    r2 = e._shared_reg
                    if w is not None:
                        e.wait_ge(sAV.h, w)
                    e.reg_load(r2, meta_s[0:1, 4 + h * 4: 5 + h * 4])
                    e.dma_start(
                        bass.AP(qro, QOF, [[NCH * 1024, 128], [1024, 4], [1, 1024]]),
                        bass.AP(pay_o, r2, [[1024, 128], [PAYR * 1024, 4], [1, 1024]])
                    ).then_inc(aq[0].h, 16)
                pha.pool(load_q)
            q_tgt = aq[0].v
            qcols = qro[:, QOF + st * 512: QOF + st * 512 + 512]

            pha.wait("act", aq[0], q_tgt)
            if it > 0:
                pha.wait("act", sSC, sSC.v)
            sQQ.inc(1)
            pha.act(lambda e, qc=qcols, sh=sQQ.h: e.activation(
                sqq[:, 0:512], qc, AF.Square).then_inc(sh, 1))
            pha.wait("pe", sQQ, sQQ.v)
            pha.wait("pe", sEP, sEP.v)
            sBC.inc(1)
            pha.pe(lambda e, sh=sBC.h: e.matmul(
                pr.ps[6][0:1, :], cons["ones_col_bf"][:, :], sqq[:, 0:512],
                start=True, stop=True).then_inc(sh, 1))
            pha.wait("dve", sBN, bounds_ready)
            pha.wait("dve", sBC, sBC.v)
            pha.dve(lambda e: e.tensor_scalar(
                rowb[0:1, 0:512], pr.ps[6][0:1, :],
                rowb[0:1, 1026:1027], -5.65685, ALU.mult, ALU.add))
            pha.dve(lambda e: e.tensor_copy(nbf[0:1, :], rowb[0:1, 0:512]))
            sEP.inc(1)
            pha.dve(lambda e, sh=sEP.h: e.nop().then_inc(sh, 1))
            negb_tgt = sEP.v

            n_tt = 4 * (st + 1)
            sAV0 = sAV.v
            sP0 = sP.v
            for tt in range(n_tt):
                par = tt % 2
                pha.wait("pe", aq[0], q_tgt)
                if tt == 0:
                    pha.wait("pe", sEP, negb_tgt)
                if tt >= 2:
                    pha.wait("pe", sP, sP0 + tt - 1)
                pha.pe(lambda e, t=tt, p=par, qc=qcols: e.matmul(
                    pr.ps[p][:, :], kT[:, KOF + t * 128: KOF + (t + 1) * 128],
                    qc, start=True, stop=False))
                sSC.inc(1)
                pha.pe(lambda e, p=par, sh=sSC.h: e.matmul(
                    pr.ps[p][:, :], cons["ones_row_bf"][:, :], nbf[0:1, :],
                    start=False, stop=True).then_inc(sh, 1))
                diag = (tt >= 4 * st)
                if diag:
                    kk = tt - 4 * st
                    pha.wait("dve", sSC, sSC.v)
                    pha.wait("dve", sP, sP0 + tt)
                    sE.inc(1)
                    pha.dve(lambda e, p=par, k=kk, sh=sE.h: e.tensor_tensor(
                        es_s, pr.ps[p][:, :],
                        cmask_s[:, k * 512:(k + 1) * 512], ALU.add).then_inc(sh, 1))
                    pha.wait("act", sE, sE.v)
                    if tt >= 2:
                        pha.wait("act", sAV, sAV0 + tt - 1)
                    sP.inc(1)
                    pha.act(lambda e, p2=par, sh=sP.h: e.activation(
                        pt_s[p2][:, :], es_s, AF.Exp).then_inc(sh, 1))
                else:
                    pha.wait("act", sSC, sSC.v)
                    if tt >= 2:
                        pha.wait("act", sAV, sAV0 + tt - 1)
                    sP.inc(1)
                    pha.act(lambda e, p=par, p2=par, sh=sP.h: e.activation(
                        pt_s[p2][:, :], pr.ps[p][:, :], AF.Exp).then_inc(sh, 1))
                pha.wait("pe", sP, sP.v)
                pha.pe(lambda e, t=tt, p2=par, st_=(tt == 0), sp2=(tt == n_tt - 1):
                       e.matmul(
                    pr.ps[4][:, 0:512], qro[:, VTOK + t * 128: VTOK + (t + 1) * 128],
                    pt_s[p2][:, :], start=st_, stop=sp2))
                sAV.inc(1)
                pha.pe(lambda e, p2=par, st_=(tt == 0), sp_=(tt == n_tt - 1), sh=sAV.h:
                       e.matmul(pr.psm[0:1, :], cons["ones_col_bf"][:, :],
                                pt_s[p2][:, :], start=st_, stop=sp_).then_inc(sh, 1))
            if st == 7:
                qld_free[0] = sAV.v

            pha.wait("dve", sAV, sAV.v)
            pha.dve(lambda e: e.tensor_scalar(rowb[0:1, 512:1024], pr.psm[0:1, :],
                                              1e-30, None, ALU.max))
            pha.dve(lambda e: e.reciprocal(rowb[0:1, 512:1024], rowb[0:1, 512:1024]))
            sEP.inc(1)
            pha.dve(lambda e, sh=sEP.h: e.nop().then_inc(sh, 1))
            pha.wait("pe", sEP, sEP.v)
            sBC.inc(1)
            pha.pe(lambda e, sh=sBC.h: e.matmul(
                pr.ps[6][:, :], cons["ones_row_f32"][:, :], rowb[0:1, 512:1024],
                start=True, stop=True).then_inc(sh, 1))
            pha.wait("dve", sBC, sBC.v)
            pha.dve(lambda e: e.tensor_copy(bcS, pr.ps[6][:, :]))
            pha.wait("dve", dsem, dsem.v)
            pha.dve(lambda e: e.tensor_tensor(outt[:, :], pr.ps[4][:, 0:512],
                                              bcS, ALU.mult))
            sEP.inc(1)
            pha.dve(lambda e, sh=sEP.h: e.nop().then_inc(sh, 1))
            out_tgt = sEP.v
            pha.wait("sp", sEP, out_tgt)
            osrc = bass.AP(outt, 0, [[512, 128], [1, 512]])
            odst = bass.AP(opay_i, h * 128 * S + st * 512, [[S, 128], [1, 512]])
            dsem.inc(16)
            pha.sp(lambda e, s1=osrc, d1=odst, sh=dsem.h: e.dma_start(d1, s1).then_inc(sh, 16))
            pha.wait("pe", sEP, out_tgt)

    pha.wait("sp", dsem, dsem.v)

    phg2 = pr.phase("gather_out")
    ccs.inc(1)
    phg2.pool(lambda e, sh=ccs.h: e.collective_compute(
        "AllGather", ALU.bypass, replica_groups=[[0, 1, 2, 3], [4, 5, 6, 7]],
        ins=[opay_i.ap()], outs=[opay_o.ap()]).then_inc(sh, 1))
    phg2.wait("pool", ccs, ccs.v)

    phr = pr.phase("read_out")
    phr.uses_reg = True
    ao = qro

    def read_own(e):
            r3 = e._shared_reg
            e.reg_load(r3, meta_s[0:1, 60:61])
            e.dma_start(bass.AP(ao, 0, [[NCH * 1024, 128], [1024, 16], [1, 1024]]),
                        bass.AP(opay_o, r3, [[S, 128], [128 * S, 16], [1, 1024]])).then_inc(sLD.h, 16)
    phr.pool(read_own)
    sLD.inc(16)
    phr.wait("pool", sLD, sLD.v)

    def resid_consumer(ph_, fg, tt, pt, pg, pgtgt, pf):
        ph_.wait("dve", pg, pgtgt)
        col = fg * 1024 + tt * 512
        ph_.dve(lambda e, pt=pt, c=col: e.tensor_tensor(
            x_s[:, c:c + 512], x_s[:, c:c + 512], pt[:, :], ALU.add))
        pf.inc(1)
        ph_.dve(lambda e, s=pf.h: e.nop().then_inc(s, 1))

    emit_proj(pr, pr.phase("proj_o"), wo, D, ao, resid_consumer, packed=True)

    phs = pr.phase("store")
    store_stream(pr, phs, x_s, y_out, dsem)
    phs.wait("sp", dsem, dsem.v)
    pr.arena_reset()


def build_l0a(upto=None):
    pr = Prog("l0a")
    nc = pr.nc
    x_in = nc.dram_tensor("x", [D, TPC], F32, kind="ExternalInput")
    y_out = nc.dram_tensor("y", [D, TPC], F32, kind="ExternalOutput")
    cons = alloc_consts(pr)
    pr.init_arena()
    emit_l0a(pr, cons, x_in, y_out)
    if upto is not None:
        names = [p.name for p in pr.phases]
        idx = names.index(upto)
        pr.phases = pr.phases[:idx + 1]
        tr = pr.sem("trunc")
        phs2 = pr.phase("store_trunc")
        x_s = nc.find_inst_by_name and None
        pr.phases = pr.phases
    pr.emit()
    return pr.nc


def rope_tables():
    hd = L0_HD
    freqs = 1.0 / (10000.0 ** (np.arange(0, hd, 2, dtype=np.float64) / hd))
    t = np.arange(S, dtype=np.float64)
    ang = t[:, None] * freqs[None, :]
    cos = np.concatenate([np.cos(ang), np.cos(ang)], axis=1)
    sin = np.sin(ang)
    ssin = np.concatenate([-sin, sin], axis=1)
    return cos.T.astype(np.float32), ssin.T.astype(np.float32)


def causal_masks():
    m = np.zeros((4, 128, 512), np.float32)
    ti = np.arange(128)[:, None]
    sj = np.arange(512)[None, :]
    for k in range(4):
        m[k] = np.where(ti + 128 * k <= sj, 0.0, NEG)
    return np.ascontiguousarray(m.reshape(512, 512))


def l0a_meta(core):
    kvg = core % 4
    m = np.zeros(128, np.int32)
    m[0] = (PAY_K + kvg * 128) * TPC
    for h in range(4):
        for rb in range(4):
            m[4 + h * 4 + rb] = (rb * PAYR + PAY_Q + (4 * kvg + h) * 128) * TPC
    m[60] = (core % 4) * TPC
    return m.reshape(1, 128)


_NC_CACHE = {}


def get_nc(name, builder):
    if name not in _NC_CACHE:
        _NC_CACHE[name] = builder()
    return _NC_CACHE[name]


def maps_l0a(inputs, c, P=""):
    cos, ssin = rope_tables()
    cm = causal_masks()
    scale = 1.0 / np.sqrt(float(L0_HD))
    t0 = (c % 4) * TPC
    return {
        P + "wq": pack_proj(np.asarray(inputs["L0_attn_wq"], np.float32),
                            NCH, D).reshape(D, D),
        P + "wk": pack_proj(np.asarray(inputs["L0_attn_wk"], np.float32),
                            NCH, 512).reshape(D, 512),
        P + "wv": pack_proj(np.asarray(inputs["L0_attn_wv"], np.float32),
                            NCH, 512).reshape(D, 512),
        P + "wo": pack_proj(np.asarray(inputs["L0_attn_wo"], np.float32),
                            NCH, D).reshape(D, D),
        P + "qcos": np.ascontiguousarray(cos[:, t0:t0 + TPC] * scale),
        P + "qsin": np.ascontiguousarray(ssin[:, t0:t0 + TPC] * scale),
        P + "kcos": np.ascontiguousarray(cos[:, t0:t0 + TPC]),
        P + "ksin": np.ascontiguousarray(ssin[:, t0:t0 + TPC]),
        P + "cmask": cm,
        P + "meta": l0a_meta(c),
    }


def run_l0a(x_shards, inputs, upto=None):
    nc = get_nc(f"l0a{upto}", lambda: build_l0a(upto))
    in_maps = [dict(x=x_shards[c], **maps_l0a(inputs, c)) for c in range(NCORE)]
    res = run_bass_kernel_spmd(nc, in_maps, core_ids=list(range(NCORE)))
    return [res.results[c]["y"] for c in range(NCORE)]


def shard_stream(x):
    out = []
    for c in range(NCORE):
        b, r = c // 4, c % 4
        out.append(np.ascontiguousarray(x[b, r * TPC:(r + 1) * TPC, :].T.astype(np.float32)))
    return out


def unshard_stream(shards):
    x = np.empty((B, S, D), np.float32)
    for c in range(NCORE):
        b, r = c // 4, c % 4
        x[b, r * TPC:(r + 1) * TPC, :] = shards[c].T
    return x


def kernel(**inputs):
    return kernel_fused(inputs)


def emit_l0f(pr, cons, x_in, y_out, P="", only=None):
    nc = pr.nc
    gwT = nc.dram_tensor(P + "gwT", [D, 8], F32, kind="ExternalInput")
    wg3 = nc.dram_tensor(P + "wg3", [8 * D, 1408], F32, kind="ExternalInput")
    wu3 = nc.dram_tensor(P + "wu3", [8 * D, 1408], F32, kind="ExternalInput")
    wd3 = nc.dram_tensor(P + "wd3", [8 * 1408, D], F32, kind="ExternalInput")
    shg = nc.dram_tensor(P + "shg", [128, 16], F32, kind="ExternalInput")
    shwg = nc.dram_tensor(P + "shwg", [D, 2816], F32, kind="ExternalInput")
    shwu = nc.dram_tensor(P + "shwu", [D, 2816], F32, kind="ExternalInput")
    shwd = nc.dram_tensor(P + "shwd", [2816, D], F32, kind="ExternalInput")
    dsem = pr.sem("dma")
    dmap = pr.sem("dmap")

    x_s = pr.ablock(P + "x_s", [128, NCH * 1024], F32)
    xn_s = pr.ablock(P + "xn_s", [128, NCH * 1024], BF16)
    hb = pr.ablock(P + "hb", [128, 22 * 512], BF16)
    y_acc = pr.ablock(P + "y_acc", [128, 16 * 512], F32)
    combrow = pr.ablock(P + "combrow", [1, 9 * 1024], BF16)
    gwT_s = pr.ablock(P + "gwT_s", [128, 16 * 8], F32)
    shg_s = pr.ablock(P + "shg_s", [128, 16], BF16)
    lgt = pr.ablock(P + "lgt", [128, 64], F32)
    cb_tok = pr.ablock(P + "cb_tok", [128, 8], F32)

    ph = pr.phase("load")
    load_stream(pr, ph, x_in, x_s, dsem)
    gsrc = bass.AP(gwT, 0, [[8, 128], [128 * 8, 16], [1, 8]])
    gdst = bass.AP(gwT_s, 0, [[128, 128], [8, 16], [1, 8]])
    dsem.inc(16)
    ph.sp(lambda e, s=gsrc, d=gdst, sh=dsem.h: e.dma_start(d, s).then_inc(sh, 16))
    dmap.inc(16)
    ph.pool(lambda e, sh=dmap.h: e.dma_start(shg_s[:, :], shg[:, :]).then_inc(sh, 16))
    ph.wait("sp", dsem, dsem.v)
    ph.wait("pool", dmap, dmap.v)

    emit_ln(pr, x_s, xn_s, cons, "l0f")
    rstd_row = pr.rowb[0:1, 2048:3072]
    negm_row = pr.rowb[0:1, 0:1024]

    phr = pr.phase("router")
    rtP = pr.sem("rtP")
    rtD = pr.sem("rtD")
    rtA = pr.sem("rtA")
    for dc in range(NCH):
        st, sp_ = dc == 0, dc == NCH - 1
        if sp_:
            rtP.inc(1)
            phr.pe(lambda e, dc=dc, sh=rtP.h: e.matmul(
                pr.ps[5][0:1, 0:8], cons["ones_col_f32"][:, :],
                gwT_s[:, dc * 8:(dc + 1) * 8], start=dc == 0, stop=True).then_inc(sh, 1))
        else:
            phr.pe(lambda e, dc=dc, s0=st: e.matmul(
                pr.ps[5][0:1, 0:8], cons["ones_col_f32"][:, :],
                gwT_s[:, dc * 8:(dc + 1) * 8], start=s0, stop=False))
    sg_done = []
    for t2 in range(2):
        for dc in range(NCH):
            sp_ = dc == NCH - 1
            if sp_:
                rtP.inc(1)
                phr.pe(lambda e, dc=dc, t2=t2, sh=rtP.h: e.matmul(
                    pr.ps[6][0:1, t2 * 256: t2 * 256 + 512] if False else pr.ps[6][0:1, 0:512],
                    shg_s[:, dc:dc + 1],
                    xn_s[:, dc * 1024 + t2 * 512: dc * 1024 + t2 * 512 + 512],
                    start=dc == 0, stop=True).then_inc(sh, 1))
            else:
                phr.pe(lambda e, dc=dc, t2=t2: e.matmul(
                    pr.ps[6][0:1, 0:512], shg_s[:, dc:dc + 1],
                    xn_s[:, dc * 1024 + t2 * 512: dc * 1024 + t2 * 512 + 512],
                    start=dc == 0, stop=False))
        phr.wait("act", rtP, rtP.v)
        rtA.inc(1)
        phr.act(lambda e, t2=t2, sh=rtA.h: e.activation(
            combrow[0:1, 8 * 1024 + t2 * 512: 8 * 1024 + (t2 + 1) * 512],
            pr.ps[6][0:1, 0:512], AF.Sigmoid).then_inc(sh, 1))
        sg_done.append(rtA.v)
        phr.wait("pe", rtA, rtA.v)

    phr.wait("dve", rtP, 1)
    rtD.inc(1)
    phr.dve(lambda e, sh=rtD.h: e.tensor_copy(
        lgt[0:1, 48:56], pr.ps[5][0:1, 0:8]).then_inc(sh, 1))

    for ts in range(8):
        for dc in range(NCH):
            sp_ = dc == NCH - 1
            lhs = x_s[:, dc * 1024 + ts * 128: dc * 1024 + ts * 128 + 128]
            if sp_:
                rtP.inc(1)
                phr.pe(lambda e, l=lhs, dc=dc, sh=rtP.h: e.matmul(
                    pr.ps[0][:, 0:8], l, gwT_s[:, dc * 8:(dc + 1) * 8],
                    start=dc == 0, stop=True).then_inc(sh, 1))
            else:
                phr.pe(lambda e, l=lhs, dc=dc: e.matmul(
                    pr.ps[0][:, 0:8], l, gwT_s[:, dc * 8:(dc + 1) * 8],
                    start=dc == 0, stop=False))
        if ts == 0:
            phr.wait("pe", rtD, rtD.v)
        rtP.inc(1)
        phr.pe(lambda e, ts=ts, sh=rtP.h: e.matmul(
            pr.ps[1][:, 0:8], negm_row[0:1, ts * 128:(ts + 1) * 128],
            lgt[0:1, 48:56], start=True, stop=True).then_inc(sh, 1))
        rtP.inc(1)
        phr.pe(lambda e, ts=ts, sh=rtP.h: e.matmul(
            pr.ps[2][:, 0:1], rstd_row[0:1, ts * 128:(ts + 1) * 128],
            cons["ones_row_f32"][0:1, 0:1], start=True, stop=True).then_inc(sh, 1))
        pe_tgt = rtP.v
        phr.wait("dve", rtP, pe_tgt)
        rtD.inc(1)

        def rmath(e, ts=ts, sh=rtD.h):
            e.tensor_copy(lgt[:, 37:38], pr.ps[2][:, 0:1])
            e.tensor_copy(lgt[:, 40:48], pr.ps[0][:, 0:8])
            e.scalar_tensor_tensor(lgt[:, 0:8], lgt[:, 40:48], lgt[:, 37:38],
                                   pr.ps[1][:, 0:8], ALU.mult, ALU.add)
            e.tensor_reduce(lgt[:, 32:33], lgt[:, 0:8], AX.X, ALU.max)
            e.tensor_scalar(lgt[:, 8:16], lgt[:, 0:8], lgt[:, 32:33], None, ALU.is_equal)
            e.scalar_tensor_tensor(lgt[:, 16:24], lgt[:, 8:16], NEG, lgt[:, 0:8],
                                   ALU.mult, ALU.add)
            e.tensor_reduce(lgt[:, 33:34], lgt[:, 16:24], AX.X, ALU.max)
            e.tensor_tensor(lgt[:, 34:35], lgt[:, 32:33], lgt[:, 33:34], ALU.subtract)
            e.tensor_scalar(lgt[:, 24:32], lgt[:, 0:8], lgt[:, 33:34], None, ALU.is_equal).then_inc(sh, 1)
        phr.dve(rmath)
        phr.wait("act", rtD, rtD.v)
        rtA.inc(1)
        phr.act(lambda e, sh=rtA.h: e.activation(
            lgt[:, 35:36], lgt[:, 34:35], AF.Sigmoid).then_inc(sh, 1))
        phr.wait("dve", rtA, rtA.v)
        rtD.inc(1)

        def rmath2(e, sh=rtD.h):
            e.tensor_scalar(lgt[:, 36:37], lgt[:, 35:36], -1.0, 1.0, ALU.mult, ALU.add)
            e.tensor_scalar(cb_tok[:, 0:8], lgt[:, 8:16], lgt[:, 35:36], None, ALU.mult)
            e.tensor_scalar(lgt[:, 24:32], lgt[:, 24:32], lgt[:, 36:37], None, ALU.mult)
            e.tensor_tensor(cb_tok[:, 0:8], cb_tok[:, 0:8], lgt[:, 24:32], ALU.add).then_inc(sh, 1)
        phr.dve(rmath2)
        phr.wait("pe", rtD, rtD.v)
        rtP.inc(1)
        phr.pe(lambda e, sh=rtP.h: e.matmul(
            pr.ps[3][0:8, 0:128], cb_tok[:, 0:8], cons["ident_f32"][:, :],
            start=True, stop=True).then_inc(sh, 1))
        phr.wait("dve", rtP, rtP.v)
        rtD.inc(1)
        phr.dve(lambda e, ts=ts, sh=rtD.h: e.tensor_copy(
            pr.bcA[0:8, ts * 128:(ts + 1) * 128], pr.ps[3][0:8, 0:128]).then_inc(sh, 1))
        phr.wait("pe", rtD, rtD.v)
    phr.wait("pool", rtD, rtD.v)
    dmap.inc(16)
    phr.pool(lambda e, sh=dmap.h: e.dma_start(
        bass.AP(combrow, 0, [[8192, 1], [1, 8192]]),
        bass.AP(pr.bcA, 0, [[1024, 8], [1, 1024]])).then_inc(sh, 16))
    phr.wait("pool", dmap, dmap.v)

    sCA = pr.sem("sCA")
    hb_free_at = [0]

    def mk_bc(ph_, e_idx, t2):
        grp = pr.grp_par % 2
        pr.grp_par += 1
        bank = pr.ps[0] if grp == 0 else pr.ps[4]
        pf = pr.sem("pf")
        ph_.wait("pe", pf, pr.pf_marks[-2])
        bcs = pr.sem("bcs")
        bcs.inc(1)
        ph_.pe(lambda e, ei=e_idx, t2=t2, b=bank, sh=bcs.h: e.matmul(
            b[:, :], cons["ones_row_bf"][:, :],
            combrow[0:1, ei * 1024 + t2 * 512: ei * 1024 + t2 * 512 + 512],
            start=True, stop=True).then_inc(sh, 1))
        ph_.wait("dve", bcs, bcs.v)
        pf.inc(1)
        ph_.dve(lambda e, b=bank, sh=pf.h: e.tensor_copy(
            pr.bcB[:, 0:512], b[:, :]).then_inc(sh, 1))
        pr.pf_marks.append(pf.v)

    def silu_consumer(hb_guard):
        def consumer(ph_, fg, tt, bank, wf, wf_tgt, pf):
            ph_.wait("act", wf, wf_tgt)
            if hb_guard[0]:
                ph_.wait("act", wf, hb_guard[0])
                hb_guard[0] = 0
            sCA.inc(1)
            ph_.act(lambda e, fg=fg, b=bank, sh=sCA.h: e.activation(
                hb[:, fg * 512:(fg + 1) * 512], b[:, :], AF.Silu).then_inc(sh, 1))
            ph_.wait("dve", sCA, sCA.v)
            pf.inc(1)
            ph_.dve(lambda e, sh=pf.h: e.nop().then_inc(sh, 1))
        return consumer

    def h_consumer(ph_, fg, tt, bank, wf, wf_tgt, pf):
        ph_.wait("dve", wf, wf_tgt)
        pf.inc(1)

        def f(e, fg=fg, b=bank, sh=pf.h):
            e.tensor_tensor(pr.tmpF[:, 0:512], b[:, :], hb[:, fg * 512:(fg + 1) * 512],
                            ALU.mult)
            e.tensor_tensor(hb[:, fg * 512:(fg + 1) * 512], pr.tmpF[:, 0:512],
                            pr.bcB[:, 0:512], ALU.mult).then_inc(sh, 1)
        ph_.dve(f)

    def yacc_consumer(first):
        def consumer(ph_, dt, bank, wf, wf_tgt, pf):
            ph_.wait("dve", wf, wf_tgt)
            pf.inc(1)
            if first:
                ph_.dve(lambda e, dt=dt, b=bank, sh=pf.h: e.tensor_copy(
                    y_acc[:, dt * 512:(dt + 1) * 512], b[:, :]).then_inc(sh, 1))
            else:
                ph_.dve(lambda e, dt=dt, b=bank, sh=pf.h: e.tensor_tensor(
                    y_acc[:, dt * 512:(dt + 1) * 512],
                    y_acc[:, dt * 512:(dt + 1) * 512], b[:, :], ALU.add).then_inc(sh, 1))
        return consumer

    experts = [("sh",)] + [(e,) for e in range(8)]
    if only is not None:
        experts = [(e,) for e in only]
    for t2 in range(2):
        phe = pr.phase(f"experts_t{t2}")
        if not hasattr(pr, "wpack_off"):
            pr.wpack_off = {}
        for w_ in (shwg, shwu, shwd, wg3, wu3, wd3):
            pr.wpack_off[w_.name] = 0
        for i, (e_id,) in enumerate(experts):
            ei = 8 if e_id == "sh" else e_id
            mk_bc(phe, ei, t2)
            hb_guard = [hb_free_at[0]]
            if e_id == "sh":
                emit_proj(pr, phe, shwg, 2816, xn_s, silu_consumer(hb_guard),
                          tts=(t2,), packed=True)
                emit_proj(pr, phe, shwu, 2816, xn_s, h_consumer, tts=(t2,), packed=True)
                emit_down(pr, phe, shwd, 22,
                          lambda hc: hb[:, hc * 512:(hc + 1) * 512],
                          yacc_consumer(i == 0), packed=True)
            else:
                emit_proj(pr, phe, wg3, 1408, xn_s, silu_consumer(hb_guard),
                          w_off=e_id * D * 1408, row_len=1408, tts=(t2,), packed=True)
                emit_proj(pr, phe, wu3, 1408, xn_s, h_consumer,
                          w_off=e_id * D * 1408, row_len=1408, tts=(t2,), packed=True)
                emit_down(pr, phe, wd3, 11,
                          lambda hc: hb[:, hc * 512:(hc + 1) * 512],
                          yacc_consumer(i == 0), w_off=e_id * 1408 * D, packed=True)
            hb_free_at[0] = pr.sem("wf").v
        pha2 = pr.phase(f"resid_t{t2}")
        for ch in range(NCH):
            pha2.dve(lambda e, ch=ch, t2=t2: e.tensor_tensor(
                x_s[:, ch * 1024 + t2 * 512: ch * 1024 + t2 * 512 + 512],
                x_s[:, ch * 1024 + t2 * 512: ch * 1024 + t2 * 512 + 512],
                y_acc[:, ch * 512:(ch + 1) * 512], ALU.add))

    phs = pr.phase("store")
    store_stream(pr, phs, x_s, y_out, dsem)
    phs.wait("sp", dsem, dsem.v)
    pr.arena_reset()


def build_l0f(upto=None, only=None):
    pr = Prog("l0f")
    nc = pr.nc
    x_in = nc.dram_tensor("x", [D, TPC], F32, kind="ExternalInput")
    y_out = nc.dram_tensor("y", [D, TPC], F32, kind="ExternalOutput")
    cons = alloc_consts(pr)
    pr.init_arena()
    emit_l0f(pr, cons, x_in, y_out, only=only)
    pr.emit()
    return pr.nc


def maps_l0f(inputs, c, P=""):
    gwT = np.ascontiguousarray(np.asarray(inputs["L0_ffn_gate_w"], np.float32).T)
    wg3 = np.ascontiguousarray(np.asarray(inputs["L0_ffn_w_gate"], np.float32).reshape(8 * D, 1408))
    wu3 = np.ascontiguousarray(np.asarray(inputs["L0_ffn_w_up"], np.float32).reshape(8 * D, 1408))
    wd3 = np.ascontiguousarray(np.asarray(inputs["L0_ffn_w_down"], np.float32).reshape(8 * 1408, D))
    shg = np.ascontiguousarray(np.asarray(inputs["L0_ffn_shgate_w"], np.float32).reshape(16, 128).T)
    wg3p = np.concatenate([pack_proj(wg3[e * D:(e + 1) * D], NCH, 1408)
                           for e in range(8)]).reshape(8 * D, 1408)
    wu3p = np.concatenate([pack_proj(wu3[e * D:(e + 1) * D], NCH, 1408)
                           for e in range(8)]).reshape(8 * D, 1408)
    wd3p = np.concatenate([pack_down(wd3, 11, row_off=e * 1408)
                           for e in range(8)]).reshape(8 * 1408, D)
    return {
        P + "gwT": gwT, P + "wg3": wg3p, P + "wu3": wu3p, P + "wd3": wd3p,
        P + "shg": shg,
        P + "shwg": pack_proj(np.asarray(inputs["L0_ffn_shw_gate"], np.float32),
                              NCH, 2816).reshape(D, 2816),
        P + "shwu": pack_proj(np.asarray(inputs["L0_ffn_shw_up"], np.float32),
                              NCH, 2816).reshape(D, 2816),
        P + "shwd": pack_down(np.asarray(inputs["L0_ffn_shw_down"], np.float32),
                              22).reshape(2816, D),
    }


def run_l0f(x_shards, inputs, upto=None):
    nc = get_nc(f"l0f{upto}", lambda: build_l0f(upto))
    in_maps = [dict(x=x_shards[c], **maps_l0f(inputs, c)) for c in range(NCORE)]
    res = run_bass_kernel_spmd(nc, in_maps, core_ids=list(range(NCORE)))
    return [res.results[c]["y"] for c in range(NCORE)]


PAY1_Q, PAY1_K, PAY1_V = 0, 1024, 2048
PAYL1 = 4096
RSQ128 = float(1.0 / np.sqrt(128.0))


def emit_l1a(pr, cons, x_in, y_out, P=""):
    nc = pr.nc
    wq = nc.dram_tensor(P + "wq", [D, 1024], F32, kind="ExternalInput")
    wk = nc.dram_tensor(P + "wk", [D, 1024], F32, kind="ExternalInput")
    wv = nc.dram_tensor(P + "wv", [D, D], F32, kind="ExternalInput")
    wig = nc.dram_tensor(P + "wig", [D, 8], F32, kind="ExternalInput")
    wfg = nc.dram_tensor(P + "wfg", [D, 8], F32, kind="ExternalInput")
    bigc = nc.dram_tensor(P + "bigc", [8, 2], F32, kind="ExternalInput")
    wog = nc.dram_tensor(P + "wog", [D, D], F32, kind="ExternalInput")
    wout = nc.dram_tensor(P + "wout", [D, D], F32, kind="ExternalInput")
    wgate = nc.dram_tensor(P + "wgate", [D, 8192], F32, kind="ExternalInput")
    wup = nc.dram_tensor(P + "wup", [D, 8192], F32, kind="ExternalInput")
    wdown = nc.dram_tensor(P + "wdown", [8192, D], F32, kind="ExternalInput")
    n1c = nc.dram_tensor(P + "n1c", [128, 16], F32, kind="ExternalInput")
    n2c = nc.dram_tensor(P + "n2c", [128, 16], F32, kind="ExternalInput")
    cmask = nc.dram_tensor(P + "cmask", [512, 512], F32, kind="ExternalInput")
    meta = nc.dram_tensor(P + "meta", [1, 128], I32, kind="ExternalInput")
    pay_i = nc.dram_tensor(P + "pay_i", [PAYL1, TPC], BF16)
    pay_o = nc.dram_tensor(P + "pay_o", [4 * PAYL1, TPC], BF16)
    pg_i = nc.dram_tensor(P + "pg_i", [16, TPC], F32)
    pg_o = nc.dram_tensor(P + "pg_o", [64, TPC], F32)
    opay_i = nc.dram_tensor(P + "opay_i", [512, S], BF16)
    opay_o = nc.dram_tensor(P + "opay_o", [2048, S], BF16)

    selmy = nc.dram_tensor(P + "selmy", [8, 256], F32, kind="ExternalInput")
    i8my = nc.dram_tensor(P + "i8my", [8, 2], F32, kind="ExternalInput")
    mhmy = nc.dram_tensor(P + "mhmy", [128, 4], F32, kind="ExternalInput")

    dsem = pr.sem("dma")
    dmap = pr.sem("dmap")
    ccs = pr.sem("cc")

    arena = pr.ablock(P + "arena", [128, NCH * 1024], F32)
    abase = nc.lookup_mloc(arena).addr
    x_s = arena
    g_in = pr.ablock(P + "g_in", [128, NCH * 1024], BF16)
    h_in = pr.ablock(P + "h_in", [128, NCH * 1024], BF16)
    B1 = pr.ablock(P + "B1", [8, 4096], F32)
    B2 = pr.ablock(P + "B2", [8, 4096], F32)
    bbase = nc.lookup_mloc(B1).addr
    acol = pr.ablock(P + "acol", [128, 64], F32)
    n1_s = pr.ablock(P + "n1_s", [128, 16], F32)
    n2_s = pr.ablock(P + "n2_s", [128, 16], F32)
    mh_s = pr.ablock(P + "mh_s", [128, 4], F32)
    big_s = pr.ablock(P + "big_s", [8, 2], F32)
    cmask_s = pr.ablock(P + "cmask_s", [128, 4 * 512], BF16)
    meta_s = pr.ablock(P + "meta_s", [1, 128], I32)
    sel_s = pr.ablock(P + "sel_s", [8, 256], F32)
    i8_s = pr.ablock(P + "i8_s", [8, 2], F32)

    qkvT = nc.alloc_sbuf_tensor_at(P + "qkvT", [128, 32 * 1024], BF16, offset=abase)
    B3 = nc.alloc_sbuf_tensor_at(P + "B3", [8, 4096], F32, offset=abase)
    gI = nc.alloc_sbuf_tensor_at(P + "gI", [8, 4096], F32, offset=abase + 16384)
    EF = nc.alloc_sbuf_tensor_at(P + "EF", [128, 512], F32, offset=abase)
    PT2 = nc.alloc_sbuf_tensor_at(P + "PT2", [128, 1024], BF16, offset=abase + 2048)
    outst = nc.alloc_sbuf_tensor_at(P + "outst", [128, 512], BF16, offset=abase + 4096)
    sqsc = nc.alloc_sbuf_tensor_at(P + "sqsc", [128, 512], BF16, offset=abase + 5120)
    kTA = nc.alloc_sbuf_tensor_at(P + "kTA", [128, 4096], BF16, offset=abase + 16384)
    vstage = nc.alloc_sbuf_tensor_at(P + "vstage", [128, 8192], BF16, offset=abase + 24576)
    vtok = nc.alloc_sbuf_tensor_at(P + "vtok", [128, 8192], BF16, offset=abase + 40960)
    qslot = nc.alloc_sbuf_tensor_at(P + "qslot", [128, 4096], BF16, offset=abase + 57344)
    hn = nc.alloc_sbuf_tensor_at(P + "hn", [128, NCH * 1024], BF16, offset=bbase)
    hb = hn

    ph = pr.phase("load")
    load_stream(pr, ph, x_in, x_s, dsem)
    for d, s in ((n1_s, n1c), (n2_s, n2c), (mh_s, mhmy), (big_s, bigc),
                 (meta_s, meta), (sel_s, selmy), (i8_s, i8my)):
        dsem.inc(16)
        ph.sp(lambda e, d=d, s=s, sh=dsem.h: e.dma_start(d[:, :], s[:, :]).then_inc(sh, 16))
    csrc = bass.AP(cmask, 0, [[512, 128], [128 * 512, 4], [1, 512]])
    cdst = bass.AP(cmask_s, 0, [[4 * 512, 128], [512, 4], [1, 512]])
    dmap.inc(16)
    ph.pool(lambda e, s=csrc, d=cdst, sh=dmap.h: e.dma_start(d, s).then_inc(sh, 16))
    ph.wait("sp", dsem, dsem.v)
    ph.wait("pool", dmap, dmap.v)

    emit_ln(pr, x_s, g_in, cons, "l1")
    emit_ln(pr, g_in, h_in, cons, "l1r", rms_w=n1_s, eps=1e-6, src_bf=True)

    def stage_consumer(colbase, scale=None):
        def consumer(ph_, fg, tt, bank, wf, wf_tgt, pf):
            ph_.wait("dve", wf, wf_tgt)
            oc = colbase + fg * 1024 + tt * 512
            pf.inc(1)
            if scale is None:
                ph_.dve(lambda e, b=bank, oc=oc, sh=pf.h: e.tensor_copy(
                    qkvT[:, oc:oc + 512], b[:, :]).then_inc(sh, 1))
            else:
                ph_.dve(lambda e, b=bank, oc=oc, sh=pf.h: e.tensor_scalar(
                    qkvT[:, oc:oc + 512], b[:, :], scale, None, ALU.mult).then_inc(sh, 1))
        return consumer

    php = pr.phase("proj_qkv")
    emit_proj(pr, php, wq, 1024, h_in, stage_consumer(0, RSQ128), packed=True)
    emit_proj(pr, php, wk, 1024, h_in, stage_consumer(8 * 1024), packed=True)
    emit_proj(pr, php, wv, D, h_in, stage_consumer(16 * 1024), packed=True)

    phgp = pr.phase("proj_gates")
    gsem = pr.sem("l1gp")
    gsd = pr.sem("l1gd")
    gsa = pr.sem("l1ga")
    gw_s = pr.ablock(P + "l1gw", [128, 2 * 16 * 8], BF16)
    for gi, w in enumerate((wig, wfg)):
        gsrc = bass.AP(w, 0, [[8, 128], [128 * 8, 16], [1, 8]])
        gdst = bass.AP(gw_s, gi * 128, [[256, 128], [8, 16], [1, 8]])
        dmap.inc(16)
        phgp.pool(lambda e, s=gsrc, d=gdst, sh=dmap.h: e.dma_start(d, s).then_inc(sh, 16))
    phgp.wait("pe", dmap, dmap.v)
    for gi in range(2):
        for t2 in range(2):
            for dc in range(NCH):
                sp_ = dc == NCH - 1
                lhs = gw_s[:, gi * 128 + dc * 8: gi * 128 + (dc + 1) * 8]
                if sp_:
                    gsem.inc(1)
                    phgp.pe(lambda e, l=lhs, dc=dc, t2=t2, sh=gsem.h: e.matmul(
                        pr.ps[6][0:8, 0:512], l,
                        h_in[:, dc * 1024 + t2 * 512: dc * 1024 + t2 * 512 + 512],
                        start=dc == 0, stop=True).then_inc(sh, 1))
                else:
                    phgp.pe(lambda e, l=lhs, dc=dc, t2=t2: e.matmul(
                        pr.ps[6][0:8, 0:512], l,
                        h_in[:, dc * 1024 + t2 * 512: dc * 1024 + t2 * 512 + 512],
                        start=dc == 0, stop=False))
            phgp.wait("dve", gsem, gsem.v)
            gsd.inc(1)
            phgp.dve(lambda e, gi=gi, sh=gsd.h: e.tensor_scalar(
                pr.bcA[0:8, 0:512], pr.ps[6][0:8, 0:512],
                big_s[0:8, gi:gi + 1], None, ALU.add).then_inc(sh, 1))
            phgp.wait("act", gsd, gsd.v)
            gsa.inc(1)
            phgp.act(lambda e, sh=gsa.h: e.activation(
                pr.bcB[0:8, 0:512], pr.bcA[0:8, 0:512], AF.Tanh,
                scale=1.0 / 15.0).then_inc(sh, 1))
            phgp.wait("dve", gsa, gsa.v)
            phgp.wait("dve", dsem, dsem.v)
            gsd.inc(1)
            phgp.dve(lambda e, sh=gsd.h: e.tensor_scalar(
                pr.bcA[0:8, 512:1024], pr.bcB[0:8, 0:512], 15.0, None,
                ALU.mult).then_inc(sh, 1))
            phgp.wait("sp", gsd, gsd.v)
            dsem.inc(16)
            phgp.sp(lambda e, gi=gi, t2=t2, sh=dsem.h: e.dma_start(
                bass.AP(pg_i, gi * 8 * TPC + t2 * 512, [[TPC, 8], [1, 512]]),
                pr.bcA[0:8, 512:1024]).then_inc(sh, 16))
            phgp.wait("pe", gsd, gsd.v)
    phgp.wait("sp", dsem, dsem.v)

    php3 = pr.phase("payload")
    s_ap = bass.AP(qkvT, 0, [[32 * 1024, 128], [1024, 32], [1, 1024]])
    d_ap = bass.AP(pay_i, 0, [[1024, 128], [128 * 1024, 32], [1, 1024]])
    dsem.inc(16)
    php3.sp(lambda e, s=s_ap, d=d_ap, sh=dsem.h: e.dma_start(d, s).then_inc(sh, 16))
    php3.wait("sp", dsem, dsem.v)

    phga = pr.phase("gather_in")
    ccs.inc(1)
    phga.pool(lambda e, sh=ccs.h: e.collective_compute(
        "AllGather", ALU.bypass, replica_groups=[[0, 1, 2, 3], [4, 5, 6, 7]],
        ins=[pay_i.ap()], outs=[pay_o.ap()]).then_inc(sh, 1))
    ccs.inc(1)
    phga.pool(lambda e, sh=ccs.h: e.collective_compute(
        "AllGather", ALU.bypass, replica_groups=[[0, 1, 2, 3], [4, 5, 6, 7]],
        ins=[pg_i.ap()], outs=[pg_o.ap()]).then_inc(sh, 1))
    phga.wait("pool", ccs, ccs.v)

    phw = pr.phase("gates")
    gisrc = bass.AP(pg_o, 0, [[TPC, 8], [16 * TPC, 4], [1, 1024]])
    gidst = bass.AP(gI, 0, [[4096, 8], [1024, 4], [1, 1024]])
    gfsrc = bass.AP(pg_o, 8 * TPC, [[TPC, 8], [16 * TPC, 4], [1, 1024]])
    gfdst = bass.AP(B2, 0, [[4096, 8], [1024, 4], [1, 1024]])
    dsem.inc(16)
    phw.sp(lambda e, s=gisrc, d=gidst, sh=dsem.h: e.dma_start(d, s).then_inc(sh, 16))
    dsem.inc(16)
    phw.sp(lambda e, s=gfsrc, d=gfdst, sh=dsem.h: e.dma_start(d, s).then_inc(sh, 16))
    phw.wait("dve", dsem, dsem.v)
    phw.wait("act", dsem, dsem.v)
    gsm = pr.sem("l1w")
    gwa = pr.sem("l1wx")
    gwa.inc(1)
    phw.act(lambda e, sh=gwa.h: e.activation(B3[0:8, :], B2[0:8, :], AF.Abs).then_inc(sh, 1))
    gwa.inc(1)
    phw.act(lambda e, sh=gwa.h: e.activation(B3[0:8, :], B3[0:8, :], AF.Exp,
                                             scale=-1.0).then_inc(sh, 1))
    gwa.inc(1)
    phw.act(lambda e, sh=gwa.h: e.activation(
        B3[0:8, :], B3[0:8, :], AF.Ln,
        bias=cons["ones_col_f32"][0:8, 0:1]).then_inc(sh, 1))
    phw.wait("dve", gwa, gwa.v)
    gsm.inc(1)
    phw.dve(lambda e, sh=gsm.h: e.tensor_scalar(B2[0:8, :], B2[0:8, :], 0.0, None,
                                                ALU.min).then_inc(sh, 1))
    gsm.inc(1)
    phw.dve(lambda e, sh=gsm.h: e.tensor_tensor(B2[0:8, :], B2[0:8, :], B3[0:8, :],
                                                ALU.subtract).then_inc(sh, 1))
    bufs = [B2, B3]
    w = 1
    pi = 0
    while w < 4096:
        a_, b_ = bufs[pi], bufs[1 - pi]
        gsm.inc(1)
        phw.dve(lambda e, a=a_, b=b_, w=w, sh=gsm.h: e.tensor_copy(
            b[0:8, 0:w], a[0:8, 0:w]).then_inc(sh, 1))
        gsm.inc(1)
        phw.dve(lambda e, a=a_, b=b_, w=w, sh=gsm.h: e.tensor_tensor(
            b[0:8, w:4096], a[0:8, w:4096], a[0:8, 0:4096 - w], ALU.add).then_inc(sh, 1))
        pi = 1 - pi
        w *= 2
    fcum = bufs[pi]
    gsm.inc(1)
    phw.dve(lambda e, f=fcum, sh=gsm.h: e.tensor_tensor(
        gI[0:8, :], gI[0:8, :], f[0:8, :], ALU.subtract).then_inc(sh, 1))
    other = bufs[1 - pi]
    gsm.inc(1)
    phw.dve(lambda e, o=other, sh=gsm.h: e.tensor_copy(o[0:8, :], gI[0:8, :]).then_inc(sh, 1))
    bufs2 = [other, B1]
    w = 1
    pi2 = 0
    while w < 4096:
        a_, b_ = bufs2[pi2], bufs2[1 - pi2]
        gsm.inc(1)
        phw.dve(lambda e, a=a_, b=b_, w=w, sh=gsm.h: e.tensor_copy(
            b[0:8, 0:w], a[0:8, 0:w]).then_inc(sh, 1))
        gsm.inc(1)
        phw.dve(lambda e, a=a_, b=b_, w=w, sh=gsm.h: e.tensor_tensor(
            b[0:8, w:4096], a[0:8, w:4096], a[0:8, 0:4096 - w], ALU.max).then_inc(sh, 1))
        pi2 = 1 - pi2
        w *= 2
    cmbuf = bufs2[pi2]
    gsm.inc(1)
    phw.dve(lambda e, c=cmbuf, f=fcum, sh=gsm.h: e.tensor_tensor(
        B2[0:8, :], f[0:8, :], c[0:8, :], ALU.add).then_inc(sh, 1))
    gsm.inc(1)
    phw.dve(lambda e, c=cmbuf, sh=gsm.h: e.tensor_scalar(
        B1[0:8, :], c[0:8, :], -1.0, None, ALU.mult).then_inc(sh, 1))
    phw.wait("pe", gsm, gsm.v)
    tsp = pr.sem("l1tp")
    tse = pr.sem("l1te")
    for blk in range(32):
        par = blk % 2
        if blk >= 2:
            phw.wait("pe", tse, tse.v - 1)
        tsp.inc(1)
        phw.pe(lambda e, b=blk, p=par, sh=tsp.h: e.matmul(
            pr.ps[2 + p][0:128, 0:2], gI[0:8, b * 128:(b + 1) * 128],
            i8_s[0:8, 0:2], start=True, stop=True).then_inc(sh, 1))
        phw.wait("dve", tsp, tsp.v)
        tse.inc(1)
        phw.dve(lambda e, b=blk, p=par, sh=tse.h: e.tensor_copy(
            acol[:, b * 2:(b + 1) * 2], pr.ps[2 + p][0:128, 0:2]).then_inc(sh, 1))

    pha = pr.phase("attn")
    pha.uses_reg = True
    sSC = pr.sem("asc")
    sE = pr.sem("aE")
    sP = pr.sem("aP")
    sM = pr.sem("aM")
    sAV = pr.sem("aAV")
    sVT = pr.sem("aVT")
    sVE = pr.sem("aVE")
    sEP = pr.sem("aEP")
    sBC = pr.sem("aBC")
    sLD = pr.sem("aload")
    aq = [pr.sem("aq0"), pr.sem("aq1")]
    rowb = pr.rowb

    qld_free_at = {}
    for hl in range(2):
        def unit_loads(e, hl=hl, w=None if hl == 0 else sAV.v):
            r0 = e._shared_reg
            if w is not None:
                e.wait_ge(sAV.h, w)
            e.reg_load(r0, meta_s[0:1, hl:hl + 1])
            e.dma_start(
                bass.AP(kTA, 0, [[4096, 128], [1024, 4], [1, 1024]]),
                bass.AP(pay_o, r0, [[1024, 128], [PAYL1 * 1024, 4], [1, 1024]])
            ).then_inc(sLD.h, 16)
            for dv in range(2):
                e.reg_load(r0, meta_s[0:1, 16 + 2 * hl + dv: 17 + 2 * hl + dv])
                e.dma_start(
                    bass.AP(vstage, dv * 4096, [[8192, 128], [1024, 4], [1, 1024]]),
                    bass.AP(pay_o, r0, [[1024, 128], [PAYL1 * 1024, 4], [1, 1024]])
                ).then_inc(sLD.h, 16)
        pha.pool(unit_loads)
        sLD.inc(48)
        unit_ld = sLD.v

        pha.wait("pe", sLD, unit_ld)
        for ti in range(64):
            tt, dv = ti // 2, ti % 2
            par = ti % 2
            if ti >= 2:
                pha.wait("pe", sVE, sVE.v - 1)
            sVT.inc(1)
            pha.pe(lambda e, t=tt, d=dv, p=par, sh=sVT.h: e.matmul(
                pr.ps[2 + p][:, 0:128],
                vstage[:, d * 4096 + t * 128: d * 4096 + (t + 1) * 128],
                cons["ident_bf"][:, :], start=True, stop=True).then_inc(sh, 1))
            pha.wait("dve", sVT, sVT.v)
            sVE.inc(1)
            pha.dve(lambda e, t=tt, d=dv, p=par, sh=sVE.h: e.tensor_copy(
                vtok[:, t * 256 + d * 128: t * 256 + (d + 1) * 128],
                pr.ps[2 + p][:, 0:128]).then_inc(sh, 1))

        for st in range(8):
            it = hl * 8 + st
            if st == 0:
                aq[0].inc(16)

                def load_q(e, hl=hl, w=qld_free_at.get(0)):
                    r2 = e._shared_reg
                    if w is not None:
                        e.wait_ge(sAV.h, w)
                    e.reg_load(r2, meta_s[0:1, 4 + hl * 4: 5 + hl * 4])
                    e.dma_start(
                        bass.AP(qslot, 0, [[4096, 128], [1024, 4], [1, 1024]]),
                        bass.AP(pay_o, r2, [[1024, 128], [PAYL1 * 1024, 4], [1, 1024]])
                    ).then_inc(aq[0].h, 16)
                pha.pool(load_q)
            q_tgt = aq[0].v
            qcols = qslot[:, st * 512: st * 512 + 512]

            pha.wait("pe", sEP, sEP.v)
            pha.wait("pe", sP, sP.v)
            sBC.inc(1)
            pha.pe(lambda e, hl=hl, st=st, sh=sBC.h: e.matmul(
                pr.ps[6][:, :], sel_s[0:8, hl * 128:(hl + 1) * 128],
                B1[0:8, st * 512:(st + 1) * 512], start=True, stop=True).then_inc(sh, 1))
            sBC.inc(1)
            pha.pe(lambda e, hl=hl, st=st, sh=sBC.h: e.matmul(
                pr.psm[32:33, 0:512], i8_s[0:8, hl:hl + 1],
                B2[0:8, st * 512:(st + 1) * 512], start=True, stop=True).then_inc(sh, 1))
            pha.wait("dve", sBC, sBC.v - 1)
            sEP.inc(1)
            pha.dve(lambda e, sh=sEP.h: e.tensor_copy(
                pr.bcB[:, 0:512], pr.ps[6][:, :]).then_inc(sh, 1))
            pha.wait("act", sBC, sBC.v)
            sP.inc(1)
            pha.act(lambda e, sh=sP.h: e.activation(
                rowb[0:1, 1024:1536], pr.psm[32:33, 0:512], AF.Exp,
                scale=-1.0).then_inc(sh, 1))
            bc_ready = sEP.v
            nf_ready = sP.v

            n_tt = 4 * (st + 1)
            sAV0 = sAV.v
            sM0 = sM.v
            sP0 = sP.v
            for tt in range(n_tt):
                par = tt % 2
                pha.wait("pe", aq[0], q_tgt)
                if tt >= 2:
                    pha.wait("pe", sM, sM0 + tt - 1)
                sSC.inc(1)
                pha.pe(lambda e, t=tt, p=par, qc=qcols, sh=sSC.h: e.matmul(
                    pr.ps[p][:, :], kTA[:, t * 128:(t + 1) * 128], qc,
                    start=True, stop=True).then_inc(sh, 1))
                pha.wait("dve", sEP, bc_ready)
                if tt >= 1:
                    pha.wait("dve", sP, sP0 + tt)
                sE.inc(1)
                pha.dve(lambda e, t=tt, hl=hl, sh=sE.h: e.tensor_scalar(
                    EF[:, :], pr.bcB[:, 0:512],
                    acol[:, t * 2 + hl: t * 2 + hl + 1], None,
                    ALU.add).then_inc(sh, 1))
                diag = (tt >= 4 * st)
                if diag:
                    kk = tt - 4 * st
                    sE.inc(1)
                    pha.dve(lambda e, k=kk, sh=sE.h: e.tensor_tensor(
                        EF[:, :], EF[:, :],
                        cmask_s[:, k * 512:(k + 1) * 512], ALU.add).then_inc(sh, 1))
                pha.wait("act", sE, sE.v)
                if tt >= 2:
                    pha.wait("act", sM, sM0 + tt - 1)
                sP.inc(1)
                pha.act(lambda e, p=par, sh=sP.h: e.activation(
                    PT2[:, p * 512:(p + 1) * 512], EF[:, :], AF.Exp).then_inc(sh, 1))
                pha.wait("dve", sP, sP.v)
                pha.wait("dve", sSC, sSC.v)
                sM.inc(1)
                pha.dve(lambda e, p=par, sh=sM.h: e.tensor_tensor(
                    PT2[:, p * 512:(p + 1) * 512], pr.ps[p][:, :],
                    PT2[:, p * 512:(p + 1) * 512], ALU.mult).then_inc(sh, 1))
                pha.wait("pe", sM, sM.v)
                pha.pe(lambda e, t=tt, p=par, st_=(tt == 0), sp_=(tt == n_tt - 1):
                       e.matmul(pr.ps[4][:, :],
                                vtok[:, t * 256: t * 256 + 128],
                                PT2[:, p * 512:(p + 1) * 512], start=st_, stop=sp_))
                pha.pe(lambda e, t=tt, p=par, st_=(tt == 0), sp_=(tt == n_tt - 1):
                       e.matmul(pr.ps[5][:, :],
                                vtok[:, t * 256 + 128: t * 256 + 256],
                                PT2[:, p * 512:(p + 1) * 512], start=st_, stop=sp_))
                sAV.inc(1)
                pha.pe(lambda e, p=par, st_=(tt == 0), sp_=(tt == n_tt - 1), sh=sAV.h:
                       e.matmul(pr.psm[0:1, 0:512], cons["ones_col_bf"][:, :],
                                PT2[:, p * 512:(p + 1) * 512], start=st_, stop=sp_)
                       .then_inc(sh, 1))
            if st == 7:
                qld_free_at[0] = sAV.v

            pha.wait("act", sAV, sAV.v)
            sP.inc(1)
            pha.act(lambda e, sh=sP.h: e.activation(
                rowb[0:1, 0:512], pr.psm[0:1, 0:512], AF.Abs).then_inc(sh, 1))
            pha.wait("dve", sP, sP.v)
            pha.wait("dve", sP, nf_ready)

            def epi_rows(e, sh=sEP.h):
                e.tensor_tensor(rowb[0:1, 0:512], rowb[0:1, 0:512],
                                rowb[0:1, 1024:1536], ALU.max)
                e.reciprocal(rowb[0:1, 512:1024], rowb[0:1, 0:512]).then_inc(sh, 1)
            sEP.inc(1)
            pha.dve(epi_rows)
            pha.wait("act", sEP, sEP.v)
            for dv in range(2):
                sP.inc(1)
                pha.act(lambda e, d=dv, sh=sP.h: e.activation(
                    sqsc[:, 0:512], pr.ps[4 + d][:, :], AF.Square).then_inc(sh, 1))
                pha.wait("pe", sP, sP.v)
                sBC.inc(1)
                pha.pe(lambda e, d=dv, sh=sBC.h: e.matmul(
                    pr.psm[64:65, 0:512], cons["ones_col_bf"][:, :], sqsc[:, 0:512],
                    start=d == 0, stop=d == 1).then_inc(sh, 1))
                if dv == 0:
                    pha.wait("act", sBC, sBC.v)
            pha.wait("dve", sBC, sBC.v)

            def epi_rows2(e, sh=sEP.h):
                e.tensor_tensor(rowb[0:1, 1536:2048], pr.psm[64:65, 0:512],
                                rowb[0:1, 512:1024], ALU.mult)
                e.tensor_tensor(rowb[0:1, 1536:2048], rowb[0:1, 1536:2048],
                                rowb[0:1, 512:1024], ALU.mult)
                e.tensor_scalar(rowb[0:1, 1536:2048], rowb[0:1, 1536:2048],
                                1.0 / 256.0, 1e-6, ALU.mult, ALU.add).then_inc(sh, 1)
            sEP.inc(1)
            pha.dve(epi_rows2)
            pha.wait("act", sEP, sEP.v)
            sP.inc(1)
            pha.act(lambda e, sh=sP.h: e.activation(
                rowb[0:1, 1536:2048], rowb[0:1, 1536:2048], AF.Sqrt).then_inc(sh, 1))
            pha.wait("dve", sP, sP.v)

            def epi_rows3(e, sh=sEP.h):
                e.reciprocal(rowb[0:1, 1536:2048], rowb[0:1, 1536:2048])
                e.tensor_tensor(rowb[0:1, 1536:2048], rowb[0:1, 1536:2048],
                                rowb[0:1, 512:1024], ALU.mult).then_inc(sh, 1)
            sEP.inc(1)
            pha.dve(epi_rows3)
            pha.wait("pe", sEP, sEP.v)
            sBC.inc(1)
            pha.pe(lambda e, sh=sBC.h: e.matmul(
                pr.ps[6][:, :], cons["ones_row_f32"][:, :], rowb[0:1, 1536:2048],
                start=True, stop=True).then_inc(sh, 1))
            pha.wait("dve", sBC, sBC.v)
            sEP.inc(1)
            pha.dve(lambda e, sh=sEP.h: e.tensor_copy(
                pr.bcA[:, 0:512], pr.ps[6][:, :]).then_inc(sh, 1))
            for dv in range(2):
                pha.wait("dve", sEP, sEP.v)
                pha.wait("dve", dsem, dsem.v)
                sEP.inc(1)
                pha.dve(lambda e, d=dv, hl=hl, sh=sEP.h: e.scalar_tensor_tensor(
                    outst[:, :], pr.ps[4 + d][:, :],
                    mh_s[:, 2 * hl + d: 2 * hl + d + 1],
                    pr.bcA[:, 0:512], ALU.mult, ALU.mult).then_inc(sh, 1))
                pha.wait("sp", sEP, sEP.v)
                dsem.inc(16)
                pha.sp(lambda e, d=dv, hl=hl, st=st, sh=dsem.h: e.dma_start(
                    bass.AP(opay_i, (hl * 256 + d * 128) * S + st * 512,
                            [[S, 128], [1, 512]]),
                    outst[:, :]).then_inc(sh, 16))
            pha.wait("pe", sEP, sEP.v)

    pha.wait("sp", dsem, dsem.v)

    phg2 = pr.phase("gather_out")
    ccs.inc(1)
    phg2.pool(lambda e, sh=ccs.h: e.collective_compute(
        "AllGather", ALU.bypass, replica_groups=[[0, 1, 2, 3], [4, 5, 6, 7]],
        ins=[opay_i.ap()], outs=[opay_o.ap()]).then_inc(sh, 1))
    phg2.wait("pool", ccs, ccs.v)

    phr = pr.phase("read_out")
    phr.uses_reg = True

    def read_own(e):
        r3 = e._shared_reg
        e.reg_load(r3, meta_s[0:1, 60:61])
        e.dma_start(bass.AP(hn, 0, [[NCH * 1024, 128], [1024, 16], [1, 1024]]),
                    bass.AP(opay_o, r3, [[S, 128], [128 * S, 16], [1, 1024]])).then_inc(sLD.h, 16)
    phr.pool(read_own)
    sLD.inc(16)
    load_stream(pr, phr, x_in, x_s, dsem)
    phr.wait("pool", sLD, sLD.v)
    phr.wait("sp", dsem, dsem.v)

    sCA = pr.sem("l1sg")

    def og_consumer(ph_, fg, tt, bank, wf, wf_tgt, pf):
        ph_.wait("act", wf, wf_tgt)
        ph_.wait("act", pf, pf.v)
        sCA.inc(1)
        ph_.act(lambda e, b=bank, sh=sCA.h: e.activation(
            pr.scbf[2][:, 0:512], b[:, :], AF.Sigmoid).then_inc(sh, 1))
        ph_.wait("dve", sCA, sCA.v)
        oc = fg * 1024 + tt * 512
        pf.inc(1)
        ph_.dve(lambda e, oc=oc, sh=pf.h: e.tensor_tensor(
            hn[:, oc:oc + 512], hn[:, oc:oc + 512], pr.scbf[2][:, 0:512],
            ALU.mult).then_inc(sh, 1))

    emit_proj(pr, pr.phase("proj_og"), wog, D, h_in, og_consumer, packed=True)

    def wout_consumer(ph_, fg, tt, bank, wf, wf_tgt, pf):
        ph_.wait("dve", wf, wf_tgt)
        oc = fg * 1024 + tt * 512
        pf.inc(1)

        def f(e, oc=oc, b=bank, sh=pf.h):
            e.tensor_tensor(g_in[:, oc:oc + 512], g_in[:, oc:oc + 512], b[:, :],
                            ALU.add)
            e.tensor_tensor(x_s[:, oc:oc + 512], x_s[:, oc:oc + 512],
                            g_in[:, oc:oc + 512], ALU.add).then_inc(sh, 1)
        ph_.dve(f)

    emit_proj(pr, pr.phase("proj_wout"), wout, D, hn, wout_consumer, packed=True)

    emit_ln(pr, g_in, h_in, cons, "l1r2", rms_w=n2_s, eps=1e-6, src_bf=True)

    phm = pr.phase("mlp")
    hb_free_at = [0]
    sMS = pr.sem("l1ms")

    def msilu_consumer(hb_guard):
        def consumer(ph_, fg, tt, bank, wf, wf_tgt, pf):
            ph_.wait("act", wf, wf_tgt)
            if hb_guard[0]:
                ph_.wait("act", wf, hb_guard[0])
                hb_guard[0] = 0
            sMS.inc(1)
            ph_.act(lambda e, fg=fg, b=bank, sh=sMS.h: e.activation(
                hb[:, fg * 512:(fg + 1) * 512], b[:, :], AF.Silu).then_inc(sh, 1))
            ph_.wait("dve", sMS, sMS.v)
            pf.inc(1)
            ph_.dve(lambda e, sh=pf.h: e.nop().then_inc(sh, 1))
        return consumer

    def mh_consumer(ph_, fg, tt, bank, wf, wf_tgt, pf):
        ph_.wait("dve", wf, wf_tgt)
        pf.inc(1)
        ph_.dve(lambda e, fg=fg, b=bank, sh=pf.h: e.tensor_tensor(
            hb[:, fg * 512:(fg + 1) * 512], b[:, :],
            hb[:, fg * 512:(fg + 1) * 512], ALU.mult).then_inc(sh, 1))

    for t2 in range(2):
        if not hasattr(pr, "wpack_off"):
            pr.wpack_off = {}
        for w_ in (wgate, wup, wdown):
            pr.wpack_off[w_.name] = 0
        for half in range(2):
            def mdown_consumer(ph_, dt, bank, wf, wf_tgt, pf, t2=t2):
                ph_.wait("dve", wf, wf_tgt)
                pf.inc(1)
                ph_.dve(lambda e, dt=dt, b=bank, sh=pf.h: e.tensor_tensor(
                    x_s[:, dt * 1024 + t2 * 512: dt * 1024 + t2 * 512 + 512],
                    x_s[:, dt * 1024 + t2 * 512: dt * 1024 + t2 * 512 + 512],
                    b[:, :], ALU.add).then_inc(sh, 1))
            hb_guard = [hb_free_at[0]]
            emit_proj(pr, phm, wgate, 4096, h_in, msilu_consumer(hb_guard),
                      row_len=8192, w_off=half * 4096, tts=(t2,), packed=True)
            emit_proj(pr, phm, wup, 4096, h_in, mh_consumer,
                      row_len=8192, w_off=half * 4096, tts=(t2,), packed=True)
            emit_down(pr, phm, wdown, 32,
                      lambda hc: hb[:, hc * 512:(hc + 1) * 512],
                      mdown_consumer, w_off=half * 4096 * D, packed=True)
            hb_free_at[0] = pr.sem("wf").v

    phs = pr.phase("store")
    store_stream(pr, phs, x_s, y_out, dsem)
    phs.wait("sp", dsem, dsem.v)
    pr.arena_reset()


def build_l1a(upto=None):
    pr = Prog("l1a")
    nc = pr.nc
    x_in = nc.dram_tensor("x", [D, TPC], F32, kind="ExternalInput")
    y_out = nc.dram_tensor("y", [D, TPC], F32, kind="ExternalOutput")
    cons = alloc_consts(pr)
    pr.init_arena()
    emit_l1a(pr, cons, x_in, y_out)
    pr.emit()
    return pr.nc


def l1a_meta(core):
    r = core % 4
    m = np.zeros(128, np.int32)
    for hl in range(2):
        hg = 2 * r + hl
        m[hl] = (PAY1_K + hg * 128) * TPC
        for dv in range(2):
            m[16 + 2 * hl + dv] = (PAY1_V + hg * 256 + dv * 128) * TPC
        for rb in range(4):
            m[4 + hl * 4 + rb] = (rb * PAYL1 + PAY1_Q + hg * 128) * TPC
    m[60] = r * TPC
    return m.reshape(1, 128)


def colw(v):
    return np.ascontiguousarray(np.asarray(v, np.float32).reshape(16, 128).T)


def maps_l1a(inputs, c, P=""):
    cm = causal_masks()
    bigc = np.stack([np.asarray(inputs["L1_attn_b_ig"], np.float32),
                     np.asarray(inputs["L1_attn_b_fg"], np.float32)], axis=1)
    bigc = np.ascontiguousarray(bigc)
    r = c % 4
    selmy = np.zeros((8, 256), np.float32)
    i8my = np.zeros((8, 2), np.float32)
    for hl in range(2):
        hg = 2 * r + hl
        selmy[hg, hl * 128:(hl + 1) * 128] = 1.0
        i8my[hg, hl] = 1.0
    mhfull = colw(inputs["L1_attn_mh_w"])
    mhmy = np.zeros((128, 4), np.float32)
    for hl in range(2):
        hg = 2 * r + hl
        for dv in range(2):
            mhmy[:, 2 * hl + dv] = mhfull[:, 2 * hg + dv]
    wgp = np.concatenate(
        [pack_proj(np.asarray(inputs["L1_attn_w_gate"], np.float32), NCH, 4096,
                   col_off=h * 4096) for h in range(2)]).reshape(D, 8192)
    wup_ = np.concatenate(
        [pack_proj(np.asarray(inputs["L1_attn_w_up"], np.float32), NCH, 4096,
                   col_off=h * 4096) for h in range(2)]).reshape(D, 8192)
    wdp = np.concatenate(
        [pack_down(np.asarray(inputs["L1_attn_w_down"], np.float32), 32,
                   row_off=h * 4096) for h in range(2)]).reshape(8192, D)
    return {
        P + "wq": pack_proj(np.asarray(inputs["L1_attn_wq"], np.float32),
                            NCH, 1024).reshape(D, 1024),
        P + "wk": pack_proj(np.asarray(inputs["L1_attn_wk"], np.float32),
                            NCH, 1024).reshape(D, 1024),
        P + "wv": pack_proj(np.asarray(inputs["L1_attn_wv"], np.float32),
                            NCH, D).reshape(D, D),
        P + "wig": inputs["L1_attn_w_ig"],
        P + "wfg": inputs["L1_attn_w_fg"], P + "bigc": bigc,
        P + "wog": pack_proj(np.asarray(inputs["L1_attn_w_og"], np.float32),
                             NCH, D).reshape(D, D),
        P + "wout": pack_proj(np.asarray(inputs["L1_attn_w_out"], np.float32),
                              NCH, D).reshape(D, D),
        P + "wgate": wgp, P + "wup": wup_,
        P + "wdown": wdp,
        P + "n1c": colw(inputs["L1_attn_norm1_w"]),
        P + "n2c": colw(inputs["L1_attn_norm2_w"]),
        P + "selmy": np.ascontiguousarray(selmy),
        P + "i8my": np.ascontiguousarray(i8my),
        P + "mhmy": np.ascontiguousarray(mhmy),
        P + "cmask": cm, P + "meta": l1a_meta(c),
    }


def run_l1a(x_shards, inputs, upto=None):
    nc = get_nc(f"l1a{upto}", lambda: build_l1a(upto))
    in_maps = [dict(x=x_shards[c], **maps_l1a(inputs, c)) for c in range(NCORE)]
    res = run_bass_kernel_spmd(nc, in_maps, core_ids=list(range(NCORE)))
    return [res.results[c]["y"] for c in range(NCORE)]


def maps_l1f(inputs, c, P=""):
    lut = np.ascontiguousarray(np.asarray(inputs["L1_ffn_lut"], np.float32).reshape(1, 16))
    gi = np.asarray(inputs["L1_ffn_gate_idx"], np.int32)
    ui = np.asarray(inputs["L1_ffn_up_idx"], np.int32)
    di = np.asarray(inputs["L1_ffn_down_idx"], np.int32)
    return {
        P + "lut": lut,
        P + "gsl": np.ascontiguousarray(gi[:, 256 * c:256 * (c + 1)]),
        P + "usl": np.ascontiguousarray(ui[:, 256 * c:256 * (c + 1)]),
        P + "dsl": np.ascontiguousarray(di[256 * c:256 * (c + 1), :]),
    }


def run_l1f(x_shards, inputs, upto=None):
    nc = get_nc(f"l1f{upto}", lambda: build_l1f(upto))
    in_maps = [dict(x=x_shards[c], **maps_l1f(inputs, c)) for c in range(NCORE)]
    res = run_bass_kernel_spmd(nc, in_maps, core_ids=list(range(NCORE)))
    return [res.results[c]["y"] for c in range(NCORE)]


def build_fused():
    pr = Prog("fused")
    nc = pr.nc
    x0 = nc.dram_tensor("x", [D, TPC], F32, kind="ExternalInput")
    y = nc.dram_tensor("y", [D, TPC], F32, kind="ExternalOutput")
    s1 = nc.dram_tensor("s1", [D, TPC], F32)
    s2 = nc.dram_tensor("s2", [D, TPC], F32)
    s3 = nc.dram_tensor("s3", [D, TPC], F32)
    cons = alloc_consts(pr)
    pr.init_arena()
    emit_l0a(pr, cons, x0, s1, P="a_")
    emit_l0f(pr, cons, s1, s2, P="f_")
    emit_l1a(pr, cons, s2, s3, P="b_")
    emit_l1f(pr, cons, s3, y, P="g_")
    pr.emit()
    return pr.nc


def kernel_fused(inputs):
    nc = get_nc("fused", build_fused)
    x_shards = shard_stream(np.asarray(inputs["L0_attn_x"], np.float32))
    in_maps = []
    for c in range(NCORE):
        m = {"x": x_shards[c]}
        m.update(maps_l0a(inputs, c, "a_"))
        m.update(maps_l0f(inputs, c, "f_"))
        m.update(maps_l1a(inputs, c, "b_"))
        m.update(maps_l1f(inputs, c, "g_"))
        in_maps.append(m)
    res = run_bass_kernel_spmd(nc, in_maps, core_ids=list(range(NCORE)))
    return unshard_stream([res.results[c]["y"] for c in range(NCORE)])


PAY1_Q, PAY1_K, PAY1_V = 0, 1024, 2048
PAYL1 = 4096
RSQ128 = float(1.0 / np.sqrt(128.0))


def emit_l1f(pr, cons, x_in, y_out, P=""):
    nc = pr.nc
    lut = nc.dram_tensor(P + "lut", [1, 16], F32, kind="ExternalInput")
    gsl = nc.dram_tensor(P + "gsl", [5504, 256], I32, kind="ExternalInput")
    usl = nc.dram_tensor(P + "usl", [5504, 256], I32, kind="ExternalInput")
    dsl = nc.dram_tensor(P + "dsl", [256, 5504], I32, kind="ExternalInput")
    wg_i = nc.dram_tensor(P + "wg_i", [256, 5504], BF16)
    wg_o = nc.dram_tensor(P + "wg_o", [2048, 5504], BF16)
    wu_i = nc.dram_tensor(P + "wu_i", [256, 5504], BF16)
    wu_o = nc.dram_tensor(P + "wu_o", [2048, 5504], BF16)
    wd_i = nc.dram_tensor(P + "wd_i", [5504, 256], BF16)
    wd_o = nc.dram_tensor(P + "wd_o", [8 * 5504, 256], BF16)

    dsem = pr.sem("dma")
    dmap = pr.sem("dmap")
    ccs = pr.sem("cc")

    x_s = pr.ablock(P + "x_s", [128, NCH * 1024], F32)
    hn = pr.ablock(P + "hn", [128, NCH * 1024], BF16)
    hb = pr.ablock(P + "hb", [128, 22 * 512], BF16)
    lutbc = pr.ablock(P + "lutbc", [128, 16], F32)
    lutrow = pr.ablock(P + "lutrow", [1, 16], F32)
    idxs = [pr.ablock(P + f"idxs{i}", [128, 128], I32) for i in range(2)]
    idxb = pr.ablock(P + "idxb", [128, 128], BF16)
    macc = pr.ablock(P + "macc", [128, 128], BF16)
    mtmp = pr.ablock(P + "mtmp", [128, 128], BF16)
    wrow = pr.ablock(P + "wrow", [128, 5504], BF16)

    ph = pr.phase("load")
    load_stream(pr, ph, x_in, x_s, dsem)
    dsem.inc(16)
    ph.sp(lambda e, sh=dsem.h: e.dma_start(lutrow[:, :], lut[:, :]).then_inc(sh, 16))
    ph.wait("sp", dsem, dsem.v)
    lsm = pr.sem("lf")
    ph.wait("pe", dsem, dsem.v)
    lsm.inc(1)
    ph.pe(lambda e, sh=lsm.h: e.matmul(pr.ps[0][:, 0:16], cons["ones_row_f32"][:, :],
                                       lutrow[0:1, :], start=True, stop=True).then_inc(sh, 1))
    ph.wait("dve", lsm, lsm.v)
    lsm.inc(1)
    ph.dve(lambda e, sh=lsm.h: e.tensor_copy(lutbc[:, :], pr.ps[0][:, 0:16]).then_inc(sh, 1))

    emit_ln(pr, x_s, hn, cons, "l1f")

    phm = pr.phase("mat")
    msd = [pr.sem("mI0"), pr.sem("mI1")]
    msv = pr.sem("mV")
    msp = pr.sem("mP")
    mse = pr.sem("mE")
    msw = pr.sem("mW")
    phm.wait("dve", lsm, lsm.v)

    mt_n = [0]

    def mat_tile(idx_dram, row_len, r0, c0, par, wcol):
        if mt_n[0] >= 2:
            phm.wait("sp", msv, mt_n[0] - 1)
        mt_n[0] += 1
        msd[par].inc(16)
        phm.sp(lambda e, s=bass.AP(idx_dram, r0 * row_len + c0,
                                   [[row_len, 128], [1, 128]]),
               d=idxs[par], sh=msd[par].h: e.dma_start(d[:, :], s).then_inc(sh, 16))
        tgt = msd[par].v
        phm.wait("dve", msd[par], tgt)
        msv.inc(1)

        def mat(e, p=par, sh=msv.h):
            e.tensor_copy(idxb[:, :], idxs[p][:, :])
            e.tensor_scalar(macc[:, :], idxb[:, :], 0.0, lutbc[:, 0:1],
                            ALU.is_equal, ALU.mult)
            for j in range(1, 16):
                e.tensor_scalar(mtmp[:, :], idxb[:, :], float(j), lutbc[:, j:j + 1],
                                ALU.is_equal, ALU.mult)
                e.tensor_tensor(macc[:, :], macc[:, :], mtmp[:, :], ALU.add)
            e.nop().then_inc(sh, 1)
        phm.dve(mat)
        phm.wait("pe", msv, msv.v)
        phm.wait("pe", mse, mse.v)
        msp.inc(1)
        phm.pe(lambda e, sh=msp.h: e.matmul(pr.ps[1][:, 0:128], macc[:, :],
                                            cons["ident_bf"][:, :],
                                            start=True, stop=True).then_inc(sh, 1))
        phm.wait("dve", msp, msp.v)
        mse.inc(1)
        phm.dve(lambda e, wc=wcol, sh=mse.h: e.tensor_copy(
            wrow[:, wc:wc + 128], pr.ps[1][:, 0:128]).then_inc(sh, 1))

    for mi, (idx_dram, ag_in) in enumerate(((gsl, wg_i), (usl, wu_i))):
        for hc2 in range(2):
            for ot in range(43):
                mat_tile(idx_dram, 256, ot * 128, hc2 * 128, ot % 2, ot * 128)
            phm.wait("sp", mse, mse.v)
            msw.inc(16)
            phm.sp(lambda e, ag=ag_in, hc2=hc2, sh=msw.h: e.dma_start(
                bass.AP(ag, hc2 * 128 * 5504, [[5504, 128], [1, 5504]]),
                wrow[:, :]).then_inc(sh, 16))
            phm.wait("dve", msw, msw.v)
    for it in range(43):
        for dt2 in range(2):
            mat_tile(dsl, 5504, dt2 * 128, it * 128, dt2 % 2, dt2 * 128)
        phm.wait("sp", mse, mse.v)
        msw.inc(16)
        phm.sp(lambda e, it=it, sh=msw.h: e.dma_start(
            bass.AP(wd_i, it * 128 * 256, [[256, 128], [1, 256]]),
            wrow[:, 0:256]).then_inc(sh, 16))
        phm.wait("dve", msw, msw.v)

    phg = pr.phase("gather_w")
    for ag_i, ag_o in ((wg_i, wg_o), (wu_i, wu_o), (wd_i, wd_o)):
        ccs.inc(1)
        phg.pool(lambda e, i_=ag_i, o_=ag_o, sh=ccs.h: e.collective_compute(
            "AllGather", ALU.bypass, replica_groups=[[0, 1, 2, 3, 4, 5, 6, 7]],
            ins=[i_.ap()], outs=[o_.ap()]).then_inc(sh, 1))
    phg.wait("pool", ccs, ccs.v)

    phf = pr.phase("ffn")
    hb_free_at = [0]
    sMS = pr.sem("lfms")

    def fsilu_consumer(hb_guard):
        def consumer(ph_, fg, tt, bank, wf, wf_tgt, pf):
            ph_.wait("act", wf, wf_tgt)
            if hb_guard[0]:
                ph_.wait("act", wf, hb_guard[0])
                hb_guard[0] = 0
            sMS.inc(1)
            ph_.act(lambda e, fg=fg, b=bank, sh=sMS.h: e.activation(
                hb[:, fg * 512:(fg + 1) * 512], b[:, :], AF.Silu).then_inc(sh, 1))
            ph_.wait("dve", sMS, sMS.v)
            pf.inc(1)
            ph_.dve(lambda e, sh=pf.h: e.nop().then_inc(sh, 1))
        return consumer

    def fh_consumer(ph_, fg, tt, bank, wf, wf_tgt, pf):
        ph_.wait("dve", wf, wf_tgt)
        pf.inc(1)
        ph_.dve(lambda e, fg=fg, b=bank, sh=pf.h: e.tensor_tensor(
            hb[:, fg * 512:(fg + 1) * 512], b[:, :],
            hb[:, fg * 512:(fg + 1) * 512], ALU.mult).then_inc(sh, 1))

    halves = [(0, 2816, 22), (2816, 2688, 21)]
    for t2 in range(2):
        for (coff, fsz, nhc) in halves:
            def fdown_consumer(ph_, dt, bank, wf, wf_tgt, pf, t2=t2):
                ph_.wait("dve", wf, wf_tgt)
                pf.inc(1)
                ph_.dve(lambda e, dt=dt, b=bank, sh=pf.h: e.tensor_tensor(
                    x_s[:, dt * 1024 + t2 * 512: dt * 1024 + t2 * 512 + 512],
                    x_s[:, dt * 1024 + t2 * 512: dt * 1024 + t2 * 512 + 512],
                    b[:, :], ALU.add).then_inc(sh, 1))
            hb_guard = [hb_free_at[0]]
            emit_proj(pr, phf, wg_o, fsz, hn, fsilu_consumer(hb_guard),
                      row_len=5504, w_off=coff, tts=(t2,))
            emit_proj(pr, phf, wu_o, fsz, hn, fh_consumer,
                      row_len=5504, w_off=coff, tts=(t2,))
            emit_down(pr, phf, wd_o, nhc,
                      lambda hc, c0=coff: hb[:, (hc - c0 // 128) * 512:
                                             (hc - c0 // 128 + 1) * 512],
                      fdown_consumer, row_len=256, hc0=coff // 128,
                      dtg_stride=5504 * 256)
            hb_free_at[0] = pr.sem("wf").v

    phs = pr.phase("store")
    store_stream(pr, phs, x_s, y_out, dsem)
    phs.wait("sp", dsem, dsem.v)
    pr.arena_reset()


def build_l1f(upto=None):
    pr = Prog("l1f")
    nc = pr.nc
    x_in = nc.dram_tensor("x", [D, TPC], F32, kind="ExternalInput")
    y_out = nc.dram_tensor("y", [D, TPC], F32, kind="ExternalOutput")
    cons = alloc_consts(pr)
    pr.init_arena()
    emit_l1f(pr, cons, x_in, y_out)
    pr.emit()
    return pr.nc


PAY1_Q, PAY1_K, PAY1_V = 0, 1024, 2048
PAYL1 = 4096
RSQ128 = float(1.0 / np.sqrt(128.0))


PAY1_Q, PAY1_K, PAY1_V = 0, 1024, 2048
PAYL1 = 4096
RSQ128 = float(1.0 / np.sqrt(128.0))



# version 04
# speedup vs baseline: 1.2711x; 1.2286x over previous
import sys
sys.path.insert(0, "/opt/trn_rl_repo")

import numpy as np
import concourse.bass as bass
import concourse.mybir as mybir
from concourse.bass_utils import run_bass_kernel_spmd

F32 = mybir.dt.float32
BF16 = mybir.dt.bfloat16
I32 = mybir.dt.int32
AF = mybir.ActivationFunctionType
ALU = mybir.AluOpType
AX = mybir.AxisListType

B, S, D = 2, 4096, 2048
NCORE = 8
TPC = 1024
WRING = 3
NCH = D // 128
NEG = -1.0e30

L0_HD = 128
PAY_Q, PAY_K, PAY_V = 0, 2048, 2560
PAYR = 3072

_uid = [0]


def uid():
    _uid[0] += 1
    return _uid[0]


class Sem:
    def __init__(self, nc, name):
        self.h = nc.alloc_semaphore(name)
        self.v = 0

    def inc(self, n=1):
        self.v += n
        return self.v



_WRAP_OPS = {
    "tensor_copy", "tensor_tensor", "tensor_scalar", "scalar_tensor_tensor",
    "tensor_reduce", "reciprocal", "activation", "copy", "mul", "add",
    "square", "sqrt", "memset", "select", "reduce_max", "reduce_sum",
    "tensor_add", "tensor_sub", "tensor_mul", "tensor_max",
}


class InstWrap:

    def __init__(self, proxy):
        self._p = proxy

    def then_inc(self, sh, n=1):
        p = self._p
        p._e.wait_ge(p._semh, p._cnt[0])
        p._e.nop().then_inc(sh, n)
        return self


class EngProxy:

    def __init__(self, e, semh, cnt):
        self._e = e
        self._semh = semh
        self._cnt = cnt

    def __getattr__(self, name):
        f = getattr(self._e, name)
        if name in _WRAP_OPS:
            def g(*a, **k):
                if self._cnt[0] > 0:
                    self._e.wait_ge(self._semh, self._cnt[0])
                inst = f(*a, **k)
                inst.then_inc(self._semh, 1)
                self._cnt[0] += 1
                return InstWrap(self)
            return g
        if name == "nop":
            def g2():
                if self._cnt[0] > 0:
                    self._e.wait_ge(self._semh, self._cnt[0])
                return f()
            return g2
        return f


class Ph:
    ENGS = ("pe", "dve", "act", "pool", "sp")

    def __init__(self, name=""):
        self.name = name
        self.uses_reg = False
        self.q = {e: [] for e in self.ENGS}

    def add(self, eng, fn):
        self.q[eng].append(fn)

    def pe(self, fn): self.add("pe", fn)
    def dve(self, fn): self.add("dve", fn)
    def act(self, fn): self.add("act", fn)
    def pool(self, fn): self.add("pool", fn)
    def sp(self, fn): self.add("sp", fn)

    def wait(self, eng, sem, val):
        self.add(eng, lambda e, s=sem.h, v=val: e.wait_ge(s, v))

    def emit(self, nc, prog=None):
        with nc.Block() as b:
            if self.q["pe"]:
                @b.tensor
                def _(e):
                    for f in self.q["pe"]:
                        f(e)
            if self.q["dve"]:
                @b.vector
                def _(e):
                    ep = EngProxy(e, prog.self_dve.h, prog.cnt_dve)
                    for f in self.q["dve"]:
                        f(ep)
            if self.q["act"]:
                @b.scalar
                def _(e):
                    ep = EngProxy(e, prog.self_act.h, prog.cnt_act)
                    for f in self.q["act"]:
                        f(ep)
            if self.q["pool"]:
                if self.uses_reg:
                    @b.gpsimd
                    def _(e):
                        with e.register(f"rp{uid()}") as r:
                            e._shared_reg = r
                            for f in self.q["pool"]:
                                f(e)
                else:
                    @b.gpsimd
                    def _(e):
                        for f in self.q["pool"]:
                            f(e)
            if self.q["sp"]:
                @b.sync
                def _(e):
                    for f in self.q["sp"]:
                        f(e)


class Prog:
    def __init__(self, name):
        self.nc = bass.Bass("TRN2", target_bir_lowering=False)
        self.phases = []
        self.sems = {}
        self.ps = [self.nc.alloc_psum_tensor(f"ps{i}", [128, 512], F32)
                   for i in range(7)]
        self.psm = self.nc.alloc_psum_tensor("psmisc", [128, 512], F32)
        nc = self.nc
        self.wbuf = [sbuf(nc, f"wbuf{i}", [128, 1024], BF16) for i in range(WRING)]
        self.scbf = [sbuf(nc, f"scbf{i}", [128, 1024], BF16) for i in range(4)]
        self.rowb = sbuf(nc, "rowb", [1, 3072], F32)
        self.gidx = 0
        self.grp_par = 0
        self.pf_marks = [0, 0]
        self.self_dve = Sem(nc, "self_dve")
        self.self_act = Sem(nc, "self_act")
        self.cnt_dve = [0]
        self.cnt_act = [0]
        self.bcA = sbuf(nc, "bcA", [128, 1024], F32)
        self.bcB = sbuf(nc, "bcB", [128, 1024], F32)
        self.tmpF = sbuf(nc, "tmpF", [128, 1024], F32)
        self.arena_h = None
        self.arena_off = 0
        self.arena_sz = 0

    def init_arena(self):
        import concourse.mybir as _mb
        rem = self.nc.sbuf_bytes_remaining - 1024
        self.arena_h = self.nc.alloc_sbuf_tensor("blockarena", [128, rem], _mb.dt.uint8)
        self.arena_base = self.nc.lookup_mloc(self.arena_h).addr
        self.arena_sz = rem
        self.arena_off = 0

    def ablock(self, name, shape, dtype):
        import concourse.mybir as _mb
        per = 1
        for s in shape[1:]:
            per *= s
        per *= _mb.dt.size(dtype)
        per = (per + 31) // 32 * 32
        assert self.arena_off + per <= self.arena_sz, \
            f"block arena overflow: {name} needs {per}, have {self.arena_sz - self.arena_off}"
        t = self.nc.alloc_sbuf_tensor_at(name, shape, dtype,
                                         offset=self.arena_base + self.arena_off)
        self.arena_off += per
        return t

    def arena_reset(self):
        self.arena_off = 0

    def sem(self, name):
        if name not in self.sems:
            self.sems[name] = Sem(self.nc, name)
        return self.sems[name]

    def phase(self, name=""):
        p = Ph(name)
        self.phases.append(p)
        return p

    def emit(self):
        for p in self.phases:
            p.emit(self.nc, self)


def sbuf(nc, name, shape, dtype):
    return nc.alloc_sbuf_tensor(name, shape, dtype)




def proj_chunks(n_dch, F):
    n_ft = F // 128
    out = []
    for i in range(0, n_ft, 2):
        nft = min(2, n_ft - i)
        for dg in range(n_dch // 4):
            out.append((nft * 128, dg, i))
    return out


def down_chunks(n_hch):
    hg = []
    i = 0
    while i < n_hch:
        n = min(4, n_hch - i)
        hg.append((i, n))
        i += n
    out = []
    for dtg in range(8):
        for (h0, nh) in hg:
            out.append((nh, h0, dtg))
    return out


def pack_proj(W, n_dch, F, col_off=0):
    parts = []
    for (cols, dg, ft0) in proj_chunks(n_dch, F):
        blk = W[dg * 512:(dg + 1) * 512, col_off + ft0 * 128: col_off + ft0 * 128 + cols]
        parts.append(np.ascontiguousarray(
            blk.reshape(4, 128, cols).transpose(1, 0, 2)).reshape(-1))
    return np.concatenate(parts)


def pack_down(W, n_hch, row_off=0):
    parts = []
    for (nh, h0, dtg) in down_chunks(n_hch):
        blk = W[row_off + h0 * 128: row_off + (h0 + nh) * 128,
                dtg * 256:(dtg + 1) * 256]
        parts.append(np.ascontiguousarray(
            blk.reshape(nh, 128, 256).transpose(1, 0, 2)).reshape(-1))
    return np.concatenate(parts)


def emit_proj(pr, ph, wdram, F, xn_s, consumer, n_dch=NCH, w_off=0, row_len=None,
              tts=(0, 1), xn_cols=None, packed=False):
    if row_len is None:
        row_len = F
    wl = [pr.sem(f"wl{i}") for i in range(WRING)]
    wf = pr.sem("wf")
    pf = pr.sem("pf")
    if xn_cols is None:
        def xn_cols(dc, tt):
            return xn_s[:, dc * 1024 + tt * 512: dc * 1024 + tt * 512 + 512]

    assert n_dch % 4 == 0
    n_dg = n_dch // 4
    n_ft = F // 128
    groups = [(i, min(2, n_ft - i)) for i in range(0, n_ft, 2)]

    gidx_load = pr.gidx
    if packed:
        if not hasattr(pr, "wpack_off"):
            pr.wpack_off = {}
        poff = pr.wpack_off.get(wdram.name, 0)
    for (ft0, nft) in groups:
        cols = nft * 128
        for dg in range(n_dg):
            gidx = gidx_load
            gidx_load += 1
            par = gidx % WRING
            if gidx >= WRING:
                ph.wait("pool", wf, gidx - WRING + 1)
            if packed:
                s = bass.AP(wdram, poff, [[4 * cols, 128], [cols, 4], [1, cols]])
                poff += 512 * cols
            else:
                s = bass.AP(wdram, w_off + dg * 512 * row_len + ft0 * 128,
                            [[row_len, 128], [128 * row_len, 4], [1, cols]])
            d = bass.AP(pr.wbuf[par], 0, [[1024, 128], [cols, 4], [1, cols]])
            wl[par].inc(16)
            ph.pool(lambda e, s=s, d=d, sh=wl[par].h:
                    e.dma_start(d, s).then_inc(sh, 16))
    if packed:
        pr.wpack_off[wdram.name] = poff

    for (ft0, nft) in groups:
        cols = nft * 128
        ntiles = nft * len(tts)
        grp = pr.grp_par % 2
        pr.grp_par += 1
        banks = [pr.ps[0], pr.ps[1], pr.ps[2], pr.ps[3]] if grp == 0 else \
                [pr.ps[4], pr.ps[5], pr.ps[6], pr.psm]
        ph.wait("pe", pf, pr.pf_marks[-2])
        for dg in range(n_dg):
            gidx = wf.v
            par = gidx % WRING
            ph.wait("pe", wl[par], 16 * (gidx // WRING + 1))
            for j in range(4):
                st = (dg == 0 and j == 0)
                sp_ = (dg == n_dg - 1 and j == 3)
                for f in range(nft):
                    for ti, tt in enumerate(tts):
                        lhs = pr.wbuf[par][:, j * cols + f * 128: j * cols + f * 128 + 128]
                        rhs = xn_cols(dg * 4 + j, tt)
                        out = banks[f * len(tts) + ti][:, :]
                        is_last = (j == 3 and f == nft - 1 and ti == len(tts) - 1)
                        if is_last:
                            wf.inc(1)
                            ph.pe(lambda e, o=out, l=lhs, r=rhs, s0=st, s1=sp_,
                                  swf=wf.h: e.matmul(o, l, r, start=s0, stop=s1)
                                  .then_inc(swf, 1))
                        else:
                            ph.pe(lambda e, o=out, l=lhs, r=rhs, s0=st, s1=sp_:
                                  e.matmul(o, l, r, start=s0, stop=s1))
        wf_tgt = wf.v
        for f in range(nft):
            for ti, tt in enumerate(tts):
                consumer(ph, ft0 + f, tt, banks[f * len(tts) + ti], wf, wf_tgt, pf)
        pr.pf_marks.append(pf.v)
    pr.gidx = gidx_load


def emit_down(pr, ph, wdram, n_hch, h_cols, consumer, w_off=0, row_len=D, tt=0,
              hc0=0, dtg_stride=None, packed=False, two_tt=False):
    wl = [pr.sem(f"wl{i}") for i in range(WRING)]
    wf = pr.sem("wf")
    pf = pr.sem("pf")
    gidx_load = pr.gidx
    hgroups = []
    i = 0
    while i < n_hch:
        n = min(4, n_hch - i)
        hgroups.append((i, n))
        i += n
    if dtg_stride is None:
        dtg_stride = 256
    if packed:
        if not hasattr(pr, "wpack_off"):
            pr.wpack_off = {}
        poff = pr.wpack_off.get(wdram.name, 0)
    for dtg in range(8):
        for (h0, nh) in hgroups:
            gidx = gidx_load
            gidx_load += 1
            par = gidx % WRING
            if gidx >= WRING:
                ph.wait("pool", wf, gidx - WRING + 1)
            if packed:
                s = bass.AP(wdram, poff, [[nh * 256, 128], [256, nh], [1, 256]])
                poff += nh * 128 * 256
            else:
                s = bass.AP(wdram, w_off + (hc0 + h0) * 128 * row_len + dtg * dtg_stride,
                            [[row_len, 128], [128 * row_len, nh], [1, 256]])
            d = bass.AP(pr.wbuf[par], 0, [[1024, 128], [256, nh], [1, 256]])
            wl[par].inc(16)
            ph.pool(lambda e, s=s, d=d, sh=wl[par].h:
                    e.dma_start(d, s).then_inc(sh, 16))
    if packed:
        pr.wpack_off[wdram.name] = poff
    h_ready = pf.v
    for dtg in range(8):
        grp = pr.grp_par % 2
        pr.grp_par += 1
        if two_tt:
            banks = [pr.ps[0], pr.ps[1], pr.ps[2], pr.ps[3]] if grp == 0 else \
                    [pr.ps[4], pr.ps[5], pr.ps[6], pr.psm]
        else:
            banks = [pr.ps[0], pr.ps[1]] if grp == 0 else [pr.ps[4], pr.ps[5]]
        ph.wait("pe", pf, max(pr.pf_marks[-2], h_ready))
        for gi, (h0, nh) in enumerate(hgroups):
            gidx = wf.v
            par = gidx % WRING
            ph.wait("pe", wl[par], 16 * (gidx // WRING + 1))
            for j in range(nh):
                hc = hc0 + h0 + j
                st = (gi == 0 and j == 0)
                sp_ = (gi == len(hgroups) - 1 and j == nh - 1)
                for f in range(2):
                    lhs = pr.wbuf[par][:, j * 256 + f * 128: j * 256 + f * 128 + 128]
                    for ti in range(2 if two_tt else 1):
                        rhs = h_cols(hc, ti) if two_tt else h_cols(hc)
                        out = banks[f * 2 + ti][:, :] if two_tt else banks[f][:, :]
                        is_last = (j == nh - 1 and f == 1 and
                                   ti == (1 if two_tt else 0))
                        if is_last:
                            wf.inc(1)
                            ph.pe(lambda e, o=out, l=lhs, r=rhs, s0=st, s1=sp_,
                                  swf=wf.h: e.matmul(o, l, r, start=s0, stop=s1)
                                  .then_inc(swf, 1))
                        else:
                            ph.pe(lambda e, o=out, l=lhs, r=rhs, s0=st, s1=sp_:
                                  e.matmul(o, l, r, start=s0, stop=s1))
        wf_tgt = wf.v
        for f in range(2):
            if two_tt:
                for ti in range(2):
                    consumer(ph, dtg * 2 + f, ti, banks[f * 2 + ti], wf, wf_tgt, pf)
            else:
                consumer(ph, dtg * 2 + f, banks[f], wf, wf_tgt, pf)
        pr.pf_marks.append(pf.v)
    pr.gidx = gidx_load


def emit_ln(pr, x_s, xn_s, cons, tag, rms_w=None, eps=1e-5, src_bf=False):
    rms = rms_w is not None
    xb = [pr.scbf[0], pr.scbf[1]]
    sq = [pr.scbf[2], pr.scbf[3]]
    rows = pr.rowb
    bc_r, bc_m, tmp = pr.bcA, pr.bcB, pr.tmpF

    sX = pr.sem(f"lnx_{tag}")
    sQ = pr.sem(f"lnq_{tag}")
    sF = pr.sem(f"lnf_{tag}")
    sS = pr.sem(f"lns_{tag}")
    sX0, sQ0, sF0, sS0 = sX.v, sQ.v, sF.v, sS.v

    ph = pr.phase(f"ln_{tag}_stats")
    for ch in range(NCH):
        par = ch % 2
        if not src_bf:
            if ch >= 2:
                ph.wait("dve", sF, sF0 + 2 * (ch - 2) + 2)
            sX.inc(1)
            ph.dve(lambda e, c=ch, p=par, sh=sX.h: e.tensor_copy(
                xb[p][:, :], x_s[:, c * 1024:(c + 1) * 1024]).then_inc(sh, 1))
        if ch >= 2:
            ph.wait("act", sF, sF0 + 2 * (ch - 2) + 2)
        if not src_bf:
            ph.wait("act", sX, sX0 + ch + 1)
        sQ.inc(1)
        if src_bf:
            ph.act(lambda e, c=ch, p=par, sh=sQ.h: e.activation(
                sq[p][:, :], x_s[:, c * 1024:(c + 1) * 1024], AF.Square).then_inc(sh, 1))
        else:
            ph.act(lambda e, p=par, sh=sQ.h: e.activation(
                sq[p][:, :], xb[p][:, :], AF.Square).then_inc(sh, 1))
        ph.wait("pe", sQ, sQ0 + ch + 1)
        st = (ch == 0)
        sp_ = (ch == NCH - 1)
        slots = [pr.psm[0:1, :], pr.psm[32:33, :], pr.psm[64:65, :], pr.ps[4][0:1, :]]
        for tt in range(2):
            if not rms:
                xbsrc = (x_s[:, ch * 1024 + tt * 512: ch * 1024 + tt * 512 + 512]
                         if src_bf else xb[par][:, tt * 512:(tt + 1) * 512])
                ph.pe(lambda e, s0=st, s1=sp_, o=slots[tt], xs=xbsrc: e.matmul(
                    o, cons["ones_col_bf"][:, :], xs, start=s0, stop=s1))
            sF.inc(1)
            ph.pe(lambda e, p=par, t=tt, s0=st, s1=sp_, sh=sF.h, o=slots[2 + tt]: e.matmul(
                o, cons["ones_col_bf"][:, :],
                sq[p][:, t * 512:(t + 1) * 512], start=s0, stop=s1).then_inc(sh, 1))

    ph2 = pr.phase(f"ln_{tag}_mid")
    inv_d = 1.0 / D

    def rowmath(e):
        if not rms:
            e.tensor_copy(rows[0:1, 0:512], pr.psm[0:1, :])
            e.tensor_copy(rows[0:1, 512:1024], pr.psm[32:33, :])
        e.tensor_copy(rows[0:1, 1024:1536], pr.psm[64:65, :])
        e.tensor_copy(rows[0:1, 1536:2048], pr.ps[4][0:1, :])
        if not rms:
            e.tensor_scalar(rows[0:1, 0:1024], rows[0:1, 0:1024], inv_d, None, ALU.mult)
        e.tensor_scalar(rows[0:1, 1024:2048], rows[0:1, 1024:2048], inv_d, None, ALU.mult)
        if not rms:
            e.tensor_tensor(rows[0:1, 2048:3072], rows[0:1, 0:1024],
                            rows[0:1, 0:1024], ALU.mult)
            e.tensor_tensor(rows[0:1, 1024:2048], rows[0:1, 1024:2048],
                            rows[0:1, 2048:3072], ALU.subtract)
    ph2.dve(rowmath)
    sS.inc(1)
    ph2.dve(lambda e, sh=sS.h: e.nop().then_inc(sh, 1))
    ph2.wait("act", sS, sS0 + 1)
    epsap = cons["cvals"][0:1, (0 if eps == 1e-5 else 1):(1 if eps == 1e-5 else 2)]
    ph2.act(lambda e, b=epsap: e.activation(rows[0:1, 2048:3072], rows[0:1, 1024:2048],
                                            AF.Sqrt, bias=b))
    sS.inc(1)
    ph2.act(lambda e, sh=sS.h: e.nop().then_inc(sh, 1))
    ph2.wait("dve", sS, sS0 + 2)
    ph2.dve(lambda e: e.reciprocal(rows[0:1, 2048:3072], rows[0:1, 2048:3072]))
    if not rms:
        ph2.dve(lambda e: e.scalar_tensor_tensor(
            rows[0:1, 0:1024], rows[0:1, 0:1024], -1.0, rows[0:1, 2048:3072],
            ALU.mult, ALU.mult))
    sS.inc(1)
    ph2.dve(lambda e, sh=sS.h: e.nop().then_inc(sh, 1))
    ph2.wait("pe", sS, sS0 + 3)
    mms = []
    for tt in range(2):
        mms.append((pr.ps[tt], rows[0:1, 2048 + tt * 512: 2048 + (tt + 1) * 512]))
        if not rms:
            mms.append((pr.ps[2 + tt], rows[0:1, tt * 512:(tt + 1) * 512]))
    for i, (bank, rsrc) in enumerate(mms):
        if i == len(mms) - 1:
            sS.inc(1)
            ph2.pe(lambda e, b=bank, r=rsrc, sh=sS.h: e.matmul(
                b[:, :], cons["ones_row_f32"][:, :], r, start=True, stop=True)
                .then_inc(sh, 1))
        else:
            ph2.pe(lambda e, b=bank, r=rsrc: e.matmul(
                b[:, :], cons["ones_row_f32"][:, :], r, start=True, stop=True))
    ph2.wait("dve", sS, sS0 + 4)
    ph2.dve(lambda e: e.tensor_copy(bc_r[:, 0:512], pr.ps[0][:, :]))
    ph2.dve(lambda e: e.tensor_copy(bc_r[:, 512:1024], pr.ps[1][:, :]))
    if not rms:
        ph2.dve(lambda e: e.tensor_copy(bc_m[:, 0:512], pr.ps[2][:, :]))
        ph2.dve(lambda e: e.tensor_copy(bc_m[:, 512:1024], pr.ps[3][:, :]))

    ph3 = pr.phase(f"ln_{tag}_norm")
    for ch in range(NCH):
        ph3.dve(lambda e, c=ch: e.tensor_tensor(
            tmp[:, :], x_s[:, c * 1024:(c + 1) * 1024], bc_r[:, :], ALU.mult))
        if rms:
            ph3.dve(lambda e, c=ch: e.tensor_scalar(
                xn_s[:, c * 1024:(c + 1) * 1024], tmp[:, :],
                rms_w[:, c:c + 1], None, ALU.mult))
        else:
            ph3.dve(lambda e, c=ch: e.tensor_tensor(
                xn_s[:, c * 1024:(c + 1) * 1024], tmp[:, :], bc_m[:, :], ALU.add))


def alloc_consts(pr):
    nc = pr.nc
    cons = {}
    cons["ones_col_bf"] = sbuf(nc, "ones_col_bf", [128, 1], BF16)
    cons["ones_row_bf"] = sbuf(nc, "ones_row_bf", [1, 128], BF16)
    cons["ones_row_f32"] = sbuf(nc, "ones_row_f32", [1, 128], F32)
    cons["ident_bf"] = sbuf(nc, "ident_bf", [128, 128], BF16)
    cons["ident_f32"] = sbuf(nc, "ident_f32", [128, 128], F32)
    cons["ones_col_f32"] = sbuf(nc, "ones_col_f32", [128, 1], F32)
    cons["cvals"] = sbuf(nc, "cvals", [1, 8], F32)
    it_cv = nc.inline_tensor(np.array([[1e-5, 1e-6, 0, 0, 0, 0, 0, 0]], np.float32), name="it_cv")
    it_onec = nc.inline_tensor(np.ones((128, 1), np.float32), name="it_onec")
    it_oner = nc.inline_tensor(np.ones((1, 128), np.float32), name="it_oner")
    it_id = nc.inline_tensor(np.eye(128, dtype=np.float32), name="it_id")
    ph = pr.phase("consts")
    ds = pr.sem("dmap")
    for dst, src in ((cons["ones_col_bf"], it_onec), (cons["ones_row_bf"], it_oner),
                     (cons["ones_row_f32"], it_oner), (cons["ident_bf"], it_id),
                     (cons["ident_f32"], it_id), (cons["ones_col_f32"], it_onec),
                     (cons["cvals"], it_cv)):
        ds.inc(16)
        ph.pool(lambda e, d=dst, s=src, sh=ds.h: e.dma_start(d[:, :], s[:, :]).then_inc(sh, 16))
    ph.wait("pool", ds, ds.v)
    return cons


def load_stream(pr, ph, x_dram, x_s, dsem):
    src = bass.AP(x_dram, 0, [[1024, 128], [128 * 1024, 16], [1, 1024]])
    dst = bass.AP(x_s, 0, [[16 * 1024, 128], [1024, 16], [1, 1024]])
    dsem.inc(16)
    ph.sp(lambda e, s=src, d=dst, sh=dsem.h: e.dma_start(d, s).then_inc(sh, 16))


def store_stream(pr, ph, x_s, y_dram, dsem):
    src = bass.AP(x_s, 0, [[16 * 1024, 128], [1024, 16], [1, 1024]])
    dst = bass.AP(y_dram, 0, [[1024, 128], [128 * 1024, 16], [1, 1024]])
    dsem.inc(16)
    ph.sp(lambda e, s=src, d=dst, sh=dsem.h: e.dma_start(d, s).then_inc(sh, 16))


def emit_l0a(pr, cons, x_in, y_out, P=""):
    nc = pr.nc
    wq = nc.dram_tensor(P + "wq", [D, D], F32, kind="ExternalInput")
    wk = nc.dram_tensor(P + "wk", [D, 512], F32, kind="ExternalInput")
    wv = nc.dram_tensor(P + "wv", [D, 512], F32, kind="ExternalInput")
    wo = nc.dram_tensor(P + "wo", [D, D], F32, kind="ExternalInput")
    qcos = nc.dram_tensor(P + "qcos", [128, TPC], F32, kind="ExternalInput")
    qsin = nc.dram_tensor(P + "qsin", [128, TPC], F32, kind="ExternalInput")
    kcos = nc.dram_tensor(P + "kcos", [128, TPC], F32, kind="ExternalInput")
    ksin = nc.dram_tensor(P + "ksin", [128, TPC], F32, kind="ExternalInput")
    cmask = nc.dram_tensor(P + "cmask", [512, 512], F32, kind="ExternalInput")
    meta = nc.dram_tensor(P + "meta", [1, 128], I32, kind="ExternalInput")
    pay_i = nc.dram_tensor(P + "pay_i", [PAYR, TPC], BF16)
    pay_o = nc.dram_tensor(P + "pay_o", [4 * PAYR, TPC], BF16)
    opay_i = nc.dram_tensor(P + "opay_i", [512, S], BF16)
    opay_o = nc.dram_tensor(P + "opay_o", [2048, S], BF16)

    dsem = pr.sem("dma")
    ccs = pr.sem("cc")

    x_s = pr.ablock(P + "x_s", [128, NCH * 1024], F32)
    xn_s = pr.ablock(P + "xn_s", [128, NCH * 1024], BF16)
    qro = pr.ablock(P + "qro", [128, NCH * 1024], BF16)
    kro = pr.ablock(P + "kro", [128, 4 * 1024], BF16)
    vts = pr.ablock(P + "vts", [128, 4 * 1024], BF16)
    ropes = {}
    for nm in ("qcos", "qsin", "kcos", "ksin"):
        ropes[nm] = pr.ablock(P + f"r_{nm}", [128, TPC], BF16)
    cmask_s = pr.ablock(P + "cmask_s", [128, 4 * 512], BF16)
    meta_s = pr.ablock(P + "meta_s", [1, 128], I32)
    normrow = pr.ablock(P + "normrow", [1, 1024], BF16)
    rsc = [pr.tmpF[:, 0:512], pr.tmpF[:, 512:1024]]

    ph = pr.phase("load")
    dmap = pr.sem("dmap")
    load_stream(pr, ph, x_in, x_s, dsem)
    for nm, t in (("qcos", qcos), ("qsin", qsin), ("kcos", kcos), ("ksin", ksin)):
        dmap.inc(16)
        ph.pool(lambda e, d=ropes[nm], s=t, sh=dmap.h: e.dma_start(d[:, :], s[:, :]).then_inc(sh, 16))
    csrc = bass.AP(cmask, 0, [[512, 128], [128 * 512, 4], [1, 512]])
    cdst = bass.AP(cmask_s, 0, [[4 * 512, 128], [512, 4], [1, 512]])
    dmap.inc(16)
    ph.pool(lambda e, s=csrc, d=cdst, sh=dmap.h: e.dma_start(d, s).then_inc(sh, 16))
    dsem.inc(16)
    ph.sp(lambda e, sh=dsem.h: e.dma_start(meta_s[:, :], meta[:, :]).then_inc(sh, 16))
    ph.wait("sp", dsem, dsem.v)
    ph.wait("pool", dmap, dmap.v)

    emit_ln(pr, x_s, xn_s, cons, "l0")

    def rope_consumer(out_s, costab, sintab):
        def consumer(ph_, fg, tt, pt, pg, pgtgt, pf):
            ph_.wait("dve", pg, pgtgt)
            co = costab[:, tt * 512:(tt + 1) * 512]
            si = sintab[:, tt * 512:(tt + 1) * 512]
            ocol = fg * 1024 + tt * 512

            def f(e, pt=pt, co=co, si=si, oc=ocol):
                e.tensor_tensor(rsc[0], pt[:, :], co, ALU.mult)
                e.tensor_tensor(rsc[1][0:64, :], pt[64:128, :], si[0:64, :], ALU.mult)
                e.tensor_tensor(rsc[1][64:128, :], pt[0:64, :], si[64:128, :], ALU.mult)
                e.tensor_tensor(out_s[:, oc:oc + 512], rsc[0], rsc[1], ALU.add)
            ph_.dve(f)
            pf.inc(1)
            ph_.dve(lambda e, s=pf.h: e.nop().then_inc(s, 1))
        return consumer

    def plain_consumer(out_s):
        def consumer(ph_, fg, tt, pt, pg, pgtgt, pf):
            ph_.wait("dve", pg, pgtgt)
            ocol = fg * 1024 + tt * 512
            ph_.dve(lambda e, pt=pt, oc=ocol: e.tensor_copy(out_s[:, oc:oc + 512], pt[:, :]))
            pf.inc(1)
            ph_.dve(lambda e, s=pf.h: e.nop().then_inc(s, 1))
        return consumer

    emit_proj(pr, pr.phase("proj_q"), wq, D, xn_s,
              rope_consumer(qro, ropes["qcos"], ropes["qsin"]), packed=True)
    emit_proj(pr, pr.phase("proj_k"), wk, 512, xn_s,
              rope_consumer(kro, ropes["kcos"], ropes["ksin"]), packed=True)
    emit_proj(pr, pr.phase("proj_v"), wv, 512, xn_s, plain_consumer(vts), packed=True)

    php3 = pr.phase("payload")
    for src, dstoff, nch, rowlen in ((qro, PAY_Q, 16, NCH * 1024),
                                     (kro, PAY_K, 4, 4 * 1024),
                                     (vts, PAY_V, 4, 4 * 1024)):
        s_ap = bass.AP(src, 0, [[rowlen, 128], [1024, nch], [1, 1024]])
        d_ap = bass.AP(pay_i, dstoff * TPC, [[1024, 128], [128 * 1024, nch], [1, 1024]])
        dsem.inc(16)
        php3.sp(lambda e, s=s_ap, d=d_ap, sh=dsem.h: e.dma_start(d, s).then_inc(sh, 16))
    php3.wait("sp", dsem, dsem.v)

    phg = pr.phase("gather_in")
    ccs.inc(1)
    phg.pool(lambda e, sh=ccs.h: e.collective_compute(
        "AllGather", ALU.bypass, replica_groups=[[0, 1, 2, 3], [4, 5, 6, 7]],
        ins=[pay_i.ap()], outs=[pay_o.ap()]).then_inc(sh, 1))
    phg.wait("pool", ccs, ccs.v)

    pha = pr.phase("attn")
    pha.uses_reg = True
    kT = qro
    KOF, VOF, VTOK, QOF = 0, 4096, 8192, 12288
    pt_s = [pr.ablock(P + f"pt{i}", [128, 512], BF16) for i in range(2)]
    es_s = pr.bcA[:, 0:512]
    nbf = pr.ablock(P + "nbf", [1, 512], BF16)
    bcS = pr.bcA[:, 512:1024]
    outt = pr.ablock(P + "outt", [128, 512], BF16)
    sqk = pr.scbf[0]
    rowb = pr.rowb

    sLD = pr.sem("aload")
    sSC = pr.sem("asc")
    sE = pr.sem("aE")
    sP = pr.sem("aP")
    sAV = pr.sem("aAV")
    sVT = pr.sem("aVT")
    sVE = pr.sem("aVE")
    sEP = pr.sem("aEP")
    sBC = pr.sem("aBC")
    sQQ = pr.sem("aQQ")
    sBN = pr.sem("aBN")
    aq = [pr.sem("aq0"), pr.sem("aq1")]

    def pool_kv(e):
        r0 = e._shared_reg
        e.reg_load(r0, meta_s[0:1, 0:1])
        e.dma_start(
            bass.AP(kT, KOF, [[NCH * 1024, 128], [1024, 4], [1, 1024]]),
            bass.AP(pay_o, r0, [[1024, 128], [PAYR * 1024, 4], [1, 1024]])
        ).then_inc(sLD.h, 16)
        e.reg_add(r0, r0, 512 * 1024)
        e.dma_start(
            bass.AP(qro, VOF, [[NCH * 1024, 128], [1024, 4], [1, 1024]]),
            bass.AP(pay_o, r0, [[1024, 128], [PAYR * 1024, 4], [1, 1024]])
        ).then_inc(sLD.h, 16)
    pha.pool(pool_kv)
    sLD.inc(32)
    unit_ld = sLD.v

    pha.wait("pe", sLD, unit_ld)
    for ti in range(32):
        par = ti % 2
        if ti >= 2:
            pha.wait("pe", sVE, sVE.v - 1)
        sVT.inc(1)
        pha.pe(lambda e, t=ti, p=par, sh=sVT.h: e.matmul(
            pr.ps[2 + p][:, 0:128], qro[:, VOF + t * 128: VOF + (t + 1) * 128],
            cons["ident_bf"][:, :], start=True, stop=True).then_inc(sh, 1))
        pha.wait("dve", sVT, sVT.v)
        sVE.inc(1)
        pha.dve(lambda e, t=ti, p=par, sh=sVE.h: e.tensor_copy(
            qro[:, VTOK + t * 128: VTOK + (t + 1) * 128],
            pr.ps[2 + p][:, 0:128]).then_inc(sh, 1))

    pha.wait("act", sLD, unit_ld)
    sEP0k = sEP.v
    for rb in range(4):
        sQQ.inc(1)
        pha.act(lambda e, rb=rb, sh=sQQ.h: e.activation(
            sqk[:, :], kT[:, KOF + rb * 1024: KOF + (rb + 1) * 1024],
            AF.Square).then_inc(sh, 1))
        pha.wait("pe", sQQ, sQQ.v)
        for tt in range(2):
            i = rb * 2 + tt
            if i > 0:
                pha.wait("pe", sEP, sEP0k + i)
            sBC.inc(1)
            pha.pe(lambda e, t=tt, sh=sBC.h: e.matmul(
                pr.ps[6][0:1, :], cons["ones_col_bf"][:, :],
                sqk[:, t * 512:(t + 1) * 512], start=True, stop=True).then_inc(sh, 1))
            pha.wait("dve", sBC, sBC.v)
            sEP.inc(1)
            pha.dve(lambda e, i=i, sh=sEP.h: e.tensor_reduce(
                rowb[0:1, 1027 + i:1028 + i], pr.ps[6][0:1, :], AX.X, ALU.max)
                .then_inc(sh, 1))
        if rb < 3:
            pha.wait("act", sBC, sBC.v)
    pha.wait("dve", sEP, sEP.v)
    pha.dve(lambda e: e.tensor_reduce(rowb[0:1, 1024:1025], rowb[0:1, 1027:1035],
                                      AX.X, ALU.max))
    pha.dve(lambda e: e.tensor_scalar(rowb[0:1, 1026:1027], rowb[0:1, 1024:1025],
                                      -1.0 / 22.62742, None, ALU.mult))
    sBN.inc(1)
    pha.dve(lambda e, sh=sBN.h: e.nop().then_inc(sh, 1))
    bounds_ready = sBN.v

    qld_free = [None]
    sqq = pr.scbf[1]
    for h in range(4):
        for st in range(8):
            it = h * 8 + st
            if st == 0:
                aq[0].inc(16)

                def load_q(e, h=h, w=qld_free[0]):
                    r2 = e._shared_reg
                    if w is not None:
                        e.wait_ge(sAV.h, w)
                    e.reg_load(r2, meta_s[0:1, 4 + h * 4: 5 + h * 4])
                    e.dma_start(
                        bass.AP(qro, QOF, [[NCH * 1024, 128], [1024, 4], [1, 1024]]),
                        bass.AP(pay_o, r2, [[1024, 128], [PAYR * 1024, 4], [1, 1024]])
                    ).then_inc(aq[0].h, 16)
                pha.pool(load_q)
            q_tgt = aq[0].v
            qcols = qro[:, QOF + st * 512: QOF + st * 512 + 512]

            pha.wait("act", aq[0], q_tgt)
            if it > 0:
                pha.wait("act", sSC, sSC.v)
            sQQ.inc(1)
            pha.act(lambda e, qc=qcols, sh=sQQ.h: e.activation(
                sqq[:, 0:512], qc, AF.Square).then_inc(sh, 1))
            pha.wait("pe", sQQ, sQQ.v)
            pha.wait("pe", sEP, sEP.v)
            sBC.inc(1)
            pha.pe(lambda e, sh=sBC.h: e.matmul(
                pr.ps[6][0:1, :], cons["ones_col_bf"][:, :], sqq[:, 0:512],
                start=True, stop=True).then_inc(sh, 1))
            pha.wait("dve", sBN, bounds_ready)
            pha.wait("dve", sBC, sBC.v)
            pha.dve(lambda e: e.tensor_scalar(
                rowb[0:1, 0:512], pr.ps[6][0:1, :],
                rowb[0:1, 1026:1027], -5.65685, ALU.mult, ALU.add))
            pha.dve(lambda e: e.tensor_copy(nbf[0:1, :], rowb[0:1, 0:512]))
            sEP.inc(1)
            pha.dve(lambda e, sh=sEP.h: e.nop().then_inc(sh, 1))
            negb_tgt = sEP.v

            n_tt = 4 * (st + 1)
            sAV0 = sAV.v
            sP0 = sP.v
            for tt in range(n_tt):
                par = tt % 2
                pha.wait("pe", aq[0], q_tgt)
                if tt == 0:
                    pha.wait("pe", sEP, negb_tgt)
                if tt >= 2:
                    pha.wait("pe", sP, sP0 + tt - 1)
                pha.pe(lambda e, t=tt, p=par, qc=qcols: e.matmul(
                    pr.ps[p][:, :], kT[:, KOF + t * 128: KOF + (t + 1) * 128],
                    qc, start=True, stop=False))
                sSC.inc(1)
                pha.pe(lambda e, p=par, sh=sSC.h: e.matmul(
                    pr.ps[p][:, :], cons["ones_row_bf"][:, :], nbf[0:1, :],
                    start=False, stop=True).then_inc(sh, 1))
                diag = (tt >= 4 * st)
                if diag:
                    kk = tt - 4 * st
                    pha.wait("dve", sSC, sSC.v)
                    pha.wait("dve", sP, sP0 + tt)
                    sE.inc(1)
                    pha.dve(lambda e, p=par, k=kk, sh=sE.h: e.tensor_tensor(
                        es_s, pr.ps[p][:, :],
                        cmask_s[:, k * 512:(k + 1) * 512], ALU.add).then_inc(sh, 1))
                    pha.wait("act", sE, sE.v)
                    if tt >= 2:
                        pha.wait("act", sAV, sAV0 + tt - 1)
                    sP.inc(1)
                    pha.act(lambda e, p2=par, sh=sP.h: e.activation(
                        pt_s[p2][:, :], es_s, AF.Exp).then_inc(sh, 1))
                else:
                    pha.wait("act", sSC, sSC.v)
                    if tt >= 2:
                        pha.wait("act", sAV, sAV0 + tt - 1)
                    sP.inc(1)
                    pha.act(lambda e, p=par, p2=par, sh=sP.h: e.activation(
                        pt_s[p2][:, :], pr.ps[p][:, :], AF.Exp).then_inc(sh, 1))
                pha.wait("pe", sP, sP.v)
                pha.pe(lambda e, t=tt, p2=par, st_=(tt == 0), sp2=(tt == n_tt - 1):
                       e.matmul(
                    pr.ps[4][:, 0:512], qro[:, VTOK + t * 128: VTOK + (t + 1) * 128],
                    pt_s[p2][:, :], start=st_, stop=sp2))
                sAV.inc(1)
                pha.pe(lambda e, p2=par, st_=(tt == 0), sp_=(tt == n_tt - 1), sh=sAV.h:
                       e.matmul(pr.psm[0:1, :], cons["ones_col_bf"][:, :],
                                pt_s[p2][:, :], start=st_, stop=sp_).then_inc(sh, 1))
            if st == 7:
                qld_free[0] = sAV.v

            pha.wait("dve", sAV, sAV.v)
            pha.dve(lambda e: e.tensor_scalar(rowb[0:1, 512:1024], pr.psm[0:1, :],
                                              1e-30, None, ALU.max))
            pha.dve(lambda e: e.reciprocal(rowb[0:1, 512:1024], rowb[0:1, 512:1024]))
            sEP.inc(1)
            pha.dve(lambda e, sh=sEP.h: e.nop().then_inc(sh, 1))
            pha.wait("pe", sEP, sEP.v)
            sBC.inc(1)
            pha.pe(lambda e, sh=sBC.h: e.matmul(
                pr.ps[6][:, :], cons["ones_row_f32"][:, :], rowb[0:1, 512:1024],
                start=True, stop=True).then_inc(sh, 1))
            pha.wait("dve", sBC, sBC.v)
            pha.dve(lambda e: e.tensor_copy(bcS, pr.ps[6][:, :]))
            pha.wait("dve", dsem, dsem.v)
            pha.dve(lambda e: e.tensor_tensor(outt[:, :], pr.ps[4][:, 0:512],
                                              bcS, ALU.mult))
            sEP.inc(1)
            pha.dve(lambda e, sh=sEP.h: e.nop().then_inc(sh, 1))
            out_tgt = sEP.v
            pha.wait("sp", sEP, out_tgt)
            osrc = bass.AP(outt, 0, [[512, 128], [1, 512]])
            odst = bass.AP(opay_i, h * 128 * S + st * 512, [[S, 128], [1, 512]])
            dsem.inc(16)
            pha.sp(lambda e, s1=osrc, d1=odst, sh=dsem.h: e.dma_start(d1, s1).then_inc(sh, 16))
            pha.wait("pe", sEP, out_tgt)

    pha.wait("sp", dsem, dsem.v)

    phg2 = pr.phase("gather_out")
    ccs.inc(1)
    phg2.pool(lambda e, sh=ccs.h: e.collective_compute(
        "AllGather", ALU.bypass, replica_groups=[[0, 1, 2, 3], [4, 5, 6, 7]],
        ins=[opay_i.ap()], outs=[opay_o.ap()]).then_inc(sh, 1))
    phg2.wait("pool", ccs, ccs.v)

    phr = pr.phase("read_out")
    phr.uses_reg = True
    ao = qro

    def read_own(e):
            r3 = e._shared_reg
            e.reg_load(r3, meta_s[0:1, 60:61])
            e.dma_start(bass.AP(ao, 0, [[NCH * 1024, 128], [1024, 16], [1, 1024]]),
                        bass.AP(opay_o, r3, [[S, 128], [128 * S, 16], [1, 1024]])).then_inc(sLD.h, 16)
    phr.pool(read_own)
    sLD.inc(16)
    phr.wait("pool", sLD, sLD.v)

    def resid_consumer(ph_, fg, tt, pt, pg, pgtgt, pf):
        ph_.wait("dve", pg, pgtgt)
        col = fg * 1024 + tt * 512
        ph_.dve(lambda e, pt=pt, c=col: e.tensor_tensor(
            x_s[:, c:c + 512], x_s[:, c:c + 512], pt[:, :], ALU.add))
        pf.inc(1)
        ph_.dve(lambda e, s=pf.h: e.nop().then_inc(s, 1))

    emit_proj(pr, pr.phase("proj_o"), wo, D, ao, resid_consumer, packed=True)

    phs = pr.phase("store")
    store_stream(pr, phs, x_s, y_out, dsem)
    phs.wait("sp", dsem, dsem.v)
    pr.arena_reset()


def build_l0a(upto=None):
    pr = Prog("l0a")
    nc = pr.nc
    x_in = nc.dram_tensor("x", [D, TPC], F32, kind="ExternalInput")
    y_out = nc.dram_tensor("y", [D, TPC], F32, kind="ExternalOutput")
    cons = alloc_consts(pr)
    pr.init_arena()
    emit_l0a(pr, cons, x_in, y_out)
    if upto is not None:
        names = [p.name for p in pr.phases]
        idx = names.index(upto)
        pr.phases = pr.phases[:idx + 1]
        tr = pr.sem("trunc")
        phs2 = pr.phase("store_trunc")
        x_s = nc.find_inst_by_name and None
        pr.phases = pr.phases
    pr.emit()
    return pr.nc


def rope_tables():
    hd = L0_HD
    freqs = 1.0 / (10000.0 ** (np.arange(0, hd, 2, dtype=np.float64) / hd))
    t = np.arange(S, dtype=np.float64)
    ang = t[:, None] * freqs[None, :]
    cos = np.concatenate([np.cos(ang), np.cos(ang)], axis=1)
    sin = np.sin(ang)
    ssin = np.concatenate([-sin, sin], axis=1)
    return cos.T.astype(np.float32), ssin.T.astype(np.float32)


def causal_masks():
    m = np.zeros((4, 128, 512), np.float32)
    ti = np.arange(128)[:, None]
    sj = np.arange(512)[None, :]
    for k in range(4):
        m[k] = np.where(ti + 128 * k <= sj, 0.0, NEG)
    return np.ascontiguousarray(m.reshape(512, 512))


def l0a_meta(core):
    kvg = core % 4
    m = np.zeros(128, np.int32)
    m[0] = (PAY_K + kvg * 128) * TPC
    for h in range(4):
        for rb in range(4):
            m[4 + h * 4 + rb] = (rb * PAYR + PAY_Q + (4 * kvg + h) * 128) * TPC
    m[60] = (core % 4) * TPC
    return m.reshape(1, 128)


_NC_CACHE = {}


def get_nc(name, builder):
    if name not in _NC_CACHE:
        _NC_CACHE[name] = builder()
    return _NC_CACHE[name]


def maps_l0a(inputs, c, P=""):
    cos, ssin = rope_tables()
    cm = causal_masks()
    scale = 1.0 / np.sqrt(float(L0_HD))
    t0 = (c % 4) * TPC
    return {
        P + "wq": pack_proj(np.asarray(inputs["L0_attn_wq"], np.float32),
                            NCH, D).reshape(D, D),
        P + "wk": pack_proj(np.asarray(inputs["L0_attn_wk"], np.float32),
                            NCH, 512).reshape(D, 512),
        P + "wv": pack_proj(np.asarray(inputs["L0_attn_wv"], np.float32),
                            NCH, 512).reshape(D, 512),
        P + "wo": pack_proj(np.asarray(inputs["L0_attn_wo"], np.float32),
                            NCH, D).reshape(D, D),
        P + "qcos": np.ascontiguousarray(cos[:, t0:t0 + TPC] * scale),
        P + "qsin": np.ascontiguousarray(ssin[:, t0:t0 + TPC] * scale),
        P + "kcos": np.ascontiguousarray(cos[:, t0:t0 + TPC]),
        P + "ksin": np.ascontiguousarray(ssin[:, t0:t0 + TPC]),
        P + "cmask": cm,
        P + "meta": l0a_meta(c),
    }


def run_l0a(x_shards, inputs, upto=None):
    nc = get_nc(f"l0a{upto}", lambda: build_l0a(upto))
    in_maps = [dict(x=x_shards[c], **maps_l0a(inputs, c)) for c in range(NCORE)]
    res = run_bass_kernel_spmd(nc, in_maps, core_ids=list(range(NCORE)))
    return [res.results[c]["y"] for c in range(NCORE)]


def shard_stream(x):
    out = []
    for c in range(NCORE):
        b, r = c // 4, c % 4
        out.append(np.ascontiguousarray(x[b, r * TPC:(r + 1) * TPC, :].T.astype(np.float32)))
    return out


def unshard_stream(shards):
    x = np.empty((B, S, D), np.float32)
    for c in range(NCORE):
        b, r = c // 4, c % 4
        x[b, r * TPC:(r + 1) * TPC, :] = shards[c].T
    return x


def kernel(**inputs):
    return kernel_fused(inputs)


def emit_l0f(pr, cons, x_in, y_out, P="", only=None):
    nc = pr.nc
    gwT = nc.dram_tensor(P + "gwT", [D, 8], F32, kind="ExternalInput")
    wg3 = nc.dram_tensor(P + "wg3", [8 * D, 1408], F32, kind="ExternalInput")
    wu3 = nc.dram_tensor(P + "wu3", [8 * D, 1408], F32, kind="ExternalInput")
    wd3 = nc.dram_tensor(P + "wd3", [8 * 1408, D], F32, kind="ExternalInput")
    shg = nc.dram_tensor(P + "shg", [128, 16], F32, kind="ExternalInput")
    shwg = nc.dram_tensor(P + "shwg", [D, 2816], F32, kind="ExternalInput")
    shwu = nc.dram_tensor(P + "shwu", [D, 2816], F32, kind="ExternalInput")
    shwd = nc.dram_tensor(P + "shwd", [2816, D], F32, kind="ExternalInput")
    dsem = pr.sem("dma")
    dmap = pr.sem("dmap")

    x_s = pr.ablock(P + "x_s", [128, NCH * 1024], F32)
    xn_s = pr.ablock(P + "xn_s", [128, NCH * 1024], BF16)
    hb = pr.ablock(P + "hb", [128, 22 * 1024], BF16)
    combrow = pr.ablock(P + "combrow", [1, 9 * 1024], BF16)
    gwT_s = pr.ablock(P + "gwT_s", [128, 16 * 8], F32)
    shg_s = pr.ablock(P + "shg_s", [128, 16], BF16)
    lgt = pr.ablock(P + "lgt", [128, 64], F32)
    cb_tok = pr.ablock(P + "cb_tok", [128, 8], F32)

    ph = pr.phase("load")
    load_stream(pr, ph, x_in, x_s, dsem)
    gsrc = bass.AP(gwT, 0, [[8, 128], [128 * 8, 16], [1, 8]])
    gdst = bass.AP(gwT_s, 0, [[128, 128], [8, 16], [1, 8]])
    dsem.inc(16)
    ph.sp(lambda e, s=gsrc, d=gdst, sh=dsem.h: e.dma_start(d, s).then_inc(sh, 16))
    dmap.inc(16)
    ph.pool(lambda e, sh=dmap.h: e.dma_start(shg_s[:, :], shg[:, :]).then_inc(sh, 16))
    ph.wait("sp", dsem, dsem.v)
    ph.wait("pool", dmap, dmap.v)

    emit_ln(pr, x_s, xn_s, cons, "l0f")
    rstd_row = pr.rowb[0:1, 2048:3072]
    negm_row = pr.rowb[0:1, 0:1024]

    phr = pr.phase("router")
    rtP = pr.sem("rtP")
    rtD = pr.sem("rtD")
    rtA = pr.sem("rtA")
    for dc in range(NCH):
        st, sp_ = dc == 0, dc == NCH - 1
        if sp_:
            rtP.inc(1)
            phr.pe(lambda e, dc=dc, sh=rtP.h: e.matmul(
                pr.ps[5][0:1, 0:8], cons["ones_col_f32"][:, :],
                gwT_s[:, dc * 8:(dc + 1) * 8], start=dc == 0, stop=True).then_inc(sh, 1))
        else:
            phr.pe(lambda e, dc=dc, s0=st: e.matmul(
                pr.ps[5][0:1, 0:8], cons["ones_col_f32"][:, :],
                gwT_s[:, dc * 8:(dc + 1) * 8], start=s0, stop=False))
    sg_done = []
    for t2 in range(2):
        for dc in range(NCH):
            sp_ = dc == NCH - 1
            if sp_:
                rtP.inc(1)
                phr.pe(lambda e, dc=dc, t2=t2, sh=rtP.h: e.matmul(
                    pr.ps[6][0:1, t2 * 256: t2 * 256 + 512] if False else pr.ps[6][0:1, 0:512],
                    shg_s[:, dc:dc + 1],
                    xn_s[:, dc * 1024 + t2 * 512: dc * 1024 + t2 * 512 + 512],
                    start=dc == 0, stop=True).then_inc(sh, 1))
            else:
                phr.pe(lambda e, dc=dc, t2=t2: e.matmul(
                    pr.ps[6][0:1, 0:512], shg_s[:, dc:dc + 1],
                    xn_s[:, dc * 1024 + t2 * 512: dc * 1024 + t2 * 512 + 512],
                    start=dc == 0, stop=False))
        phr.wait("act", rtP, rtP.v)
        rtA.inc(1)
        phr.act(lambda e, t2=t2, sh=rtA.h: e.activation(
            combrow[0:1, 8 * 1024 + t2 * 512: 8 * 1024 + (t2 + 1) * 512],
            pr.ps[6][0:1, 0:512], AF.Sigmoid).then_inc(sh, 1))
        sg_done.append(rtA.v)
        phr.wait("pe", rtA, rtA.v)

    phr.wait("dve", rtP, 1)
    rtD.inc(1)
    phr.dve(lambda e, sh=rtD.h: e.tensor_copy(
        lgt[0:1, 48:56], pr.ps[5][0:1, 0:8]).then_inc(sh, 1))

    for ts in range(8):
        for dc in range(NCH):
            sp_ = dc == NCH - 1
            lhs = x_s[:, dc * 1024 + ts * 128: dc * 1024 + ts * 128 + 128]
            if sp_:
                rtP.inc(1)
                phr.pe(lambda e, l=lhs, dc=dc, sh=rtP.h: e.matmul(
                    pr.ps[0][:, 0:8], l, gwT_s[:, dc * 8:(dc + 1) * 8],
                    start=dc == 0, stop=True).then_inc(sh, 1))
            else:
                phr.pe(lambda e, l=lhs, dc=dc: e.matmul(
                    pr.ps[0][:, 0:8], l, gwT_s[:, dc * 8:(dc + 1) * 8],
                    start=dc == 0, stop=False))
        if ts == 0:
            phr.wait("pe", rtD, rtD.v)
        rtP.inc(1)
        phr.pe(lambda e, ts=ts, sh=rtP.h: e.matmul(
            pr.ps[1][:, 0:8], negm_row[0:1, ts * 128:(ts + 1) * 128],
            lgt[0:1, 48:56], start=True, stop=True).then_inc(sh, 1))
        rtP.inc(1)
        phr.pe(lambda e, ts=ts, sh=rtP.h: e.matmul(
            pr.ps[2][:, 0:1], rstd_row[0:1, ts * 128:(ts + 1) * 128],
            cons["ones_row_f32"][0:1, 0:1], start=True, stop=True).then_inc(sh, 1))
        pe_tgt = rtP.v
        phr.wait("dve", rtP, pe_tgt)
        rtD.inc(1)

        def rmath(e, ts=ts, sh=rtD.h):
            e.tensor_copy(lgt[:, 37:38], pr.ps[2][:, 0:1])
            e.tensor_copy(lgt[:, 40:48], pr.ps[0][:, 0:8])
            e.scalar_tensor_tensor(lgt[:, 0:8], lgt[:, 40:48], lgt[:, 37:38],
                                   pr.ps[1][:, 0:8], ALU.mult, ALU.add)
            e.tensor_reduce(lgt[:, 32:33], lgt[:, 0:8], AX.X, ALU.max)
            e.tensor_scalar(lgt[:, 8:16], lgt[:, 0:8], lgt[:, 32:33], None, ALU.is_equal)
            e.scalar_tensor_tensor(lgt[:, 16:24], lgt[:, 8:16], NEG, lgt[:, 0:8],
                                   ALU.mult, ALU.add)
            e.tensor_reduce(lgt[:, 33:34], lgt[:, 16:24], AX.X, ALU.max)
            e.tensor_tensor(lgt[:, 34:35], lgt[:, 32:33], lgt[:, 33:34], ALU.subtract)
            e.tensor_scalar(lgt[:, 24:32], lgt[:, 0:8], lgt[:, 33:34], None, ALU.is_equal).then_inc(sh, 1)
        phr.dve(rmath)
        phr.wait("act", rtD, rtD.v)
        rtA.inc(1)
        phr.act(lambda e, sh=rtA.h: e.activation(
            lgt[:, 35:36], lgt[:, 34:35], AF.Sigmoid).then_inc(sh, 1))
        phr.wait("dve", rtA, rtA.v)
        rtD.inc(1)

        def rmath2(e, sh=rtD.h):
            e.tensor_scalar(lgt[:, 36:37], lgt[:, 35:36], -1.0, 1.0, ALU.mult, ALU.add)
            e.tensor_scalar(cb_tok[:, 0:8], lgt[:, 8:16], lgt[:, 35:36], None, ALU.mult)
            e.tensor_scalar(lgt[:, 24:32], lgt[:, 24:32], lgt[:, 36:37], None, ALU.mult)
            e.tensor_tensor(cb_tok[:, 0:8], cb_tok[:, 0:8], lgt[:, 24:32], ALU.add).then_inc(sh, 1)
        phr.dve(rmath2)
        phr.wait("pe", rtD, rtD.v)
        rtP.inc(1)
        phr.pe(lambda e, sh=rtP.h: e.matmul(
            pr.ps[3][0:8, 0:128], cb_tok[:, 0:8], cons["ident_f32"][:, :],
            start=True, stop=True).then_inc(sh, 1))
        phr.wait("dve", rtP, rtP.v)
        rtD.inc(1)
        phr.dve(lambda e, ts=ts, sh=rtD.h: e.tensor_copy(
            pr.bcA[0:8, ts * 128:(ts + 1) * 128], pr.ps[3][0:8, 0:128]).then_inc(sh, 1))
        phr.wait("pe", rtD, rtD.v)
    phr.wait("pool", rtD, rtD.v)
    dmap.inc(16)
    phr.pool(lambda e, sh=dmap.h: e.dma_start(
        bass.AP(combrow, 0, [[8192, 1], [1, 8192]]),
        bass.AP(pr.bcA, 0, [[1024, 8], [1, 1024]])).then_inc(sh, 16))
    phr.wait("pool", dmap, dmap.v)

    sCA = pr.sem("sCA")
    hb_free_at = [0]

    def mk_bc(ph_, e_idx):
        grp = pr.grp_par % 2
        pr.grp_par += 1
        banks = (pr.ps[0], pr.ps[1]) if grp == 0 else (pr.ps[4], pr.ps[5])
        pf = pr.sem("pf")
        ph_.wait("pe", pf, pr.pf_marks[-2])
        bcs = pr.sem("bcs")
        for t2 in range(2):
            bcs.inc(1)
            ph_.pe(lambda e, ei=e_idx, t2=t2, b=banks[t2], sh=bcs.h: e.matmul(
                b[:, :], cons["ones_row_bf"][:, :],
                combrow[0:1, ei * 1024 + t2 * 512: ei * 1024 + t2 * 512 + 512],
                start=True, stop=True).then_inc(sh, 1))
        ph_.wait("dve", bcs, bcs.v)
        pf.inc(1)

        def cp(e, b0=banks[0], b1=banks[1], sh=pf.h):
            e.tensor_copy(pr.bcB[:, 0:512], b0[:, :])
            e.tensor_copy(pr.bcB[:, 512:1024], b1[:, :]).then_inc(sh, 1)
        ph_.dve(cp)
        pr.pf_marks.append(pf.v)

    def silu_consumer(hb_guard):
        def consumer(ph_, fg, tt, bank, wf, wf_tgt, pf):
            ph_.wait("act", wf, wf_tgt)
            if hb_guard[0]:
                ph_.wait("act", wf, hb_guard[0])
                hb_guard[0] = 0
            sCA.inc(1)
            ph_.act(lambda e, fg=fg, tt=tt, b=bank, sh=sCA.h: e.activation(
                hb[:, fg * 1024 + tt * 512: fg * 1024 + tt * 512 + 512],
                b[:, :], AF.Silu).then_inc(sh, 1))
            ph_.wait("dve", sCA, sCA.v)
            pf.inc(1)
            ph_.dve(lambda e, sh=pf.h: e.nop().then_inc(sh, 1))
        return consumer

    def h_consumer(ph_, fg, tt, bank, wf, wf_tgt, pf):
        ph_.wait("dve", wf, wf_tgt)
        pf.inc(1)

        def f(e, fg=fg, tt=tt, b=bank, sh=pf.h):
            hc_ = hb[:, fg * 1024 + tt * 512: fg * 1024 + tt * 512 + 512]
            e.tensor_tensor(pr.tmpF[:, 0:512], b[:, :], hc_, ALU.mult)
            e.tensor_tensor(hc_, pr.tmpF[:, 0:512],
                            pr.bcB[:, tt * 512:(tt + 1) * 512], ALU.mult).then_inc(sh, 1)
        ph_.dve(f)

    def xacc_consumer(ph_, dt, ti, bank, wf, wf_tgt, pf):
        ph_.wait("dve", wf, wf_tgt)
        pf.inc(1)
        ph_.dve(lambda e, dt=dt, ti=ti, b=bank, sh=pf.h: e.tensor_tensor(
            x_s[:, dt * 1024 + ti * 512: dt * 1024 + ti * 512 + 512],
            x_s[:, dt * 1024 + ti * 512: dt * 1024 + ti * 512 + 512],
            b[:, :], ALU.add).then_inc(sh, 1))

    experts = [("sh",)] + [(e,) for e in range(8)]
    if only is not None:
        experts = [(e,) for e in only]
    phe = pr.phase("experts")
    for i, (e_id,) in enumerate(experts):
        ei = 8 if e_id == "sh" else e_id
        mk_bc(phe, ei)
        hb_guard = [hb_free_at[0]]
        if e_id == "sh":
            emit_proj(pr, phe, shwg, 2816, xn_s, silu_consumer(hb_guard),
                      packed=True)
            emit_proj(pr, phe, shwu, 2816, xn_s, h_consumer, packed=True)
            emit_down(pr, phe, shwd, 22,
                      lambda hc, ti: hb[:, hc * 1024 + ti * 512: hc * 1024 + ti * 512 + 512],
                      xacc_consumer, packed=True, two_tt=True)
        else:
            emit_proj(pr, phe, wg3, 1408, xn_s, silu_consumer(hb_guard),
                      w_off=e_id * D * 1408, row_len=1408, packed=True)
            emit_proj(pr, phe, wu3, 1408, xn_s, h_consumer,
                      w_off=e_id * D * 1408, row_len=1408, packed=True)
            emit_down(pr, phe, wd3, 11,
                      lambda hc, ti: hb[:, hc * 1024 + ti * 512: hc * 1024 + ti * 512 + 512],
                      xacc_consumer, w_off=e_id * 1408 * D, packed=True, two_tt=True)
        hb_free_at[0] = pr.sem("wf").v

    phs = pr.phase("store")
    store_stream(pr, phs, x_s, y_out, dsem)
    phs.wait("sp", dsem, dsem.v)
    pr.arena_reset()


def build_l0f(upto=None, only=None):
    pr = Prog("l0f")
    nc = pr.nc
    x_in = nc.dram_tensor("x", [D, TPC], F32, kind="ExternalInput")
    y_out = nc.dram_tensor("y", [D, TPC], F32, kind="ExternalOutput")
    cons = alloc_consts(pr)
    pr.init_arena()
    emit_l0f(pr, cons, x_in, y_out, only=only)
    pr.emit()
    return pr.nc


def maps_l0f(inputs, c, P=""):
    gwT = np.ascontiguousarray(np.asarray(inputs["L0_ffn_gate_w"], np.float32).T)
    wg3 = np.ascontiguousarray(np.asarray(inputs["L0_ffn_w_gate"], np.float32).reshape(8 * D, 1408))
    wu3 = np.ascontiguousarray(np.asarray(inputs["L0_ffn_w_up"], np.float32).reshape(8 * D, 1408))
    wd3 = np.ascontiguousarray(np.asarray(inputs["L0_ffn_w_down"], np.float32).reshape(8 * 1408, D))
    shg = np.ascontiguousarray(np.asarray(inputs["L0_ffn_shgate_w"], np.float32).reshape(16, 128).T)
    wg3p = np.concatenate([pack_proj(wg3[e * D:(e + 1) * D], NCH, 1408)
                           for e in range(8)]).reshape(8 * D, 1408)
    wu3p = np.concatenate([pack_proj(wu3[e * D:(e + 1) * D], NCH, 1408)
                           for e in range(8)]).reshape(8 * D, 1408)
    wd3p = np.concatenate([pack_down(wd3, 11, row_off=e * 1408)
                           for e in range(8)]).reshape(8 * 1408, D)
    return {
        P + "gwT": gwT, P + "wg3": wg3p, P + "wu3": wu3p, P + "wd3": wd3p,
        P + "shg": shg,
        P + "shwg": pack_proj(np.asarray(inputs["L0_ffn_shw_gate"], np.float32),
                              NCH, 2816).reshape(D, 2816),
        P + "shwu": pack_proj(np.asarray(inputs["L0_ffn_shw_up"], np.float32),
                              NCH, 2816).reshape(D, 2816),
        P + "shwd": pack_down(np.asarray(inputs["L0_ffn_shw_down"], np.float32),
                              22).reshape(2816, D),
    }


def run_l0f(x_shards, inputs, upto=None):
    nc = get_nc(f"l0f{upto}", lambda: build_l0f(upto))
    in_maps = [dict(x=x_shards[c], **maps_l0f(inputs, c)) for c in range(NCORE)]
    res = run_bass_kernel_spmd(nc, in_maps, core_ids=list(range(NCORE)))
    return [res.results[c]["y"] for c in range(NCORE)]


PAY1_Q, PAY1_K, PAY1_V = 0, 1024, 2048
PAYL1 = 4096
RSQ128 = float(1.0 / np.sqrt(128.0))


def emit_l1a(pr, cons, x_in, y_out, P=""):
    nc = pr.nc
    wq = nc.dram_tensor(P + "wq", [D, 1024], F32, kind="ExternalInput")
    wk = nc.dram_tensor(P + "wk", [D, 1024], F32, kind="ExternalInput")
    wv = nc.dram_tensor(P + "wv", [D, D], F32, kind="ExternalInput")
    wig = nc.dram_tensor(P + "wig", [D, 8], F32, kind="ExternalInput")
    wfg = nc.dram_tensor(P + "wfg", [D, 8], F32, kind="ExternalInput")
    bigc = nc.dram_tensor(P + "bigc", [8, 2], F32, kind="ExternalInput")
    wog = nc.dram_tensor(P + "wog", [D, D], F32, kind="ExternalInput")
    wout = nc.dram_tensor(P + "wout", [D, D], F32, kind="ExternalInput")
    wgate = nc.dram_tensor(P + "wgate", [D, 8192], F32, kind="ExternalInput")
    wup = nc.dram_tensor(P + "wup", [D, 8192], F32, kind="ExternalInput")
    wdown = nc.dram_tensor(P + "wdown", [8192, D], F32, kind="ExternalInput")
    n1c = nc.dram_tensor(P + "n1c", [128, 16], F32, kind="ExternalInput")
    n2c = nc.dram_tensor(P + "n2c", [128, 16], F32, kind="ExternalInput")
    cmask = nc.dram_tensor(P + "cmask", [512, 512], F32, kind="ExternalInput")
    meta = nc.dram_tensor(P + "meta", [1, 128], I32, kind="ExternalInput")
    pay_i = nc.dram_tensor(P + "pay_i", [PAYL1, TPC], BF16)
    pay_o = nc.dram_tensor(P + "pay_o", [4 * PAYL1, TPC], BF16)
    pg_i = nc.dram_tensor(P + "pg_i", [16, TPC], F32)
    pg_o = nc.dram_tensor(P + "pg_o", [64, TPC], F32)
    opay_i = nc.dram_tensor(P + "opay_i", [512, S], BF16)
    opay_o = nc.dram_tensor(P + "opay_o", [2048, S], BF16)

    selmy = nc.dram_tensor(P + "selmy", [8, 256], F32, kind="ExternalInput")
    i8my = nc.dram_tensor(P + "i8my", [8, 2], F32, kind="ExternalInput")
    mhmy = nc.dram_tensor(P + "mhmy", [128, 4], F32, kind="ExternalInput")

    dsem = pr.sem("dma")
    dmap = pr.sem("dmap")
    ccs = pr.sem("cc")

    arena = pr.ablock(P + "arena", [128, NCH * 1024], F32)
    abase = nc.lookup_mloc(arena).addr
    x_s = arena
    g_in = pr.ablock(P + "g_in", [128, NCH * 1024], BF16)
    h_in = pr.ablock(P + "h_in", [128, NCH * 1024], BF16)
    B1 = pr.ablock(P + "B1", [8, 4096], F32)
    B2 = pr.ablock(P + "B2", [8, 4096], F32)
    bbase = nc.lookup_mloc(B1).addr
    acol = pr.ablock(P + "acol", [128, 64], F32)
    n1_s = pr.ablock(P + "n1_s", [128, 16], F32)
    n2_s = pr.ablock(P + "n2_s", [128, 16], F32)
    mh_s = pr.ablock(P + "mh_s", [128, 4], F32)
    big_s = pr.ablock(P + "big_s", [8, 2], F32)
    cmask_s = pr.ablock(P + "cmask_s", [128, 4 * 512], BF16)
    meta_s = pr.ablock(P + "meta_s", [1, 128], I32)
    sel_s = pr.ablock(P + "sel_s", [8, 256], F32)
    i8_s = pr.ablock(P + "i8_s", [8, 2], F32)

    qkvT = nc.alloc_sbuf_tensor_at(P + "qkvT", [128, 32 * 1024], BF16, offset=abase)
    B3 = nc.alloc_sbuf_tensor_at(P + "B3", [8, 4096], F32, offset=abase)
    gI = nc.alloc_sbuf_tensor_at(P + "gI", [8, 4096], F32, offset=abase + 16384)
    EF = nc.alloc_sbuf_tensor_at(P + "EF", [128, 512], F32, offset=abase)
    PT2 = nc.alloc_sbuf_tensor_at(P + "PT2", [128, 1024], BF16, offset=abase + 2048)
    outst = nc.alloc_sbuf_tensor_at(P + "outst", [128, 512], BF16, offset=abase + 4096)
    sqsc = nc.alloc_sbuf_tensor_at(P + "sqsc", [128, 512], BF16, offset=abase + 5120)
    kTA = nc.alloc_sbuf_tensor_at(P + "kTA", [128, 4096], BF16, offset=abase + 16384)
    vstage = nc.alloc_sbuf_tensor_at(P + "vstage", [128, 8192], BF16, offset=abase + 24576)
    vtok = nc.alloc_sbuf_tensor_at(P + "vtok", [128, 8192], BF16, offset=abase + 40960)
    qslot = nc.alloc_sbuf_tensor_at(P + "qslot", [128, 4096], BF16, offset=abase + 57344)
    hn = nc.alloc_sbuf_tensor_at(P + "hn", [128, NCH * 1024], BF16, offset=bbase)
    hb = hn

    ph = pr.phase("load")
    load_stream(pr, ph, x_in, x_s, dsem)
    for d, s in ((n1_s, n1c), (n2_s, n2c), (mh_s, mhmy), (big_s, bigc),
                 (meta_s, meta), (sel_s, selmy), (i8_s, i8my)):
        dsem.inc(16)
        ph.sp(lambda e, d=d, s=s, sh=dsem.h: e.dma_start(d[:, :], s[:, :]).then_inc(sh, 16))
    csrc = bass.AP(cmask, 0, [[512, 128], [128 * 512, 4], [1, 512]])
    cdst = bass.AP(cmask_s, 0, [[4 * 512, 128], [512, 4], [1, 512]])
    dmap.inc(16)
    ph.pool(lambda e, s=csrc, d=cdst, sh=dmap.h: e.dma_start(d, s).then_inc(sh, 16))
    ph.wait("sp", dsem, dsem.v)
    ph.wait("pool", dmap, dmap.v)

    emit_ln(pr, x_s, g_in, cons, "l1")
    emit_ln(pr, g_in, h_in, cons, "l1r", rms_w=n1_s, eps=1e-6, src_bf=True)

    def stage_consumer(colbase, scale=None):
        def consumer(ph_, fg, tt, bank, wf, wf_tgt, pf):
            ph_.wait("dve", wf, wf_tgt)
            oc = colbase + fg * 1024 + tt * 512
            pf.inc(1)
            if scale is None:
                ph_.dve(lambda e, b=bank, oc=oc, sh=pf.h: e.tensor_copy(
                    qkvT[:, oc:oc + 512], b[:, :]).then_inc(sh, 1))
            else:
                ph_.dve(lambda e, b=bank, oc=oc, sh=pf.h: e.tensor_scalar(
                    qkvT[:, oc:oc + 512], b[:, :], scale, None, ALU.mult).then_inc(sh, 1))
        return consumer

    php = pr.phase("proj_qkv")
    emit_proj(pr, php, wq, 1024, h_in, stage_consumer(0, RSQ128), packed=True)
    emit_proj(pr, php, wk, 1024, h_in, stage_consumer(8 * 1024), packed=True)
    emit_proj(pr, php, wv, D, h_in, stage_consumer(16 * 1024), packed=True)

    phgp = pr.phase("proj_gates")
    gsem = pr.sem("l1gp")
    gsd = pr.sem("l1gd")
    gsa = pr.sem("l1ga")
    gw_s = pr.ablock(P + "l1gw", [128, 2 * 16 * 8], BF16)
    for gi, w in enumerate((wig, wfg)):
        gsrc = bass.AP(w, 0, [[8, 128], [128 * 8, 16], [1, 8]])
        gdst = bass.AP(gw_s, gi * 128, [[256, 128], [8, 16], [1, 8]])
        dmap.inc(16)
        phgp.pool(lambda e, s=gsrc, d=gdst, sh=dmap.h: e.dma_start(d, s).then_inc(sh, 16))
    phgp.wait("pe", dmap, dmap.v)
    for gi in range(2):
        for t2 in range(2):
            for dc in range(NCH):
                sp_ = dc == NCH - 1
                lhs = gw_s[:, gi * 128 + dc * 8: gi * 128 + (dc + 1) * 8]
                if sp_:
                    gsem.inc(1)
                    phgp.pe(lambda e, l=lhs, dc=dc, t2=t2, sh=gsem.h: e.matmul(
                        pr.ps[6][0:8, 0:512], l,
                        h_in[:, dc * 1024 + t2 * 512: dc * 1024 + t2 * 512 + 512],
                        start=dc == 0, stop=True).then_inc(sh, 1))
                else:
                    phgp.pe(lambda e, l=lhs, dc=dc, t2=t2: e.matmul(
                        pr.ps[6][0:8, 0:512], l,
                        h_in[:, dc * 1024 + t2 * 512: dc * 1024 + t2 * 512 + 512],
                        start=dc == 0, stop=False))
            phgp.wait("dve", gsem, gsem.v)
            gsd.inc(1)
            phgp.dve(lambda e, gi=gi, sh=gsd.h: e.tensor_scalar(
                pr.bcA[0:8, 0:512], pr.ps[6][0:8, 0:512],
                big_s[0:8, gi:gi + 1], None, ALU.add).then_inc(sh, 1))
            phgp.wait("act", gsd, gsd.v)
            gsa.inc(1)
            phgp.act(lambda e, sh=gsa.h: e.activation(
                pr.bcB[0:8, 0:512], pr.bcA[0:8, 0:512], AF.Tanh,
                scale=1.0 / 15.0).then_inc(sh, 1))
            phgp.wait("dve", gsa, gsa.v)
            phgp.wait("dve", dsem, dsem.v)
            gsd.inc(1)
            phgp.dve(lambda e, sh=gsd.h: e.tensor_scalar(
                pr.bcA[0:8, 512:1024], pr.bcB[0:8, 0:512], 15.0, None,
                ALU.mult).then_inc(sh, 1))
            phgp.wait("sp", gsd, gsd.v)
            dsem.inc(16)
            phgp.sp(lambda e, gi=gi, t2=t2, sh=dsem.h: e.dma_start(
                bass.AP(pg_i, gi * 8 * TPC + t2 * 512, [[TPC, 8], [1, 512]]),
                pr.bcA[0:8, 512:1024]).then_inc(sh, 16))
            phgp.wait("pe", gsd, gsd.v)
    phgp.wait("sp", dsem, dsem.v)

    php3 = pr.phase("payload")
    s_ap = bass.AP(qkvT, 0, [[32 * 1024, 128], [1024, 32], [1, 1024]])
    d_ap = bass.AP(pay_i, 0, [[1024, 128], [128 * 1024, 32], [1, 1024]])
    dsem.inc(16)
    php3.sp(lambda e, s=s_ap, d=d_ap, sh=dsem.h: e.dma_start(d, s).then_inc(sh, 16))
    php3.wait("sp", dsem, dsem.v)

    phga = pr.phase("gather_in")
    ccs.inc(1)
    phga.pool(lambda e, sh=ccs.h: e.collective_compute(
        "AllGather", ALU.bypass, replica_groups=[[0, 1, 2, 3], [4, 5, 6, 7]],
        ins=[pay_i.ap()], outs=[pay_o.ap()]).then_inc(sh, 1))
    ccs.inc(1)
    phga.pool(lambda e, sh=ccs.h: e.collective_compute(
        "AllGather", ALU.bypass, replica_groups=[[0, 1, 2, 3], [4, 5, 6, 7]],
        ins=[pg_i.ap()], outs=[pg_o.ap()]).then_inc(sh, 1))
    phga.wait("pool", ccs, ccs.v)

    phw = pr.phase("gates")
    gisrc = bass.AP(pg_o, 0, [[TPC, 8], [16 * TPC, 4], [1, 1024]])
    gidst = bass.AP(gI, 0, [[4096, 8], [1024, 4], [1, 1024]])
    gfsrc = bass.AP(pg_o, 8 * TPC, [[TPC, 8], [16 * TPC, 4], [1, 1024]])
    gfdst = bass.AP(B2, 0, [[4096, 8], [1024, 4], [1, 1024]])
    dsem.inc(16)
    phw.sp(lambda e, s=gisrc, d=gidst, sh=dsem.h: e.dma_start(d, s).then_inc(sh, 16))
    dsem.inc(16)
    phw.sp(lambda e, s=gfsrc, d=gfdst, sh=dsem.h: e.dma_start(d, s).then_inc(sh, 16))
    phw.wait("dve", dsem, dsem.v)
    phw.wait("act", dsem, dsem.v)
    gsm = pr.sem("l1w")
    gwa = pr.sem("l1wx")
    gwa.inc(1)
    phw.act(lambda e, sh=gwa.h: e.activation(B3[0:8, :], B2[0:8, :], AF.Abs).then_inc(sh, 1))
    gwa.inc(1)
    phw.act(lambda e, sh=gwa.h: e.activation(B3[0:8, :], B3[0:8, :], AF.Exp,
                                             scale=-1.0).then_inc(sh, 1))
    gwa.inc(1)
    phw.act(lambda e, sh=gwa.h: e.activation(
        B3[0:8, :], B3[0:8, :], AF.Ln,
        bias=cons["ones_col_f32"][0:8, 0:1]).then_inc(sh, 1))
    phw.wait("dve", gwa, gwa.v)
    gsm.inc(1)
    phw.dve(lambda e, sh=gsm.h: e.tensor_scalar(B2[0:8, :], B2[0:8, :], 0.0, None,
                                                ALU.min).then_inc(sh, 1))
    gsm.inc(1)
    phw.dve(lambda e, sh=gsm.h: e.tensor_tensor(B2[0:8, :], B2[0:8, :], B3[0:8, :],
                                                ALU.subtract).then_inc(sh, 1))
    bufs = [B2, B3]
    w = 1
    pi = 0
    while w < 4096:
        a_, b_ = bufs[pi], bufs[1 - pi]
        gsm.inc(1)
        phw.dve(lambda e, a=a_, b=b_, w=w, sh=gsm.h: e.tensor_copy(
            b[0:8, 0:w], a[0:8, 0:w]).then_inc(sh, 1))
        gsm.inc(1)
        phw.dve(lambda e, a=a_, b=b_, w=w, sh=gsm.h: e.tensor_tensor(
            b[0:8, w:4096], a[0:8, w:4096], a[0:8, 0:4096 - w], ALU.add).then_inc(sh, 1))
        pi = 1 - pi
        w *= 2
    fcum = bufs[pi]
    gsm.inc(1)
    phw.dve(lambda e, f=fcum, sh=gsm.h: e.tensor_tensor(
        gI[0:8, :], gI[0:8, :], f[0:8, :], ALU.subtract).then_inc(sh, 1))
    other = bufs[1 - pi]
    gsm.inc(1)
    phw.dve(lambda e, o=other, sh=gsm.h: e.tensor_copy(o[0:8, :], gI[0:8, :]).then_inc(sh, 1))
    bufs2 = [other, B1]
    w = 1
    pi2 = 0
    while w < 4096:
        a_, b_ = bufs2[pi2], bufs2[1 - pi2]
        gsm.inc(1)
        phw.dve(lambda e, a=a_, b=b_, w=w, sh=gsm.h: e.tensor_copy(
            b[0:8, 0:w], a[0:8, 0:w]).then_inc(sh, 1))
        gsm.inc(1)
        phw.dve(lambda e, a=a_, b=b_, w=w, sh=gsm.h: e.tensor_tensor(
            b[0:8, w:4096], a[0:8, w:4096], a[0:8, 0:4096 - w], ALU.max).then_inc(sh, 1))
        pi2 = 1 - pi2
        w *= 2
    cmbuf = bufs2[pi2]
    gsm.inc(1)
    phw.dve(lambda e, c=cmbuf, f=fcum, sh=gsm.h: e.tensor_tensor(
        B2[0:8, :], f[0:8, :], c[0:8, :], ALU.add).then_inc(sh, 1))
    gsm.inc(1)
    phw.dve(lambda e, c=cmbuf, sh=gsm.h: e.tensor_scalar(
        B1[0:8, :], c[0:8, :], -1.0, None, ALU.mult).then_inc(sh, 1))
    phw.wait("pe", gsm, gsm.v)
    tsp = pr.sem("l1tp")
    tse = pr.sem("l1te")
    for blk in range(32):
        par = blk % 2
        if blk >= 2:
            phw.wait("pe", tse, tse.v - 1)
        tsp.inc(1)
        phw.pe(lambda e, b=blk, p=par, sh=tsp.h: e.matmul(
            pr.ps[2 + p][0:128, 0:2], gI[0:8, b * 128:(b + 1) * 128],
            i8_s[0:8, 0:2], start=True, stop=True).then_inc(sh, 1))
        phw.wait("dve", tsp, tsp.v)
        tse.inc(1)
        phw.dve(lambda e, b=blk, p=par, sh=tse.h: e.tensor_copy(
            acol[:, b * 2:(b + 1) * 2], pr.ps[2 + p][0:128, 0:2]).then_inc(sh, 1))

    pha = pr.phase("attn")
    pha.uses_reg = True
    sSC = pr.sem("asc")
    sE = pr.sem("aE")
    sP = pr.sem("aP")
    sM = pr.sem("aM")
    sAV = pr.sem("aAV")
    sVT = pr.sem("aVT")
    sVE = pr.sem("aVE")
    sEP = pr.sem("aEP")
    sBC = pr.sem("aBC")
    sLD = pr.sem("aload")
    aq = [pr.sem("aq0"), pr.sem("aq1")]
    rowb = pr.rowb

    qld_free_at = {}
    for hl in range(2):
        def unit_loads(e, hl=hl, w=None if hl == 0 else sAV.v):
            r0 = e._shared_reg
            if w is not None:
                e.wait_ge(sAV.h, w)
            e.reg_load(r0, meta_s[0:1, hl:hl + 1])
            e.dma_start(
                bass.AP(kTA, 0, [[4096, 128], [1024, 4], [1, 1024]]),
                bass.AP(pay_o, r0, [[1024, 128], [PAYL1 * 1024, 4], [1, 1024]])
            ).then_inc(sLD.h, 16)
            for dv in range(2):
                e.reg_load(r0, meta_s[0:1, 16 + 2 * hl + dv: 17 + 2 * hl + dv])
                e.dma_start(
                    bass.AP(vstage, dv * 4096, [[8192, 128], [1024, 4], [1, 1024]]),
                    bass.AP(pay_o, r0, [[1024, 128], [PAYL1 * 1024, 4], [1, 1024]])
                ).then_inc(sLD.h, 16)
        pha.pool(unit_loads)
        sLD.inc(48)
        unit_ld = sLD.v

        pha.wait("pe", sLD, unit_ld)
        for ti in range(64):
            tt, dv = ti // 2, ti % 2
            par = ti % 2
            if ti >= 2:
                pha.wait("pe", sVE, sVE.v - 1)
            sVT.inc(1)
            pha.pe(lambda e, t=tt, d=dv, p=par, sh=sVT.h: e.matmul(
                pr.ps[2 + p][:, 0:128],
                vstage[:, d * 4096 + t * 128: d * 4096 + (t + 1) * 128],
                cons["ident_bf"][:, :], start=True, stop=True).then_inc(sh, 1))
            pha.wait("dve", sVT, sVT.v)
            sVE.inc(1)
            pha.dve(lambda e, t=tt, d=dv, p=par, sh=sVE.h: e.tensor_copy(
                vtok[:, t * 256 + d * 128: t * 256 + (d + 1) * 128],
                pr.ps[2 + p][:, 0:128]).then_inc(sh, 1))

        for st in range(8):
            it = hl * 8 + st
            if st == 0:
                aq[0].inc(16)

                def load_q(e, hl=hl, w=qld_free_at.get(0)):
                    r2 = e._shared_reg
                    if w is not None:
                        e.wait_ge(sAV.h, w)
                    e.reg_load(r2, meta_s[0:1, 4 + hl * 4: 5 + hl * 4])
                    e.dma_start(
                        bass.AP(qslot, 0, [[4096, 128], [1024, 4], [1, 1024]]),
                        bass.AP(pay_o, r2, [[1024, 128], [PAYL1 * 1024, 4], [1, 1024]])
                    ).then_inc(aq[0].h, 16)
                pha.pool(load_q)
            q_tgt = aq[0].v
            qcols = qslot[:, st * 512: st * 512 + 512]

            pha.wait("pe", sEP, sEP.v)
            pha.wait("pe", sP, sP.v)
            sBC.inc(1)
            pha.pe(lambda e, hl=hl, st=st, sh=sBC.h: e.matmul(
                pr.ps[6][:, :], sel_s[0:8, hl * 128:(hl + 1) * 128],
                B1[0:8, st * 512:(st + 1) * 512], start=True, stop=True).then_inc(sh, 1))
            sBC.inc(1)
            pha.pe(lambda e, hl=hl, st=st, sh=sBC.h: e.matmul(
                pr.psm[32:33, 0:512], i8_s[0:8, hl:hl + 1],
                B2[0:8, st * 512:(st + 1) * 512], start=True, stop=True).then_inc(sh, 1))
            pha.wait("dve", sBC, sBC.v - 1)
            sEP.inc(1)
            pha.dve(lambda e, sh=sEP.h: e.tensor_copy(
                pr.bcB[:, 0:512], pr.ps[6][:, :]).then_inc(sh, 1))
            pha.wait("act", sBC, sBC.v)
            sP.inc(1)
            pha.act(lambda e, sh=sP.h: e.activation(
                rowb[0:1, 1024:1536], pr.psm[32:33, 0:512], AF.Exp,
                scale=-1.0).then_inc(sh, 1))
            bc_ready = sEP.v
            nf_ready = sP.v

            n_tt = 4 * (st + 1)
            sAV0 = sAV.v
            sM0 = sM.v
            sP0 = sP.v
            for tt in range(n_tt):
                par = tt % 2
                pha.wait("pe", aq[0], q_tgt)
                if tt >= 2:
                    pha.wait("pe", sM, sM0 + tt - 1)
                sSC.inc(1)
                pha.pe(lambda e, t=tt, p=par, qc=qcols, sh=sSC.h: e.matmul(
                    pr.ps[p][:, :], kTA[:, t * 128:(t + 1) * 128], qc,
                    start=True, stop=True).then_inc(sh, 1))
                pha.wait("dve", sEP, bc_ready)
                if tt >= 1:
                    pha.wait("dve", sP, sP0 + tt)
                sE.inc(1)
                pha.dve(lambda e, t=tt, hl=hl, sh=sE.h: e.tensor_scalar(
                    EF[:, :], pr.bcB[:, 0:512],
                    acol[:, t * 2 + hl: t * 2 + hl + 1], None,
                    ALU.add).then_inc(sh, 1))
                diag = (tt >= 4 * st)
                if diag:
                    kk = tt - 4 * st
                    sE.inc(1)
                    pha.dve(lambda e, k=kk, sh=sE.h: e.tensor_tensor(
                        EF[:, :], EF[:, :],
                        cmask_s[:, k * 512:(k + 1) * 512], ALU.add).then_inc(sh, 1))
                pha.wait("act", sE, sE.v)
                if tt >= 2:
                    pha.wait("act", sM, sM0 + tt - 1)
                sP.inc(1)
                pha.act(lambda e, p=par, sh=sP.h: e.activation(
                    PT2[:, p * 512:(p + 1) * 512], EF[:, :], AF.Exp).then_inc(sh, 1))
                pha.wait("dve", sP, sP.v)
                pha.wait("dve", sSC, sSC.v)
                sM.inc(1)
                pha.dve(lambda e, p=par, sh=sM.h: e.tensor_tensor(
                    PT2[:, p * 512:(p + 1) * 512], pr.ps[p][:, :],
                    PT2[:, p * 512:(p + 1) * 512], ALU.mult).then_inc(sh, 1))
                pha.wait("pe", sM, sM.v)
                pha.pe(lambda e, t=tt, p=par, st_=(tt == 0), sp_=(tt == n_tt - 1):
                       e.matmul(pr.ps[4][:, :],
                                vtok[:, t * 256: t * 256 + 128],
                                PT2[:, p * 512:(p + 1) * 512], start=st_, stop=sp_))
                pha.pe(lambda e, t=tt, p=par, st_=(tt == 0), sp_=(tt == n_tt - 1):
                       e.matmul(pr.ps[5][:, :],
                                vtok[:, t * 256 + 128: t * 256 + 256],
                                PT2[:, p * 512:(p + 1) * 512], start=st_, stop=sp_))
                sAV.inc(1)
                pha.pe(lambda e, p=par, st_=(tt == 0), sp_=(tt == n_tt - 1), sh=sAV.h:
                       e.matmul(pr.psm[0:1, 0:512], cons["ones_col_bf"][:, :],
                                PT2[:, p * 512:(p + 1) * 512], start=st_, stop=sp_)
                       .then_inc(sh, 1))
            if st == 7:
                qld_free_at[0] = sAV.v

            pha.wait("act", sAV, sAV.v)
            sP.inc(1)
            pha.act(lambda e, sh=sP.h: e.activation(
                rowb[0:1, 0:512], pr.psm[0:1, 0:512], AF.Abs).then_inc(sh, 1))
            pha.wait("dve", sP, sP.v)
            pha.wait("dve", sP, nf_ready)

            def epi_rows(e, sh=sEP.h):
                e.tensor_tensor(rowb[0:1, 0:512], rowb[0:1, 0:512],
                                rowb[0:1, 1024:1536], ALU.max)
                e.reciprocal(rowb[0:1, 512:1024], rowb[0:1, 0:512]).then_inc(sh, 1)
            sEP.inc(1)
            pha.dve(epi_rows)
            pha.wait("act", sEP, sEP.v)
            for dv in range(2):
                sP.inc(1)
                pha.act(lambda e, d=dv, sh=sP.h: e.activation(
                    sqsc[:, 0:512], pr.ps[4 + d][:, :], AF.Square).then_inc(sh, 1))
                pha.wait("pe", sP, sP.v)
                sBC.inc(1)
                pha.pe(lambda e, d=dv, sh=sBC.h: e.matmul(
                    pr.psm[64:65, 0:512], cons["ones_col_bf"][:, :], sqsc[:, 0:512],
                    start=d == 0, stop=d == 1).then_inc(sh, 1))
                if dv == 0:
                    pha.wait("act", sBC, sBC.v)
            pha.wait("dve", sBC, sBC.v)

            def epi_rows2(e, sh=sEP.h):
                e.tensor_tensor(rowb[0:1, 1536:2048], pr.psm[64:65, 0:512],
                                rowb[0:1, 512:1024], ALU.mult)
                e.tensor_tensor(rowb[0:1, 1536:2048], rowb[0:1, 1536:2048],
                                rowb[0:1, 512:1024], ALU.mult)
                e.tensor_scalar(rowb[0:1, 1536:2048], rowb[0:1, 1536:2048],
                                1.0 / 256.0, 1e-6, ALU.mult, ALU.add).then_inc(sh, 1)
            sEP.inc(1)
            pha.dve(epi_rows2)
            pha.wait("act", sEP, sEP.v)
            sP.inc(1)
            pha.act(lambda e, sh=sP.h: e.activation(
                rowb[0:1, 1536:2048], rowb[0:1, 1536:2048], AF.Sqrt).then_inc(sh, 1))
            pha.wait("dve", sP, sP.v)

            def epi_rows3(e, sh=sEP.h):
                e.reciprocal(rowb[0:1, 1536:2048], rowb[0:1, 1536:2048])
                e.tensor_tensor(rowb[0:1, 1536:2048], rowb[0:1, 1536:2048],
                                rowb[0:1, 512:1024], ALU.mult).then_inc(sh, 1)
            sEP.inc(1)
            pha.dve(epi_rows3)
            pha.wait("pe", sEP, sEP.v)
            sBC.inc(1)
            pha.pe(lambda e, sh=sBC.h: e.matmul(
                pr.ps[6][:, :], cons["ones_row_f32"][:, :], rowb[0:1, 1536:2048],
                start=True, stop=True).then_inc(sh, 1))
            pha.wait("dve", sBC, sBC.v)
            sEP.inc(1)
            pha.dve(lambda e, sh=sEP.h: e.tensor_copy(
                pr.bcA[:, 0:512], pr.ps[6][:, :]).then_inc(sh, 1))
            for dv in range(2):
                pha.wait("dve", sEP, sEP.v)
                pha.wait("dve", dsem, dsem.v)
                sEP.inc(1)
                pha.dve(lambda e, d=dv, hl=hl, sh=sEP.h: e.scalar_tensor_tensor(
                    outst[:, :], pr.ps[4 + d][:, :],
                    mh_s[:, 2 * hl + d: 2 * hl + d + 1],
                    pr.bcA[:, 0:512], ALU.mult, ALU.mult).then_inc(sh, 1))
                pha.wait("sp", sEP, sEP.v)
                dsem.inc(16)
                pha.sp(lambda e, d=dv, hl=hl, st=st, sh=dsem.h: e.dma_start(
                    bass.AP(opay_i, (hl * 256 + d * 128) * S + st * 512,
                            [[S, 128], [1, 512]]),
                    outst[:, :]).then_inc(sh, 16))
            pha.wait("pe", sEP, sEP.v)

    pha.wait("sp", dsem, dsem.v)

    phg2 = pr.phase("gather_out")
    ccs.inc(1)
    phg2.pool(lambda e, sh=ccs.h: e.collective_compute(
        "AllGather", ALU.bypass, replica_groups=[[0, 1, 2, 3], [4, 5, 6, 7]],
        ins=[opay_i.ap()], outs=[opay_o.ap()]).then_inc(sh, 1))
    phg2.wait("pool", ccs, ccs.v)

    phr = pr.phase("read_out")
    phr.uses_reg = True

    def read_own(e):
        r3 = e._shared_reg
        e.reg_load(r3, meta_s[0:1, 60:61])
        e.dma_start(bass.AP(hn, 0, [[NCH * 1024, 128], [1024, 16], [1, 1024]]),
                    bass.AP(opay_o, r3, [[S, 128], [128 * S, 16], [1, 1024]])).then_inc(sLD.h, 16)
    phr.pool(read_own)
    sLD.inc(16)
    load_stream(pr, phr, x_in, x_s, dsem)
    phr.wait("pool", sLD, sLD.v)
    phr.wait("sp", dsem, dsem.v)

    sCA = pr.sem("l1sg")

    def og_consumer(ph_, fg, tt, bank, wf, wf_tgt, pf):
        ph_.wait("act", wf, wf_tgt)
        ph_.wait("act", pf, pf.v)
        sCA.inc(1)
        ph_.act(lambda e, b=bank, sh=sCA.h: e.activation(
            pr.scbf[2][:, 0:512], b[:, :], AF.Sigmoid).then_inc(sh, 1))
        ph_.wait("dve", sCA, sCA.v)
        oc = fg * 1024 + tt * 512
        pf.inc(1)
        ph_.dve(lambda e, oc=oc, sh=pf.h: e.tensor_tensor(
            hn[:, oc:oc + 512], hn[:, oc:oc + 512], pr.scbf[2][:, 0:512],
            ALU.mult).then_inc(sh, 1))

    emit_proj(pr, pr.phase("proj_og"), wog, D, h_in, og_consumer, packed=True)

    def wout_consumer(ph_, fg, tt, bank, wf, wf_tgt, pf):
        ph_.wait("dve", wf, wf_tgt)
        oc = fg * 1024 + tt * 512
        pf.inc(1)

        def f(e, oc=oc, b=bank, sh=pf.h):
            e.tensor_tensor(g_in[:, oc:oc + 512], g_in[:, oc:oc + 512], b[:, :],
                            ALU.add)
            e.tensor_tensor(x_s[:, oc:oc + 512], x_s[:, oc:oc + 512],
                            g_in[:, oc:oc + 512], ALU.add).then_inc(sh, 1)
        ph_.dve(f)

    emit_proj(pr, pr.phase("proj_wout"), wout, D, hn, wout_consumer, packed=True)

    emit_ln(pr, g_in, h_in, cons, "l1r2", rms_w=n2_s, eps=1e-6, src_bf=True)

    phm = pr.phase("mlp")
    hb_free_at = [0]
    sMS = pr.sem("l1ms")

    def msilu_consumer(hb_guard):
        def consumer(ph_, fg, tt, bank, wf, wf_tgt, pf):
            ph_.wait("act", wf, wf_tgt)
            if hb_guard[0]:
                ph_.wait("act", wf, hb_guard[0])
                hb_guard[0] = 0
            sMS.inc(1)
            ph_.act(lambda e, fg=fg, tt=tt, b=bank, sh=sMS.h: e.activation(
                hb[:, fg * 1024 + tt * 512: fg * 1024 + tt * 512 + 512],
                b[:, :], AF.Silu).then_inc(sh, 1))
            ph_.wait("dve", sMS, sMS.v)
            pf.inc(1)
            ph_.dve(lambda e, sh=pf.h: e.nop().then_inc(sh, 1))
        return consumer

    def mh_consumer(ph_, fg, tt, bank, wf, wf_tgt, pf):
        ph_.wait("dve", wf, wf_tgt)
        pf.inc(1)
        ph_.dve(lambda e, fg=fg, tt=tt, b=bank, sh=pf.h: e.tensor_tensor(
            hb[:, fg * 1024 + tt * 512: fg * 1024 + tt * 512 + 512], b[:, :],
            hb[:, fg * 1024 + tt * 512: fg * 1024 + tt * 512 + 512],
            ALU.mult).then_inc(sh, 1))

    def mdown_consumer(ph_, dt, ti, bank, wf, wf_tgt, pf):
        ph_.wait("dve", wf, wf_tgt)
        pf.inc(1)
        ph_.dve(lambda e, dt=dt, ti=ti, b=bank, sh=pf.h: e.tensor_tensor(
            x_s[:, dt * 1024 + ti * 512: dt * 1024 + ti * 512 + 512],
            x_s[:, dt * 1024 + ti * 512: dt * 1024 + ti * 512 + 512],
            b[:, :], ALU.add).then_inc(sh, 1))

    for q in range(4):
        hb_guard = [hb_free_at[0]]
        emit_proj(pr, phm, wgate, 2048, h_in, msilu_consumer(hb_guard),
                  row_len=8192, w_off=q * 2048, packed=True)
        emit_proj(pr, phm, wup, 2048, h_in, mh_consumer,
                  row_len=8192, w_off=q * 2048, packed=True)
        emit_down(pr, phm, wdown, 16,
                  lambda hc, ti: hb[:, hc * 1024 + ti * 512: hc * 1024 + ti * 512 + 512],
                  mdown_consumer, w_off=q * 2048 * D, packed=True, two_tt=True)
        hb_free_at[0] = pr.sem("wf").v

    phs = pr.phase("store")
    store_stream(pr, phs, x_s, y_out, dsem)
    phs.wait("sp", dsem, dsem.v)
    pr.arena_reset()


def build_l1a(upto=None):
    pr = Prog("l1a")
    nc = pr.nc
    x_in = nc.dram_tensor("x", [D, TPC], F32, kind="ExternalInput")
    y_out = nc.dram_tensor("y", [D, TPC], F32, kind="ExternalOutput")
    cons = alloc_consts(pr)
    pr.init_arena()
    emit_l1a(pr, cons, x_in, y_out)
    pr.emit()
    return pr.nc


def l1a_meta(core):
    r = core % 4
    m = np.zeros(128, np.int32)
    for hl in range(2):
        hg = 2 * r + hl
        m[hl] = (PAY1_K + hg * 128) * TPC
        for dv in range(2):
            m[16 + 2 * hl + dv] = (PAY1_V + hg * 256 + dv * 128) * TPC
        for rb in range(4):
            m[4 + hl * 4 + rb] = (rb * PAYL1 + PAY1_Q + hg * 128) * TPC
    m[60] = r * TPC
    return m.reshape(1, 128)


def colw(v):
    return np.ascontiguousarray(np.asarray(v, np.float32).reshape(16, 128).T)


def maps_l1a(inputs, c, P=""):
    cm = causal_masks()
    bigc = np.stack([np.asarray(inputs["L1_attn_b_ig"], np.float32),
                     np.asarray(inputs["L1_attn_b_fg"], np.float32)], axis=1)
    bigc = np.ascontiguousarray(bigc)
    r = c % 4
    selmy = np.zeros((8, 256), np.float32)
    i8my = np.zeros((8, 2), np.float32)
    for hl in range(2):
        hg = 2 * r + hl
        selmy[hg, hl * 128:(hl + 1) * 128] = 1.0
        i8my[hg, hl] = 1.0
    mhfull = colw(inputs["L1_attn_mh_w"])
    mhmy = np.zeros((128, 4), np.float32)
    for hl in range(2):
        hg = 2 * r + hl
        for dv in range(2):
            mhmy[:, 2 * hl + dv] = mhfull[:, 2 * hg + dv]
    wgp = np.concatenate(
        [pack_proj(np.asarray(inputs["L1_attn_w_gate"], np.float32), NCH, 2048,
                   col_off=q * 2048) for q in range(4)]).reshape(D, 8192)
    wup_ = np.concatenate(
        [pack_proj(np.asarray(inputs["L1_attn_w_up"], np.float32), NCH, 2048,
                   col_off=q * 2048) for q in range(4)]).reshape(D, 8192)
    wdp = np.concatenate(
        [pack_down(np.asarray(inputs["L1_attn_w_down"], np.float32), 16,
                   row_off=q * 2048) for q in range(4)]).reshape(8192, D)
    return {
        P + "wq": pack_proj(np.asarray(inputs["L1_attn_wq"], np.float32),
                            NCH, 1024).reshape(D, 1024),
        P + "wk": pack_proj(np.asarray(inputs["L1_attn_wk"], np.float32),
                            NCH, 1024).reshape(D, 1024),
        P + "wv": pack_proj(np.asarray(inputs["L1_attn_wv"], np.float32),
                            NCH, D).reshape(D, D),
        P + "wig": inputs["L1_attn_w_ig"],
        P + "wfg": inputs["L1_attn_w_fg"], P + "bigc": bigc,
        P + "wog": pack_proj(np.asarray(inputs["L1_attn_w_og"], np.float32),
                             NCH, D).reshape(D, D),
        P + "wout": pack_proj(np.asarray(inputs["L1_attn_w_out"], np.float32),
                              NCH, D).reshape(D, D),
        P + "wgate": wgp, P + "wup": wup_,
        P + "wdown": wdp,
        P + "n1c": colw(inputs["L1_attn_norm1_w"]),
        P + "n2c": colw(inputs["L1_attn_norm2_w"]),
        P + "selmy": np.ascontiguousarray(selmy),
        P + "i8my": np.ascontiguousarray(i8my),
        P + "mhmy": np.ascontiguousarray(mhmy),
        P + "cmask": cm, P + "meta": l1a_meta(c),
    }


def run_l1a(x_shards, inputs, upto=None):
    nc = get_nc(f"l1a{upto}", lambda: build_l1a(upto))
    in_maps = [dict(x=x_shards[c], **maps_l1a(inputs, c)) for c in range(NCORE)]
    res = run_bass_kernel_spmd(nc, in_maps, core_ids=list(range(NCORE)))
    return [res.results[c]["y"] for c in range(NCORE)]


def maps_l1f(inputs, c, P=""):
    lut = np.ascontiguousarray(np.asarray(inputs["L1_ffn_lut"], np.float32).reshape(1, 16))
    gi = np.asarray(inputs["L1_ffn_gate_idx"], np.int32)
    ui = np.asarray(inputs["L1_ffn_up_idx"], np.int32)
    di = np.asarray(inputs["L1_ffn_down_idx"], np.int32)
    return {
        P + "lut": lut,
        P + "gsl": np.ascontiguousarray(gi[:, 256 * c:256 * (c + 1)]),
        P + "usl": np.ascontiguousarray(ui[:, 256 * c:256 * (c + 1)]),
        P + "dsl": np.ascontiguousarray(di[256 * c:256 * (c + 1), :]),
    }


def run_l1f(x_shards, inputs, upto=None):
    nc = get_nc(f"l1f{upto}", lambda: build_l1f(upto))
    in_maps = [dict(x=x_shards[c], **maps_l1f(inputs, c)) for c in range(NCORE)]
    res = run_bass_kernel_spmd(nc, in_maps, core_ids=list(range(NCORE)))
    return [res.results[c]["y"] for c in range(NCORE)]


def build_fused():
    pr = Prog("fused")
    nc = pr.nc
    x0 = nc.dram_tensor("x", [D, TPC], F32, kind="ExternalInput")
    y = nc.dram_tensor("y", [D, TPC], F32, kind="ExternalOutput")
    s1 = nc.dram_tensor("s1", [D, TPC], F32)
    s2 = nc.dram_tensor("s2", [D, TPC], F32)
    s3 = nc.dram_tensor("s3", [D, TPC], F32)
    cons = alloc_consts(pr)
    pr.init_arena()
    emit_l0a(pr, cons, x0, s1, P="a_")
    emit_l0f(pr, cons, s1, s2, P="f_")
    emit_l1a(pr, cons, s2, s3, P="b_")
    emit_l1f(pr, cons, s3, y, P="g_")
    pr.emit()
    return pr.nc


def kernel_fused(inputs):
    nc = get_nc("fused", build_fused)
    x_shards = shard_stream(np.asarray(inputs["L0_attn_x"], np.float32))
    in_maps = []
    for c in range(NCORE):
        m = {"x": x_shards[c]}
        m.update(maps_l0a(inputs, c, "a_"))
        m.update(maps_l0f(inputs, c, "f_"))
        m.update(maps_l1a(inputs, c, "b_"))
        m.update(maps_l1f(inputs, c, "g_"))
        in_maps.append(m)
    res = run_bass_kernel_spmd(nc, in_maps, core_ids=list(range(NCORE)))
    return unshard_stream([res.results[c]["y"] for c in range(NCORE)])


PAY1_Q, PAY1_K, PAY1_V = 0, 1024, 2048
PAYL1 = 4096
RSQ128 = float(1.0 / np.sqrt(128.0))


def emit_l1f(pr, cons, x_in, y_out, P=""):
    nc = pr.nc
    lut = nc.dram_tensor(P + "lut", [1, 16], F32, kind="ExternalInput")
    gsl = nc.dram_tensor(P + "gsl", [5504, 256], I32, kind="ExternalInput")
    usl = nc.dram_tensor(P + "usl", [5504, 256], I32, kind="ExternalInput")
    dsl = nc.dram_tensor(P + "dsl", [256, 5504], I32, kind="ExternalInput")
    wg_i = nc.dram_tensor(P + "wg_i", [256, 5504], BF16)
    wg_o = nc.dram_tensor(P + "wg_o", [2048, 5504], BF16)
    wu_i = nc.dram_tensor(P + "wu_i", [256, 5504], BF16)
    wu_o = nc.dram_tensor(P + "wu_o", [2048, 5504], BF16)
    wd_i = nc.dram_tensor(P + "wd_i", [5504, 256], BF16)
    wd_o = nc.dram_tensor(P + "wd_o", [8 * 5504, 256], BF16)

    dsem = pr.sem("dma")
    dmap = pr.sem("dmap")
    ccs = pr.sem("cc")

    x_s = pr.ablock(P + "x_s", [128, NCH * 1024], F32)
    hn = pr.ablock(P + "hn", [128, NCH * 1024], BF16)
    hb = pr.ablock(P + "hb", [128, 22 * 1024], BF16)
    lutbc = pr.ablock(P + "lutbc", [128, 16], F32)
    lutrow = pr.ablock(P + "lutrow", [1, 16], F32)
    idxs = [pr.ablock(P + f"idxs{i}", [128, 128], I32) for i in range(2)]
    idxb = pr.ablock(P + "idxb", [128, 128], BF16)
    macc = pr.ablock(P + "macc", [128, 128], BF16)
    mtmp = pr.ablock(P + "mtmp", [128, 128], BF16)
    wrow = pr.ablock(P + "wrow", [128, 5504], BF16)

    ph = pr.phase("load")
    load_stream(pr, ph, x_in, x_s, dsem)
    dsem.inc(16)
    ph.sp(lambda e, sh=dsem.h: e.dma_start(lutrow[:, :], lut[:, :]).then_inc(sh, 16))
    ph.wait("sp", dsem, dsem.v)
    lsm = pr.sem("lf")
    ph.wait("pe", dsem, dsem.v)
    lsm.inc(1)
    ph.pe(lambda e, sh=lsm.h: e.matmul(pr.ps[0][:, 0:16], cons["ones_row_f32"][:, :],
                                       lutrow[0:1, :], start=True, stop=True).then_inc(sh, 1))
    ph.wait("dve", lsm, lsm.v)
    lsm.inc(1)
    ph.dve(lambda e, sh=lsm.h: e.tensor_copy(lutbc[:, :], pr.ps[0][:, 0:16]).then_inc(sh, 1))

    emit_ln(pr, x_s, hn, cons, "l1f")

    phm = pr.phase("mat")
    msd = [pr.sem("mI0"), pr.sem("mI1")]
    msv = pr.sem("mV")
    msp = pr.sem("mP")
    mse = pr.sem("mE")
    msw = pr.sem("mW")
    phm.wait("dve", lsm, lsm.v)

    mt_n = [0]

    def mat_tile(idx_dram, row_len, r0, c0, par, wcol):
        if mt_n[0] >= 2:
            phm.wait("sp", msv, mt_n[0] - 1)
        mt_n[0] += 1
        msd[par].inc(16)
        phm.sp(lambda e, s=bass.AP(idx_dram, r0 * row_len + c0,
                                   [[row_len, 128], [1, 128]]),
               d=idxs[par], sh=msd[par].h: e.dma_start(d[:, :], s).then_inc(sh, 16))
        tgt = msd[par].v
        phm.wait("dve", msd[par], tgt)
        msv.inc(1)

        def mat(e, p=par, sh=msv.h):
            e.tensor_copy(idxb[:, :], idxs[p][:, :])
            e.tensor_scalar(macc[:, :], idxb[:, :], 0.0, lutbc[:, 0:1],
                            ALU.is_equal, ALU.mult)
            for j in range(1, 16):
                e.tensor_scalar(mtmp[:, :], idxb[:, :], float(j), lutbc[:, j:j + 1],
                                ALU.is_equal, ALU.mult)
                e.tensor_tensor(macc[:, :], macc[:, :], mtmp[:, :], ALU.add)
            e.nop().then_inc(sh, 1)
        phm.dve(mat)
        phm.wait("pe", msv, msv.v)
        phm.wait("pe", mse, mse.v)
        msp.inc(1)
        phm.pe(lambda e, sh=msp.h: e.matmul(pr.ps[1][:, 0:128], macc[:, :],
                                            cons["ident_bf"][:, :],
                                            start=True, stop=True).then_inc(sh, 1))
        phm.wait("dve", msp, msp.v)
        mse.inc(1)
        phm.dve(lambda e, wc=wcol, sh=mse.h: e.tensor_copy(
            wrow[:, wc:wc + 128], pr.ps[1][:, 0:128]).then_inc(sh, 1))

    for mi, (idx_dram, ag_in) in enumerate(((gsl, wg_i), (usl, wu_i))):
        for hc2 in range(2):
            for ot in range(43):
                mat_tile(idx_dram, 256, ot * 128, hc2 * 128, ot % 2, ot * 128)
            phm.wait("sp", mse, mse.v)
            msw.inc(16)
            phm.sp(lambda e, ag=ag_in, hc2=hc2, sh=msw.h: e.dma_start(
                bass.AP(ag, hc2 * 128 * 5504, [[5504, 128], [1, 5504]]),
                wrow[:, :]).then_inc(sh, 16))
            phm.wait("dve", msw, msw.v)
    for it in range(43):
        for dt2 in range(2):
            mat_tile(dsl, 5504, dt2 * 128, it * 128, dt2 % 2, dt2 * 128)
        phm.wait("sp", mse, mse.v)
        msw.inc(16)
        phm.sp(lambda e, it=it, sh=msw.h: e.dma_start(
            bass.AP(wd_i, it * 128 * 256, [[256, 128], [1, 256]]),
            wrow[:, 0:256]).then_inc(sh, 16))
        phm.wait("dve", msw, msw.v)

    phg = pr.phase("gather_w")
    for ag_i, ag_o in ((wg_i, wg_o), (wu_i, wu_o), (wd_i, wd_o)):
        ccs.inc(1)
        phg.pool(lambda e, i_=ag_i, o_=ag_o, sh=ccs.h: e.collective_compute(
            "AllGather", ALU.bypass, replica_groups=[[0, 1, 2, 3, 4, 5, 6, 7]],
            ins=[i_.ap()], outs=[o_.ap()]).then_inc(sh, 1))
    phg.wait("pool", ccs, ccs.v)

    phf = pr.phase("ffn")
    hb_free_at = [0]
    sMS = pr.sem("lfms")

    def fsilu_consumer(hb_guard):
        def consumer(ph_, fg, tt, bank, wf, wf_tgt, pf):
            ph_.wait("act", wf, wf_tgt)
            if hb_guard[0]:
                ph_.wait("act", wf, hb_guard[0])
                hb_guard[0] = 0
            sMS.inc(1)
            ph_.act(lambda e, fg=fg, tt=tt, b=bank, sh=sMS.h: e.activation(
                hb[:, fg * 1024 + tt * 512: fg * 1024 + tt * 512 + 512],
                b[:, :], AF.Silu).then_inc(sh, 1))
            ph_.wait("dve", sMS, sMS.v)
            pf.inc(1)
            ph_.dve(lambda e, sh=pf.h: e.nop().then_inc(sh, 1))
        return consumer

    def fh_consumer(ph_, fg, tt, bank, wf, wf_tgt, pf):
        ph_.wait("dve", wf, wf_tgt)
        pf.inc(1)
        ph_.dve(lambda e, fg=fg, tt=tt, b=bank, sh=pf.h: e.tensor_tensor(
            hb[:, fg * 1024 + tt * 512: fg * 1024 + tt * 512 + 512], b[:, :],
            hb[:, fg * 1024 + tt * 512: fg * 1024 + tt * 512 + 512],
            ALU.mult).then_inc(sh, 1))

    def fdown_consumer(ph_, dt, ti, bank, wf, wf_tgt, pf):
        ph_.wait("dve", wf, wf_tgt)
        pf.inc(1)
        ph_.dve(lambda e, dt=dt, ti=ti, b=bank, sh=pf.h: e.tensor_tensor(
            x_s[:, dt * 1024 + ti * 512: dt * 1024 + ti * 512 + 512],
            x_s[:, dt * 1024 + ti * 512: dt * 1024 + ti * 512 + 512],
            b[:, :], ALU.add).then_inc(sh, 1))

    halves = [(0, 2816, 22), (2816, 2688, 21)]
    for (coff, fsz, nhc) in halves:
        hb_guard = [hb_free_at[0]]
        emit_proj(pr, phf, wg_o, fsz, hn, fsilu_consumer(hb_guard),
                  row_len=5504, w_off=coff)
        emit_proj(pr, phf, wu_o, fsz, hn, fh_consumer,
                  row_len=5504, w_off=coff)
        emit_down(pr, phf, wd_o, nhc,
                  lambda hc, ti, c0=coff: hb[:, (hc - c0 // 128) * 1024 + ti * 512:
                                             (hc - c0 // 128) * 1024 + ti * 512 + 512],
                  fdown_consumer, row_len=256, hc0=coff // 128,
                  dtg_stride=5504 * 256, two_tt=True)
        hb_free_at[0] = pr.sem("wf").v

    phs = pr.phase("store")
    store_stream(pr, phs, x_s, y_out, dsem)
    phs.wait("sp", dsem, dsem.v)
    pr.arena_reset()


def build_l1f(upto=None):
    pr = Prog("l1f")
    nc = pr.nc
    x_in = nc.dram_tensor("x", [D, TPC], F32, kind="ExternalInput")
    y_out = nc.dram_tensor("y", [D, TPC], F32, kind="ExternalOutput")
    cons = alloc_consts(pr)
    pr.init_arena()
    emit_l1f(pr, cons, x_in, y_out)
    pr.emit()
    return pr.nc


PAY1_Q, PAY1_K, PAY1_V = 0, 1024, 2048
PAYL1 = 4096
RSQ128 = float(1.0 / np.sqrt(128.0))


PAY1_Q, PAY1_K, PAY1_V = 0, 1024, 2048
PAYL1 = 4096
RSQ128 = float(1.0 / np.sqrt(128.0))



# version 05 (final)
# speedup vs baseline: 1.2806x; 1.0075x over previous
import sys
sys.path.insert(0, "/opt/trn_rl_repo")

import numpy as np
import concourse.bass as bass
import concourse.mybir as mybir
from concourse.bass_utils import run_bass_kernel_spmd

F32 = mybir.dt.float32
BF16 = mybir.dt.bfloat16
I32 = mybir.dt.int32
AF = mybir.ActivationFunctionType
ALU = mybir.AluOpType
AX = mybir.AxisListType

B, S, D = 2, 4096, 2048
NCORE = 8
TPC = 1024
WRING = 3
NCH = D // 128
NEG = -1.0e30

L0_HD = 128
PAY_Q, PAY_K, PAY_V = 0, 2048, 2560
PAYR = 3072

_uid = [0]


def uid():
    _uid[0] += 1
    return _uid[0]


class Sem:
    def __init__(self, nc, name):
        self.h = nc.alloc_semaphore(name)
        self.v = 0

    def inc(self, n=1):
        self.v += n
        return self.v



_WRAP_OPS = {
    "tensor_copy", "tensor_tensor", "tensor_scalar", "scalar_tensor_tensor",
    "tensor_reduce", "reciprocal", "activation", "copy", "mul", "add",
    "square", "sqrt", "memset", "select", "reduce_max", "reduce_sum",
    "tensor_add", "tensor_sub", "tensor_mul", "tensor_max",
}


class InstWrap:

    def __init__(self, proxy):
        self._p = proxy

    def then_inc(self, sh, n=1):
        p = self._p
        p._e.wait_ge(p._semh, p._cnt[0])
        p._e.nop().then_inc(sh, n)
        return self


class EngProxy:

    def __init__(self, e, semh, cnt):
        self._e = e
        self._semh = semh
        self._cnt = cnt

    def __getattr__(self, name):
        f = getattr(self._e, name)
        if name in _WRAP_OPS:
            def g(*a, **k):
                if self._cnt[0] > 0:
                    self._e.wait_ge(self._semh, self._cnt[0])
                inst = f(*a, **k)
                inst.then_inc(self._semh, 1)
                self._cnt[0] += 1
                return InstWrap(self)
            return g
        if name == "nop":
            def g2():
                if self._cnt[0] > 0:
                    self._e.wait_ge(self._semh, self._cnt[0])
                return f()
            return g2
        return f


class Ph:
    ENGS = ("pe", "dve", "act", "pool", "sp")

    def __init__(self, name=""):
        self.name = name
        self.uses_reg = False
        self.q = {e: [] for e in self.ENGS}

    def add(self, eng, fn):
        self.q[eng].append(fn)

    def pe(self, fn): self.add("pe", fn)
    def dve(self, fn): self.add("dve", fn)
    def act(self, fn): self.add("act", fn)
    def pool(self, fn): self.add("pool", fn)
    def sp(self, fn): self.add("sp", fn)

    def wait(self, eng, sem, val):
        self.add(eng, lambda e, s=sem.h, v=val: e.wait_ge(s, v))

    def emit(self, nc, prog=None):
        with nc.Block() as b:
            if self.q["pe"]:
                @b.tensor
                def _(e):
                    for f in self.q["pe"]:
                        f(e)
            if self.q["dve"]:
                @b.vector
                def _(e):
                    ep = EngProxy(e, prog.self_dve.h, prog.cnt_dve)
                    for f in self.q["dve"]:
                        f(ep)
            if self.q["act"]:
                @b.scalar
                def _(e):
                    ep = EngProxy(e, prog.self_act.h, prog.cnt_act)
                    for f in self.q["act"]:
                        f(ep)
            if self.q["pool"]:
                if self.uses_reg:
                    @b.gpsimd
                    def _(e):
                        with e.register(f"rp{uid()}") as r:
                            e._shared_reg = r
                            for f in self.q["pool"]:
                                f(e)
                else:
                    @b.gpsimd
                    def _(e):
                        for f in self.q["pool"]:
                            f(e)
            if self.q["sp"]:
                @b.sync
                def _(e):
                    for f in self.q["sp"]:
                        f(e)


class Prog:
    def __init__(self, name):
        self.nc = bass.Bass("TRN2", target_bir_lowering=False)
        self.phases = []
        self.sems = {}
        self.ps = [self.nc.alloc_psum_tensor(f"ps{i}", [128, 512], F32)
                   for i in range(7)]
        self.psm = self.nc.alloc_psum_tensor("psmisc", [128, 512], F32)
        nc = self.nc
        self.wbuf = [sbuf(nc, f"wbuf{i}", [128, 1024], BF16) for i in range(WRING)]
        self.scbf = [sbuf(nc, f"scbf{i}", [128, 1024], BF16) for i in range(4)]
        self.rowb = sbuf(nc, "rowb", [1, 3072], F32)
        self.gidx = 0
        self.grp_par = 0
        self.pf_marks = [0, 0]
        self.self_dve = Sem(nc, "self_dve")
        self.self_act = Sem(nc, "self_act")
        self.cnt_dve = [0]
        self.cnt_act = [0]
        self.bcA = sbuf(nc, "bcA", [128, 1024], F32)
        self.bcB = sbuf(nc, "bcB", [128, 1024], F32)
        self.tmpF = sbuf(nc, "tmpF", [128, 1024], F32)
        self.arena_h = None
        self.arena_off = 0
        self.arena_sz = 0

    def init_arena(self):
        import concourse.mybir as _mb
        rem = self.nc.sbuf_bytes_remaining - 1024
        self.arena_h = self.nc.alloc_sbuf_tensor("blockarena", [128, rem], _mb.dt.uint8)
        self.arena_base = self.nc.lookup_mloc(self.arena_h).addr
        self.arena_sz = rem
        self.arena_off = 0

    def ablock(self, name, shape, dtype):
        import concourse.mybir as _mb
        per = 1
        for s in shape[1:]:
            per *= s
        per *= _mb.dt.size(dtype)
        per = (per + 31) // 32 * 32
        assert self.arena_off + per <= self.arena_sz, \
            f"block arena overflow: {name} needs {per}, have {self.arena_sz - self.arena_off}"
        t = self.nc.alloc_sbuf_tensor_at(name, shape, dtype,
                                         offset=self.arena_base + self.arena_off)
        self.arena_off += per
        return t

    def arena_reset(self):
        self.arena_off = 0

    def sem(self, name):
        if name not in self.sems:
            self.sems[name] = Sem(self.nc, name)
        return self.sems[name]

    def phase(self, name=""):
        p = Ph(name)
        self.phases.append(p)
        return p

    def emit(self):
        for p in self.phases:
            p.emit(self.nc, self)


def sbuf(nc, name, shape, dtype):
    return nc.alloc_sbuf_tensor(name, shape, dtype)




def proj_chunks(n_dch, F):
    n_ft = F // 128
    out = []
    for i in range(0, n_ft, 2):
        nft = min(2, n_ft - i)
        for dg in range(n_dch // 4):
            out.append((nft * 128, dg, i))
    return out


def down_chunks(n_hch):
    hg = []
    i = 0
    while i < n_hch:
        n = min(4, n_hch - i)
        hg.append((i, n))
        i += n
    out = []
    for dtg in range(8):
        for (h0, nh) in hg:
            out.append((nh, h0, dtg))
    return out


def pack_proj(W, n_dch, F, col_off=0):
    parts = []
    for (cols, dg, ft0) in proj_chunks(n_dch, F):
        blk = W[dg * 512:(dg + 1) * 512, col_off + ft0 * 128: col_off + ft0 * 128 + cols]
        parts.append(np.ascontiguousarray(
            blk.reshape(4, 128, cols).transpose(1, 0, 2)).reshape(-1))
    return np.concatenate(parts)


def pack_down(W, n_hch, row_off=0):
    parts = []
    for (nh, h0, dtg) in down_chunks(n_hch):
        blk = W[row_off + h0 * 128: row_off + (h0 + nh) * 128,
                dtg * 256:(dtg + 1) * 256]
        parts.append(np.ascontiguousarray(
            blk.reshape(nh, 128, 256).transpose(1, 0, 2)).reshape(-1))
    return np.concatenate(parts)


def emit_proj(pr, ph, wdram, F, xn_s, consumer, n_dch=NCH, w_off=0, row_len=None,
              tts=(0, 1), xn_cols=None, packed=False):
    if row_len is None:
        row_len = F
    wl = [pr.sem(f"wl{i}") for i in range(WRING)]
    wf = pr.sem("wf")
    pf = pr.sem("pf")
    if xn_cols is None:
        def xn_cols(dc, tt):
            return xn_s[:, dc * 1024 + tt * 512: dc * 1024 + tt * 512 + 512]

    assert n_dch % 4 == 0
    n_dg = n_dch // 4
    n_ft = F // 128
    groups = [(i, min(2, n_ft - i)) for i in range(0, n_ft, 2)]

    gidx_load = pr.gidx
    if packed:
        if not hasattr(pr, "wpack_off"):
            pr.wpack_off = {}
        poff = pr.wpack_off.get(wdram.name, 0)
    for (ft0, nft) in groups:
        cols = nft * 128
        for dg in range(n_dg):
            gidx = gidx_load
            gidx_load += 1
            par = gidx % WRING
            if gidx >= WRING:
                ph.wait("pool", wf, gidx - WRING + 1)
            if packed:
                s = bass.AP(wdram, poff, [[4 * cols, 128], [cols, 4], [1, cols]])
                poff += 512 * cols
            else:
                s = bass.AP(wdram, w_off + dg * 512 * row_len + ft0 * 128,
                            [[row_len, 128], [128 * row_len, 4], [1, cols]])
            d = bass.AP(pr.wbuf[par], 0, [[1024, 128], [cols, 4], [1, cols]])
            wl[par].inc(16)
            ph.pool(lambda e, s=s, d=d, sh=wl[par].h:
                    e.dma_start(d, s).then_inc(sh, 16))
    if packed:
        pr.wpack_off[wdram.name] = poff

    for (ft0, nft) in groups:
        cols = nft * 128
        ntiles = nft * len(tts)
        grp = pr.grp_par % 2
        pr.grp_par += 1
        banks = [pr.ps[0], pr.ps[1], pr.ps[2], pr.ps[3]] if grp == 0 else \
                [pr.ps[4], pr.ps[5], pr.ps[6], pr.psm]
        ph.wait("pe", pf, pr.pf_marks[-2])
        for dg in range(n_dg):
            gidx = wf.v
            par = gidx % WRING
            ph.wait("pe", wl[par], 16 * (gidx // WRING + 1))
            for j in range(4):
                st = (dg == 0 and j == 0)
                sp_ = (dg == n_dg - 1 and j == 3)
                for f in range(nft):
                    for ti, tt in enumerate(tts):
                        lhs = pr.wbuf[par][:, j * cols + f * 128: j * cols + f * 128 + 128]
                        rhs = xn_cols(dg * 4 + j, tt)
                        out = banks[f * len(tts) + ti][:, :]
                        is_last = (j == 3 and f == nft - 1 and ti == len(tts) - 1)
                        if is_last:
                            wf.inc(1)
                            ph.pe(lambda e, o=out, l=lhs, r=rhs, s0=st, s1=sp_,
                                  swf=wf.h: e.matmul(o, l, r, start=s0, stop=s1)
                                  .then_inc(swf, 1))
                        else:
                            ph.pe(lambda e, o=out, l=lhs, r=rhs, s0=st, s1=sp_:
                                  e.matmul(o, l, r, start=s0, stop=s1))
        wf_tgt = wf.v
        for f in range(nft):
            for ti, tt in enumerate(tts):
                consumer(ph, ft0 + f, tt, banks[f * len(tts) + ti], wf, wf_tgt, pf)
        pr.pf_marks.append(pf.v)
    pr.gidx = gidx_load


def emit_down(pr, ph, wdram, n_hch, h_cols, consumer, w_off=0, row_len=D, tt=0,
              hc0=0, dtg_stride=None, packed=False, two_tt=False):
    wl = [pr.sem(f"wl{i}") for i in range(WRING)]
    wf = pr.sem("wf")
    pf = pr.sem("pf")
    gidx_load = pr.gidx
    hgroups = []
    i = 0
    while i < n_hch:
        n = min(4, n_hch - i)
        hgroups.append((i, n))
        i += n
    if dtg_stride is None:
        dtg_stride = 256
    if packed:
        if not hasattr(pr, "wpack_off"):
            pr.wpack_off = {}
        poff = pr.wpack_off.get(wdram.name, 0)
    for dtg in range(8):
        for (h0, nh) in hgroups:
            gidx = gidx_load
            gidx_load += 1
            par = gidx % WRING
            if gidx >= WRING:
                ph.wait("pool", wf, gidx - WRING + 1)
            if packed:
                s = bass.AP(wdram, poff, [[nh * 256, 128], [256, nh], [1, 256]])
                poff += nh * 128 * 256
            else:
                s = bass.AP(wdram, w_off + (hc0 + h0) * 128 * row_len + dtg * dtg_stride,
                            [[row_len, 128], [128 * row_len, nh], [1, 256]])
            d = bass.AP(pr.wbuf[par], 0, [[1024, 128], [256, nh], [1, 256]])
            wl[par].inc(16)
            ph.pool(lambda e, s=s, d=d, sh=wl[par].h:
                    e.dma_start(d, s).then_inc(sh, 16))
    if packed:
        pr.wpack_off[wdram.name] = poff
    h_ready = pf.v
    for dtg in range(8):
        grp = pr.grp_par % 2
        pr.grp_par += 1
        if two_tt:
            banks = [pr.ps[0], pr.ps[1], pr.ps[2], pr.ps[3]] if grp == 0 else \
                    [pr.ps[4], pr.ps[5], pr.ps[6], pr.psm]
        else:
            banks = [pr.ps[0], pr.ps[1]] if grp == 0 else [pr.ps[4], pr.ps[5]]
        ph.wait("pe", pf, max(pr.pf_marks[-2], h_ready))
        for gi, (h0, nh) in enumerate(hgroups):
            gidx = wf.v
            par = gidx % WRING
            ph.wait("pe", wl[par], 16 * (gidx // WRING + 1))
            for j in range(nh):
                hc = hc0 + h0 + j
                st = (gi == 0 and j == 0)
                sp_ = (gi == len(hgroups) - 1 and j == nh - 1)
                for f in range(2):
                    lhs = pr.wbuf[par][:, j * 256 + f * 128: j * 256 + f * 128 + 128]
                    for ti in range(2 if two_tt else 1):
                        rhs = h_cols(hc, ti) if two_tt else h_cols(hc)
                        out = banks[f * 2 + ti][:, :] if two_tt else banks[f][:, :]
                        is_last = (j == nh - 1 and f == 1 and
                                   ti == (1 if two_tt else 0))
                        if is_last:
                            wf.inc(1)
                            ph.pe(lambda e, o=out, l=lhs, r=rhs, s0=st, s1=sp_,
                                  swf=wf.h: e.matmul(o, l, r, start=s0, stop=s1)
                                  .then_inc(swf, 1))
                        else:
                            ph.pe(lambda e, o=out, l=lhs, r=rhs, s0=st, s1=sp_:
                                  e.matmul(o, l, r, start=s0, stop=s1))
        wf_tgt = wf.v
        for f in range(2):
            if two_tt:
                for ti in range(2):
                    consumer(ph, dtg * 2 + f, ti, banks[f * 2 + ti], wf, wf_tgt, pf)
            else:
                consumer(ph, dtg * 2 + f, banks[f], wf, wf_tgt, pf)
        pr.pf_marks.append(pf.v)
    pr.gidx = gidx_load


def emit_ln(pr, x_s, xn_s, cons, tag, rms_w=None, eps=1e-5, src_bf=False):
    rms = rms_w is not None
    xb = [pr.scbf[0], pr.scbf[1]]
    sq = [pr.scbf[2], pr.scbf[3]]
    rows = pr.rowb
    bc_r, bc_m, tmp = pr.bcA, pr.bcB, pr.tmpF

    sX = pr.sem(f"lnx_{tag}")
    sQ = pr.sem(f"lnq_{tag}")
    sF = pr.sem(f"lnf_{tag}")
    sS = pr.sem(f"lns_{tag}")
    sX0, sQ0, sF0, sS0 = sX.v, sQ.v, sF.v, sS.v

    ph = pr.phase(f"ln_{tag}_stats")
    for ch in range(NCH):
        par = ch % 2
        if not src_bf:
            if ch >= 2:
                ph.wait("dve", sF, sF0 + 2 * (ch - 2) + 2)
            sX.inc(1)
            ph.dve(lambda e, c=ch, p=par, sh=sX.h: e.tensor_copy(
                xb[p][:, :], x_s[:, c * 1024:(c + 1) * 1024]).then_inc(sh, 1))
        if ch >= 2:
            ph.wait("act", sF, sF0 + 2 * (ch - 2) + 2)
        if not src_bf:
            ph.wait("act", sX, sX0 + ch + 1)
        sQ.inc(1)
        if src_bf:
            ph.act(lambda e, c=ch, p=par, sh=sQ.h: e.activation(
                sq[p][:, :], x_s[:, c * 1024:(c + 1) * 1024], AF.Square).then_inc(sh, 1))
        else:
            ph.act(lambda e, p=par, sh=sQ.h: e.activation(
                sq[p][:, :], xb[p][:, :], AF.Square).then_inc(sh, 1))
        ph.wait("pe", sQ, sQ0 + ch + 1)
        st = (ch == 0)
        sp_ = (ch == NCH - 1)
        slots = [pr.psm[0:1, :], pr.psm[32:33, :], pr.psm[64:65, :], pr.ps[4][0:1, :]]
        for tt in range(2):
            if not rms:
                xbsrc = (x_s[:, ch * 1024 + tt * 512: ch * 1024 + tt * 512 + 512]
                         if src_bf else xb[par][:, tt * 512:(tt + 1) * 512])
                ph.pe(lambda e, s0=st, s1=sp_, o=slots[tt], xs=xbsrc: e.matmul(
                    o, cons["ones_col_bf"][:, :], xs, start=s0, stop=s1))
            sF.inc(1)
            ph.pe(lambda e, p=par, t=tt, s0=st, s1=sp_, sh=sF.h, o=slots[2 + tt]: e.matmul(
                o, cons["ones_col_bf"][:, :],
                sq[p][:, t * 512:(t + 1) * 512], start=s0, stop=s1).then_inc(sh, 1))

    ph2 = pr.phase(f"ln_{tag}_mid")
    inv_d = 1.0 / D

    def rowmath(e):
        if not rms:
            e.tensor_copy(rows[0:1, 0:512], pr.psm[0:1, :])
            e.tensor_copy(rows[0:1, 512:1024], pr.psm[32:33, :])
        e.tensor_copy(rows[0:1, 1024:1536], pr.psm[64:65, :])
        e.tensor_copy(rows[0:1, 1536:2048], pr.ps[4][0:1, :])
        if not rms:
            e.tensor_scalar(rows[0:1, 0:1024], rows[0:1, 0:1024], inv_d, None, ALU.mult)
        e.tensor_scalar(rows[0:1, 1024:2048], rows[0:1, 1024:2048], inv_d, None, ALU.mult)
        if not rms:
            e.tensor_tensor(rows[0:1, 2048:3072], rows[0:1, 0:1024],
                            rows[0:1, 0:1024], ALU.mult)
            e.tensor_tensor(rows[0:1, 1024:2048], rows[0:1, 1024:2048],
                            rows[0:1, 2048:3072], ALU.subtract)
    ph2.dve(rowmath)
    sS.inc(1)
    ph2.dve(lambda e, sh=sS.h: e.nop().then_inc(sh, 1))
    ph2.wait("act", sS, sS0 + 1)
    epsap = cons["cvals"][0:1, (0 if eps == 1e-5 else 1):(1 if eps == 1e-5 else 2)]
    ph2.act(lambda e, b=epsap: e.activation(rows[0:1, 2048:3072], rows[0:1, 1024:2048],
                                            AF.Sqrt, bias=b))
    sS.inc(1)
    ph2.act(lambda e, sh=sS.h: e.nop().then_inc(sh, 1))
    ph2.wait("dve", sS, sS0 + 2)
    ph2.dve(lambda e: e.reciprocal(rows[0:1, 2048:3072], rows[0:1, 2048:3072]))
    if not rms:
        ph2.dve(lambda e: e.scalar_tensor_tensor(
            rows[0:1, 0:1024], rows[0:1, 0:1024], -1.0, rows[0:1, 2048:3072],
            ALU.mult, ALU.mult))
    sS.inc(1)
    ph2.dve(lambda e, sh=sS.h: e.nop().then_inc(sh, 1))
    ph2.wait("pe", sS, sS0 + 3)
    mms = []
    for tt in range(2):
        mms.append((pr.ps[tt], rows[0:1, 2048 + tt * 512: 2048 + (tt + 1) * 512]))
        if not rms:
            mms.append((pr.ps[2 + tt], rows[0:1, tt * 512:(tt + 1) * 512]))
    for i, (bank, rsrc) in enumerate(mms):
        if i == len(mms) - 1:
            sS.inc(1)
            ph2.pe(lambda e, b=bank, r=rsrc, sh=sS.h: e.matmul(
                b[:, :], cons["ones_row_f32"][:, :], r, start=True, stop=True)
                .then_inc(sh, 1))
        else:
            ph2.pe(lambda e, b=bank, r=rsrc: e.matmul(
                b[:, :], cons["ones_row_f32"][:, :], r, start=True, stop=True))
    ph2.wait("dve", sS, sS0 + 4)
    ph2.dve(lambda e: e.tensor_copy(bc_r[:, 0:512], pr.ps[0][:, :]))
    ph2.dve(lambda e: e.tensor_copy(bc_r[:, 512:1024], pr.ps[1][:, :]))
    if not rms:
        ph2.dve(lambda e: e.tensor_copy(bc_m[:, 0:512], pr.ps[2][:, :]))
        ph2.dve(lambda e: e.tensor_copy(bc_m[:, 512:1024], pr.ps[3][:, :]))

    ph3 = pr.phase(f"ln_{tag}_norm")
    for ch in range(NCH):
        ph3.dve(lambda e, c=ch: e.tensor_tensor(
            tmp[:, :], x_s[:, c * 1024:(c + 1) * 1024], bc_r[:, :], ALU.mult))
        if rms:
            ph3.dve(lambda e, c=ch: e.tensor_scalar(
                xn_s[:, c * 1024:(c + 1) * 1024], tmp[:, :],
                rms_w[:, c:c + 1], None, ALU.mult))
        else:
            ph3.dve(lambda e, c=ch: e.tensor_tensor(
                xn_s[:, c * 1024:(c + 1) * 1024], tmp[:, :], bc_m[:, :], ALU.add))


def alloc_consts(pr):
    nc = pr.nc
    cons = {}
    cons["ones_col_bf"] = sbuf(nc, "ones_col_bf", [128, 1], BF16)
    cons["ones_row_bf"] = sbuf(nc, "ones_row_bf", [1, 128], BF16)
    cons["ones_row_f32"] = sbuf(nc, "ones_row_f32", [1, 128], F32)
    cons["ident_bf"] = sbuf(nc, "ident_bf", [128, 128], BF16)
    cons["ident_f32"] = sbuf(nc, "ident_f32", [128, 128], F32)
    cons["ones_col_f32"] = sbuf(nc, "ones_col_f32", [128, 1], F32)
    cons["cvals"] = sbuf(nc, "cvals", [1, 8], F32)
    it_cv = nc.inline_tensor(np.array([[1e-5, 1e-6, 0, 0, 0, 0, 0, 0]], np.float32), name="it_cv")
    it_onec = nc.inline_tensor(np.ones((128, 1), np.float32), name="it_onec")
    it_oner = nc.inline_tensor(np.ones((1, 128), np.float32), name="it_oner")
    it_id = nc.inline_tensor(np.eye(128, dtype=np.float32), name="it_id")
    ph = pr.phase("consts")
    ds = pr.sem("dmap")
    for dst, src in ((cons["ones_col_bf"], it_onec), (cons["ones_row_bf"], it_oner),
                     (cons["ones_row_f32"], it_oner), (cons["ident_bf"], it_id),
                     (cons["ident_f32"], it_id), (cons["ones_col_f32"], it_onec),
                     (cons["cvals"], it_cv)):
        ds.inc(16)
        ph.pool(lambda e, d=dst, s=src, sh=ds.h: e.dma_start(d[:, :], s[:, :]).then_inc(sh, 16))
    ph.wait("pool", ds, ds.v)
    return cons


def load_stream(pr, ph, x_dram, x_s, dsem):
    src = bass.AP(x_dram, 0, [[1024, 128], [128 * 1024, 16], [1, 1024]])
    dst = bass.AP(x_s, 0, [[16 * 1024, 128], [1024, 16], [1, 1024]])
    dsem.inc(16)
    ph.sp(lambda e, s=src, d=dst, sh=dsem.h: e.dma_start(d, s).then_inc(sh, 16))


def store_stream(pr, ph, x_s, y_dram, dsem):
    src = bass.AP(x_s, 0, [[16 * 1024, 128], [1024, 16], [1, 1024]])
    dst = bass.AP(y_dram, 0, [[1024, 128], [128 * 1024, 16], [1, 1024]])
    dsem.inc(16)
    ph.sp(lambda e, s=src, d=dst, sh=dsem.h: e.dma_start(d, s).then_inc(sh, 16))


def emit_l0a(pr, cons, x_in, y_out, P="", store=True):
    nc = pr.nc
    wq = nc.dram_tensor(P + "wq", [D, D], F32, kind="ExternalInput")
    wk = nc.dram_tensor(P + "wk", [D, 512], F32, kind="ExternalInput")
    wv = nc.dram_tensor(P + "wv", [D, 512], F32, kind="ExternalInput")
    wo = nc.dram_tensor(P + "wo", [D, D], F32, kind="ExternalInput")
    qcos = nc.dram_tensor(P + "qcos", [128, TPC], F32, kind="ExternalInput")
    qsin = nc.dram_tensor(P + "qsin", [128, TPC], F32, kind="ExternalInput")
    kcos = nc.dram_tensor(P + "kcos", [128, TPC], F32, kind="ExternalInput")
    ksin = nc.dram_tensor(P + "ksin", [128, TPC], F32, kind="ExternalInput")
    cmask = nc.dram_tensor(P + "cmask", [512, 512], F32, kind="ExternalInput")
    meta = nc.dram_tensor(P + "meta", [1, 128], I32, kind="ExternalInput")
    pay_i = nc.dram_tensor(P + "pay_i", [PAYR, TPC], BF16)
    pay_o = nc.dram_tensor(P + "pay_o", [4 * PAYR, TPC], BF16)
    opay_i = nc.dram_tensor(P + "opay_i", [512, S], BF16)
    opay_o = nc.dram_tensor(P + "opay_o", [2048, S], BF16)

    dsem = pr.sem("dma")
    ccs = pr.sem("cc")

    x_s = pr.ablock(P + "x_s", [128, NCH * 1024], F32)
    xn_s = pr.ablock(P + "xn_s", [128, NCH * 1024], BF16)
    qro = pr.ablock(P + "qro", [128, NCH * 1024], BF16)
    kro = pr.ablock(P + "kro", [128, 4 * 1024], BF16)
    vts = pr.ablock(P + "vts", [128, 4 * 1024], BF16)
    ropes = {}
    for nm in ("qcos", "qsin", "kcos", "ksin"):
        ropes[nm] = pr.ablock(P + f"r_{nm}", [128, TPC], BF16)
    cmask_s = pr.ablock(P + "cmask_s", [128, 4 * 512], BF16)
    meta_s = pr.ablock(P + "meta_s", [1, 128], I32)
    normrow = pr.ablock(P + "normrow", [1, 1024], BF16)
    rsc = [pr.tmpF[:, 0:512], pr.tmpF[:, 512:1024]]

    ph = pr.phase("load")
    dmap = pr.sem("dmap")
    load_stream(pr, ph, x_in, x_s, dsem)
    for nm, t in (("qcos", qcos), ("qsin", qsin), ("kcos", kcos), ("ksin", ksin)):
        dmap.inc(16)
        ph.pool(lambda e, d=ropes[nm], s=t, sh=dmap.h: e.dma_start(d[:, :], s[:, :]).then_inc(sh, 16))
    csrc = bass.AP(cmask, 0, [[512, 128], [128 * 512, 4], [1, 512]])
    cdst = bass.AP(cmask_s, 0, [[4 * 512, 128], [512, 4], [1, 512]])
    dmap.inc(16)
    ph.pool(lambda e, s=csrc, d=cdst, sh=dmap.h: e.dma_start(d, s).then_inc(sh, 16))
    dsem.inc(16)
    ph.sp(lambda e, sh=dsem.h: e.dma_start(meta_s[:, :], meta[:, :]).then_inc(sh, 16))
    ph.wait("sp", dsem, dsem.v)
    ph.wait("pool", dmap, dmap.v)

    emit_ln(pr, x_s, xn_s, cons, "l0")

    def rope_consumer(out_s, costab, sintab):
        def consumer(ph_, fg, tt, pt, pg, pgtgt, pf):
            ph_.wait("dve", pg, pgtgt)
            co = costab[:, tt * 512:(tt + 1) * 512]
            si = sintab[:, tt * 512:(tt + 1) * 512]
            ocol = fg * 1024 + tt * 512

            def f(e, pt=pt, co=co, si=si, oc=ocol):
                e.tensor_tensor(rsc[0], pt[:, :], co, ALU.mult)
                e.tensor_tensor(rsc[1][0:64, :], pt[64:128, :], si[0:64, :], ALU.mult)
                e.tensor_tensor(rsc[1][64:128, :], pt[0:64, :], si[64:128, :], ALU.mult)
                e.tensor_tensor(out_s[:, oc:oc + 512], rsc[0], rsc[1], ALU.add)
            ph_.dve(f)
            pf.inc(1)
            ph_.dve(lambda e, s=pf.h: e.nop().then_inc(s, 1))
        return consumer

    def plain_consumer(out_s):
        def consumer(ph_, fg, tt, pt, pg, pgtgt, pf):
            ph_.wait("dve", pg, pgtgt)
            ocol = fg * 1024 + tt * 512
            ph_.dve(lambda e, pt=pt, oc=ocol: e.tensor_copy(out_s[:, oc:oc + 512], pt[:, :]))
            pf.inc(1)
            ph_.dve(lambda e, s=pf.h: e.nop().then_inc(s, 1))
        return consumer

    emit_proj(pr, pr.phase("proj_q"), wq, D, xn_s,
              rope_consumer(qro, ropes["qcos"], ropes["qsin"]), packed=True)
    emit_proj(pr, pr.phase("proj_k"), wk, 512, xn_s,
              rope_consumer(kro, ropes["kcos"], ropes["ksin"]), packed=True)
    emit_proj(pr, pr.phase("proj_v"), wv, 512, xn_s, plain_consumer(vts), packed=True)

    php3 = pr.phase("payload")
    for src, dstoff, nch, rowlen in ((qro, PAY_Q, 16, NCH * 1024),
                                     (kro, PAY_K, 4, 4 * 1024),
                                     (vts, PAY_V, 4, 4 * 1024)):
        s_ap = bass.AP(src, 0, [[rowlen, 128], [1024, nch], [1, 1024]])
        d_ap = bass.AP(pay_i, dstoff * TPC, [[1024, 128], [128 * 1024, nch], [1, 1024]])
        dsem.inc(16)
        php3.sp(lambda e, s=s_ap, d=d_ap, sh=dsem.h: e.dma_start(d, s).then_inc(sh, 16))
    php3.wait("sp", dsem, dsem.v)

    phg = pr.phase("gather_in")
    ccs.inc(1)
    phg.pool(lambda e, sh=ccs.h: e.collective_compute(
        "AllGather", ALU.bypass, replica_groups=[[0, 1, 2, 3], [4, 5, 6, 7]],
        ins=[pay_i.ap()], outs=[pay_o.ap()]).then_inc(sh, 1))
    phg.wait("pool", ccs, ccs.v)

    pha = pr.phase("attn")
    pha.uses_reg = True
    kT = qro
    KOF, VOF, VTOK, QOF = 0, 4096, 8192, 12288
    pt_s = [pr.ablock(P + f"pt{i}", [128, 512], BF16) for i in range(2)]
    es_s = pr.bcA[:, 0:512]
    nbf = pr.ablock(P + "nbf", [1, 512], BF16)
    bcS = pr.bcA[:, 512:1024]
    outt = pr.ablock(P + "outt", [128, 512], BF16)
    sqk = pr.scbf[0]
    rowb = pr.rowb

    sLD = pr.sem("aload")
    sSC = pr.sem("asc")
    sE = pr.sem("aE")
    sP = pr.sem("aP")
    sAV = pr.sem("aAV")
    sVT = pr.sem("aVT")
    sVE = pr.sem("aVE")
    sEP = pr.sem("aEP")
    sBC = pr.sem("aBC")
    sQQ = pr.sem("aQQ")
    sBN = pr.sem("aBN")
    aq = [pr.sem("aq0"), pr.sem("aq1")]

    def pool_kv(e):
        r0 = e._shared_reg
        e.reg_load(r0, meta_s[0:1, 0:1])
        e.dma_start(
            bass.AP(kT, KOF, [[NCH * 1024, 128], [1024, 4], [1, 1024]]),
            bass.AP(pay_o, r0, [[1024, 128], [PAYR * 1024, 4], [1, 1024]])
        ).then_inc(sLD.h, 16)
        e.reg_add(r0, r0, 512 * 1024)
        e.dma_start(
            bass.AP(qro, VOF, [[NCH * 1024, 128], [1024, 4], [1, 1024]]),
            bass.AP(pay_o, r0, [[1024, 128], [PAYR * 1024, 4], [1, 1024]])
        ).then_inc(sLD.h, 16)
    pha.pool(pool_kv)
    sLD.inc(32)
    unit_ld = sLD.v

    pha.wait("pe", sLD, unit_ld)
    for ti in range(32):
        par = ti % 2
        if ti >= 2:
            pha.wait("pe", sVE, sVE.v - 1)
        sVT.inc(1)
        pha.pe(lambda e, t=ti, p=par, sh=sVT.h: e.matmul(
            pr.ps[2 + p][:, 0:128], qro[:, VOF + t * 128: VOF + (t + 1) * 128],
            cons["ident_bf"][:, :], start=True, stop=True).then_inc(sh, 1))
        pha.wait("dve", sVT, sVT.v)
        sVE.inc(1)
        pha.dve(lambda e, t=ti, p=par, sh=sVE.h: e.tensor_copy(
            qro[:, VTOK + t * 128: VTOK + (t + 1) * 128],
            pr.ps[2 + p][:, 0:128]).then_inc(sh, 1))

    pha.wait("act", sLD, unit_ld)
    sEP0k = sEP.v
    for rb in range(4):
        sQQ.inc(1)
        pha.act(lambda e, rb=rb, sh=sQQ.h: e.activation(
            sqk[:, :], kT[:, KOF + rb * 1024: KOF + (rb + 1) * 1024],
            AF.Square).then_inc(sh, 1))
        pha.wait("pe", sQQ, sQQ.v)
        for tt in range(2):
            i = rb * 2 + tt
            if i > 0:
                pha.wait("pe", sEP, sEP0k + i)
            sBC.inc(1)
            pha.pe(lambda e, t=tt, sh=sBC.h: e.matmul(
                pr.ps[6][0:1, :], cons["ones_col_bf"][:, :],
                sqk[:, t * 512:(t + 1) * 512], start=True, stop=True).then_inc(sh, 1))
            pha.wait("dve", sBC, sBC.v)
            sEP.inc(1)
            pha.dve(lambda e, i=i, sh=sEP.h: e.tensor_reduce(
                rowb[0:1, 1027 + i:1028 + i], pr.ps[6][0:1, :], AX.X, ALU.max)
                .then_inc(sh, 1))
        if rb < 3:
            pha.wait("act", sBC, sBC.v)
    pha.wait("dve", sEP, sEP.v)
    pha.dve(lambda e: e.tensor_reduce(rowb[0:1, 1024:1025], rowb[0:1, 1027:1035],
                                      AX.X, ALU.max))
    pha.dve(lambda e: e.tensor_scalar(rowb[0:1, 1026:1027], rowb[0:1, 1024:1025],
                                      -1.0 / 22.62742, None, ALU.mult))
    sBN.inc(1)
    pha.dve(lambda e, sh=sBN.h: e.nop().then_inc(sh, 1))
    bounds_ready = sBN.v

    qld_free = [None]
    sqq = pr.scbf[1]
    for h in range(4):
        for st in range(8):
            it = h * 8 + st
            if st == 0:
                aq[0].inc(16)

                def load_q(e, h=h, w=qld_free[0]):
                    r2 = e._shared_reg
                    if w is not None:
                        e.wait_ge(sAV.h, w)
                    e.reg_load(r2, meta_s[0:1, 4 + h * 4: 5 + h * 4])
                    e.dma_start(
                        bass.AP(qro, QOF, [[NCH * 1024, 128], [1024, 4], [1, 1024]]),
                        bass.AP(pay_o, r2, [[1024, 128], [PAYR * 1024, 4], [1, 1024]])
                    ).then_inc(aq[0].h, 16)
                pha.pool(load_q)
            q_tgt = aq[0].v
            qcols = qro[:, QOF + st * 512: QOF + st * 512 + 512]

            pha.wait("act", aq[0], q_tgt)
            if it > 0:
                pha.wait("act", sSC, sSC.v)
            sQQ.inc(1)
            pha.act(lambda e, qc=qcols, sh=sQQ.h: e.activation(
                sqq[:, 0:512], qc, AF.Square).then_inc(sh, 1))
            pha.wait("pe", sQQ, sQQ.v)
            pha.wait("pe", sEP, sEP.v)
            sBC.inc(1)
            pha.pe(lambda e, sh=sBC.h: e.matmul(
                pr.ps[6][0:1, :], cons["ones_col_bf"][:, :], sqq[:, 0:512],
                start=True, stop=True).then_inc(sh, 1))
            pha.wait("dve", sBN, bounds_ready)
            pha.wait("dve", sBC, sBC.v)
            pha.dve(lambda e: e.tensor_scalar(
                rowb[0:1, 0:512], pr.ps[6][0:1, :],
                rowb[0:1, 1026:1027], -5.65685, ALU.mult, ALU.add))
            pha.dve(lambda e: e.tensor_copy(nbf[0:1, :], rowb[0:1, 0:512]))
            sEP.inc(1)
            pha.dve(lambda e, sh=sEP.h: e.nop().then_inc(sh, 1))
            negb_tgt = sEP.v

            n_tt = 4 * (st + 1)
            sAV0 = sAV.v
            sP0 = sP.v
            for tt in range(n_tt):
                par = tt % 2
                pha.wait("pe", aq[0], q_tgt)
                if tt == 0:
                    pha.wait("pe", sEP, negb_tgt)
                if tt >= 2:
                    pha.wait("pe", sP, sP0 + tt - 1)
                pha.pe(lambda e, t=tt, p=par, qc=qcols: e.matmul(
                    pr.ps[p][:, :], kT[:, KOF + t * 128: KOF + (t + 1) * 128],
                    qc, start=True, stop=False))
                sSC.inc(1)
                pha.pe(lambda e, p=par, sh=sSC.h: e.matmul(
                    pr.ps[p][:, :], cons["ones_row_bf"][:, :], nbf[0:1, :],
                    start=False, stop=True).then_inc(sh, 1))
                diag = (tt >= 4 * st)
                if diag:
                    kk = tt - 4 * st
                    pha.wait("dve", sSC, sSC.v)
                    pha.wait("dve", sP, sP0 + tt)
                    sE.inc(1)
                    pha.dve(lambda e, p=par, k=kk, sh=sE.h: e.tensor_tensor(
                        es_s, pr.ps[p][:, :],
                        cmask_s[:, k * 512:(k + 1) * 512], ALU.add).then_inc(sh, 1))
                    pha.wait("act", sE, sE.v)
                    if tt >= 2:
                        pha.wait("act", sAV, sAV0 + tt - 1)
                    sP.inc(1)
                    pha.act(lambda e, p2=par, sh=sP.h: e.activation(
                        pt_s[p2][:, :], es_s, AF.Exp).then_inc(sh, 1))
                else:
                    pha.wait("act", sSC, sSC.v)
                    if tt >= 2:
                        pha.wait("act", sAV, sAV0 + tt - 1)
                    sP.inc(1)
                    pha.act(lambda e, p=par, p2=par, sh=sP.h: e.activation(
                        pt_s[p2][:, :], pr.ps[p][:, :], AF.Exp).then_inc(sh, 1))
                pha.wait("pe", sP, sP.v)
                pha.pe(lambda e, t=tt, p2=par, st_=(tt == 0), sp2=(tt == n_tt - 1):
                       e.matmul(
                    pr.ps[4][:, 0:512], qro[:, VTOK + t * 128: VTOK + (t + 1) * 128],
                    pt_s[p2][:, :], start=st_, stop=sp2))
                sAV.inc(1)
                pha.pe(lambda e, p2=par, st_=(tt == 0), sp_=(tt == n_tt - 1), sh=sAV.h:
                       e.matmul(pr.psm[0:1, :], cons["ones_col_bf"][:, :],
                                pt_s[p2][:, :], start=st_, stop=sp_).then_inc(sh, 1))
            if st == 7:
                qld_free[0] = sAV.v

            pha.wait("dve", sAV, sAV.v)
            pha.dve(lambda e: e.tensor_scalar(rowb[0:1, 512:1024], pr.psm[0:1, :],
                                              1e-30, None, ALU.max))
            pha.dve(lambda e: e.reciprocal(rowb[0:1, 512:1024], rowb[0:1, 512:1024]))
            sEP.inc(1)
            pha.dve(lambda e, sh=sEP.h: e.nop().then_inc(sh, 1))
            pha.wait("pe", sEP, sEP.v)
            sBC.inc(1)
            pha.pe(lambda e, sh=sBC.h: e.matmul(
                pr.ps[6][:, :], cons["ones_row_f32"][:, :], rowb[0:1, 512:1024],
                start=True, stop=True).then_inc(sh, 1))
            pha.wait("dve", sBC, sBC.v)
            pha.dve(lambda e: e.tensor_copy(bcS, pr.ps[6][:, :]))
            pha.wait("dve", dsem, dsem.v)
            pha.dve(lambda e: e.tensor_tensor(outt[:, :], pr.ps[4][:, 0:512],
                                              bcS, ALU.mult))
            sEP.inc(1)
            pha.dve(lambda e, sh=sEP.h: e.nop().then_inc(sh, 1))
            out_tgt = sEP.v
            pha.wait("sp", sEP, out_tgt)
            osrc = bass.AP(outt, 0, [[512, 128], [1, 512]])
            odst = bass.AP(opay_i, h * 128 * S + st * 512, [[S, 128], [1, 512]])
            dsem.inc(16)
            pha.sp(lambda e, s1=osrc, d1=odst, sh=dsem.h: e.dma_start(d1, s1).then_inc(sh, 16))
            pha.wait("pe", sEP, out_tgt)

    pha.wait("sp", dsem, dsem.v)

    phg2 = pr.phase("gather_out")
    ccs.inc(1)
    phg2.pool(lambda e, sh=ccs.h: e.collective_compute(
        "AllGather", ALU.bypass, replica_groups=[[0, 1, 2, 3], [4, 5, 6, 7]],
        ins=[opay_i.ap()], outs=[opay_o.ap()]).then_inc(sh, 1))
    phg2.wait("pool", ccs, ccs.v)

    phr = pr.phase("read_out")
    phr.uses_reg = True
    ao = qro

    def read_own(e):
            r3 = e._shared_reg
            e.reg_load(r3, meta_s[0:1, 60:61])
            e.dma_start(bass.AP(ao, 0, [[NCH * 1024, 128], [1024, 16], [1, 1024]]),
                        bass.AP(opay_o, r3, [[S, 128], [128 * S, 16], [1, 1024]])).then_inc(sLD.h, 16)
    phr.pool(read_own)
    sLD.inc(16)
    phr.wait("pool", sLD, sLD.v)

    def resid_consumer(ph_, fg, tt, pt, pg, pgtgt, pf):
        ph_.wait("dve", pg, pgtgt)
        col = fg * 1024 + tt * 512
        ph_.dve(lambda e, pt=pt, c=col: e.tensor_tensor(
            x_s[:, c:c + 512], x_s[:, c:c + 512], pt[:, :], ALU.add))
        pf.inc(1)
        ph_.dve(lambda e, s=pf.h: e.nop().then_inc(s, 1))

    emit_proj(pr, pr.phase("proj_o"), wo, D, ao, resid_consumer, packed=True)

    if store:
        phs = pr.phase("store")
        store_stream(pr, phs, x_s, y_out, dsem)
        phs.wait("sp", dsem, dsem.v)
    pr.arena_reset()


def build_l0a(upto=None):
    pr = Prog("l0a")
    nc = pr.nc
    x_in = nc.dram_tensor("x", [D, TPC], F32, kind="ExternalInput")
    y_out = nc.dram_tensor("y", [D, TPC], F32, kind="ExternalOutput")
    cons = alloc_consts(pr)
    pr.init_arena()
    emit_l0a(pr, cons, x_in, y_out)
    if upto is not None:
        names = [p.name for p in pr.phases]
        idx = names.index(upto)
        pr.phases = pr.phases[:idx + 1]
        tr = pr.sem("trunc")
        phs2 = pr.phase("store_trunc")
        x_s = nc.find_inst_by_name and None
        pr.phases = pr.phases
    pr.emit()
    return pr.nc


def rope_tables():
    hd = L0_HD
    freqs = 1.0 / (10000.0 ** (np.arange(0, hd, 2, dtype=np.float64) / hd))
    t = np.arange(S, dtype=np.float64)
    ang = t[:, None] * freqs[None, :]
    cos = np.concatenate([np.cos(ang), np.cos(ang)], axis=1)
    sin = np.sin(ang)
    ssin = np.concatenate([-sin, sin], axis=1)
    return cos.T.astype(np.float32), ssin.T.astype(np.float32)


def causal_masks():
    m = np.zeros((4, 128, 512), np.float32)
    ti = np.arange(128)[:, None]
    sj = np.arange(512)[None, :]
    for k in range(4):
        m[k] = np.where(ti + 128 * k <= sj, 0.0, NEG)
    return np.ascontiguousarray(m.reshape(512, 512))


def l0a_meta(core):
    kvg = core % 4
    m = np.zeros(128, np.int32)
    m[0] = (PAY_K + kvg * 128) * TPC
    for h in range(4):
        for rb in range(4):
            m[4 + h * 4 + rb] = (rb * PAYR + PAY_Q + (4 * kvg + h) * 128) * TPC
    m[60] = (core % 4) * TPC
    return m.reshape(1, 128)


_NC_CACHE = {}


def get_nc(name, builder):
    if name not in _NC_CACHE:
        _NC_CACHE[name] = builder()
    return _NC_CACHE[name]


def maps_l0a(inputs, c, P=""):
    cos, ssin = rope_tables()
    cm = causal_masks()
    scale = 1.0 / np.sqrt(float(L0_HD))
    t0 = (c % 4) * TPC
    return {
        P + "wq": pack_proj(np.asarray(inputs["L0_attn_wq"], np.float32),
                            NCH, D).reshape(D, D),
        P + "wk": pack_proj(np.asarray(inputs["L0_attn_wk"], np.float32),
                            NCH, 512).reshape(D, 512),
        P + "wv": pack_proj(np.asarray(inputs["L0_attn_wv"], np.float32),
                            NCH, 512).reshape(D, 512),
        P + "wo": pack_proj(np.asarray(inputs["L0_attn_wo"], np.float32),
                            NCH, D).reshape(D, D),
        P + "qcos": np.ascontiguousarray(cos[:, t0:t0 + TPC] * scale),
        P + "qsin": np.ascontiguousarray(ssin[:, t0:t0 + TPC] * scale),
        P + "kcos": np.ascontiguousarray(cos[:, t0:t0 + TPC]),
        P + "ksin": np.ascontiguousarray(ssin[:, t0:t0 + TPC]),
        P + "cmask": cm,
        P + "meta": l0a_meta(c),
    }


def run_l0a(x_shards, inputs, upto=None):
    nc = get_nc(f"l0a{upto}", lambda: build_l0a(upto))
    in_maps = [dict(x=x_shards[c], **maps_l0a(inputs, c)) for c in range(NCORE)]
    res = run_bass_kernel_spmd(nc, in_maps, core_ids=list(range(NCORE)))
    return [res.results[c]["y"] for c in range(NCORE)]


def shard_stream(x):
    out = []
    for c in range(NCORE):
        b, r = c // 4, c % 4
        out.append(np.ascontiguousarray(x[b, r * TPC:(r + 1) * TPC, :].T.astype(np.float32)))
    return out


def unshard_stream(shards):
    x = np.empty((B, S, D), np.float32)
    for c in range(NCORE):
        b, r = c // 4, c % 4
        x[b, r * TPC:(r + 1) * TPC, :] = shards[c].T
    return x


def kernel(**inputs):
    return kernel_fused(inputs)


def emit_l0f(pr, cons, x_in, y_out, P="", only=None, load=True, store=True):
    nc = pr.nc
    gwT = nc.dram_tensor(P + "gwT", [D, 8], F32, kind="ExternalInput")
    wg3 = nc.dram_tensor(P + "wg3", [8 * D, 1408], F32, kind="ExternalInput")
    wu3 = nc.dram_tensor(P + "wu3", [8 * D, 1408], F32, kind="ExternalInput")
    wd3 = nc.dram_tensor(P + "wd3", [8 * 1408, D], F32, kind="ExternalInput")
    shg = nc.dram_tensor(P + "shg", [128, 16], F32, kind="ExternalInput")
    shwg = nc.dram_tensor(P + "shwg", [D, 2816], F32, kind="ExternalInput")
    shwu = nc.dram_tensor(P + "shwu", [D, 2816], F32, kind="ExternalInput")
    shwd = nc.dram_tensor(P + "shwd", [2816, D], F32, kind="ExternalInput")
    dsem = pr.sem("dma")
    dmap = pr.sem("dmap")

    x_s = pr.ablock(P + "x_s", [128, NCH * 1024], F32)
    xn_s = pr.ablock(P + "xn_s", [128, NCH * 1024], BF16)
    hb = pr.ablock(P + "hb", [128, 22 * 1024], BF16)
    combrow = pr.ablock(P + "combrow", [1, 9 * 1024], BF16)
    gwT_s = pr.ablock(P + "gwT_s", [128, 16 * 8], F32)
    shg_s = pr.ablock(P + "shg_s", [128, 16], BF16)
    lgt = pr.ablock(P + "lgt", [128, 64], F32)
    cb_tok = pr.ablock(P + "cb_tok", [128, 8], F32)

    ph = pr.phase("load")
    if load:
        load_stream(pr, ph, x_in, x_s, dsem)
    gsrc = bass.AP(gwT, 0, [[8, 128], [128 * 8, 16], [1, 8]])
    gdst = bass.AP(gwT_s, 0, [[128, 128], [8, 16], [1, 8]])
    dsem.inc(16)
    ph.sp(lambda e, s=gsrc, d=gdst, sh=dsem.h: e.dma_start(d, s).then_inc(sh, 16))
    dmap.inc(16)
    ph.pool(lambda e, sh=dmap.h: e.dma_start(shg_s[:, :], shg[:, :]).then_inc(sh, 16))
    ph.wait("sp", dsem, dsem.v)
    ph.wait("pool", dmap, dmap.v)

    emit_ln(pr, x_s, xn_s, cons, "l0f")
    rstd_row = pr.rowb[0:1, 2048:3072]
    negm_row = pr.rowb[0:1, 0:1024]

    phr = pr.phase("router")
    rtP = pr.sem("rtP")
    rtD = pr.sem("rtD")
    rtA = pr.sem("rtA")
    for dc in range(NCH):
        st, sp_ = dc == 0, dc == NCH - 1
        if sp_:
            rtP.inc(1)
            phr.pe(lambda e, dc=dc, sh=rtP.h: e.matmul(
                pr.ps[5][0:1, 0:8], cons["ones_col_f32"][:, :],
                gwT_s[:, dc * 8:(dc + 1) * 8], start=dc == 0, stop=True).then_inc(sh, 1))
        else:
            phr.pe(lambda e, dc=dc, s0=st: e.matmul(
                pr.ps[5][0:1, 0:8], cons["ones_col_f32"][:, :],
                gwT_s[:, dc * 8:(dc + 1) * 8], start=s0, stop=False))
    sg_done = []
    for t2 in range(2):
        for dc in range(NCH):
            sp_ = dc == NCH - 1
            if sp_:
                rtP.inc(1)
                phr.pe(lambda e, dc=dc, t2=t2, sh=rtP.h: e.matmul(
                    pr.ps[6][0:1, t2 * 256: t2 * 256 + 512] if False else pr.ps[6][0:1, 0:512],
                    shg_s[:, dc:dc + 1],
                    xn_s[:, dc * 1024 + t2 * 512: dc * 1024 + t2 * 512 + 512],
                    start=dc == 0, stop=True).then_inc(sh, 1))
            else:
                phr.pe(lambda e, dc=dc, t2=t2: e.matmul(
                    pr.ps[6][0:1, 0:512], shg_s[:, dc:dc + 1],
                    xn_s[:, dc * 1024 + t2 * 512: dc * 1024 + t2 * 512 + 512],
                    start=dc == 0, stop=False))
        phr.wait("act", rtP, rtP.v)
        rtA.inc(1)
        phr.act(lambda e, t2=t2, sh=rtA.h: e.activation(
            combrow[0:1, 8 * 1024 + t2 * 512: 8 * 1024 + (t2 + 1) * 512],
            pr.ps[6][0:1, 0:512], AF.Sigmoid).then_inc(sh, 1))
        sg_done.append(rtA.v)
        phr.wait("pe", rtA, rtA.v)

    phr.wait("dve", rtP, 1)
    rtD.inc(1)
    phr.dve(lambda e, sh=rtD.h: e.tensor_copy(
        lgt[0:1, 48:56], pr.ps[5][0:1, 0:8]).then_inc(sh, 1))

    for ts in range(8):
        for dc in range(NCH):
            sp_ = dc == NCH - 1
            lhs = x_s[:, dc * 1024 + ts * 128: dc * 1024 + ts * 128 + 128]
            if sp_:
                rtP.inc(1)
                phr.pe(lambda e, l=lhs, dc=dc, sh=rtP.h: e.matmul(
                    pr.ps[0][:, 0:8], l, gwT_s[:, dc * 8:(dc + 1) * 8],
                    start=dc == 0, stop=True).then_inc(sh, 1))
            else:
                phr.pe(lambda e, l=lhs, dc=dc: e.matmul(
                    pr.ps[0][:, 0:8], l, gwT_s[:, dc * 8:(dc + 1) * 8],
                    start=dc == 0, stop=False))
        if ts == 0:
            phr.wait("pe", rtD, rtD.v)
        rtP.inc(1)
        phr.pe(lambda e, ts=ts, sh=rtP.h: e.matmul(
            pr.ps[1][:, 0:8], negm_row[0:1, ts * 128:(ts + 1) * 128],
            lgt[0:1, 48:56], start=True, stop=True).then_inc(sh, 1))
        rtP.inc(1)
        phr.pe(lambda e, ts=ts, sh=rtP.h: e.matmul(
            pr.ps[2][:, 0:1], rstd_row[0:1, ts * 128:(ts + 1) * 128],
            cons["ones_row_f32"][0:1, 0:1], start=True, stop=True).then_inc(sh, 1))
        pe_tgt = rtP.v
        phr.wait("dve", rtP, pe_tgt)
        rtD.inc(1)

        def rmath(e, ts=ts, sh=rtD.h):
            e.tensor_copy(lgt[:, 37:38], pr.ps[2][:, 0:1])
            e.tensor_copy(lgt[:, 40:48], pr.ps[0][:, 0:8])
            e.scalar_tensor_tensor(lgt[:, 0:8], lgt[:, 40:48], lgt[:, 37:38],
                                   pr.ps[1][:, 0:8], ALU.mult, ALU.add)
            e.tensor_reduce(lgt[:, 32:33], lgt[:, 0:8], AX.X, ALU.max)
            e.tensor_scalar(lgt[:, 8:16], lgt[:, 0:8], lgt[:, 32:33], None, ALU.is_equal)
            e.scalar_tensor_tensor(lgt[:, 16:24], lgt[:, 8:16], NEG, lgt[:, 0:8],
                                   ALU.mult, ALU.add)
            e.tensor_reduce(lgt[:, 33:34], lgt[:, 16:24], AX.X, ALU.max)
            e.tensor_tensor(lgt[:, 34:35], lgt[:, 32:33], lgt[:, 33:34], ALU.subtract)
            e.tensor_scalar(lgt[:, 24:32], lgt[:, 0:8], lgt[:, 33:34], None, ALU.is_equal).then_inc(sh, 1)
        phr.dve(rmath)
        phr.wait("act", rtD, rtD.v)
        rtA.inc(1)
        phr.act(lambda e, sh=rtA.h: e.activation(
            lgt[:, 35:36], lgt[:, 34:35], AF.Sigmoid).then_inc(sh, 1))
        phr.wait("dve", rtA, rtA.v)
        rtD.inc(1)

        def rmath2(e, sh=rtD.h):
            e.tensor_scalar(lgt[:, 36:37], lgt[:, 35:36], -1.0, 1.0, ALU.mult, ALU.add)
            e.tensor_scalar(cb_tok[:, 0:8], lgt[:, 8:16], lgt[:, 35:36], None, ALU.mult)
            e.tensor_scalar(lgt[:, 24:32], lgt[:, 24:32], lgt[:, 36:37], None, ALU.mult)
            e.tensor_tensor(cb_tok[:, 0:8], cb_tok[:, 0:8], lgt[:, 24:32], ALU.add).then_inc(sh, 1)
        phr.dve(rmath2)
        phr.wait("pe", rtD, rtD.v)
        rtP.inc(1)
        phr.pe(lambda e, sh=rtP.h: e.matmul(
            pr.ps[3][0:8, 0:128], cb_tok[:, 0:8], cons["ident_f32"][:, :],
            start=True, stop=True).then_inc(sh, 1))
        phr.wait("dve", rtP, rtP.v)
        rtD.inc(1)
        phr.dve(lambda e, ts=ts, sh=rtD.h: e.tensor_copy(
            pr.bcA[0:8, ts * 128:(ts + 1) * 128], pr.ps[3][0:8, 0:128]).then_inc(sh, 1))
        phr.wait("pe", rtD, rtD.v)
    phr.wait("pool", rtD, rtD.v)
    dmap.inc(16)
    phr.pool(lambda e, sh=dmap.h: e.dma_start(
        bass.AP(combrow, 0, [[8192, 1], [1, 8192]]),
        bass.AP(pr.bcA, 0, [[1024, 8], [1, 1024]])).then_inc(sh, 16))
    phr.wait("pool", dmap, dmap.v)

    sCA = pr.sem("sCA")
    hb_free_at = [0]

    def mk_bc(ph_, e_idx):
        grp = pr.grp_par % 2
        pr.grp_par += 1
        banks = (pr.ps[0], pr.ps[1]) if grp == 0 else (pr.ps[4], pr.ps[5])
        pf = pr.sem("pf")
        ph_.wait("pe", pf, pr.pf_marks[-2])
        bcs = pr.sem("bcs")
        for t2 in range(2):
            bcs.inc(1)
            ph_.pe(lambda e, ei=e_idx, t2=t2, b=banks[t2], sh=bcs.h: e.matmul(
                b[:, :], cons["ones_row_bf"][:, :],
                combrow[0:1, ei * 1024 + t2 * 512: ei * 1024 + t2 * 512 + 512],
                start=True, stop=True).then_inc(sh, 1))
        ph_.wait("dve", bcs, bcs.v)
        pf.inc(1)

        def cp(e, b0=banks[0], b1=banks[1], sh=pf.h):
            e.tensor_copy(pr.bcB[:, 0:512], b0[:, :])
            e.tensor_copy(pr.bcB[:, 512:1024], b1[:, :]).then_inc(sh, 1)
        ph_.dve(cp)
        pr.pf_marks.append(pf.v)

    def silu_consumer(hb_guard):
        def consumer(ph_, fg, tt, bank, wf, wf_tgt, pf):
            ph_.wait("act", wf, wf_tgt)
            if hb_guard[0]:
                ph_.wait("act", wf, hb_guard[0])
                hb_guard[0] = 0
            sCA.inc(1)
            ph_.act(lambda e, fg=fg, tt=tt, b=bank, sh=sCA.h: e.activation(
                hb[:, fg * 1024 + tt * 512: fg * 1024 + tt * 512 + 512],
                b[:, :], AF.Silu).then_inc(sh, 1))
            ph_.wait("dve", sCA, sCA.v)
            pf.inc(1)
            ph_.dve(lambda e, sh=pf.h: e.nop().then_inc(sh, 1))
        return consumer

    def h_consumer(ph_, fg, tt, bank, wf, wf_tgt, pf):
        ph_.wait("dve", wf, wf_tgt)
        pf.inc(1)

        def f(e, fg=fg, tt=tt, b=bank, sh=pf.h):
            hc_ = hb[:, fg * 1024 + tt * 512: fg * 1024 + tt * 512 + 512]
            e.tensor_tensor(pr.tmpF[:, 0:512], b[:, :], hc_, ALU.mult)
            e.tensor_tensor(hc_, pr.tmpF[:, 0:512],
                            pr.bcB[:, tt * 512:(tt + 1) * 512], ALU.mult).then_inc(sh, 1)
        ph_.dve(f)

    def xacc_consumer(ph_, dt, ti, bank, wf, wf_tgt, pf):
        ph_.wait("dve", wf, wf_tgt)
        pf.inc(1)
        ph_.dve(lambda e, dt=dt, ti=ti, b=bank, sh=pf.h: e.tensor_tensor(
            x_s[:, dt * 1024 + ti * 512: dt * 1024 + ti * 512 + 512],
            x_s[:, dt * 1024 + ti * 512: dt * 1024 + ti * 512 + 512],
            b[:, :], ALU.add).then_inc(sh, 1))

    experts = [("sh",)] + [(e,) for e in range(8)]
    if only is not None:
        experts = [(e,) for e in only]
    phe = pr.phase("experts")
    for i, (e_id,) in enumerate(experts):
        ei = 8 if e_id == "sh" else e_id
        mk_bc(phe, ei)
        hb_guard = [hb_free_at[0]]
        if e_id == "sh":
            emit_proj(pr, phe, shwg, 2816, xn_s, silu_consumer(hb_guard),
                      packed=True)
            emit_proj(pr, phe, shwu, 2816, xn_s, h_consumer, packed=True)
            emit_down(pr, phe, shwd, 22,
                      lambda hc, ti: hb[:, hc * 1024 + ti * 512: hc * 1024 + ti * 512 + 512],
                      xacc_consumer, packed=True, two_tt=True)
        else:
            emit_proj(pr, phe, wg3, 1408, xn_s, silu_consumer(hb_guard),
                      w_off=e_id * D * 1408, row_len=1408, packed=True)
            emit_proj(pr, phe, wu3, 1408, xn_s, h_consumer,
                      w_off=e_id * D * 1408, row_len=1408, packed=True)
            emit_down(pr, phe, wd3, 11,
                      lambda hc, ti: hb[:, hc * 1024 + ti * 512: hc * 1024 + ti * 512 + 512],
                      xacc_consumer, w_off=e_id * 1408 * D, packed=True, two_tt=True)
        hb_free_at[0] = pr.sem("wf").v

    if store:
        phs = pr.phase("store")
        store_stream(pr, phs, x_s, y_out, dsem)
        phs.wait("sp", dsem, dsem.v)
    pr.arena_reset()


def build_l0f(upto=None, only=None):
    pr = Prog("l0f")
    nc = pr.nc
    x_in = nc.dram_tensor("x", [D, TPC], F32, kind="ExternalInput")
    y_out = nc.dram_tensor("y", [D, TPC], F32, kind="ExternalOutput")
    cons = alloc_consts(pr)
    pr.init_arena()
    emit_l0f(pr, cons, x_in, y_out, only=only)
    pr.emit()
    return pr.nc


def maps_l0f(inputs, c, P=""):
    gwT = np.ascontiguousarray(np.asarray(inputs["L0_ffn_gate_w"], np.float32).T)
    wg3 = np.ascontiguousarray(np.asarray(inputs["L0_ffn_w_gate"], np.float32).reshape(8 * D, 1408))
    wu3 = np.ascontiguousarray(np.asarray(inputs["L0_ffn_w_up"], np.float32).reshape(8 * D, 1408))
    wd3 = np.ascontiguousarray(np.asarray(inputs["L0_ffn_w_down"], np.float32).reshape(8 * 1408, D))
    shg = np.ascontiguousarray(np.asarray(inputs["L0_ffn_shgate_w"], np.float32).reshape(16, 128).T)
    wg3p = np.concatenate([pack_proj(wg3[e * D:(e + 1) * D], NCH, 1408)
                           for e in range(8)]).reshape(8 * D, 1408)
    wu3p = np.concatenate([pack_proj(wu3[e * D:(e + 1) * D], NCH, 1408)
                           for e in range(8)]).reshape(8 * D, 1408)
    wd3p = np.concatenate([pack_down(wd3, 11, row_off=e * 1408)
                           for e in range(8)]).reshape(8 * 1408, D)
    return {
        P + "gwT": gwT, P + "wg3": wg3p, P + "wu3": wu3p, P + "wd3": wd3p,
        P + "shg": shg,
        P + "shwg": pack_proj(np.asarray(inputs["L0_ffn_shw_gate"], np.float32),
                              NCH, 2816).reshape(D, 2816),
        P + "shwu": pack_proj(np.asarray(inputs["L0_ffn_shw_up"], np.float32),
                              NCH, 2816).reshape(D, 2816),
        P + "shwd": pack_down(np.asarray(inputs["L0_ffn_shw_down"], np.float32),
                              22).reshape(2816, D),
    }


def run_l0f(x_shards, inputs, upto=None):
    nc = get_nc(f"l0f{upto}", lambda: build_l0f(upto))
    in_maps = [dict(x=x_shards[c], **maps_l0f(inputs, c)) for c in range(NCORE)]
    res = run_bass_kernel_spmd(nc, in_maps, core_ids=list(range(NCORE)))
    return [res.results[c]["y"] for c in range(NCORE)]


PAY1_Q, PAY1_K, PAY1_V = 0, 1024, 2048
PAYL1 = 4096
RSQ128 = float(1.0 / np.sqrt(128.0))


def emit_l1a(pr, cons, x_in, y_out, P="", load=True, store=True):
    nc = pr.nc
    wq = nc.dram_tensor(P + "wq", [D, 1024], F32, kind="ExternalInput")
    wk = nc.dram_tensor(P + "wk", [D, 1024], F32, kind="ExternalInput")
    wv = nc.dram_tensor(P + "wv", [D, D], F32, kind="ExternalInput")
    wig = nc.dram_tensor(P + "wig", [D, 8], F32, kind="ExternalInput")
    wfg = nc.dram_tensor(P + "wfg", [D, 8], F32, kind="ExternalInput")
    bigc = nc.dram_tensor(P + "bigc", [8, 2], F32, kind="ExternalInput")
    wog = nc.dram_tensor(P + "wog", [D, D], F32, kind="ExternalInput")
    wout = nc.dram_tensor(P + "wout", [D, D], F32, kind="ExternalInput")
    wgate = nc.dram_tensor(P + "wgate", [D, 8192], F32, kind="ExternalInput")
    wup = nc.dram_tensor(P + "wup", [D, 8192], F32, kind="ExternalInput")
    wdown = nc.dram_tensor(P + "wdown", [8192, D], F32, kind="ExternalInput")
    n1c = nc.dram_tensor(P + "n1c", [128, 16], F32, kind="ExternalInput")
    n2c = nc.dram_tensor(P + "n2c", [128, 16], F32, kind="ExternalInput")
    cmask = nc.dram_tensor(P + "cmask", [512, 512], F32, kind="ExternalInput")
    meta = nc.dram_tensor(P + "meta", [1, 128], I32, kind="ExternalInput")
    pay_i = nc.dram_tensor(P + "pay_i", [PAYL1, TPC], BF16)
    pay_o = nc.dram_tensor(P + "pay_o", [4 * PAYL1, TPC], BF16)
    pg_i = nc.dram_tensor(P + "pg_i", [16, TPC], F32)
    pg_o = nc.dram_tensor(P + "pg_o", [64, TPC], F32)
    opay_i = nc.dram_tensor(P + "opay_i", [512, S], BF16)
    opay_o = nc.dram_tensor(P + "opay_o", [2048, S], BF16)

    selmy = nc.dram_tensor(P + "selmy", [8, 256], F32, kind="ExternalInput")
    i8my = nc.dram_tensor(P + "i8my", [8, 2], F32, kind="ExternalInput")
    mhmy = nc.dram_tensor(P + "mhmy", [128, 4], F32, kind="ExternalInput")

    dsem = pr.sem("dma")
    dmap = pr.sem("dmap")
    ccs = pr.sem("cc")

    arena = pr.ablock(P + "arena", [128, NCH * 1024], F32)
    abase = nc.lookup_mloc(arena).addr
    x_s = arena
    g_in = pr.ablock(P + "g_in", [128, NCH * 1024], BF16)
    h_in = pr.ablock(P + "h_in", [128, NCH * 1024], BF16)
    B1 = pr.ablock(P + "B1", [8, 4096], F32)
    B2 = pr.ablock(P + "B2", [8, 4096], F32)
    bbase = nc.lookup_mloc(B1).addr
    acol = pr.ablock(P + "acol", [128, 64], F32)
    n1_s = pr.ablock(P + "n1_s", [128, 16], F32)
    n2_s = pr.ablock(P + "n2_s", [128, 16], F32)
    mh_s = pr.ablock(P + "mh_s", [128, 4], F32)
    big_s = pr.ablock(P + "big_s", [8, 2], F32)
    cmask_s = pr.ablock(P + "cmask_s", [128, 4 * 512], BF16)
    meta_s = pr.ablock(P + "meta_s", [1, 128], I32)
    sel_s = pr.ablock(P + "sel_s", [8, 256], F32)
    i8_s = pr.ablock(P + "i8_s", [8, 2], F32)

    qkvT = nc.alloc_sbuf_tensor_at(P + "qkvT", [128, 32 * 1024], BF16, offset=abase)
    B3 = nc.alloc_sbuf_tensor_at(P + "B3", [8, 4096], F32, offset=abase)
    gI = nc.alloc_sbuf_tensor_at(P + "gI", [8, 4096], F32, offset=abase + 16384)
    EF = nc.alloc_sbuf_tensor_at(P + "EF", [128, 512], F32, offset=abase)
    PT2 = nc.alloc_sbuf_tensor_at(P + "PT2", [128, 1024], BF16, offset=abase + 2048)
    outst = nc.alloc_sbuf_tensor_at(P + "outst", [128, 512], BF16, offset=abase + 4096)
    sqsc = nc.alloc_sbuf_tensor_at(P + "sqsc", [128, 512], BF16, offset=abase + 5120)
    kTA = nc.alloc_sbuf_tensor_at(P + "kTA", [128, 4096], BF16, offset=abase + 16384)
    vstage = nc.alloc_sbuf_tensor_at(P + "vstage", [128, 8192], BF16, offset=abase + 24576)
    vtok = nc.alloc_sbuf_tensor_at(P + "vtok", [128, 8192], BF16, offset=abase + 40960)
    qslot = nc.alloc_sbuf_tensor_at(P + "qslot", [128, 4096], BF16, offset=abase + 57344)
    hn = nc.alloc_sbuf_tensor_at(P + "hn", [128, NCH * 1024], BF16, offset=bbase)
    hb = hn

    ph = pr.phase("load")
    if load:
        load_stream(pr, ph, x_in, x_s, dsem)
    for d, s in ((n1_s, n1c), (n2_s, n2c), (mh_s, mhmy), (big_s, bigc),
                 (meta_s, meta), (sel_s, selmy), (i8_s, i8my)):
        dsem.inc(16)
        ph.sp(lambda e, d=d, s=s, sh=dsem.h: e.dma_start(d[:, :], s[:, :]).then_inc(sh, 16))
    csrc = bass.AP(cmask, 0, [[512, 128], [128 * 512, 4], [1, 512]])
    cdst = bass.AP(cmask_s, 0, [[4 * 512, 128], [512, 4], [1, 512]])
    dmap.inc(16)
    ph.pool(lambda e, s=csrc, d=cdst, sh=dmap.h: e.dma_start(d, s).then_inc(sh, 16))
    ph.wait("sp", dsem, dsem.v)
    ph.wait("pool", dmap, dmap.v)

    emit_ln(pr, x_s, g_in, cons, "l1")
    emit_ln(pr, g_in, h_in, cons, "l1r", rms_w=n1_s, eps=1e-6, src_bf=True)

    def stage_consumer(colbase, scale=None):
        def consumer(ph_, fg, tt, bank, wf, wf_tgt, pf):
            ph_.wait("dve", wf, wf_tgt)
            oc = colbase + fg * 1024 + tt * 512
            pf.inc(1)
            if scale is None:
                ph_.dve(lambda e, b=bank, oc=oc, sh=pf.h: e.tensor_copy(
                    qkvT[:, oc:oc + 512], b[:, :]).then_inc(sh, 1))
            else:
                ph_.dve(lambda e, b=bank, oc=oc, sh=pf.h: e.tensor_scalar(
                    qkvT[:, oc:oc + 512], b[:, :], scale, None, ALU.mult).then_inc(sh, 1))
        return consumer

    php = pr.phase("proj_qkv")
    emit_proj(pr, php, wq, 1024, h_in, stage_consumer(0, RSQ128), packed=True)
    emit_proj(pr, php, wk, 1024, h_in, stage_consumer(8 * 1024), packed=True)
    emit_proj(pr, php, wv, D, h_in, stage_consumer(16 * 1024), packed=True)

    phgp = pr.phase("proj_gates")
    gsem = pr.sem("l1gp")
    gsd = pr.sem("l1gd")
    gsa = pr.sem("l1ga")
    gw_s = pr.ablock(P + "l1gw", [128, 2 * 16 * 8], BF16)
    for gi, w in enumerate((wig, wfg)):
        gsrc = bass.AP(w, 0, [[8, 128], [128 * 8, 16], [1, 8]])
        gdst = bass.AP(gw_s, gi * 128, [[256, 128], [8, 16], [1, 8]])
        dmap.inc(16)
        phgp.pool(lambda e, s=gsrc, d=gdst, sh=dmap.h: e.dma_start(d, s).then_inc(sh, 16))
    phgp.wait("pe", dmap, dmap.v)
    for gi in range(2):
        for t2 in range(2):
            for dc in range(NCH):
                sp_ = dc == NCH - 1
                lhs = gw_s[:, gi * 128 + dc * 8: gi * 128 + (dc + 1) * 8]
                if sp_:
                    gsem.inc(1)
                    phgp.pe(lambda e, l=lhs, dc=dc, t2=t2, sh=gsem.h: e.matmul(
                        pr.ps[6][0:8, 0:512], l,
                        h_in[:, dc * 1024 + t2 * 512: dc * 1024 + t2 * 512 + 512],
                        start=dc == 0, stop=True).then_inc(sh, 1))
                else:
                    phgp.pe(lambda e, l=lhs, dc=dc, t2=t2: e.matmul(
                        pr.ps[6][0:8, 0:512], l,
                        h_in[:, dc * 1024 + t2 * 512: dc * 1024 + t2 * 512 + 512],
                        start=dc == 0, stop=False))
            phgp.wait("dve", gsem, gsem.v)
            gsd.inc(1)
            phgp.dve(lambda e, gi=gi, sh=gsd.h: e.tensor_scalar(
                pr.bcA[0:8, 0:512], pr.ps[6][0:8, 0:512],
                big_s[0:8, gi:gi + 1], None, ALU.add).then_inc(sh, 1))
            phgp.wait("act", gsd, gsd.v)
            gsa.inc(1)
            phgp.act(lambda e, sh=gsa.h: e.activation(
                pr.bcB[0:8, 0:512], pr.bcA[0:8, 0:512], AF.Tanh,
                scale=1.0 / 15.0).then_inc(sh, 1))
            phgp.wait("dve", gsa, gsa.v)
            phgp.wait("dve", dsem, dsem.v)
            gsd.inc(1)
            phgp.dve(lambda e, sh=gsd.h: e.tensor_scalar(
                pr.bcA[0:8, 512:1024], pr.bcB[0:8, 0:512], 15.0, None,
                ALU.mult).then_inc(sh, 1))
            phgp.wait("sp", gsd, gsd.v)
            dsem.inc(16)
            phgp.sp(lambda e, gi=gi, t2=t2, sh=dsem.h: e.dma_start(
                bass.AP(pg_i, gi * 8 * TPC + t2 * 512, [[TPC, 8], [1, 512]]),
                pr.bcA[0:8, 512:1024]).then_inc(sh, 16))
            phgp.wait("pe", gsd, gsd.v)
    phgp.wait("sp", dsem, dsem.v)

    php3 = pr.phase("payload")
    s_ap = bass.AP(qkvT, 0, [[32 * 1024, 128], [1024, 32], [1, 1024]])
    d_ap = bass.AP(pay_i, 0, [[1024, 128], [128 * 1024, 32], [1, 1024]])
    dsem.inc(16)
    php3.sp(lambda e, s=s_ap, d=d_ap, sh=dsem.h: e.dma_start(d, s).then_inc(sh, 16))
    php3.wait("sp", dsem, dsem.v)

    phga = pr.phase("gather_in")
    ccs.inc(1)
    phga.pool(lambda e, sh=ccs.h: e.collective_compute(
        "AllGather", ALU.bypass, replica_groups=[[0, 1, 2, 3], [4, 5, 6, 7]],
        ins=[pay_i.ap()], outs=[pay_o.ap()]).then_inc(sh, 1))
    ccs.inc(1)
    phga.pool(lambda e, sh=ccs.h: e.collective_compute(
        "AllGather", ALU.bypass, replica_groups=[[0, 1, 2, 3], [4, 5, 6, 7]],
        ins=[pg_i.ap()], outs=[pg_o.ap()]).then_inc(sh, 1))
    phga.wait("pool", ccs, ccs.v)

    phw = pr.phase("gates")
    gisrc = bass.AP(pg_o, 0, [[TPC, 8], [16 * TPC, 4], [1, 1024]])
    gidst = bass.AP(gI, 0, [[4096, 8], [1024, 4], [1, 1024]])
    gfsrc = bass.AP(pg_o, 8 * TPC, [[TPC, 8], [16 * TPC, 4], [1, 1024]])
    gfdst = bass.AP(B2, 0, [[4096, 8], [1024, 4], [1, 1024]])
    dsem.inc(16)
    phw.sp(lambda e, s=gisrc, d=gidst, sh=dsem.h: e.dma_start(d, s).then_inc(sh, 16))
    dsem.inc(16)
    phw.sp(lambda e, s=gfsrc, d=gfdst, sh=dsem.h: e.dma_start(d, s).then_inc(sh, 16))
    phw.wait("dve", dsem, dsem.v)
    phw.wait("act", dsem, dsem.v)
    gsm = pr.sem("l1w")
    gwa = pr.sem("l1wx")
    gwa.inc(1)
    phw.act(lambda e, sh=gwa.h: e.activation(B3[0:8, :], B2[0:8, :], AF.Abs).then_inc(sh, 1))
    gwa.inc(1)
    phw.act(lambda e, sh=gwa.h: e.activation(B3[0:8, :], B3[0:8, :], AF.Exp,
                                             scale=-1.0).then_inc(sh, 1))
    gwa.inc(1)
    phw.act(lambda e, sh=gwa.h: e.activation(
        B3[0:8, :], B3[0:8, :], AF.Ln,
        bias=cons["ones_col_f32"][0:8, 0:1]).then_inc(sh, 1))
    phw.wait("dve", gwa, gwa.v)
    gsm.inc(1)
    phw.dve(lambda e, sh=gsm.h: e.tensor_scalar(B2[0:8, :], B2[0:8, :], 0.0, None,
                                                ALU.min).then_inc(sh, 1))
    gsm.inc(1)
    phw.dve(lambda e, sh=gsm.h: e.tensor_tensor(B2[0:8, :], B2[0:8, :], B3[0:8, :],
                                                ALU.subtract).then_inc(sh, 1))
    bufs = [B2, B3]
    w = 1
    pi = 0
    while w < 4096:
        a_, b_ = bufs[pi], bufs[1 - pi]
        gsm.inc(1)
        phw.dve(lambda e, a=a_, b=b_, w=w, sh=gsm.h: e.tensor_copy(
            b[0:8, 0:w], a[0:8, 0:w]).then_inc(sh, 1))
        gsm.inc(1)
        phw.dve(lambda e, a=a_, b=b_, w=w, sh=gsm.h: e.tensor_tensor(
            b[0:8, w:4096], a[0:8, w:4096], a[0:8, 0:4096 - w], ALU.add).then_inc(sh, 1))
        pi = 1 - pi
        w *= 2
    fcum = bufs[pi]
    gsm.inc(1)
    phw.dve(lambda e, f=fcum, sh=gsm.h: e.tensor_tensor(
        gI[0:8, :], gI[0:8, :], f[0:8, :], ALU.subtract).then_inc(sh, 1))
    other = bufs[1 - pi]
    gsm.inc(1)
    phw.dve(lambda e, o=other, sh=gsm.h: e.tensor_copy(o[0:8, :], gI[0:8, :]).then_inc(sh, 1))
    bufs2 = [other, B1]
    w = 1
    pi2 = 0
    while w < 4096:
        a_, b_ = bufs2[pi2], bufs2[1 - pi2]
        gsm.inc(1)
        phw.dve(lambda e, a=a_, b=b_, w=w, sh=gsm.h: e.tensor_copy(
            b[0:8, 0:w], a[0:8, 0:w]).then_inc(sh, 1))
        gsm.inc(1)
        phw.dve(lambda e, a=a_, b=b_, w=w, sh=gsm.h: e.tensor_tensor(
            b[0:8, w:4096], a[0:8, w:4096], a[0:8, 0:4096 - w], ALU.max).then_inc(sh, 1))
        pi2 = 1 - pi2
        w *= 2
    cmbuf = bufs2[pi2]
    gsm.inc(1)
    phw.dve(lambda e, c=cmbuf, f=fcum, sh=gsm.h: e.tensor_tensor(
        B2[0:8, :], f[0:8, :], c[0:8, :], ALU.add).then_inc(sh, 1))
    gsm.inc(1)
    phw.dve(lambda e, c=cmbuf, sh=gsm.h: e.tensor_scalar(
        B1[0:8, :], c[0:8, :], -1.0, None, ALU.mult).then_inc(sh, 1))
    phw.wait("pe", gsm, gsm.v)
    tsp = pr.sem("l1tp")
    tse = pr.sem("l1te")
    for blk in range(32):
        par = blk % 2
        if blk >= 2:
            phw.wait("pe", tse, tse.v - 1)
        tsp.inc(1)
        phw.pe(lambda e, b=blk, p=par, sh=tsp.h: e.matmul(
            pr.ps[2 + p][0:128, 0:2], gI[0:8, b * 128:(b + 1) * 128],
            i8_s[0:8, 0:2], start=True, stop=True).then_inc(sh, 1))
        phw.wait("dve", tsp, tsp.v)
        tse.inc(1)
        phw.dve(lambda e, b=blk, p=par, sh=tse.h: e.tensor_copy(
            acol[:, b * 2:(b + 1) * 2], pr.ps[2 + p][0:128, 0:2]).then_inc(sh, 1))

    pha = pr.phase("attn")
    pha.uses_reg = True
    sSC = pr.sem("asc")
    sE = pr.sem("aE")
    sP = pr.sem("aP")
    sM = pr.sem("aM")
    sAV = pr.sem("aAV")
    sVT = pr.sem("aVT")
    sVE = pr.sem("aVE")
    sEP = pr.sem("aEP")
    sBC = pr.sem("aBC")
    sLD = pr.sem("aload")
    aq = [pr.sem("aq0"), pr.sem("aq1")]
    rowb = pr.rowb

    qld_free_at = {}
    for hl in range(2):
        def unit_loads(e, hl=hl, w=None if hl == 0 else sAV.v):
            r0 = e._shared_reg
            if w is not None:
                e.wait_ge(sAV.h, w)
            e.reg_load(r0, meta_s[0:1, hl:hl + 1])
            e.dma_start(
                bass.AP(kTA, 0, [[4096, 128], [1024, 4], [1, 1024]]),
                bass.AP(pay_o, r0, [[1024, 128], [PAYL1 * 1024, 4], [1, 1024]])
            ).then_inc(sLD.h, 16)
            for dv in range(2):
                e.reg_load(r0, meta_s[0:1, 16 + 2 * hl + dv: 17 + 2 * hl + dv])
                e.dma_start(
                    bass.AP(vstage, dv * 4096, [[8192, 128], [1024, 4], [1, 1024]]),
                    bass.AP(pay_o, r0, [[1024, 128], [PAYL1 * 1024, 4], [1, 1024]])
                ).then_inc(sLD.h, 16)
        pha.pool(unit_loads)
        sLD.inc(48)
        unit_ld = sLD.v

        pha.wait("pe", sLD, unit_ld)
        for ti in range(64):
            tt, dv = ti // 2, ti % 2
            par = ti % 2
            if ti >= 2:
                pha.wait("pe", sVE, sVE.v - 1)
            sVT.inc(1)
            pha.pe(lambda e, t=tt, d=dv, p=par, sh=sVT.h: e.matmul(
                pr.ps[2 + p][:, 0:128],
                vstage[:, d * 4096 + t * 128: d * 4096 + (t + 1) * 128],
                cons["ident_bf"][:, :], start=True, stop=True).then_inc(sh, 1))
            pha.wait("dve", sVT, sVT.v)
            sVE.inc(1)
            pha.dve(lambda e, t=tt, d=dv, p=par, sh=sVE.h: e.tensor_copy(
                vtok[:, t * 256 + d * 128: t * 256 + (d + 1) * 128],
                pr.ps[2 + p][:, 0:128]).then_inc(sh, 1))

        for st in range(8):
            it = hl * 8 + st
            if st == 0:
                aq[0].inc(16)

                def load_q(e, hl=hl, w=qld_free_at.get(0)):
                    r2 = e._shared_reg
                    if w is not None:
                        e.wait_ge(sAV.h, w)
                    e.reg_load(r2, meta_s[0:1, 4 + hl * 4: 5 + hl * 4])
                    e.dma_start(
                        bass.AP(qslot, 0, [[4096, 128], [1024, 4], [1, 1024]]),
                        bass.AP(pay_o, r2, [[1024, 128], [PAYL1 * 1024, 4], [1, 1024]])
                    ).then_inc(aq[0].h, 16)
                pha.pool(load_q)
            q_tgt = aq[0].v
            qcols = qslot[:, st * 512: st * 512 + 512]

            pha.wait("pe", sEP, sEP.v)
            pha.wait("pe", sP, sP.v)
            sBC.inc(1)
            pha.pe(lambda e, hl=hl, st=st, sh=sBC.h: e.matmul(
                pr.ps[6][:, :], sel_s[0:8, hl * 128:(hl + 1) * 128],
                B1[0:8, st * 512:(st + 1) * 512], start=True, stop=True).then_inc(sh, 1))
            sBC.inc(1)
            pha.pe(lambda e, hl=hl, st=st, sh=sBC.h: e.matmul(
                pr.psm[32:33, 0:512], i8_s[0:8, hl:hl + 1],
                B2[0:8, st * 512:(st + 1) * 512], start=True, stop=True).then_inc(sh, 1))
            pha.wait("dve", sBC, sBC.v - 1)
            sEP.inc(1)
            pha.dve(lambda e, sh=sEP.h: e.tensor_copy(
                pr.bcB[:, 0:512], pr.ps[6][:, :]).then_inc(sh, 1))
            pha.wait("act", sBC, sBC.v)
            sP.inc(1)
            pha.act(lambda e, sh=sP.h: e.activation(
                rowb[0:1, 1024:1536], pr.psm[32:33, 0:512], AF.Exp,
                scale=-1.0).then_inc(sh, 1))
            bc_ready = sEP.v
            nf_ready = sP.v

            n_tt = 4 * (st + 1)
            sAV0 = sAV.v
            sM0 = sM.v
            sP0 = sP.v
            for tt in range(n_tt):
                par = tt % 2
                pha.wait("pe", aq[0], q_tgt)
                if tt >= 2:
                    pha.wait("pe", sM, sM0 + tt - 1)
                sSC.inc(1)
                pha.pe(lambda e, t=tt, p=par, qc=qcols, sh=sSC.h: e.matmul(
                    pr.ps[p][:, :], kTA[:, t * 128:(t + 1) * 128], qc,
                    start=True, stop=True).then_inc(sh, 1))
                pha.wait("dve", sEP, bc_ready)
                if tt >= 1:
                    pha.wait("dve", sP, sP0 + tt)
                sE.inc(1)
                pha.dve(lambda e, t=tt, hl=hl, sh=sE.h: e.tensor_scalar(
                    EF[:, :], pr.bcB[:, 0:512],
                    acol[:, t * 2 + hl: t * 2 + hl + 1], None,
                    ALU.add).then_inc(sh, 1))
                diag = (tt >= 4 * st)
                if diag:
                    kk = tt - 4 * st
                    sE.inc(1)
                    pha.dve(lambda e, k=kk, sh=sE.h: e.tensor_tensor(
                        EF[:, :], EF[:, :],
                        cmask_s[:, k * 512:(k + 1) * 512], ALU.add).then_inc(sh, 1))
                pha.wait("act", sE, sE.v)
                if tt >= 2:
                    pha.wait("act", sM, sM0 + tt - 1)
                sP.inc(1)
                pha.act(lambda e, p=par, sh=sP.h: e.activation(
                    PT2[:, p * 512:(p + 1) * 512], EF[:, :], AF.Exp).then_inc(sh, 1))
                pha.wait("dve", sP, sP.v)
                pha.wait("dve", sSC, sSC.v)
                sM.inc(1)
                pha.dve(lambda e, p=par, sh=sM.h: e.tensor_tensor(
                    PT2[:, p * 512:(p + 1) * 512], pr.ps[p][:, :],
                    PT2[:, p * 512:(p + 1) * 512], ALU.mult).then_inc(sh, 1))
                pha.wait("pe", sM, sM.v)
                pha.pe(lambda e, t=tt, p=par, st_=(tt == 0), sp_=(tt == n_tt - 1):
                       e.matmul(pr.ps[4][:, :],
                                vtok[:, t * 256: t * 256 + 128],
                                PT2[:, p * 512:(p + 1) * 512], start=st_, stop=sp_))
                pha.pe(lambda e, t=tt, p=par, st_=(tt == 0), sp_=(tt == n_tt - 1):
                       e.matmul(pr.ps[5][:, :],
                                vtok[:, t * 256 + 128: t * 256 + 256],
                                PT2[:, p * 512:(p + 1) * 512], start=st_, stop=sp_))
                sAV.inc(1)
                pha.pe(lambda e, p=par, st_=(tt == 0), sp_=(tt == n_tt - 1), sh=sAV.h:
                       e.matmul(pr.psm[0:1, 0:512], cons["ones_col_bf"][:, :],
                                PT2[:, p * 512:(p + 1) * 512], start=st_, stop=sp_)
                       .then_inc(sh, 1))
            if st == 7:
                qld_free_at[0] = sAV.v

            pha.wait("act", sAV, sAV.v)
            sP.inc(1)
            pha.act(lambda e, sh=sP.h: e.activation(
                rowb[0:1, 0:512], pr.psm[0:1, 0:512], AF.Abs).then_inc(sh, 1))
            pha.wait("dve", sP, sP.v)
            pha.wait("dve", sP, nf_ready)

            def epi_rows(e, sh=sEP.h):
                e.tensor_tensor(rowb[0:1, 0:512], rowb[0:1, 0:512],
                                rowb[0:1, 1024:1536], ALU.max)
                e.reciprocal(rowb[0:1, 512:1024], rowb[0:1, 0:512]).then_inc(sh, 1)
            sEP.inc(1)
            pha.dve(epi_rows)
            pha.wait("act", sEP, sEP.v)
            for dv in range(2):
                sP.inc(1)
                pha.act(lambda e, d=dv, sh=sP.h: e.activation(
                    sqsc[:, 0:512], pr.ps[4 + d][:, :], AF.Square).then_inc(sh, 1))
                pha.wait("pe", sP, sP.v)
                sBC.inc(1)
                pha.pe(lambda e, d=dv, sh=sBC.h: e.matmul(
                    pr.psm[64:65, 0:512], cons["ones_col_bf"][:, :], sqsc[:, 0:512],
                    start=d == 0, stop=d == 1).then_inc(sh, 1))
                if dv == 0:
                    pha.wait("act", sBC, sBC.v)
            pha.wait("dve", sBC, sBC.v)

            def epi_rows2(e, sh=sEP.h):
                e.tensor_tensor(rowb[0:1, 1536:2048], pr.psm[64:65, 0:512],
                                rowb[0:1, 512:1024], ALU.mult)
                e.tensor_tensor(rowb[0:1, 1536:2048], rowb[0:1, 1536:2048],
                                rowb[0:1, 512:1024], ALU.mult)
                e.tensor_scalar(rowb[0:1, 1536:2048], rowb[0:1, 1536:2048],
                                1.0 / 256.0, 1e-6, ALU.mult, ALU.add).then_inc(sh, 1)
            sEP.inc(1)
            pha.dve(epi_rows2)
            pha.wait("act", sEP, sEP.v)
            sP.inc(1)
            pha.act(lambda e, sh=sP.h: e.activation(
                rowb[0:1, 1536:2048], rowb[0:1, 1536:2048], AF.Sqrt).then_inc(sh, 1))
            pha.wait("dve", sP, sP.v)

            def epi_rows3(e, sh=sEP.h):
                e.reciprocal(rowb[0:1, 1536:2048], rowb[0:1, 1536:2048])
                e.tensor_tensor(rowb[0:1, 1536:2048], rowb[0:1, 1536:2048],
                                rowb[0:1, 512:1024], ALU.mult).then_inc(sh, 1)
            sEP.inc(1)
            pha.dve(epi_rows3)
            pha.wait("pe", sEP, sEP.v)
            sBC.inc(1)
            pha.pe(lambda e, sh=sBC.h: e.matmul(
                pr.ps[6][:, :], cons["ones_row_f32"][:, :], rowb[0:1, 1536:2048],
                start=True, stop=True).then_inc(sh, 1))
            pha.wait("dve", sBC, sBC.v)
            sEP.inc(1)
            pha.dve(lambda e, sh=sEP.h: e.tensor_copy(
                pr.bcA[:, 0:512], pr.ps[6][:, :]).then_inc(sh, 1))
            for dv in range(2):
                pha.wait("dve", sEP, sEP.v)
                pha.wait("dve", dsem, dsem.v)
                sEP.inc(1)
                pha.dve(lambda e, d=dv, hl=hl, sh=sEP.h: e.scalar_tensor_tensor(
                    outst[:, :], pr.ps[4 + d][:, :],
                    mh_s[:, 2 * hl + d: 2 * hl + d + 1],
                    pr.bcA[:, 0:512], ALU.mult, ALU.mult).then_inc(sh, 1))
                pha.wait("sp", sEP, sEP.v)
                dsem.inc(16)
                pha.sp(lambda e, d=dv, hl=hl, st=st, sh=dsem.h: e.dma_start(
                    bass.AP(opay_i, (hl * 256 + d * 128) * S + st * 512,
                            [[S, 128], [1, 512]]),
                    outst[:, :]).then_inc(sh, 16))
            pha.wait("pe", sEP, sEP.v)

    pha.wait("sp", dsem, dsem.v)

    phg2 = pr.phase("gather_out")
    ccs.inc(1)
    phg2.pool(lambda e, sh=ccs.h: e.collective_compute(
        "AllGather", ALU.bypass, replica_groups=[[0, 1, 2, 3], [4, 5, 6, 7]],
        ins=[opay_i.ap()], outs=[opay_o.ap()]).then_inc(sh, 1))
    phg2.wait("pool", ccs, ccs.v)

    phr = pr.phase("read_out")
    phr.uses_reg = True

    def read_own(e):
        r3 = e._shared_reg
        e.reg_load(r3, meta_s[0:1, 60:61])
        e.dma_start(bass.AP(hn, 0, [[NCH * 1024, 128], [1024, 16], [1, 1024]]),
                    bass.AP(opay_o, r3, [[S, 128], [128 * S, 16], [1, 1024]])).then_inc(sLD.h, 16)
    phr.pool(read_own)
    sLD.inc(16)
    load_stream(pr, phr, x_in, x_s, dsem)
    phr.wait("pool", sLD, sLD.v)
    phr.wait("sp", dsem, dsem.v)

    sCA = pr.sem("l1sg")

    def og_consumer(ph_, fg, tt, bank, wf, wf_tgt, pf):
        ph_.wait("act", wf, wf_tgt)
        ph_.wait("act", pf, pf.v)
        sCA.inc(1)
        ph_.act(lambda e, b=bank, sh=sCA.h: e.activation(
            pr.scbf[2][:, 0:512], b[:, :], AF.Sigmoid).then_inc(sh, 1))
        ph_.wait("dve", sCA, sCA.v)
        oc = fg * 1024 + tt * 512
        pf.inc(1)
        ph_.dve(lambda e, oc=oc, sh=pf.h: e.tensor_tensor(
            hn[:, oc:oc + 512], hn[:, oc:oc + 512], pr.scbf[2][:, 0:512],
            ALU.mult).then_inc(sh, 1))

    emit_proj(pr, pr.phase("proj_og"), wog, D, h_in, og_consumer, packed=True)

    def wout_consumer(ph_, fg, tt, bank, wf, wf_tgt, pf):
        ph_.wait("dve", wf, wf_tgt)
        oc = fg * 1024 + tt * 512
        pf.inc(1)

        def f(e, oc=oc, b=bank, sh=pf.h):
            e.tensor_tensor(g_in[:, oc:oc + 512], g_in[:, oc:oc + 512], b[:, :],
                            ALU.add)
            e.tensor_tensor(x_s[:, oc:oc + 512], x_s[:, oc:oc + 512],
                            g_in[:, oc:oc + 512], ALU.add).then_inc(sh, 1)
        ph_.dve(f)

    emit_proj(pr, pr.phase("proj_wout"), wout, D, hn, wout_consumer, packed=True)

    emit_ln(pr, g_in, h_in, cons, "l1r2", rms_w=n2_s, eps=1e-6, src_bf=True)

    phm = pr.phase("mlp")
    hb_free_at = [0]
    sMS = pr.sem("l1ms")

    def msilu_consumer(hb_guard):
        def consumer(ph_, fg, tt, bank, wf, wf_tgt, pf):
            ph_.wait("act", wf, wf_tgt)
            if hb_guard[0]:
                ph_.wait("act", wf, hb_guard[0])
                hb_guard[0] = 0
            sMS.inc(1)
            ph_.act(lambda e, fg=fg, tt=tt, b=bank, sh=sMS.h: e.activation(
                hb[:, fg * 1024 + tt * 512: fg * 1024 + tt * 512 + 512],
                b[:, :], AF.Silu).then_inc(sh, 1))
            ph_.wait("dve", sMS, sMS.v)
            pf.inc(1)
            ph_.dve(lambda e, sh=pf.h: e.nop().then_inc(sh, 1))
        return consumer

    def mh_consumer(ph_, fg, tt, bank, wf, wf_tgt, pf):
        ph_.wait("dve", wf, wf_tgt)
        pf.inc(1)
        ph_.dve(lambda e, fg=fg, tt=tt, b=bank, sh=pf.h: e.tensor_tensor(
            hb[:, fg * 1024 + tt * 512: fg * 1024 + tt * 512 + 512], b[:, :],
            hb[:, fg * 1024 + tt * 512: fg * 1024 + tt * 512 + 512],
            ALU.mult).then_inc(sh, 1))

    def mdown_consumer(ph_, dt, ti, bank, wf, wf_tgt, pf):
        ph_.wait("dve", wf, wf_tgt)
        pf.inc(1)
        ph_.dve(lambda e, dt=dt, ti=ti, b=bank, sh=pf.h: e.tensor_tensor(
            x_s[:, dt * 1024 + ti * 512: dt * 1024 + ti * 512 + 512],
            x_s[:, dt * 1024 + ti * 512: dt * 1024 + ti * 512 + 512],
            b[:, :], ALU.add).then_inc(sh, 1))

    for q in range(4):
        hb_guard = [hb_free_at[0]]
        emit_proj(pr, phm, wgate, 2048, h_in, msilu_consumer(hb_guard),
                  row_len=8192, w_off=q * 2048, packed=True)
        emit_proj(pr, phm, wup, 2048, h_in, mh_consumer,
                  row_len=8192, w_off=q * 2048, packed=True)
        emit_down(pr, phm, wdown, 16,
                  lambda hc, ti: hb[:, hc * 1024 + ti * 512: hc * 1024 + ti * 512 + 512],
                  mdown_consumer, w_off=q * 2048 * D, packed=True, two_tt=True)
        hb_free_at[0] = pr.sem("wf").v

    if store:
        phs = pr.phase("store")
        store_stream(pr, phs, x_s, y_out, dsem)
        phs.wait("sp", dsem, dsem.v)
    pr.arena_reset()


def build_l1a(upto=None):
    pr = Prog("l1a")
    nc = pr.nc
    x_in = nc.dram_tensor("x", [D, TPC], F32, kind="ExternalInput")
    y_out = nc.dram_tensor("y", [D, TPC], F32, kind="ExternalOutput")
    cons = alloc_consts(pr)
    pr.init_arena()
    emit_l1a(pr, cons, x_in, y_out)
    pr.emit()
    return pr.nc


def l1a_meta(core):
    r = core % 4
    m = np.zeros(128, np.int32)
    for hl in range(2):
        hg = 2 * r + hl
        m[hl] = (PAY1_K + hg * 128) * TPC
        for dv in range(2):
            m[16 + 2 * hl + dv] = (PAY1_V + hg * 256 + dv * 128) * TPC
        for rb in range(4):
            m[4 + hl * 4 + rb] = (rb * PAYL1 + PAY1_Q + hg * 128) * TPC
    m[60] = r * TPC
    return m.reshape(1, 128)


def colw(v):
    return np.ascontiguousarray(np.asarray(v, np.float32).reshape(16, 128).T)


def maps_l1a(inputs, c, P=""):
    cm = causal_masks()
    bigc = np.stack([np.asarray(inputs["L1_attn_b_ig"], np.float32),
                     np.asarray(inputs["L1_attn_b_fg"], np.float32)], axis=1)
    bigc = np.ascontiguousarray(bigc)
    r = c % 4
    selmy = np.zeros((8, 256), np.float32)
    i8my = np.zeros((8, 2), np.float32)
    for hl in range(2):
        hg = 2 * r + hl
        selmy[hg, hl * 128:(hl + 1) * 128] = 1.0
        i8my[hg, hl] = 1.0
    mhfull = colw(inputs["L1_attn_mh_w"])
    mhmy = np.zeros((128, 4), np.float32)
    for hl in range(2):
        hg = 2 * r + hl
        for dv in range(2):
            mhmy[:, 2 * hl + dv] = mhfull[:, 2 * hg + dv]
    wgp = np.concatenate(
        [pack_proj(np.asarray(inputs["L1_attn_w_gate"], np.float32), NCH, 2048,
                   col_off=q * 2048) for q in range(4)]).reshape(D, 8192)
    wup_ = np.concatenate(
        [pack_proj(np.asarray(inputs["L1_attn_w_up"], np.float32), NCH, 2048,
                   col_off=q * 2048) for q in range(4)]).reshape(D, 8192)
    wdp = np.concatenate(
        [pack_down(np.asarray(inputs["L1_attn_w_down"], np.float32), 16,
                   row_off=q * 2048) for q in range(4)]).reshape(8192, D)
    return {
        P + "wq": pack_proj(np.asarray(inputs["L1_attn_wq"], np.float32),
                            NCH, 1024).reshape(D, 1024),
        P + "wk": pack_proj(np.asarray(inputs["L1_attn_wk"], np.float32),
                            NCH, 1024).reshape(D, 1024),
        P + "wv": pack_proj(np.asarray(inputs["L1_attn_wv"], np.float32),
                            NCH, D).reshape(D, D),
        P + "wig": inputs["L1_attn_w_ig"],
        P + "wfg": inputs["L1_attn_w_fg"], P + "bigc": bigc,
        P + "wog": pack_proj(np.asarray(inputs["L1_attn_w_og"], np.float32),
                             NCH, D).reshape(D, D),
        P + "wout": pack_proj(np.asarray(inputs["L1_attn_w_out"], np.float32),
                              NCH, D).reshape(D, D),
        P + "wgate": wgp, P + "wup": wup_,
        P + "wdown": wdp,
        P + "n1c": colw(inputs["L1_attn_norm1_w"]),
        P + "n2c": colw(inputs["L1_attn_norm2_w"]),
        P + "selmy": np.ascontiguousarray(selmy),
        P + "i8my": np.ascontiguousarray(i8my),
        P + "mhmy": np.ascontiguousarray(mhmy),
        P + "cmask": cm, P + "meta": l1a_meta(c),
    }


def run_l1a(x_shards, inputs, upto=None):
    nc = get_nc(f"l1a{upto}", lambda: build_l1a(upto))
    in_maps = [dict(x=x_shards[c], **maps_l1a(inputs, c)) for c in range(NCORE)]
    res = run_bass_kernel_spmd(nc, in_maps, core_ids=list(range(NCORE)))
    return [res.results[c]["y"] for c in range(NCORE)]


def maps_l1f(inputs, c, P=""):
    lut = np.ascontiguousarray(np.asarray(inputs["L1_ffn_lut"], np.float32).reshape(1, 16))
    gi = np.asarray(inputs["L1_ffn_gate_idx"], np.int32)
    ui = np.asarray(inputs["L1_ffn_up_idx"], np.int32)
    di = np.asarray(inputs["L1_ffn_down_idx"], np.int32)
    return {
        P + "lut": lut,
        P + "gsl": np.ascontiguousarray(gi[:, 256 * c:256 * (c + 1)]),
        P + "usl": np.ascontiguousarray(ui[:, 256 * c:256 * (c + 1)]),
        P + "dsl": np.ascontiguousarray(di[256 * c:256 * (c + 1), :]),
    }


def run_l1f(x_shards, inputs, upto=None):
    nc = get_nc(f"l1f{upto}", lambda: build_l1f(upto))
    in_maps = [dict(x=x_shards[c], **maps_l1f(inputs, c)) for c in range(NCORE)]
    res = run_bass_kernel_spmd(nc, in_maps, core_ids=list(range(NCORE)))
    return [res.results[c]["y"] for c in range(NCORE)]


def build_fused():
    pr = Prog("fused")
    nc = pr.nc
    x0 = nc.dram_tensor("x", [D, TPC], F32, kind="ExternalInput")
    y = nc.dram_tensor("y", [D, TPC], F32, kind="ExternalOutput")
    s1 = nc.dram_tensor("s1", [D, TPC], F32)
    s2 = nc.dram_tensor("s2", [D, TPC], F32)
    s3 = nc.dram_tensor("s3", [D, TPC], F32)
    cons = alloc_consts(pr)
    pr.init_arena()
    emit_l0a(pr, cons, x0, s1, P="a_", store=False)
    emit_l0f(pr, cons, s1, s2, P="f_", load=False)
    emit_l1a(pr, cons, s2, s3, P="b_", load=False, store=False)
    emit_l1f(pr, cons, s3, y, P="g_", load=False)
    pr.emit()
    return pr.nc


def kernel_fused(inputs):
    nc = get_nc("fused", build_fused)
    x_shards = shard_stream(np.asarray(inputs["L0_attn_x"], np.float32))
    in_maps = []
    for c in range(NCORE):
        m = {"x": x_shards[c]}
        m.update(maps_l0a(inputs, c, "a_"))
        m.update(maps_l0f(inputs, c, "f_"))
        m.update(maps_l1a(inputs, c, "b_"))
        m.update(maps_l1f(inputs, c, "g_"))
        in_maps.append(m)
    res = run_bass_kernel_spmd(nc, in_maps, core_ids=list(range(NCORE)))
    return unshard_stream([res.results[c]["y"] for c in range(NCORE)])


PAY1_Q, PAY1_K, PAY1_V = 0, 1024, 2048
PAYL1 = 4096
RSQ128 = float(1.0 / np.sqrt(128.0))


def emit_l1f(pr, cons, x_in, y_out, P="", load=True):
    nc = pr.nc
    lut = nc.dram_tensor(P + "lut", [1, 16], F32, kind="ExternalInput")
    gsl = nc.dram_tensor(P + "gsl", [5504, 256], I32, kind="ExternalInput")
    usl = nc.dram_tensor(P + "usl", [5504, 256], I32, kind="ExternalInput")
    dsl = nc.dram_tensor(P + "dsl", [256, 5504], I32, kind="ExternalInput")
    wg_i = nc.dram_tensor(P + "wg_i", [256, 5504], BF16)
    wg_o = nc.dram_tensor(P + "wg_o", [2048, 5504], BF16)
    wu_i = nc.dram_tensor(P + "wu_i", [256, 5504], BF16)
    wu_o = nc.dram_tensor(P + "wu_o", [2048, 5504], BF16)
    wd_i = nc.dram_tensor(P + "wd_i", [5504, 256], BF16)
    wd_o = nc.dram_tensor(P + "wd_o", [8 * 5504, 256], BF16)

    dsem = pr.sem("dma")
    dmap = pr.sem("dmap")
    ccs = pr.sem("cc")

    x_s = pr.ablock(P + "x_s", [128, NCH * 1024], F32)
    hn = pr.ablock(P + "hn", [128, NCH * 1024], BF16)
    hb = pr.ablock(P + "hb", [128, 22 * 1024], BF16)
    lutbc = pr.ablock(P + "lutbc", [128, 16], F32)
    lutrow = pr.ablock(P + "lutrow", [1, 16], F32)
    idxs = [pr.ablock(P + f"idxs{i}", [128, 128], I32) for i in range(2)]
    idxb = pr.ablock(P + "idxb", [128, 128], BF16)
    macc = pr.ablock(P + "macc", [128, 128], BF16)
    mtmp = pr.ablock(P + "mtmp", [128, 128], BF16)
    wrow = pr.ablock(P + "wrow", [128, 5504], BF16)

    ph = pr.phase("load")
    if load:
        load_stream(pr, ph, x_in, x_s, dsem)
    dsem.inc(16)
    ph.sp(lambda e, sh=dsem.h: e.dma_start(lutrow[:, :], lut[:, :]).then_inc(sh, 16))
    ph.wait("sp", dsem, dsem.v)
    lsm = pr.sem("lf")
    ph.wait("pe", dsem, dsem.v)
    lsm.inc(1)
    ph.pe(lambda e, sh=lsm.h: e.matmul(pr.ps[0][:, 0:16], cons["ones_row_f32"][:, :],
                                       lutrow[0:1, :], start=True, stop=True).then_inc(sh, 1))
    ph.wait("dve", lsm, lsm.v)
    lsm.inc(1)
    ph.dve(lambda e, sh=lsm.h: e.tensor_copy(lutbc[:, :], pr.ps[0][:, 0:16]).then_inc(sh, 1))

    emit_ln(pr, x_s, hn, cons, "l1f")

    phm = pr.phase("mat")
    msd = [pr.sem("mI0"), pr.sem("mI1")]
    msv = pr.sem("mV")
    msp = pr.sem("mP")
    mse = pr.sem("mE")
    msw = pr.sem("mW")
    phm.wait("dve", lsm, lsm.v)

    mt_n = [0]

    def mat_tile(idx_dram, row_len, r0, c0, par, wcol):
        if mt_n[0] >= 2:
            phm.wait("sp", msv, mt_n[0] - 1)
        mt_n[0] += 1
        msd[par].inc(16)
        phm.sp(lambda e, s=bass.AP(idx_dram, r0 * row_len + c0,
                                   [[row_len, 128], [1, 128]]),
               d=idxs[par], sh=msd[par].h: e.dma_start(d[:, :], s).then_inc(sh, 16))
        tgt = msd[par].v
        phm.wait("dve", msd[par], tgt)
        msv.inc(1)

        def mat(e, p=par, sh=msv.h):
            e.tensor_copy(idxb[:, :], idxs[p][:, :])
            e.tensor_scalar(macc[:, :], idxb[:, :], 0.0, lutbc[:, 0:1],
                            ALU.is_equal, ALU.mult)
            for j in range(1, 16):
                e.tensor_scalar(mtmp[:, :], idxb[:, :], float(j), lutbc[:, j:j + 1],
                                ALU.is_equal, ALU.mult)
                e.tensor_tensor(macc[:, :], macc[:, :], mtmp[:, :], ALU.add)
            e.nop().then_inc(sh, 1)
        phm.dve(mat)
        phm.wait("pe", msv, msv.v)
        phm.wait("pe", mse, mse.v)
        msp.inc(1)
        phm.pe(lambda e, sh=msp.h: e.matmul(pr.ps[1][:, 0:128], macc[:, :],
                                            cons["ident_bf"][:, :],
                                            start=True, stop=True).then_inc(sh, 1))
        phm.wait("dve", msp, msp.v)
        mse.inc(1)
        phm.dve(lambda e, wc=wcol, sh=mse.h: e.tensor_copy(
            wrow[:, wc:wc + 128], pr.ps[1][:, 0:128]).then_inc(sh, 1))

    for mi, (idx_dram, ag_in) in enumerate(((gsl, wg_i), (usl, wu_i))):
        for hc2 in range(2):
            for ot in range(43):
                mat_tile(idx_dram, 256, ot * 128, hc2 * 128, ot % 2, ot * 128)
            phm.wait("sp", mse, mse.v)
            msw.inc(16)
            phm.sp(lambda e, ag=ag_in, hc2=hc2, sh=msw.h: e.dma_start(
                bass.AP(ag, hc2 * 128 * 5504, [[5504, 128], [1, 5504]]),
                wrow[:, :]).then_inc(sh, 16))
            phm.wait("dve", msw, msw.v)
    for it in range(43):
        for dt2 in range(2):
            mat_tile(dsl, 5504, dt2 * 128, it * 128, dt2 % 2, dt2 * 128)
        phm.wait("sp", mse, mse.v)
        msw.inc(16)
        phm.sp(lambda e, it=it, sh=msw.h: e.dma_start(
            bass.AP(wd_i, it * 128 * 256, [[256, 128], [1, 256]]),
            wrow[:, 0:256]).then_inc(sh, 16))
        phm.wait("dve", msw, msw.v)

    phg = pr.phase("gather_w")
    for ag_i, ag_o in ((wg_i, wg_o), (wu_i, wu_o), (wd_i, wd_o)):
        ccs.inc(1)
        phg.pool(lambda e, i_=ag_i, o_=ag_o, sh=ccs.h: e.collective_compute(
            "AllGather", ALU.bypass, replica_groups=[[0, 1, 2, 3, 4, 5, 6, 7]],
            ins=[i_.ap()], outs=[o_.ap()]).then_inc(sh, 1))
    phg.wait("pool", ccs, ccs.v)

    phf = pr.phase("ffn")
    hb_free_at = [0]
    sMS = pr.sem("lfms")

    def fsilu_consumer(hb_guard):
        def consumer(ph_, fg, tt, bank, wf, wf_tgt, pf):
            ph_.wait("act", wf, wf_tgt)
            if hb_guard[0]:
                ph_.wait("act", wf, hb_guard[0])
                hb_guard[0] = 0
            sMS.inc(1)
            ph_.act(lambda e, fg=fg, tt=tt, b=bank, sh=sMS.h: e.activation(
                hb[:, fg * 1024 + tt * 512: fg * 1024 + tt * 512 + 512],
                b[:, :], AF.Silu).then_inc(sh, 1))
            ph_.wait("dve", sMS, sMS.v)
            pf.inc(1)
            ph_.dve(lambda e, sh=pf.h: e.nop().then_inc(sh, 1))
        return consumer

    def fh_consumer(ph_, fg, tt, bank, wf, wf_tgt, pf):
        ph_.wait("dve", wf, wf_tgt)
        pf.inc(1)
        ph_.dve(lambda e, fg=fg, tt=tt, b=bank, sh=pf.h: e.tensor_tensor(
            hb[:, fg * 1024 + tt * 512: fg * 1024 + tt * 512 + 512], b[:, :],
            hb[:, fg * 1024 + tt * 512: fg * 1024 + tt * 512 + 512],
            ALU.mult).then_inc(sh, 1))

    def fdown_consumer(ph_, dt, ti, bank, wf, wf_tgt, pf):
        ph_.wait("dve", wf, wf_tgt)
        pf.inc(1)
        ph_.dve(lambda e, dt=dt, ti=ti, b=bank, sh=pf.h: e.tensor_tensor(
            x_s[:, dt * 1024 + ti * 512: dt * 1024 + ti * 512 + 512],
            x_s[:, dt * 1024 + ti * 512: dt * 1024 + ti * 512 + 512],
            b[:, :], ALU.add).then_inc(sh, 1))

    halves = [(0, 2816, 22), (2816, 2688, 21)]
    for (coff, fsz, nhc) in halves:
        hb_guard = [hb_free_at[0]]
        emit_proj(pr, phf, wg_o, fsz, hn, fsilu_consumer(hb_guard),
                  row_len=5504, w_off=coff)
        emit_proj(pr, phf, wu_o, fsz, hn, fh_consumer,
                  row_len=5504, w_off=coff)
        emit_down(pr, phf, wd_o, nhc,
                  lambda hc, ti, c0=coff: hb[:, (hc - c0 // 128) * 1024 + ti * 512:
                                             (hc - c0 // 128) * 1024 + ti * 512 + 512],
                  fdown_consumer, row_len=256, hc0=coff // 128,
                  dtg_stride=5504 * 256, two_tt=True)
        hb_free_at[0] = pr.sem("wf").v

    phs = pr.phase("store")
    store_stream(pr, phs, x_s, y_out, dsem)
    phs.wait("sp", dsem, dsem.v)
    pr.arena_reset()


def build_l1f(upto=None):
    pr = Prog("l1f")
    nc = pr.nc
    x_in = nc.dram_tensor("x", [D, TPC], F32, kind="ExternalInput")
    y_out = nc.dram_tensor("y", [D, TPC], F32, kind="ExternalOutput")
    cons = alloc_consts(pr)
    pr.init_arena()
    emit_l1f(pr, cons, x_in, y_out)
    pr.emit()
    return pr.nc


PAY1_Q, PAY1_K, PAY1_V = 0, 1024, 2048
PAYL1 = 4096
RSQ128 = float(1.0 / np.sqrt(128.0))


PAY1_Q, PAY1_K, PAY1_V = 0, 1024, 2048
PAYL1 = 4096
RSQ128 = float(1.0 / np.sqrt(128.0))

